# version 03
# baseline (speedup 1.0000x reference)
import numpy as np
from contextlib import ExitStack

import concourse.bass as bass
import concourse.mybir as mybir
from concourse.bass_utils import run_bass_kernel_spmd

F32 = mybir.dt.float32
BF16 = mybir.dt.bfloat16
I32 = mybir.dt.int32
U8 = mybir.dt.uint8
U32 = mybir.dt.uint32
AF = mybir.ActivationFunctionType
ALU = mybir.AluOpType
AXX = mybir.AxisListType.X

N_CORES = 8
B, TFULL, D = 4, 8192, 1024
T = 4096
NT = T // 128
KD = D // 128
H, LH, SH, DH = 16, 4, 12, 64
INNER = 4096
NST, DTR = 16, 64
NEXP, TOPK = 8, 2
CAP = int(1.2 * TOPK * TFULL // NEXP)
ESL = 2464
TRASH = NEXP * ESL
SROWS = TRASH + 128

C32 = 12582912.0
CDELTA = 6755399441055744.0

SK_SZ = 32 * 128 * 768
LK_SZ = T * 256

ENGS = ("pe", "dve", "act", "pool", "sp")
_ENG_ATTR = {"pe": "tensor", "dve": "vector", "act": "scalar", "pool": "gpsimd",
             "sp": "sync"}
DMA_QUEUES = ("qsp", "qact", "qpool")
_DMA_ISSUER = {"qsp": "sp", "qact": "act", "qpool": "pool"}


class Buf:
    __slots__ = ("name", "writer", "readers")

    def __init__(self, name):
        self.name = name
        self.writer = None
        self.readers = []


class Em:
    NSLOT = 12

    def __init__(self, nc):
        self.nc = nc
        self.steps = {e: [] for e in ENGS}
        self.cnt = {e: 0 for e in ENGS}
        self.qidx = {q: 0 for q in DMA_QUEUES}
        self._sems = {}
        self._waited = {e: {} for e in ENGS}
        self._fin = False

    def sem(self, key):
        if key not in self._sems:
            self._sems[key] = self.nc.alloc_semaphore(name=f"bk_{key}")
        return self._sems[key]

    def buf(self, name):
        return Buf(name)

    def _edge(self, eng, dep):
        if dep is None:
            return None
        sk, c = dep
        if self._waited[eng].get(sk, -1) >= c:
            return None
        return dep

    def _deps(self, eng, reads, writes):
        deps = []
        for b in reads:
            d = self._edge(eng, b.writer)
            if d:
                deps.append(d)
        for b in writes:
            d = self._edge(eng, b.writer)
            if d:
                deps.append(d)
            for r in b.readers:
                d = self._edge(eng, r)
                if d:
                    deps.append(d)
        dd = {}
        for sk, c in deps:
            dd[sk] = max(dd.get(sk, 0), c)
        return dd

    def _commit(self, eng, dd):
        waits = sorted(dd.items())
        for sk, c in waits:
            self._waited[eng][sk] = max(self._waited[eng].get(sk, -1), c)
        return waits

    def emit(self, eng, fn, reads=(), writes=()):
        assert not self._fin
        dd = self._deps(eng, reads, writes)
        waits = self._commit(eng, dd)
        self.cnt[eng] += 1
        self.steps[eng].append((waits, fn, eng, 1))
        me = (eng, self.cnt[eng])
        for b in writes:
            b.writer = me
            b.readers = []
        for b in reads:
            b.readers.append(me)
        return me

    def dma(self, q, fn, reads=(), writes=()):
        assert not self._fin
        eng = _DMA_ISSUER[q]
        dd = self._deps(eng, reads, writes)
        i = self.qidx[q]
        self.qidx[q] = i + 1
        slot = f"{q}{i % self.NSLOT}"
        prev = 16 * (i // self.NSLOT)
        if prev > 0 and self._waited[eng].get(slot, -1) < prev:
            dd[slot] = max(dd.get(slot, 0), prev)
        waits = self._commit(eng, dd)
        self.steps[eng].append((waits, fn, slot, 16))
        me = (slot, prev + 16)
        for b in writes:
            b.writer = me
            b.readers = []
        for b in reads:
            b.readers.append(me)
        return me

    def barrier_token(self):
        toks = [(e, self.cnt[e]) for e in ENGS if self.cnt[e] > 0]
        for q in DMA_QUEUES:
            n = self.qidx[q]
            for s in range(min(n, self.NSLOT)):
                i = (n - 1 - ((n - 1 - s) % self.NSLOT))
                toks.append((f"{q}{s}", 16 * (i // self.NSLOT + 1)))
        return toks

    def all_barrier(self):
        toks = self.barrier_token()
        for e in ENGS:
            self.wait_tokens(e, toks)

    def wait_tokens(self, eng, toks):
        dd = {}
        for sk, c in toks:
            if sk == eng:
                continue
            if self._waited[eng].get(sk, -1) >= c:
                continue
            dd[sk] = max(dd.get(sk, 0), c)
        if not dd:
            return
        waits = self._commit(eng, dd)
        self.steps[eng].append((waits, None, None, 0))

    def finalize(self):
        assert not self._fin
        self._fin = True
        with self.nc.Block() as blk:
            for eng in ENGS:
                steps = self.steps[eng]
                if not steps:
                    continue

                def mk(eng_steps):
                    def body(engobj):
                        for waits, fn, sem_key, inc in eng_steps:
                            for sk, c in waits:
                                engobj.wait_ge(self.sem(sk), c)
                            if fn is None:
                                continue
                            instr = fn(engobj)
                            if instr is not None and sem_key is not None:
                                instr.then_inc(self.sem(sem_key), inc)
                    return body
                getattr(blk, _ENG_ATTR[eng])(mk(steps))


class K:
    def __init__(self, nc):
        self.nc = nc
        self.em = Em(nc)
        self.scopes = [ExitStack()]
        self.bufs = {}

    def push(self):
        self.scopes.append(ExitStack())

    def pop(self, barrier=True):
        if barrier:
            self.em.all_barrier()
        self.scopes.pop().close()

    def B(self, name):
        if name not in self.bufs:
            self.bufs[name] = self.em.buf(name)
        return self.bufs[name]

    def sb(self, name, shape, dt=F32):
        return self.scopes[-1].enter_context(
            self.nc.sbuf_tensor(name, list(shape), dt))

    def ps(self, name, shape, dt=F32):
        return self.scopes[-1].enter_context(
            self.nc.psum_tensor(name, list(shape), dt))

    def dram(self, name, shape, dt=F32):
        return self.nc.dram_tensor(name, list(shape), dt)

    def v(self, fn, reads=(), writes=()):
        return self.em.emit("dve", fn, reads, writes)

    def a(self, fn, reads=(), writes=()):
        return self.em.emit("act", fn, reads, writes)

    def p(self, fn, reads=(), writes=()):
        return self.em.emit("pe", fn, reads, writes)

    def g(self, fn, reads=(), writes=()):
        return self.em.emit("pool", fn, reads, writes)


def rsqrt_rows(k, dst, src, scratch, bufs):
    sbuf, dbuf, cbuf = bufs
    k.a(lambda e: e.activation(out=scratch, in_=src, func=AF.Sqrt),
        reads=[sbuf], writes=[cbuf])
    k.v(lambda e: e.reciprocal(dst, scratch), reads=[cbuf], writes=[dbuf])
    for _ in range(2):
        k.v(lambda e: e.tensor_tensor(scratch, dst, dst, ALU.mult),
            reads=[dbuf], writes=[cbuf])
        k.v(lambda e: e.tensor_tensor(scratch, src, scratch, ALU.mult),
            reads=[sbuf, cbuf], writes=[cbuf])
        k.v(lambda e: e.tensor_scalar(scratch, scratch, 3.0, None, ALU.subtract),
            reads=[cbuf], writes=[cbuf])
        k.v(lambda e: e.tensor_scalar(scratch, scratch, -0.5, None, ALU.mult),
            reads=[cbuf], writes=[cbuf])
        k.v(lambda e: e.tensor_tensor(dst, dst, scratch, ALU.mult),
            reads=[dbuf, cbuf], writes=[dbuf])


def build_program(n_phases=4):
    nc = bass.Bass("TRN2", target_bir_lowering=False)
    k = K(nc)
    em, B_ = k.em, k.B

    def inp(name, shape, dt=F32):
        return nc.dram_tensor(name, list(shape), dt, kind="ExternalInput").ap()

    x_h = inp("x_fm", [D, T])
    wq_h = inp("l0_wq", [D, D])
    wkv_h = inp("l0_wkv", [D, 2 * D])
    wout_h = inp("l0_wout", [D, D])
    bout_h = inp("l0_bout", [128, KD])
    sortwq_h = inp("l0_sortwq", [SH, 64, 64])
    sortwk_h = inp("l0_sortwk", [SH, 64, 64])
    ownhot_h = inp("ownhot", [64, 32])
    mask_first_h = inp("mask_first", [128, 384])
    mask_last_h = inp("mask_last", [128, 384])
    identb_h = inp("ident_bf16", [128, 128], BF16)
    ones1f_h = inp("ones1_f32", [1, 128])
    onescol_h = inp("onescol_f32", [128, 1])
    kbm_idx_h = inp("kbm_idx", [64, 4], I32)
    sktile_idx_h = inp("sktile_idx", [128, 32], I32)
    lkb_idx_h = inp("lkb_idx", [128, 34], I32)
    identf_h = inp("ident_f32", [128, 128])
    wg_h = inp("l0_wg_t", [D, INNER])
    wu_h = inp("l0_wu_t", [D, INNER])
    wd_h = inp("l0_wd_t", [INNER, D])
    if n_phases < 4:
        out_h = nc.dram_tensor("out_fm", [D, T], F32, kind="ExternalOutput").ap()
    else:
        out_h = nc.dram_tensor("out_tm", [T, D], F32, kind="ExternalOutput").ap()

    ag1_sk_in = k.dram("ag1_sk_in", [32 * 128, 768], BF16)
    ag1_sv_in = k.dram("ag1_sv_in", [32 * 128, 768], BF16)
    ag1_lk_in = k.dram("ag1_lk_in", [T, 256], BF16)
    ag1_lv_in = k.dram("ag1_lv_in", [T, 256], BF16)
    ag1_sk = k.dram("ag1_sk", [N_CORES * 32 * 128, 768], BF16)
    ag1_sv = k.dram("ag1_sv", [N_CORES * 32 * 128, 768], BF16)
    ag1_lk = k.dram("ag1_lk", [N_CORES * T, 256], BF16)
    ag1_lv = k.dram("ag1_lv", [N_CORES * T, 256], BF16)
    x1_hbm = k.dram("x1_hbm", [D, T])
    attnout_hbm = k.dram("attnout_hbm", [D, T], BF16)
    kblob_hbm = k.dram("kblob_hbm", [6, 64 * 128, 128], BF16)
    vblob_hbm = k.dram("vblob_hbm", [3, 64 * 128, 256], BF16)
    lkstrip_hbm = k.dram("lkstrip_hbm", [34 * 128, 256], BF16)

    identb = k.sb("identb_s", [128, 128], BF16)
    identf = k.sb("identf_s", [128, 128])
    ones1f = k.sb("ones1f_s", [1, 128])
    onescol = k.sb("onescol_s", [128, 1])
    onescol_b = k.sb("onescol_bs", [128, 1], BF16)
    probe = k.sb("probe", [128, 1])
    magicC = k.sb("magicC", [128, 1])
    kbm_idx = k.sb("kbm_idx_s", [64, 4], I32)
    sktile_idx = k.sb("sktile_idx_s", [128, 32], I32)
    lkb_idx = k.sb("lkb_idx_s", [128, 34], I32)
    mask_first = k.sb("mask_first_s", [128, 384])
    mask_last = k.sb("mask_last_s", [128, 384])
    boutc = k.sb("boutc_s", [128, KD])

    for dst, src, nm in [(identb, identb_h, "identb"), (identf, identf_h, "identf"),
                         (ones1f, ones1f_h, "ones1f"),
                         (onescol, onescol_h, "onescol"), (kbm_idx, kbm_idx_h, "kbm_idx"),
                         (lkb_idx, lkb_idx_h, "lkb_idx"),
                         (sktile_idx, sktile_idx_h, "sktile_idx"),
                         (mask_first, mask_first_h, "mask_first"),
                         (mask_last, mask_last_h, "mask_last"), (boutc, bout_h, "boutc")]:
        em.dma("qsp", (lambda d, s: lambda e: e.dma_start(out=d[:], in_=s[:, :]))(dst, src),
               writes=[B_(nm)])
    k.v(lambda e: e.tensor_copy(onescol_b[:, :], onescol[:, :]),
        reads=[B_("onescol")], writes=[B_("onescol_b")])

    k.v(lambda e: e.memset(probe[:], 16777216.0), writes=[B_("probe")])
    k.v(lambda e: e.tensor_scalar(probe[:], probe[:], 1.0, None, ALU.add),
        reads=[B_("probe")], writes=[B_("probe")])
    k.v(lambda e: e.tensor_scalar(probe[:], probe[:], 16777216.0, None, ALU.subtract),
        reads=[B_("probe")], writes=[B_("probe")])
    k.v(lambda e: e.tensor_scalar(magicC[:], probe[:], CDELTA, C32, ALU.mult, ALU.add),
        reads=[B_("probe")], writes=[B_("magicC")])

    k.push()
    q_fm = k.sb("q_fm", [128, KD * T], BF16)

    k.push()
    xa = k.sb("xa", [128, KD * 512])
    xb = k.sb("xb", [128, KD * 512])
    xsq = k.sb("xsq", [128, KD * 512])
    xnr = k.sb("xnr", [128, 2 * KD * 512], BF16)
    wq_sb = k.sb("wq_sb", [128, KD * D], BF16)
    wkv_sb = k.sb("wkv_sb", [128, KD * 2048], BF16)
    stat = k.sb("stat", [1, 4 * 512])
    mbc = k.sb("mbc", [128, 512])
    rbc = k.sb("rbc", [128, 512])
    kvev = k.sb("kvev", [128, 2 * 2048], BF16)
    ps_s1 = k.ps("ps_s1", [1, 512])
    ps_bc2 = k.ps("ps_bc2", [128, 512])
    ps_q = [k.ps(f"ps_q{i}", [128, 512]) for i in range(2)]
    ps_kv = [k.ps(f"ps_kv{i}", [128, 512]) for i in range(2)]

    for kc in range(KD):
        em.dma("qpool", (lambda kc: lambda e: e.dma_start(
            out=wq_sb[:, kc * D:(kc + 1) * D],
            in_=wq_h[kc * 128:(kc + 1) * 128, :]))(kc), writes=[B_("wq_sb")])
        em.dma("qpool", (lambda kc: lambda e: e.dma_start(
            out=wkv_sb[:, kc * 2048:(kc + 1) * 2048],
            in_=wkv_h[kc * 128:(kc + 1) * 128, :]))(kc), writes=[B_("wkv_sb")])

    for tc in range(T // 512):
        xsb, xbuf = (xa, B_("xa")) if tc % 2 == 0 else (xb, B_("xb"))
        xnb = (tc % 2) * KD * 512
        xnbuf = B_(f"xnr{tc % 2}")
        em.dma("qsp", (lambda xsb, tc: lambda e: e.dma_start(
            out=xsb[:, :].rearrange("p (kc t) -> p kc t", kc=KD),
            in_=x_h[:, tc * 512:(tc + 1) * 512].rearrange("(kc p) t -> p kc t", p=128)))(xsb, tc),
            writes=[xbuf])
        k.p((lambda xsb: lambda e: [e.matmul(
            ps_s1[:, :], onescol[:, :], xsb[:, kc * 512:(kc + 1) * 512],
            start=(kc == 0), stop=(kc == KD - 1)) for kc in range(KD)][-1])(xsb),
            reads=[B_("onescol"), xbuf], writes=[B_("ps_s1")])
        k.a((lambda xsb: lambda e: e.activation(out=xsq[:, :], in_=xsb[:, :],
                                                func=AF.Square))(xsb),
            reads=[xbuf], writes=[B_("xsq")])
        sm = stat[0:1, 0:512]
        sv = stat[0:1, 512:1024]
        sr = stat[0:1, 1024:1536]
        sc_ = stat[0:1, 1536:2048]
        k.v(lambda e: e.tensor_scalar(sm, ps_s1[:, :], 1.0 / D, None, ALU.mult),
            reads=[B_("ps_s1")], writes=[B_("stat")])
        k.p(lambda e: [e.matmul(
            ps_s1[:, :], onescol[:, :], xsq[:, kc * 512:(kc + 1) * 512],
            start=(kc == 0), stop=(kc == KD - 1)) for kc in range(KD)][-1],
            reads=[B_("onescol"), B_("xsq")], writes=[B_("ps_s1")])
        k.v(lambda e: e.tensor_scalar(sv, ps_s1[:, :], 1.0 / D, None, ALU.mult),
            reads=[B_("ps_s1")], writes=[B_("stat")])
        k.v(lambda e: e.tensor_tensor(sc_, sm, sm, ALU.mult),
            reads=[B_("stat")], writes=[B_("stat")])
        k.v(lambda e: e.tensor_tensor(sv, sv, sc_, ALU.subtract),
            reads=[B_("stat")], writes=[B_("stat")])
        k.v(lambda e: e.tensor_scalar(sv, sv, 1e-5, None, ALU.add),
            reads=[B_("stat")], writes=[B_("stat")])
        rsqrt_rows(k, sr, sv, sc_, (B_("stat"), B_("stat"), B_("stat")))
        k.p(lambda e: e.matmul(ps_bc2[:, :], ones1f[:, :], sm, start=True, stop=True),
            reads=[B_("ones1f"), B_("stat")], writes=[B_("ps_bc2")])
        k.v(lambda e: e.tensor_copy(mbc[:, :], ps_bc2[:, :]),
            reads=[B_("ps_bc2")], writes=[B_("mbc")])
        k.p(lambda e: e.matmul(ps_bc2[:, :], ones1f[:, :], sr, start=True, stop=True),
            reads=[B_("ones1f"), B_("stat")], writes=[B_("ps_bc2")])
        k.v(lambda e: e.tensor_copy(rbc[:, :], ps_bc2[:, :]),
            reads=[B_("ps_bc2")], writes=[B_("rbc")])
        for kc in range(KD):
            k.v((lambda xsb, kc: lambda e: e.tensor_tensor(
                xsq[:, kc * 512:(kc + 1) * 512], xsb[:, kc * 512:(kc + 1) * 512],
                mbc[:, :], ALU.subtract))(xsb, kc),
                reads=[xbuf, B_("mbc")], writes=[B_("xsq")])
            k.v((lambda kc, xnb, xnbuf: lambda e: e.tensor_tensor(
                xnr[:, xnb + kc * 512:xnb + kc * 512 + 512],
                xsq[:, kc * 512:(kc + 1) * 512], rbc[:, :], ALU.mult))(kc, xnb, xnbuf),
                reads=[B_("xsq"), B_("rbc")], writes=[xnbuf])
        for oc in range(KD):
            pq, pqb = ps_q[oc % 2], B_(f"ps_q{oc % 2}")
            k.p((lambda oc, xnb, pq: lambda e: [e.matmul(
                pq[:, :], wq_sb[:, (kc * KD + oc) * 128:(kc * KD + oc) * 128 + 128],
                xnr[:, xnb + kc * 512:xnb + kc * 512 + 512],
                start=(kc == 0), stop=(kc == KD - 1)) for kc in range(KD)][-1])(oc, xnb, pq),
                reads=[B_("wq_sb"), xnbuf], writes=[pqb])
            k.v((lambda oc, tc, pq: lambda e: e.tensor_copy(
                q_fm[:, oc * T + tc * 512:oc * T + tc * 512 + 512], pq[:, :]))(oc, tc, pq),
                reads=[pqb], writes=[B_("q_fm")])
        for ti in range(4):
            vtile = tc * 4 + ti
            for nb in range(4):
                pkv, pkb = ps_kv[nb % 2], B_(f"ps_kv{nb % 2}")
                k.p((lambda xnb, ti, nb, pkv: lambda e: [e.matmul(
                    pkv[:, :],
                    xnr[:, xnb + kc * 512 + ti * 128:xnb + kc * 512 + ti * 128 + 128],
                    wkv_sb[:, kc * 2048 + nb * 512:kc * 2048 + nb * 512 + 512],
                    start=(kc == 0), stop=(kc == KD - 1)) for kc in range(KD)][-1])(xnb, ti, nb, pkv),
                    reads=[xnbuf, B_("wkv_sb")], writes=[pkb])
                k.v((lambda vtile, nb, pkv: lambda e: e.tensor_copy(
                    kvev[:, (vtile % 2) * 2048 + nb * 512:(vtile % 2) * 2048 + nb * 512 + 512],
                    pkv[:, :]))(vtile, nb, pkv),
                    reads=[pkb], writes=[B_(f"kvev{vtile % 2}")])
            db = (vtile % 2) * 2048
            kvb = B_(f"kvev{vtile % 2}")
            em.dma("qsp", (lambda vtile, db: lambda e: e.dma_start(
                out=ag1_sk_in[vtile * 128:(vtile + 1) * 128, :],
                in_=kvev[:, db + 256:db + 1024]))(vtile, db),
                reads=[kvb], writes=[B_("ag1_sk_in")])
            em.dma("qsp", (lambda vtile, db: lambda e: e.dma_start(
                out=ag1_sv_in[vtile * 128:(vtile + 1) * 128, :],
                in_=kvev[:, db + 1280:db + 2048]))(vtile, db),
                reads=[kvb], writes=[B_("ag1_sv_in")])
            em.dma("qact", (lambda vtile, db: lambda e: e.dma_start(
                out=ag1_lk_in[vtile * 128:(vtile + 1) * 128, :],
                in_=kvev[:, db:db + 256]))(vtile, db),
                reads=[kvb], writes=[B_("ag1_lk_in")])
            em.dma("qact", (lambda vtile, db: lambda e: e.dma_start(
                out=ag1_lv_in[vtile * 128:(vtile + 1) * 128, :],
                in_=kvev[:, db + 1024:db + 1280]))(vtile, db),
                reads=[kvb], writes=[B_("ag1_lv_in")])

    for _i, _o, _ib in [(ag1_sk_in, ag1_sk, "ag1_sk_in"), (ag1_sv_in, ag1_sv, "ag1_sv_in"),
                        (ag1_lk_in, ag1_lk, "ag1_lk_in"), (ag1_lv_in, ag1_lv, "ag1_lv_in")]:
        k.g((lambda _i, _o: lambda e: e.collective_compute(
            "AllGather", ALU.bypass, replica_groups=[list(range(N_CORES))],
            ins=[_i[:, :]], outs=[_o[:, :]]))(_i, _o),
            reads=[B_(_ib)], writes=[B_("ag1")])
    k.pop()

    env = dict(locals())
    if n_phases >= 1:
        build_p1(k, env)
    k.pop()
    if n_phases >= 2:
        build_p2(k, env)
    if n_phases >= 3:
        build_p3(k, env)
    if n_phases >= 4:
        build_p4(k, env)

    if n_phases < 4:
        res_map = {0: None, 1: "x1_hbm", 2: "x2_hbm", 3: "x3_hbm"}
        nm = res_map.get(n_phases)
        k.push()
        dbg = k.sb("dbgout", [128, KD * 512])
        for tc in range(T // 512):
            if nm is None:
                src_ap = x_h[:, tc * 512:(tc + 1) * 512]
                rb = []
            else:
                src_ap = env[nm][:, tc * 512:(tc + 1) * 512]
                rb = [B_(nm)]
            em.dma("qact", (lambda src_ap: lambda e: e.dma_start(
                out=dbg[:, :].rearrange("p (kc t) -> p kc t", kc=KD),
                in_=src_ap.rearrange("(kc p) t -> p kc t", p=128)))(src_ap),
                reads=rb, writes=[B_("dbgout")])
            em.dma("qact", (lambda tc: lambda e: e.dma_start(
                out=out_h[:, tc * 512:(tc + 1) * 512].rearrange("(kc p) t -> p kc t", p=128),
                in_=dbg[:, :].rearrange("p (kc t) -> p kc t", kc=KD)))(tc),
                reads=[B_("dbgout")], writes=[B_("out")])
        k.pop(barrier=False)

    em.wait_tokens("sp", em.barrier_token())
    em.finalize()
    while k.scopes:
        k.scopes.pop().close()
    return nc

def build_p1(k, env):
    em, B_ = k.em, k.B
    g = env
    q_fm = g["q_fm"]; identb = g["identb"]; onescol = g["onescol"]
    onescol_b = g["onescol_b"]
    ag1_sk, ag1_sv = g["ag1_sk"], g["ag1_sv"]
    ag1_lk, ag1_lv = g["ag1_lk"], g["ag1_lv"]
    kbm_idx, lkb_idx, sktile_idx = g["kbm_idx"], g["lkb_idx"], g["sktile_idx"]
    mask_first, mask_last = g["mask_first"], g["mask_last"]
    kblob_hbm, vblob_hbm, lkstrip_hbm = g["kblob_hbm"], g["vblob_hbm"], g["lkstrip_hbm"]
    attnout_hbm = g["attnout_hbm"]
    sortwq_h, sortwk_h, ownhot_h = g["sortwq_h"], g["sortwk_h"], g["ownhot_h"]
    wout_h, boutc = g["wout_h"], g["boutc"]
    x_h, x1_hbm = g["x_h"], g["x1_hbm"]

    k.push()
    sortwq = k.sb("sortwq", [64, SH * 64], BF16)
    sortwk = k.sb("sortwk", [64, SH * 64], BF16)
    ownhot = k.sb("ownhot_s", [64, 32], BF16)
    em.dma("qpool", lambda e: e.dma_start(
        out=sortwq[:, :].rearrange("p (s c) -> p s c", s=SH),
        in_=sortwq_h[:, :, :].rearrange("s p c -> p s c")), writes=[B_("sortwq")])
    em.dma("qpool", lambda e: e.dma_start(
        out=sortwk[:, :].rearrange("p (s c) -> p s c", s=SH),
        in_=sortwk_h[:, :, :].rearrange("s p c -> p s c")), writes=[B_("sortwk")])
    em.dma("qpool", lambda e: e.dma_start(out=ownhot[:, :], in_=ownhot_h[:, :]),
           writes=[B_("ownhot")])

    qsum_b = k.sb("qsum_b", [64, SH * 32], BF16)
    ksum_b = k.sb("ksum_b", [64, SH * 32], BF16)
    qsum_f = k.sb("qsum_f", [64, 32])
    for s in range(SH):
        h = s + 4
        hc, hb = h // 2, 64 * (h % 2)
        k.v((lambda hc, hb: lambda e: e.tensor_reduce(
            qsum_f[:, :].rearrange("p (u o) -> p u o", o=1),
            q_fm[hb:hb + 64, hc * T:(hc + 1) * T].rearrange("p (u t) -> p u t", u=32),
            AXX, ALU.add))(hc, hb),
            reads=[B_("q_fm")], writes=[B_("qsum_f")])
        k.v((lambda s: lambda e: e.tensor_copy(qsum_b[:, s * 32:(s + 1) * 32],
                                               qsum_f[:, :]))(s),
            reads=[B_("qsum_f")], writes=[B_("qsum_b")])

    sktile = k.sb("sktile", [128, 2 * 768], BF16)
    ps_ks = k.ps("ps_ks", [64, SH])
    for u in range(32):
        db = (u % 2) * 768
        em.dma("qpool", (lambda u, db: lambda e: e.indirect_dma_start(
            sktile[:, db:db + 768], None, ag1_sk[:, :],
            bass.IndirectOffsetOnAxis(sktile_idx[:, u:u + 1], 0)))(u, db),
            reads=[B_("sktile_idx"), B_("ag1")], writes=[B_(f"sktile{u % 2}")])
        k.p((lambda u, db: lambda e: [e.matmul(
            ps_ks[:, s:s + 1], sktile[:, db + s * 64:db + s * 64 + 64],
            onescol_b[:, :], start=True, stop=True) for s in range(SH)][-1])(u, db),
            reads=[B_(f"sktile{u % 2}"), B_("onescol_b")], writes=[B_("ps_ks")])
        k.v((lambda u: lambda e: e.tensor_copy(
            ksum_b[:, :].rearrange("p (s u2) -> p s u2", s=SH)[:, :, u:u + 1],
            ps_ks[:, :].rearrange("p (s o) -> p s o", o=1)))(u),
            reads=[B_("ps_ks")], writes=[B_("ksum_b")])

    r_sb = k.sb("r_sb", [32, 64])
    r_bb = k.sb("r_bb", [32, 64], BF16)
    r_row = k.sb("r_row", [32, 2])
    r2all = k.sb("r2all", [64, SH * 64], BF16)
    ps_r = k.ps("ps_r", [32, 64])
    ps_rt = k.ps("ps_rt", [64, 32], BF16)
    for s in range(SH):
        k.p((lambda s: lambda e: [
            e.matmul(ps_r[:, :], qsum_b[:, s * 32:(s + 1) * 32],
                     sortwq[:, s * 64:(s + 1) * 64], start=True, stop=False),
            e.matmul(ps_r[:, :], ksum_b[:, s * 32:(s + 1) * 32],
                     sortwk[:, s * 64:(s + 1) * 64], start=False, stop=True)][-1])(s),
            reads=[B_("qsum_b"), B_("ksum_b"), B_("sortwq"), B_("sortwk")],
            writes=[B_("ps_r")])
        k.v(lambda e: e.tensor_single_scalar(r_sb[:, :], ps_r[:, :], 0.0, ALU.max),
            reads=[B_("ps_r")], writes=[B_("r_sb")])
        k.v(lambda e: e.tensor_reduce(r_row[:, 0:1], r_sb[:, :], AXX, ALU.max),
            reads=[B_("r_sb")], writes=[B_("r_row")])
        k.v(lambda e: e.tensor_scalar(r_row[:, 0:1], r_row[:, 0:1], -1.0, None, ALU.mult),
            reads=[B_("r_row")], writes=[B_("r_row")])
        k.a(lambda e: e.activation(out=r_sb[:, :], in_=r_sb[:, :], func=AF.Exp,
                                   bias=r_row[:, 0:1]),
            reads=[B_("r_row"), B_("r_sb")], writes=[B_("r_sb")])
        k.v(lambda e: e.tensor_reduce(r_row[:, 1:2], r_sb[:, :], AXX, ALU.add),
            reads=[B_("r_sb")], writes=[B_("r_row")])
        k.v(lambda e: e.reciprocal(r_row[:, 1:2], r_row[:, 1:2]),
            reads=[B_("r_row")], writes=[B_("r_row")])
        k.v(lambda e: e.tensor_scalar(r_sb[:, :], r_sb[:, :], r_row[:, 1:2], None,
                                      ALU.mult),
            reads=[B_("r_row"), B_("r_sb")], writes=[B_("r_sb")])
        k.v(lambda e: e.tensor_copy(r_bb[:, :], r_sb[:, :]),
            reads=[B_("r_sb")], writes=[B_("r_bb")])
        k.p(lambda e: e.transpose(ps_rt[:, :], r_bb[:, :], identb[0:32, 0:32]),
            reads=[B_("r_bb"), B_("identb")], writes=[B_("ps_rt")])
        k.v((lambda s: lambda e: e.tensor_copy(r2all[:, s * 64:s * 64 + 32],
                                               ps_rt[:, :]))(s),
            reads=[B_("ps_rt")], writes=[B_("r2all")])
        k.v((lambda s: lambda e: e.tensor_copy(r2all[:, s * 64 + 32:s * 64 + 64],
                                               ownhot[:, :]))(s),
            reads=[B_("ownhot")], writes=[B_("r2all")])

    kbm = k.sb("kbm", [64, 32 * 768], BF16)
    kblob = k.sb("kblob", [64, 2 * 32 * 128], BF16)
    vblob = k.sb("vblob", [64, 32 * 256], BF16)
    ps_mx = [k.ps(f"ps_mx{i}", [64, 512]) for i in range(2)]

    def mix_head(s, dstblob, dstbuf, colw, coloff):
        for c in range(4):
            pm, pmb = ps_mx[c % 2], B_(f"ps_mx{c % 2}")
            k.p((lambda s, c, pm: lambda e: e.matmul(
                pm[:, :], r2all[:, s * 64:(s + 1) * 64],
                kbm[:, :].rearrange("p (t s2 e) -> p t s2 e", t=32, s2=SH)[
                    :, c * 8:(c + 1) * 8, s:s + 1, :],
                start=True, stop=True))(s, c, pm),
                reads=[B_("kbm"), B_("r2all")], writes=[pmb])
            k.v((lambda c, pm, dstblob, colw, coloff: lambda e: e.tensor_copy(
                dstblob[:, :].rearrange("p (t c2) -> p t c2", c2=colw)[
                    :, c * 8:(c + 1) * 8, coloff:coloff + 64],
                pm[:, :].rearrange("p (t e) -> p t e", t=8)))(c, pm, dstblob, colw, coloff),
                reads=[pmb], writes=[dstbuf])

    for th in range(4):
        em.dma("qpool", (lambda th: lambda e: e.indirect_dma_start(
            kbm[:, :], None,
            ag1_sk[:, :].rearrange("(r t) c -> r (t c)", t=32),
            bass.IndirectOffsetOnAxis(kbm_idx[:, th:th + 1], 0)))(th),
            reads=[B_("kbm_idx"), B_("ag1")], writes=[B_("kbm")])
        for pr in range(6):
            blb = kblob[:, (pr % 2) * 4096:(pr % 2) * 4096 + 4096]
            blbuf = B_(f"kblob{pr % 2}")
            mix_head(2 * pr, blb, blbuf, 128, 0)
            mix_head(2 * pr + 1, blb, blbuf, 128, 64)
            em.dma("qsp", (lambda pr, th, blb: lambda e: e.dma_start(
                out=kblob_hbm[pr, :, :].rearrange("(u t) c -> u (t c)", t=128)[
                    :, th * 32 * 128:(th + 1) * 32 * 128],
                in_=blb))(pr, th, blb), reads=[blbuf], writes=[B_("kblob_hbm")])
        em.dma("qpool", (lambda th: lambda e: e.indirect_dma_start(
            kbm[:, :], None,
            ag1_sv[:, :].rearrange("(r t) c -> r (t c)", t=32),
            bass.IndirectOffsetOnAxis(kbm_idx[:, th:th + 1], 0)))(th),
            reads=[B_("kbm_idx"), B_("ag1")], writes=[B_("kbm")])
        for grp in range(3):
            for si in range(4):
                mix_head(grp * 4 + si, vblob, B_("vblob"), 256, si * 64)
            em.dma("qsp", (lambda grp, th: lambda e: e.dma_start(
                out=vblob_hbm[grp, :, :].rearrange("(u t) c -> u (t c)", t=128)[
                    :, th * 32 * 256:(th + 1) * 32 * 256],
                in_=vblob[:, :]))(grp, th), reads=[B_("vblob")], writes=[B_("vblob_hbm")])
    k.pop()

    k.push()
    lv_sb = k.sb("lv_sb", [128, 34 * 256], BF16)
    lk_sb = k.sb("lk_sb", [128, 34 * 256], BF16)
    for w in range(34):
        em.dma("qpool", (lambda w: lambda e: e.indirect_dma_start(
            lv_sb[:, w * 256:(w + 1) * 256], None, ag1_lv[:, :],
            bass.IndirectOffsetOnAxis(lkb_idx[:, w:w + 1], 0)))(w),
            reads=[B_("lkb_idx"), B_("ag1")], writes=[B_("lv_sb")])
        em.dma("qpool", (lambda w: lambda e: e.indirect_dma_start(
            lk_sb[:, w * 256:(w + 1) * 256], None, ag1_lk[:, :],
            bass.IndirectOffsetOnAxis(lkb_idx[:, w:w + 1], 0)))(w),
            reads=[B_("lkb_idx"), B_("ag1")], writes=[B_("lk_sb")])
    em.dma("qact", lambda e: e.dma_start(
        out=lkstrip_hbm[:, :].rearrange("(w t) c -> t w c", t=128),
        in_=lk_sb[:, :].rearrange("p (w c) -> p w c", w=34)),
        reads=[B_("lk_sb")], writes=[B_("lkstrip_hbm")])
    lkfm = k.sb("lkfm", [128, 2 * 4352], BF16)
    for pr2 in range(2):
        em.dma("qsp", (lambda pr2: lambda e: e.dma_start_transpose(
            lkfm[:, pr2 * 4352:(pr2 + 1) * 4352],
            lkstrip_hbm[:, pr2 * 128:(pr2 + 1) * 128]))(pr2),
            reads=[B_("lkstrip_hbm")], writes=[B_("lkfm")])

    ps_d = k.ps("ps_d", [128, 512])
    ps_t = [k.ps(f"ps_t{i}", [128, 128], BF16) for i in range(2)]
    ps_av = k.ps("ps_av", [64, 128])
    smx = k.sb("smx", [128, 512])
    scol = k.sb("scol", [128, 2])
    attnb = k.sb("attnb", [128, 512], BF16)
    attnT = k.sb("attnT", [128, 3 * 128], BF16)
    av_sb = k.sb("av_sb", [64, 128], BF16)
    vt_r = k.sb("vt_r", [128, 256], BF16)
    vt_o = k.sb("vt_o", [128, 256], BF16)

    def softmax_av(h, u, nj, vt_tiles, scale):
        nch = nj // 128
        k.v((lambda nj: lambda e: e.tensor_reduce(
            scol[:, 0:1], smx[:, 0:nj], AXX, ALU.max))(nj),
            reads=[B_("smx")], writes=[B_("scol")])
        k.v((lambda scale: lambda e: e.tensor_scalar(
            scol[:, 0:1], scol[:, 0:1], -scale, None, ALU.mult))(scale),
            reads=[B_("scol")], writes=[B_("scol")])
        k.a((lambda nj, scale: lambda e: e.activation(
            out=smx[:, 0:nj], in_=smx[:, 0:nj], func=AF.Exp,
            bias=scol[:, 0:1], scale=scale))(nj, scale),
            reads=[B_("scol"), B_("smx")], writes=[B_("smx")])
        k.v((lambda nj: lambda e: e.tensor_reduce(
            scol[:, 1:2], smx[:, 0:nj], AXX, ALU.add))(nj),
            reads=[B_("smx")], writes=[B_("scol")])
        k.v(lambda e: e.reciprocal(scol[:, 1:2], scol[:, 1:2]),
            reads=[B_("scol")], writes=[B_("scol")])
        k.v((lambda nj: lambda e: e.tensor_scalar(
            attnb[:, 0:nj], smx[:, 0:nj], scol[:, 1:2], None, ALU.mult))(nj),
            reads=[B_("scol"), B_("smx")], writes=[B_("attnb")])
        for jc in range(nch):
            pt, ptb = ps_t[jc % 2], B_(f"ps_t{jc % 2}")
            k.p((lambda jc, pt: lambda e: e.transpose(
                pt[:, :], attnb[:, jc * 128:(jc + 1) * 128], identb[:, :]))(jc, pt),
                reads=[B_("attnb"), B_("identb")], writes=[ptb])
            k.v((lambda jc, pt: lambda e: e.tensor_copy(
                attnT[:, jc * 128:(jc + 1) * 128], pt[:, :]))(jc, pt),
                reads=[ptb], writes=[B_("attnT")])
        k.p((lambda nch, vt_tiles: lambda e: [e.matmul(
            ps_av[:, :], vt_tiles[jc][0][:, vt_tiles[jc][1]:vt_tiles[jc][1] + 64],
            attnT[:, jc * 128:(jc + 1) * 128],
            start=(jc == 0), stop=(jc == nch - 1)) for jc in range(nch)][-1])(nch, vt_tiles),
            reads=[B_("attnT")] + [B_(vt_tiles[jc][2]) for jc in range(nch)],
            writes=[B_("ps_av")])
        k.v(lambda e: e.tensor_copy(av_sb[:, :], ps_av[:, :]),
            reads=[B_("ps_av")], writes=[B_("av_sb")])
        em.dma("qact", (lambda h, u: lambda e: e.dma_start(
            out=attnout_hbm[h * 64:(h + 1) * 64, u * 128:(u + 1) * 128],
            in_=av_sb[:, :]))(h, u), reads=[B_("av_sb")], writes=[B_("attnout_hbm")])

    for h in range(LH):
        hb = 64 * (h % 2)
        for u in range(NT):
            k.p((lambda h, hb, u: lambda e: e.matmul(
                ps_d[:, 0:384],
                q_fm[hb:hb + 64, (h // 2) * T + u * 128:(h // 2) * T + u * 128 + 128],
                lkfm[hb:hb + 64, (h // 2) * 4352 + u * 128:(h // 2) * 4352 + u * 128 + 384],
                start=True, stop=True))(h, hb, u),
                reads=[B_("q_fm"), B_("lkfm")], writes=[B_("ps_d")])
            if u == 0:
                k.v(lambda e: e.tensor_tensor(smx[:, 0:384], ps_d[:, 0:384],
                                              mask_first[:, :], ALU.add),
                    reads=[B_("ps_d"), B_("mask_first")], writes=[B_("smx")])
            elif u == NT - 1:
                k.v(lambda e: e.tensor_tensor(smx[:, 0:384], ps_d[:, 0:384],
                                              mask_last[:, :], ALU.add),
                    reads=[B_("ps_d"), B_("mask_last")], writes=[B_("smx")])
            else:
                k.v(lambda e: e.tensor_copy(smx[:, 0:384], ps_d[:, 0:384]),
                    reads=[B_("ps_d")], writes=[B_("smx")])
            vts = [(lv_sb, (u + jc) * 256 + h * 64, "lv_sb") for jc in range(3)]
            softmax_av(h, u, 384, vts, 1.0)

    kfm = k.sb("kfm", [128, 2 * 8192], BF16)
    for pr in range(6):
        kb = (pr % 2) * 8192
        kfb = B_(f"kfm{pr % 2}")
        em.dma("qsp", (lambda pr, kb: lambda e: e.dma_start_transpose(
            kfm[:, kb:kb + 8192], kblob_hbm[pr, :, :]))(pr, kb),
            reads=[B_("kblob_hbm")], writes=[kfb])
        grp = pr // 2
        for u in range(NT):
            em.dma("qsp", (lambda grp, u: lambda e: e.dma_start(
                out=vt_r[:, :], in_=vblob_hbm[grp, u * 128:(u + 1) * 128, :]))(grp, u),
                reads=[B_("vblob_hbm")], writes=[B_("vt_r")])
            em.dma("qsp", (lambda grp, u: lambda e: e.dma_start(
                out=vt_o[:, :], in_=vblob_hbm[grp, (32 + u) * 128:(33 + u) * 128, :]))(grp, u),
                reads=[B_("vblob_hbm")], writes=[B_("vt_o")])
            for sp in range(2):
                s = 2 * pr + sp
                h = s + 4
                hb = 64 * (h % 2)
                si = s % 4
                k.p((lambda h, hb, u, kb, sp: lambda e: [
                    e.matmul(ps_d[:, 0:128],
                             q_fm[hb:hb + 64, (h // 2) * T + u * 128:(h // 2) * T + u * 128 + 128],
                             kfm[hb:hb + 64, kb + u * 128:kb + u * 128 + 128],
                             start=True, stop=True),
                    e.matmul(ps_d[:, 128:256],
                             q_fm[hb:hb + 64, (h // 2) * T + u * 128:(h // 2) * T + u * 128 + 128],
                             kfm[hb:hb + 64, kb + (32 + u) * 128:kb + (32 + u) * 128 + 128],
                             start=True, stop=True)][-1])(h, hb, u, kb, sp),
                    reads=[B_("q_fm"), kfb], writes=[B_("ps_d")])
                k.v(lambda e: e.tensor_copy(smx[:, 0:256], ps_d[:, 0:256]),
                    reads=[B_("ps_d")], writes=[B_("smx")])
                vts = [(vt_r, si * 64, "vt_r"), (vt_o, si * 64, "vt_o")]
                softmax_av(h, u, 256, vts, float(DH) ** -0.5)
    k.pop()

    k.push()
    aout = k.sb("aout", [128, KD * 512], BF16)
    wot = k.sb("wot", [128, KD * KD * 128], BF16)
    xres = k.sb("xres", [128, KD * 512])
    x1sb = k.sb("x1sb", [128, 512])
    ps_o = [k.ps(f"ps_o{i}", [128, 512]) for i in range(2)]
    for tc in range(T // 512):
        em.dma("qsp", (lambda tc: lambda e: e.dma_start(
            out=aout[:, :].rearrange("p (kc t) -> p kc t", kc=KD),
            in_=attnout_hbm[:, tc * 512:(tc + 1) * 512].rearrange(
                "(kc p) t -> p kc t", p=128)))(tc),
            reads=[B_("attnout_hbm")], writes=[B_("aout")])
        em.dma("qact", (lambda tc: lambda e: e.dma_start(
            out=xres[:, :].rearrange("p (kc t) -> p kc t", kc=KD),
            in_=x_h[:, tc * 512:(tc + 1) * 512].rearrange("(kc p) t -> p kc t", p=128)))(tc),
            writes=[B_("xres")])
        for oc in range(KD):
            wb = oc * KD * 128
            if tc == 0:
                em.dma("qpool", (lambda oc, wb: lambda e: e.dma_start(
                    out=wot[:, wb:wb + KD * 128].rearrange("p (kc c) -> p kc c", kc=KD),
                    in_=wout_h[:, oc * 128:(oc + 1) * 128].rearrange(
                        "(kc p) c -> p kc c", p=128)))(oc, wb),
                    writes=[B_(f"wot{oc}")])
            po, pob = ps_o[oc % 2], B_(f"ps_o{oc % 2}")
            k.p((lambda oc, wb, po: lambda e: [e.matmul(
                po[:, :], wot[:, wb + kc * 128:wb + kc * 128 + 128],
                aout[:, kc * 512:(kc + 1) * 512],
                start=(kc == 0), stop=(kc == KD - 1)) for kc in range(KD)][-1])(oc, wb, po),
                reads=[B_(f"wot{oc}"), B_("aout")], writes=[pob])
            k.v((lambda oc, po: lambda e: e.scalar_tensor_tensor(
                x1sb[:, :], po[:, :], boutc[:, oc:oc + 1],
                xres[:, oc * 512:(oc + 1) * 512], ALU.add, ALU.add))(oc, po),
                reads=[pob, B_("boutc"), B_("xres")], writes=[B_("x1sb")])
            em.dma("qsp", (lambda oc, tc: lambda e: e.dma_start(
                out=x1_hbm[oc * 128:(oc + 1) * 128, tc * 512:(tc + 1) * 512],
                in_=x1sb[:, :]))(oc, tc),
                reads=[B_("x1sb")], writes=[B_("x1_hbm")])
    k.pop(barrier=False)

def build_p2(k, env):
    em, B_ = k.em, k.B
    g = env
    onescol, ones1f = g["onescol"], g["ones1f"]
    magicC = g["magicC"]
    x1_hbm = g["x1_hbm"]
    wg_h, wu_h, wd_h = g["wg_h"], g["wu_h"], g["wd_h"]
    identf = g["identf"]
    nc = k.nc
    x2_hbm = k.dram("x2_hbm", [D, T])
    wqg_hbm = k.dram("wqg_hbm", [D, INNER], BF16)
    wqu_hbm = k.dram("wqu_hbm", [D, INNER], BF16)
    wqd_hbm = k.dram("wqd_hbm", [INNER, D], BF16)
    prod_hbm = k.dram("prod_hbm", [INNER, 512])
    scr_hbm = k.dram("p2scr", [1, 1024])

    k.push()
    mrow = k.sb("mrow", [1, 8])
    mcol = k.sb("mcol", [128, 6])
    k.push()
    wload = k.sb("wload", [128, 4096])
    wqt = k.sb("wqt", [128, 4096], BF16)
    acc = k.sb("acc", [128, 3])
    accs = k.sb("accs", [128, 1])
    ps_m = k.ps("ps_m", [1, 1])
    ps_mb = k.ps("ps_mb", [128, 6])
    wabs = k.sb("wabs", [128, 4096])

    Ws = [(wg_h, wqg_hbm, D, "wg"), (wu_h, wqu_hbm, D, "wu"), (wd_h, wqd_hbm, INNER, "wd")]
    for wi, (wh, wqh, nrows, tag) in enumerate(Ws):
        nld = nrows // 128
        ncols = (D * INNER) // nrows
        for ld in range(nld):
            em.dma("qsp", (lambda wh, ld, ncols: lambda e: e.dma_start(
                out=wload[:, 0:ncols], in_=wh[ld * 128:(ld + 1) * 128, :]))(wh, ld, ncols),
                writes=[B_("wload")])
            k.a((lambda ncols: lambda e: e.activation(
                out=wabs[:, 0:ncols], in_=wload[:, 0:ncols], func=AF.Abs,
                accum_out=accs[:, 0:1]))(ncols),
                reads=[B_("wload")], writes=[B_("wabs"), B_("accs")])
            if ld == 0:
                k.v((lambda wi: lambda e: e.tensor_copy(acc[:, wi:wi + 1],
                                                        accs[:, 0:1]))(wi),
                    reads=[B_("accs")], writes=[B_("acc")])
            else:
                k.v((lambda wi: lambda e: e.tensor_tensor(
                    acc[:, wi:wi + 1], acc[:, wi:wi + 1], accs[:, 0:1], ALU.add))(wi),
                    reads=[B_("accs"), B_("acc")], writes=[B_("acc")])
        k.p((lambda wi: lambda e: e.matmul(ps_m[:, :], onescol[:, :],
                                           acc[:, wi:wi + 1], start=True, stop=True))(wi),
            reads=[B_("onescol"), B_("acc")], writes=[B_("ps_m")])
        k.v((lambda wi: lambda e: e.tensor_scalar(
            mrow[0:1, wi:wi + 1], ps_m[:, :], 1.0 / (D * INNER), None, ALU.mult))(wi),
            reads=[B_("ps_m")], writes=[B_("mrow")])
        k.v((lambda wi: lambda e: e.tensor_scalar(
            mrow[0:1, wi:wi + 1], mrow[0:1, wi:wi + 1], 1e-5, None, ALU.max))(wi),
            reads=[B_("mrow")], writes=[B_("mrow")])
        k.v((lambda wi: lambda e: e.reciprocal(mrow[0:1, 3 + wi:4 + wi],
                                               mrow[0:1, wi:wi + 1]))(wi),
            reads=[B_("mrow")], writes=[B_("mrow")])
    k.p(lambda e: e.matmul(ps_mb[:, :], ones1f[:, :], mrow[0:1, 0:6],
                           start=True, stop=True),
        reads=[B_("ones1f"), B_("mrow")], writes=[B_("ps_mb")])
    k.v(lambda e: e.tensor_copy(mcol[:, :], ps_mb[:, :]),
        reads=[B_("ps_mb")], writes=[B_("mcol")])
    for wi, (wh, wqh, nrows, tag) in enumerate(Ws):
        nld = nrows // 128
        ncols = (D * INNER) // nrows
        for ld in range(nld):
            em.dma("qsp", (lambda wh, ld, ncols: lambda e: e.dma_start(
                out=wload[:, 0:ncols], in_=wh[ld * 128:(ld + 1) * 128, :]))(wh, ld, ncols),
                writes=[B_("wload")])
            k.v((lambda wi, ncols: lambda e: e.tensor_scalar(
                wabs[:, 0:ncols], wload[:, 0:ncols], mcol[:, 3 + wi:4 + wi],
                None, ALU.mult))(wi, ncols),
                reads=[B_("wload"), B_("mcol")], writes=[B_("wabs")])
            k.v((lambda ncols: lambda e: e.tensor_scalar(
                wabs[:, 0:ncols], wabs[:, 0:ncols], magicC[:, 0:1], magicC[:, 0:1],
                ALU.add, ALU.subtract))(ncols),
                reads=[B_("wabs"), B_("magicC")], writes=[B_("wabs")])
            k.v((lambda ncols: lambda e: e.tensor_scalar(
                wqt[:, 0:ncols], wabs[:, 0:ncols], 1.0, -1.0, ALU.min, ALU.max))(ncols),
                reads=[B_("wabs")], writes=[B_("wqt")])
            em.dma("qsp", (lambda wqh, ld, ncols: lambda e: e.dma_start(
                out=wqh[ld * 128:(ld + 1) * 128, :], in_=wqt[:, 0:ncols]))(wqh, ld, ncols),
                reads=[B_("wqt")], writes=[B_(tag + "q")])
    k.pop()

    k.push()
    x1c = k.sb("x1c", [128, KD * 512])
    xnq = k.sb("xnq", [128, KD * 512])
    xq_b = k.sb("xq_b", [128, KD * 512], BF16)
    stat2 = k.sb("stat2", [1, 4 * 512])
    mbc2 = k.sb("mbc2", [128, 512])
    rbc2 = k.sb("rbc2", [128, 512])
    amax = k.sb("amax", [128, 512])
    amt = k.sb("amt", [128, 512])
    mx4 = k.sb("mx4", [128, 4])
    row4 = k.sb("row4", [4, 128])
    qrow = k.sb("qrow", [1, 3 * 512])
    scbc = k.sb("scbc", [128, 512])
    rsbc = k.sb("rsbc", [128, 512])
    wslab = k.sb("wslab", [128, 2 * 8 * 1024], BF16)
    gout = k.sb("gout", [128, 512])
    uout = k.sb("uout", [128, 512])
    prodc = k.sb("prodc", [128, 512])
    pq_sb = k.sb("pq_sb", [128, 32 * 512], BF16)
    wdsl = k.sb("wdsl", [128, 2 * 1024], BF16)
    x2sb = k.sb("x2sb", [128, 512])
    ps_s1 = k.ps("p2ps_s1", [1, 512])
    ps_bc = k.ps("p2ps_bc", [128, 512])
    ps_tr = k.ps("p2ps_tr", [128, 128])
    ps_g = k.ps("p2ps_g", [128, 512])
    ps_u = k.ps("p2ps_u", [128, 512])
    ps_dn = [k.ps(f"p2ps_dn{i}", [128, 512]) for i in range(2)]

    def pmax_to_rows(src_sb, srcbuf, n, rowdst, rowbuf, scr_off):
        nb = n // 128
        for blk in range(nb):
            k.p((lambda blk: lambda e: e.transpose(
                ps_tr[:, :], src_sb[:, blk * 128:(blk + 1) * 128], identf[:, :]))(blk),
                reads=[srcbuf, B_("identf")], writes=[B_("p2ps_tr")])
            k.v((lambda blk: lambda e: e.tensor_reduce(
                mx4[:, blk:blk + 1], ps_tr[:, :], AXX, ALU.max))(blk),
                reads=[B_("p2ps_tr")], writes=[B_("mx4")])
        k.p((lambda nb: lambda e: e.transpose(
            ps_tr[0:nb, 0:128], mx4[:, 0:nb], identf[:, :]))(nb),
            reads=[B_("mx4"), B_("identf")], writes=[B_("p2ps_tr")])
        k.v((lambda nb: lambda e: e.tensor_copy(row4[0:nb, :], ps_tr[0:nb, 0:128]))(nb),
            reads=[B_("p2ps_tr")], writes=[B_("row4")])
        em.dma("qact", (lambda nb, scr_off: lambda e: e.dma_start(
            out=scr_hbm[0:1, scr_off:scr_off + nb * 128].rearrange(
                "o (b c) -> (o b) c", b=nb),
            in_=row4[0:nb, :]))(nb, scr_off),
            reads=[B_("row4")], writes=[B_("p2scr")])
        em.dma("qact", (lambda n, scr_off: lambda e: e.dma_start(
            out=rowdst, in_=scr_hbm[0:1, scr_off:scr_off + n]))(n, scr_off),
            reads=[B_("p2scr")], writes=[rowbuf])

    for tc in range(T // 512):
        em.dma("qsp", (lambda tc: lambda e: e.dma_start(
            out=x1c[:, :].rearrange("p (kc t) -> p kc t", kc=KD),
            in_=x1_hbm[:, tc * 512:(tc + 1) * 512].rearrange("(kc p) t -> p kc t", p=128)))(tc),
            reads=[B_("x1_hbm")], writes=[B_("x1c")])
        k.p(lambda e: [e.matmul(
            ps_s1[:, :], onescol[:, :], x1c[:, kc * 512:(kc + 1) * 512],
            start=(kc == 0), stop=(kc == KD - 1)) for kc in range(KD)][-1],
            reads=[B_("onescol"), B_("x1c")], writes=[B_("p2ps_s1")])
        sm = stat2[0:1, 0:512]
        sv = stat2[0:1, 512:1024]
        sr = stat2[0:1, 1024:1536]
        sc2 = stat2[0:1, 1536:2048]
        k.v(lambda e: e.tensor_scalar(sm, ps_s1[:, :], 1.0 / D, None, ALU.mult),
            reads=[B_("p2ps_s1")], writes=[B_("stat2")])
        k.a(lambda e: e.activation(out=xnq[:, :], in_=x1c[:, :], func=AF.Square),
            reads=[B_("x1c")], writes=[B_("xnq")])
        k.p(lambda e: [e.matmul(
            ps_s1[:, :], onescol[:, :], xnq[:, kc * 512:(kc + 1) * 512],
            start=(kc == 0), stop=(kc == KD - 1)) for kc in range(KD)][-1],
            reads=[B_("onescol"), B_("xnq")], writes=[B_("p2ps_s1")])
        k.v(lambda e: e.tensor_scalar(sv, ps_s1[:, :], 1.0 / D, None, ALU.mult),
            reads=[B_("p2ps_s1")], writes=[B_("stat2")])
        k.v(lambda e: e.tensor_tensor(sc2, sm, sm, ALU.mult),
            reads=[B_("stat2")], writes=[B_("stat2")])
        k.v(lambda e: e.tensor_tensor(sv, sv, sc2, ALU.subtract),
            reads=[B_("stat2")], writes=[B_("stat2")])
        k.v(lambda e: e.tensor_scalar(sv, sv, 1e-5, None, ALU.add),
            reads=[B_("stat2")], writes=[B_("stat2")])
        rsqrt_rows(k, sr, sv, sc2, (B_("stat2"), B_("stat2"), B_("stat2")))
        k.p(lambda e: e.matmul(ps_bc[:, :], ones1f[:, :], sm, start=True, stop=True),
            reads=[B_("ones1f"), B_("stat2")], writes=[B_("p2ps_bc")])
        k.v(lambda e: e.tensor_copy(mbc2[:, :], ps_bc[:, :]),
            reads=[B_("p2ps_bc")], writes=[B_("mbc2")])
        k.p(lambda e: e.matmul(ps_bc[:, :], ones1f[:, :], sr, start=True, stop=True),
            reads=[B_("ones1f"), B_("stat2")], writes=[B_("p2ps_bc")])
        k.v(lambda e: e.tensor_copy(rbc2[:, :], ps_bc[:, :]),
            reads=[B_("p2ps_bc")], writes=[B_("rbc2")])
        for kc in range(KD):
            k.v((lambda kc: lambda e: e.tensor_tensor(
                xnq[:, kc * 512:(kc + 1) * 512], x1c[:, kc * 512:(kc + 1) * 512],
                mbc2[:, :], ALU.subtract))(kc),
                reads=[B_("x1c"), B_("mbc2")], writes=[B_("xnq")])
            k.v((lambda kc: lambda e: e.tensor_tensor(
                xnq[:, kc * 512:(kc + 1) * 512], xnq[:, kc * 512:(kc + 1) * 512],
                rbc2[:, :], ALU.mult))(kc),
                reads=[B_("xnq"), B_("rbc2")], writes=[B_("xnq")])
        k.a(lambda e: e.activation(out=amax[:, :], in_=xnq[:, 0:512],
                                   func=AF.Abs),
            reads=[B_("xnq")], writes=[B_("amax")])
        for kc in range(1, KD):
            k.a((lambda kc: lambda e: e.activation(
                out=amt[:, :], in_=xnq[:, kc * 512:(kc + 1) * 512],
                func=AF.Abs))(kc),
                reads=[B_("xnq")], writes=[B_("amt")])
            k.v(lambda e: e.tensor_tensor(amax[:, :], amax[:, :], amt[:, :],
                                          ALU.max),
                reads=[B_("amt"), B_("amax")], writes=[B_("amax")])
        pmax_to_rows(amax, B_("amax"), 512, qrow[0:1, 0:512], B_("qrow"), 0)
        q_m = qrow[0:1, 0:512]
        q_rs = qrow[0:1, 512:1024]
        q_sc = qrow[0:1, 1024:1536]
        k.v(lambda e: e.tensor_scalar(q_m, q_m, 1e-5, None, ALU.max),
            reads=[B_("qrow")], writes=[B_("qrow")])
        k.v(lambda e: e.reciprocal(q_sc, q_m), reads=[B_("qrow")], writes=[B_("qrow")])
        k.v(lambda e: e.tensor_scalar(q_sc, q_sc, 127.0, None, ALU.mult),
            reads=[B_("qrow")], writes=[B_("qrow")])
        k.v(lambda e: e.tensor_scalar(q_rs, q_m, 1.0 / 127.0, None, ALU.mult),
            reads=[B_("qrow")], writes=[B_("qrow")])
        k.p(lambda e: e.matmul(ps_bc[:, :], ones1f[:, :], q_sc, start=True, stop=True),
            reads=[B_("ones1f"), B_("qrow")], writes=[B_("p2ps_bc")])
        k.v(lambda e: e.tensor_copy(scbc[:, :], ps_bc[:, :]),
            reads=[B_("p2ps_bc")], writes=[B_("scbc")])
        k.p(lambda e: e.matmul(ps_bc[:, :], ones1f[:, :], q_rs, start=True, stop=True),
            reads=[B_("ones1f"), B_("qrow")], writes=[B_("p2ps_bc")])
        k.v(lambda e: e.tensor_copy(rsbc[:, :], ps_bc[:, :]),
            reads=[B_("p2ps_bc")], writes=[B_("rsbc")])
        for kc in range(KD):
            k.v((lambda kc: lambda e: e.tensor_tensor(
                xnq[:, kc * 512:(kc + 1) * 512], xnq[:, kc * 512:(kc + 1) * 512],
                scbc[:, :], ALU.mult))(kc),
                reads=[B_("xnq"), B_("scbc")], writes=[B_("xnq")])
        k.v(lambda e: e.tensor_scalar(xnq[:, :], xnq[:, :], magicC[:, 0:1],
                                      magicC[:, 0:1], ALU.add, ALU.subtract),
            reads=[B_("xnq"), B_("magicC")], writes=[B_("xnq")])
        k.v(lambda e: e.tensor_scalar(xq_b[:, :], xnq[:, :], 127.0, -128.0,
                                      ALU.min, ALU.max),
            reads=[B_("xnq")], writes=[B_("xq_b")])
        for icg in range(4):
            em.dma("qpool", (lambda icg: lambda e: e.dma_start(
                out=wslab[:, 0:8 * 1024].rearrange("p (kc c) -> p kc c", kc=KD),
                in_=wqg_hbm[:, icg * 1024:(icg + 1) * 1024].rearrange(
                    "(kc p) c -> p kc c", p=128)))(icg),
                reads=[B_("wgq")], writes=[B_("wslabg")])
            em.dma("qpool", (lambda icg: lambda e: e.dma_start(
                out=wslab[:, 8 * 1024:16 * 1024].rearrange("p (kc c) -> p kc c", kc=KD),
                in_=wqu_hbm[:, icg * 1024:(icg + 1) * 1024].rearrange(
                    "(kc p) c -> p kc c", p=128)))(icg),
                reads=[B_("wuq")], writes=[B_("wslabu")])
            for ici in range(8):
                ic = icg * 8 + ici
                k.p((lambda ici: lambda e: [e.matmul(
                    ps_g[:, :], wslab[:, kc * 1024 + ici * 128:kc * 1024 + ici * 128 + 128],
                    xq_b[:, kc * 512:(kc + 1) * 512],
                    start=(kc == 0), stop=(kc == KD - 1)) for kc in range(KD)][-1])(ici),
                    reads=[B_("wslabg"), B_("xq_b")], writes=[B_("p2ps_g")])
                k.p((lambda ici: lambda e: [e.matmul(
                    ps_u[:, :], wslab[:, 8192 + kc * 1024 + ici * 128:8192 + kc * 1024 + ici * 128 + 128],
                    xq_b[:, kc * 512:(kc + 1) * 512],
                    start=(kc == 0), stop=(kc == KD - 1)) for kc in range(KD)][-1])(ici),
                    reads=[B_("wslabu"), B_("xq_b")], writes=[B_("p2ps_u")])
                k.v(lambda e: e.scalar_tensor_tensor(
                    gout[:, :], ps_g[:, :], mcol[:, 0:1], rsbc[:, :],
                    ALU.mult, ALU.mult),
                    reads=[B_("p2ps_g"), B_("mcol"), B_("rsbc")], writes=[B_("gout")])
                k.a(lambda e: e.activation(out=gout[:, :], in_=gout[:, :],
                                           func=AF.Silu),
                    reads=[B_("gout")], writes=[B_("gout")])
                k.v(lambda e: e.scalar_tensor_tensor(
                    uout[:, :], ps_u[:, :], mcol[:, 1:2], rsbc[:, :],
                    ALU.mult, ALU.mult),
                    reads=[B_("p2ps_u"), B_("mcol"), B_("rsbc")], writes=[B_("uout")])
                k.v(lambda e: e.tensor_tensor(prodc[:, :], gout[:, :], uout[:, :],
                                              ALU.mult),
                    reads=[B_("gout"), B_("uout")], writes=[B_("prodc")])
                em.dma("qsp", (lambda ic: lambda e: e.dma_start(
                    out=prod_hbm[ic * 128:(ic + 1) * 128, :], in_=prodc[:, :]))(ic),
                    reads=[B_("prodc")], writes=[B_("prod_hbm")])
                if ic == 0:
                    k.a(lambda e: e.activation(out=amax[:, :], in_=prodc[:, :],
                                               func=AF.Abs),
                        reads=[B_("prodc")], writes=[B_("amax")])
                else:
                    k.a(lambda e: e.activation(out=amt[:, :], in_=prodc[:, :],
                                               func=AF.Abs),
                        reads=[B_("prodc")], writes=[B_("amt")])
                    k.v(lambda e: e.tensor_tensor(amax[:, :], amax[:, :],
                                                  amt[:, :], ALU.max),
                        reads=[B_("amt"), B_("amax")], writes=[B_("amax")])
        pmax_to_rows(amax, B_("amax"), 512, qrow[0:1, 0:512], B_("qrow"), 512)
        k.v(lambda e: e.tensor_scalar(q_m, q_m, 1e-5, None, ALU.max),
            reads=[B_("qrow")], writes=[B_("qrow")])
        k.v(lambda e: e.reciprocal(q_sc, q_m), reads=[B_("qrow")], writes=[B_("qrow")])
        k.v(lambda e: e.tensor_scalar(q_sc, q_sc, 127.0, None, ALU.mult),
            reads=[B_("qrow")], writes=[B_("qrow")])
        k.v(lambda e: e.tensor_scalar(q_rs, q_m, 1.0 / 127.0, None, ALU.mult),
            reads=[B_("qrow")], writes=[B_("qrow")])
        k.p(lambda e: e.matmul(ps_bc[:, :], ones1f[:, :], q_sc, start=True, stop=True),
            reads=[B_("ones1f"), B_("qrow")], writes=[B_("p2ps_bc")])
        k.v(lambda e: e.tensor_copy(scbc[:, :], ps_bc[:, :]),
            reads=[B_("p2ps_bc")], writes=[B_("scbc")])
        k.p(lambda e: e.matmul(ps_bc[:, :], ones1f[:, :], q_rs, start=True, stop=True),
            reads=[B_("ones1f"), B_("qrow")], writes=[B_("p2ps_bc")])
        k.v(lambda e: e.tensor_copy(rsbc[:, :], ps_bc[:, :]),
            reads=[B_("p2ps_bc")], writes=[B_("rsbc")])
        for ic in range(32):
            em.dma("qsp", (lambda ic: lambda e: e.dma_start(
                out=prodc[:, :], in_=prod_hbm[ic * 128:(ic + 1) * 128, :]))(ic),
                reads=[B_("prod_hbm")], writes=[B_("prodc")])
            k.v(lambda e: e.tensor_tensor(prodc[:, :], prodc[:, :], scbc[:, :],
                                          ALU.mult),
                reads=[B_("prodc"), B_("scbc")], writes=[B_("prodc")])
            k.v(lambda e: e.tensor_scalar(prodc[:, :], prodc[:, :], magicC[:, 0:1],
                                          magicC[:, 0:1], ALU.add, ALU.subtract),
                reads=[B_("prodc"), B_("magicC")], writes=[B_("prodc")])
            k.v((lambda ic: lambda e: e.tensor_scalar(
                pq_sb[:, ic * 512:(ic + 1) * 512], prodc[:, :], 127.0, -128.0,
                ALU.min, ALU.max))(ic),
                reads=[B_("prodc")], writes=[B_("pq_sb")])
        for ocp in range(4):
            for ic in range(32):
                em.dma("qpool", (lambda ic: lambda e: e.dma_start(
                    out=wdsl[:, (ic % 2) * 1024:(ic % 2) * 1024 + 1024],
                    in_=wqd_hbm[ic * 128:(ic + 1) * 128, :]))(ic),
                    reads=[B_("wdq")], writes=[B_(f"wdsl{ic % 2}")])
                for oi in range(2):
                    oc = ocp * 2 + oi
                    pd, pdb = ps_dn[oi], B_(f"p2ps_dn{oi}")
                    k.p((lambda ic, oc, pd: lambda e: e.matmul(
                        pd[:, :], wdsl[:, (ic % 2) * 1024 + oc * 128:(ic % 2) * 1024 + oc * 128 + 128],
                        pq_sb[:, ic * 512:(ic + 1) * 512],
                        start=(ic == 0), stop=(ic == 31)))(ic, oc, pd),
                        reads=[B_(f"wdsl{ic % 2}"), B_("pq_sb")], writes=[pdb])
            for oi in range(2):
                oc = ocp * 2 + oi
                pd, pdb = ps_dn[oi], B_(f"p2ps_dn{oi}")
                k.v((lambda oc, pd: lambda e: e.scalar_tensor_tensor(
                    x2sb[:, :], pd[:, :], mcol[:, 2:3], rsbc[:, :],
                    ALU.mult, ALU.mult))(oc, pd),
                    reads=[pdb, B_("mcol"), B_("rsbc")], writes=[B_("x2sb")])
                k.v((lambda oc: lambda e: e.tensor_tensor(
                    x2sb[:, :], x2sb[:, :], x1c[:, oc * 512:(oc + 1) * 512],
                    ALU.add))(oc),
                    reads=[B_("x2sb"), B_("x1c")], writes=[B_("x2sb")])
                em.dma("qsp", (lambda oc, tc: lambda e: e.dma_start(
                    out=x2_hbm[oc * 128:(oc + 1) * 128, tc * 512:(tc + 1) * 512],
                    in_=x2sb[:, :]))(oc, tc),
                    reads=[B_("x2sb")], writes=[B_("x2_hbm")])
    k.pop()
    k.pop(barrier=False)
    env["x2_hbm"] = x2_hbm


def build_p3(k, env):
    em, B_ = k.em, k.B
    g = env
    onescol, ones1f = g["onescol"], g["ones1f"]
    x2_hbm = g["x2_hbm"]
    nc = k.nc

    def inp(name, shape, dt=F32):
        return nc.dram_tensor(name, list(shape), dt, kind="ExternalInput").ap()

    wx_h = inp("l1_wx_t", [D, D])
    wz_h = inp("l1_wz_t", [D, D])
    wp_h = inp("l1_wp_t", [D, 96])
    wdt_h = inp("l1_wdt_sl", [DTR, 512])
    wo_h = inp("l1_wo_t", [D, D])
    alog_h = inp("l1_alog_sl", [128, 4 * NST])
    bdt_h = inp("l1_bdt_sl", [128, 4])
    dp_h = inp("l1_dp_sl", [128, 4])
    convw_h = inp("l1_convw_sl", [128, 16])
    xpg_idx_h = inp("xpg_idx", [128, 8], I32)
    z_idx_h = inp("z_idx", [128, 64], I32)
    dtun_idx_h = inp("dtun_idx", [64, 2], I32)
    bc_idx_h = inp("bc_idx", [32, 2], I32)
    y_idx_h = inp("y_idx", [128, 64], I32)
    bchot_h = inp("bchot", [32, 32 * 128], BF16)

    ag2_in = k.dram("ag2_in", [1120, T])
    ag2_out = k.dram("ag2_out", [N_CORES * 1120, T])
    ag2z_in = k.dram("ag2z_in", [1024, T], BF16)
    ag2z_out = k.dram("ag2z_out", [N_CORES * 1024, T], BF16)
    ag3_in = k.dram("ag3_in", [512, TFULL], BF16)
    ag3_out = k.dram("ag3_out", [N_CORES * 512, TFULL], BF16)
    xp_loc = k.dram("xp_loc", [512, TFULL + 8])
    x3_hbm = k.dram("x3_hbm", [D, T])
    env["x3_hbm"] = x3_hbm

    k.push()
    x2c = k.sb("p3x2c", [128, KD * 512])
    xnb = k.sb("p3xn", [128, KD * 512], BF16)
    xsq3 = k.sb("p3xsq", [128, KD * 512])
    stat3 = k.sb("p3stat", [1, 4 * 512])
    mbc3 = k.sb("p3mbc", [128, 512])
    rbc3 = k.sb("p3rbc", [128, 512])
    wx_sb = k.sb("wx_sb", [128, KD * D], BF16)
    wz_sb = k.sb("wz_sb", [128, KD * D], BF16)
    wp_sb = k.sb("wp_sb", [128, KD * 96], BF16)
    prj = k.sb("prj", [128, 512])
    prjz = k.sb("prjz", [128, 512], BF16)
    ps3_s = k.ps("p3ps_s", [1, 512])
    ps3_b = k.ps("p3ps_b", [128, 512])
    ps3_m = [k.ps(f"p3ps_m{i}", [128, 512]) for i in range(2)]
    ps3_p = k.ps("p3ps_p", [96, 512])

    for kc in range(KD):
        em.dma("qpool", (lambda kc: lambda e: e.dma_start(
            out=wx_sb[:, kc * D:(kc + 1) * D], in_=wx_h[kc * 128:(kc + 1) * 128, :]))(kc),
            writes=[B_("wx_sb")])
        em.dma("qpool", (lambda kc: lambda e: e.dma_start(
            out=wz_sb[:, kc * D:(kc + 1) * D], in_=wz_h[kc * 128:(kc + 1) * 128, :]))(kc),
            writes=[B_("wz_sb")])
        em.dma("qpool", (lambda kc: lambda e: e.dma_start(
            out=wp_sb[:, kc * 96:(kc + 1) * 96], in_=wp_h[kc * 128:(kc + 1) * 128, :]))(kc),
            writes=[B_("wp_sb")])

    for tc in range(T // 512):
        em.dma("qsp", (lambda tc: lambda e: e.dma_start(
            out=x2c[:, :].rearrange("p (kc t) -> p kc t", kc=KD),
            in_=x2_hbm[:, tc * 512:(tc + 1) * 512].rearrange("(kc p) t -> p kc t", p=128)))(tc),
            reads=[B_("x2_hbm")], writes=[B_("p3x2c")])
        k.p(lambda e: [e.matmul(
            ps3_s[:, :], onescol[:, :], x2c[:, kc * 512:(kc + 1) * 512],
            start=(kc == 0), stop=(kc == KD - 1)) for kc in range(KD)][-1],
            reads=[B_("onescol"), B_("p3x2c")], writes=[B_("p3ps_s")])
        sm = stat3[0:1, 0:512]
        sv = stat3[0:1, 512:1024]
        sr = stat3[0:1, 1024:1536]
        s4 = stat3[0:1, 1536:2048]
        k.v(lambda e: e.tensor_scalar(sm, ps3_s[:, :], 1.0 / D, None, ALU.mult),
            reads=[B_("p3ps_s")], writes=[B_("p3stat")])
        k.a(lambda e: e.activation(out=xsq3[:, :], in_=x2c[:, :], func=AF.Square),
            reads=[B_("p3x2c")], writes=[B_("p3xsq")])
        k.p(lambda e: [e.matmul(
            ps3_s[:, :], onescol[:, :], xsq3[:, kc * 512:(kc + 1) * 512],
            start=(kc == 0), stop=(kc == KD - 1)) for kc in range(KD)][-1],
            reads=[B_("onescol"), B_("p3xsq")], writes=[B_("p3ps_s")])
        k.v(lambda e: e.tensor_scalar(sv, ps3_s[:, :], 1.0 / D, None, ALU.mult),
            reads=[B_("p3ps_s")], writes=[B_("p3stat")])
        k.v(lambda e: e.tensor_tensor(s4, sm, sm, ALU.mult),
            reads=[B_("p3stat")], writes=[B_("p3stat")])
        k.v(lambda e: e.tensor_tensor(sv, sv, s4, ALU.subtract),
            reads=[B_("p3stat")], writes=[B_("p3stat")])
        k.v(lambda e: e.tensor_scalar(sv, sv, 1e-5, None, ALU.add),
            reads=[B_("p3stat")], writes=[B_("p3stat")])
        rsqrt_rows(k, sr, sv, s4, (B_("p3stat"), B_("p3stat"), B_("p3stat")))
        k.p(lambda e: e.matmul(ps3_b[:, :], ones1f[:, :], sm, start=True, stop=True),
            reads=[B_("ones1f"), B_("p3stat")], writes=[B_("p3ps_b")])
        k.v(lambda e: e.tensor_copy(mbc3[:, :], ps3_b[:, :]),
            reads=[B_("p3ps_b")], writes=[B_("p3mbc")])
        k.p(lambda e: e.matmul(ps3_b[:, :], ones1f[:, :], sr, start=True, stop=True),
            reads=[B_("ones1f"), B_("p3stat")], writes=[B_("p3ps_b")])
        k.v(lambda e: e.tensor_copy(rbc3[:, :], ps3_b[:, :]),
            reads=[B_("p3ps_b")], writes=[B_("p3rbc")])
        for kc in range(KD):
            k.v((lambda kc: lambda e: e.tensor_tensor(
                xsq3[:, kc * 512:(kc + 1) * 512], x2c[:, kc * 512:(kc + 1) * 512],
                mbc3[:, :], ALU.subtract))(kc),
                reads=[B_("p3x2c"), B_("p3mbc")], writes=[B_("p3xsq")])
            k.v((lambda kc: lambda e: e.tensor_tensor(
                xnb[:, kc * 512:(kc + 1) * 512], xsq3[:, kc * 512:(kc + 1) * 512],
                rbc3[:, :], ALU.mult))(kc),
                reads=[B_("p3xsq"), B_("p3rbc")], writes=[B_("p3xn")])
        for wi, (w_sb, wbuf) in enumerate([(wx_sb, "wx_sb"), (wz_sb, "wz_sb")]):
            for oc in range(KD):
                pm, pmb = ps3_m[oc % 2], B_(f"p3ps_m{oc % 2}")
                k.p((lambda oc, w_sb, pm: lambda e: [e.matmul(
                    pm[:, :], w_sb[:, (kc * KD + oc) * 128:(kc * KD + oc) * 128 + 128],
                    xnb[:, kc * 512:(kc + 1) * 512],
                    start=(kc == 0), stop=(kc == KD - 1)) for kc in range(KD)][-1])(oc, w_sb, pm),
                    reads=[B_(wbuf), B_("p3xn")], writes=[pmb])
                if wi == 0:
                    k.v((lambda pm: lambda e: e.tensor_copy(prj[:, :], pm[:, :]))(pm),
                        reads=[pmb], writes=[B_("prj")])
                    em.dma("qsp", (lambda oc, tc: lambda e: e.dma_start(
                        out=ag2_in[oc * 128:oc * 128 + 128, tc * 512:(tc + 1) * 512],
                        in_=prj[:, :]))(oc, tc),
                        reads=[B_("prj")], writes=[B_("ag2_in")])
                else:
                    k.v((lambda pm: lambda e: e.tensor_copy(prjz[:, :], pm[:, :]))(pm),
                        reads=[pmb], writes=[B_("prjz")])
                    em.dma("qsp", (lambda oc, tc: lambda e: e.dma_start(
                        out=ag2z_in[oc * 128:oc * 128 + 128, tc * 512:(tc + 1) * 512],
                        in_=prjz[:, :]))(oc, tc),
                        reads=[B_("prjz")], writes=[B_("ag2z_in")])
        k.p(lambda e: [e.matmul(
            ps3_p[:, :], wp_sb[:, kc * 96:(kc + 1) * 96],
            xnb[:, kc * 512:(kc + 1) * 512],
            start=(kc == 0), stop=(kc == KD - 1)) for kc in range(KD)][-1],
            reads=[B_("wp_sb"), B_("p3xn")], writes=[B_("p3ps_p")])
        k.v(lambda e: e.tensor_copy(prj[0:96, :], ps3_p[:, :]),
            reads=[B_("p3ps_p")], writes=[B_("prj")])
        em.dma("qsp", (lambda tc: lambda e: e.dma_start(
            out=ag2_in[1024:1120, tc * 512:(tc + 1) * 512], in_=prj[0:96, :]))(tc),
            reads=[B_("prj")], writes=[B_("ag2_in")])
    k.pop()

    k.g(lambda e: e.collective_compute(
        "AllGather", ALU.bypass, replica_groups=[list(range(N_CORES))],
        ins=[ag2_in[:, :]], outs=[ag2_out[:, :]]),
        reads=[B_("ag2_in")], writes=[B_("ag2")])
    k.g(lambda e: e.collective_compute(
        "AllGather", ALU.bypass, replica_groups=[list(range(N_CORES))],
        ins=[ag2z_in[:, :]], outs=[ag2z_out[:, :]]),
        reads=[B_("ag2z_in")], writes=[B_("ag2")])

    k.push()
    xpg_idx = k.sb("xpg_idx_s", [128, 8], I32)
    z_idx = k.sb("z_idx_s", [128, 64], I32)
    dtun_idx = k.sb("dtun_idx_s", [64, 2], I32)
    bc_idx = k.sb("bc_idx_s", [32, 2], I32)
    alog = k.sb("alog_s", [128, 4 * NST])
    acol = k.sb("acol", [128, 4 * NST])
    invA = k.sb("invA", [128, 4 * NST])
    bdtc = k.sb("bdtc_s", [128, 4])
    dpc = k.sb("dpc_s", [128, 4])
    cwc = k.sb("cwc_s", [128, 16])
    wdt_sb = k.sb("wdt_sb", [64, 512], BF16)
    dtun_sb = k.sb("dtun_sb", [64, TFULL], BF16)
    bc_f = k.sb("bc_f", [32, TFULL])
    bc_sb = k.sb("bc_sb", [32, TFULL], BF16)
    bchot = k.sb("bchot_s", [32, 32 * 128], BF16)
    gath = k.sb("gath", [128, T])
    for dst, src, nm in [(xpg_idx, xpg_idx_h, "xpg_idx"), (z_idx, z_idx_h, "z_idx"),
                         (dtun_idx, dtun_idx_h, "dtun_idx"), (bc_idx, bc_idx_h, "bc_idx"),
                         (alog, alog_h, "alog"), (bdtc, bdt_h, "bdtc"),
                         (dpc, dp_h, "dpc"), (cwc, convw_h, "cwc"),
                         (bchot, bchot_h, "bchot")]:
        em.dma("qsp", (lambda d, s: lambda e: e.dma_start(out=d[:], in_=s[:, :]))(dst, src),
               writes=[B_(nm)])
    em.dma("qpool", lambda e: e.dma_start(out=wdt_sb[:, :], in_=wdt_h[:, :]),
           writes=[B_("wdt_sb")])
    k.a(lambda e: e.activation(out=acol[:, :], in_=alog[:, :], func=AF.Exp),
        reads=[B_("alog")], writes=[B_("acol")])
    k.v(lambda e: e.tensor_scalar(acol[:, :], acol[:, :], -1.0, None, ALU.mult),
        reads=[B_("acol")], writes=[B_("acol")])
    k.v(lambda e: e.tensor_scalar(invA[:, :], acol[:, :], 1e-8, None, ALU.add),
        reads=[B_("acol")], writes=[B_("invA")])
    k.v(lambda e: e.reciprocal(invA[:, :], invA[:, :]),
        reads=[B_("invA")], writes=[B_("invA")])
    for r01 in range(2):
        em.dma("qpool", (lambda r01: lambda e: e.indirect_dma_start(
            dtun_sb[:, r01 * T:(r01 + 1) * T], None, ag2_out[:, :],
            bass.IndirectOffsetOnAxis(dtun_idx[:, r01:r01 + 1], 0)))(r01),
            reads=[B_("dtun_idx"), B_("ag2")], writes=[B_("dtun_sb")])
        em.dma("qpool", (lambda r01: lambda e: e.indirect_dma_start(
            bc_f[:, r01 * T:(r01 + 1) * T], None, ag2_out[:, :],
            bass.IndirectOffsetOnAxis(bc_idx[:, r01:r01 + 1], 0)))(r01),
            reads=[B_("bc_idx"), B_("ag2")], writes=[B_("bc_f")])
        k.v((lambda r01: lambda e: e.tensor_copy(
            bc_sb[:, r01 * T:(r01 + 1) * T], bc_f[:, r01 * T:(r01 + 1) * T]))(r01),
            reads=[B_("bc_f")], writes=[B_("bc_sb")])
        for dc in range(4):
            em.dma("qpool", (lambda r01, dc: lambda e: e.indirect_dma_start(
                gath[:, :], None, ag2_out[:, :],
                bass.IndirectOffsetOnAxis(xpg_idx[:, dc * 2 + r01:dc * 2 + r01 + 1], 0)))(r01, dc),
                reads=[B_("xpg_idx"), B_("ag2")], writes=[B_("gath")])
            em.dma("qsp", (lambda r01, dc: lambda e: e.dma_start(
                out=xp_loc[dc * 128:(dc + 1) * 128, 8 + r01 * T:8 + (r01 + 1) * T],
                in_=gath[:, :]))(r01, dc),
                reads=[B_("gath")], writes=[B_("xp_loc")])
    zpad = k.sb("zpad", [128, 8])
    k.v(lambda e: e.memset(zpad[:, :], 0.0), writes=[B_("zpad")])
    em.dma("qsp", lambda e: e.dma_start(out=xp_loc[0:128, 0:8], in_=zpad[:, :]),
           reads=[B_("zpad")], writes=[B_("xp_loc")])
    em.dma("qsp", lambda e: e.dma_start(out=xp_loc[128:256, 0:8], in_=zpad[:, :]),
           reads=[B_("zpad")], writes=[B_("xp_loc")])
    em.dma("qsp", lambda e: e.dma_start(out=xp_loc[256:384, 0:8], in_=zpad[:, :]),
           reads=[B_("zpad")], writes=[B_("xp_loc")])
    em.dma("qsp", lambda e: e.dma_start(out=xp_loc[384:512, 0:8], in_=zpad[:, :]),
           reads=[B_("zpad")], writes=[B_("xp_loc")])

    xps = k.sb("xps", [128, 2 * 520])
    zs = k.sb("zs", [128, 2 * 512], BF16)
    dtf = k.sb("dtf", [128, 512])
    sp0 = k.sb("sp0", [128, 512])
    sp1 = k.sb("sp1", [128, 512])
    uf = k.sb("uf", [128, 512])
    a_f = k.sb("a_f", [128, 512])
    bu2 = k.sb("bu2", [128, 512])
    inpt = k.sb("inpt", [128, 512])
    hstr = k.sb("hstr", [128, 512])
    yacc = k.sb("yacc", [128, 512])
    ytmp = k.sb("ytmp", [128, 512])
    hcarry = k.sb("hcarry", [128, 4 * NST])
    bbc = k.sb("bbc", [128, NST * 512], BF16)
    cbc = k.sb("cbc", [128, NST * 512], BF16)
    ygb = k.sb("ygb", [128, 512], BF16)
    ps3_d = k.ps("p3ps_d", [128, 512])
    ps3_bb = k.ps("p3ps_bb", [128, 512])

    for tch in range(TFULL // 512):
        for n in range(NST):
            k.p((lambda n, tch: lambda e: e.matmul(
                ps3_bb[:, :], bchot[:, n * 128:(n + 1) * 128],
                bc_sb[:, tch * 512:(tch + 1) * 512],
                start=True, stop=True))(n, tch),
                reads=[B_("bchot"), B_("bc_sb")], writes=[B_("p3ps_bb")])
            k.v((lambda n: lambda e: e.tensor_copy(
                bbc[:, n * 512:(n + 1) * 512], ps3_bb[:, :]))(n),
                reads=[B_("p3ps_bb")], writes=[B_("bbc")])
            k.p((lambda n, tch: lambda e: e.matmul(
                ps3_bb[:, :], bchot[:, (NST + n) * 128:(NST + n + 1) * 128],
                bc_sb[:, tch * 512:(tch + 1) * 512],
                start=True, stop=True))(n, tch),
                reads=[B_("bchot"), B_("bc_sb")], writes=[B_("p3ps_bb")])
            k.v((lambda n: lambda e: e.tensor_copy(
                cbc[:, n * 512:(n + 1) * 512], ps3_bb[:, :]))(n),
                reads=[B_("p3ps_bb")], writes=[B_("cbc")])
        for dc in range(4):
            k.p((lambda dc, tch: lambda e: e.matmul(
                ps3_d[:, :], wdt_sb[:, dc * 128:(dc + 1) * 128],
                dtun_sb[:, tch * 512:(tch + 1) * 512], start=True, stop=True))(dc, tch),
                reads=[B_("wdt_sb"), B_("dtun_sb")], writes=[B_("p3ps_d")])
            k.v((lambda dc: lambda e: e.tensor_scalar(
                dtf[:, :], ps3_d[:, :], bdtc[:, dc:dc + 1], None, ALU.add))(dc),
                reads=[B_("p3ps_d"), B_("bdtc")], writes=[B_("dtf")])
            k.a(lambda e: e.activation(out=sp0[:, :], in_=dtf[:, :],
                                       func=AF.Abs),
                reads=[B_("dtf")], writes=[B_("sp0")])
            k.a(lambda e: e.activation(out=sp1[:, :], in_=sp0[:, :], func=AF.Exp,
                                       scale=-1.0),
                reads=[B_("sp0")], writes=[B_("sp1")])
            k.a(lambda e: e.activation(out=sp1[:, :], in_=sp1[:, :], func=AF.Ln,
                                       bias=1.0),
                reads=[B_("sp1")], writes=[B_("sp1")])
            k.v(lambda e: e.tensor_scalar(sp0[:, :], dtf[:, :], 0.0, None, ALU.max),
                reads=[B_("dtf")], writes=[B_("sp0")])
            k.v(lambda e: e.tensor_tensor(dtf[:, :], sp0[:, :], sp1[:, :], ALU.add),
                reads=[B_("sp0"), B_("sp1")], writes=[B_("dtf")])
            em.dma("qsp", (lambda dc, tch: lambda e: e.dma_start(
                out=xps[:, (tch % 2) * 520:(tch % 2) * 520 + 515],
                in_=xp_loc[dc * 128:(dc + 1) * 128, 5 + tch * 512:5 + tch * 512 + 515]))(dc, tch),
                reads=[B_("xp_loc")], writes=[B_(f"xps{tch % 2}")])
            xb0 = (tch % 2) * 520
            k.v((lambda dc, xb0: lambda e: e.tensor_scalar(
                uf[:, :], xps[:, xb0:xb0 + 512], cwc[:, dc * 4:dc * 4 + 1],
                None, ALU.mult))(dc, xb0),
                reads=[B_(f"xps{tch % 2}"), B_("cwc")], writes=[B_("uf")])
            for j in range(1, 4):
                k.v((lambda dc, j, xb0: lambda e: e.scalar_tensor_tensor(
                    uf[:, :], xps[:, xb0 + j:xb0 + j + 512],
                    cwc[:, dc * 4 + j:dc * 4 + j + 1], uf[:, :],
                    ALU.mult, ALU.add))(dc, j, xb0),
                    reads=[B_(f"xps{tch % 2}"), B_("cwc"), B_("uf")], writes=[B_("uf")])
            k.a(lambda e: e.activation(out=uf[:, :], in_=uf[:, :], func=AF.Silu),
                reads=[B_("uf")], writes=[B_("uf")])
            if tch == 0 and dc == 0:
                k.v(lambda e: e.memset(hcarry[:, :], 0.0), writes=[B_("hcarry")])
            for n in range(NST):
                an = dc * NST + n
                k.a((lambda an: lambda e: e.activation(
                    out=a_f[:, :], in_=dtf[:, :], func=AF.Exp,
                    scale=acol[:, an:an + 1]))(an),
                    reads=[B_("dtf"), B_("acol")], writes=[B_("a_f")])
                k.v((lambda an, n: lambda e: e.scalar_tensor_tensor(
                    bu2[:, :], uf[:, :], invA[:, an:an + 1],
                    bbc[:, n * 512:(n + 1) * 512], ALU.mult, ALU.mult))(an, n),
                    reads=[B_("uf"), B_("invA"), B_("bbc")], writes=[B_("bu2")])
                k.v(lambda e: e.scalar_tensor_tensor(
                    inpt[:, :], a_f[:, :], 1.0, bu2[:, :], ALU.subtract, ALU.mult),
                    reads=[B_("a_f"), B_("bu2")], writes=[B_("inpt")])
                k.v((lambda an: lambda e: e.tensor_tensor_scan(
                    hstr[:, :], a_f[:, :], inpt[:, :], hcarry[:, an:an + 1],
                    ALU.mult, ALU.add))(an),
                    reads=[B_("a_f"), B_("inpt"), B_("hcarry")], writes=[B_("hstr")])
                k.v((lambda an: lambda e: e.tensor_copy(
                    hcarry[:, an:an + 1], hstr[:, 511:512]))(an),
                    reads=[B_("hstr")], writes=[B_("hcarry")])
                if n == 0:
                    k.v((lambda n: lambda e: e.tensor_tensor(
                        yacc[:, :], hstr[:, :], cbc[:, n * 512:(n + 1) * 512],
                        ALU.mult))(n),
                        reads=[B_("hstr"), B_("cbc")], writes=[B_("yacc")])
                else:
                    k.g((lambda n: lambda e: e.tensor_tensor(
                        ytmp[:, :], hstr[:, :], cbc[:, n * 512:(n + 1) * 512],
                        ALU.mult))(n),
                        reads=[B_("hstr"), B_("cbc")], writes=[B_("ytmp")])
                    k.g(lambda e: e.tensor_tensor(yacc[:, :], yacc[:, :],
                                                  ytmp[:, :], ALU.add),
                        reads=[B_("ytmp"), B_("yacc")], writes=[B_("yacc")])
            em.dma("qpool", (lambda dc, tch: lambda e: e.indirect_dma_start(
                zs[:, (tch % 2) * 512:(tch % 2) * 512 + 512], None,
                ag2z_out[:, :].rearrange("r (c t) -> (r c) t", t=512),
                bass.IndirectOffsetOnAxis(z_idx[:, dc * 16 + tch:dc * 16 + tch + 1], 0)))(dc, tch),
                reads=[B_("z_idx"), B_("ag2")], writes=[B_(f"zs{tch % 2}")])
            k.v((lambda dc: lambda e: e.scalar_tensor_tensor(
                ytmp[:, :], uf[:, :], dpc[:, dc:dc + 1], yacc[:, :],
                ALU.mult, ALU.add))(dc),
                reads=[B_("uf"), B_("dpc"), B_("yacc")], writes=[B_("ytmp")])
            k.a((lambda tch: lambda e: e.activation(
                out=sp0[:, :], in_=zs[:, (tch % 2) * 512:(tch % 2) * 512 + 512],
                func=AF.Silu))(tch),
                reads=[B_(f"zs{tch % 2}")], writes=[B_("sp0")])
            k.v(lambda e: e.tensor_tensor(ygb[:, :], ytmp[:, :], sp0[:, :], ALU.mult),
                reads=[B_("ytmp"), B_("sp0")], writes=[B_("ygb")])
            em.dma("qsp", (lambda dc, tch: lambda e: e.dma_start(
                out=ag3_in[dc * 128:(dc + 1) * 128, tch * 512:(tch + 1) * 512],
                in_=ygb[:, :]))(dc, tch),
                reads=[B_("ygb")], writes=[B_("ag3_in")])
    k.pop()

    k.g(lambda e: e.collective_compute(
        "AllGather", ALU.bypass, replica_groups=[list(range(N_CORES))],
        ins=[ag3_in[:, :]], outs=[ag3_out[:, :]]),
        reads=[B_("ag3_in")], writes=[B_("ag3")])

    k.push()
    y_idx = k.sb("y_idx_s", [128, 64], I32)
    em.dma("qsp", lambda e: e.dma_start(out=y_idx[:, :], in_=y_idx_h[:, :]),
           writes=[B_("y_idx")])
    wo_sb = k.sb("wo_sb", [128, KD * D], BF16)
    ysl = k.sb("ysl", [128, KD * 512], BF16)
    x2c2 = k.sb("p3x2c2", [128, KD * 512])
    x3sb = k.sb("x3sb", [128, 512])
    ps3_o = [k.ps(f"p3ps_o{i}", [128, 512]) for i in range(2)]
    for kc in range(KD):
        em.dma("qpool", (lambda kc: lambda e: e.dma_start(
            out=wo_sb[:, kc * D:(kc + 1) * D], in_=wo_h[kc * 128:(kc + 1) * 128, :]))(kc),
            writes=[B_("wo_sb")])
    for tc in range(T // 512):
        for kc in range(KD):
            em.dma("qpool", (lambda kc, tc: lambda e: e.indirect_dma_start(
                ysl[:, kc * 512:(kc + 1) * 512], None,
                ag3_out[:, :].rearrange("r (c t) -> (r c) t", t=512),
                bass.IndirectOffsetOnAxis(y_idx[:, kc * 8 + tc:kc * 8 + tc + 1], 0)))(kc, tc),
                reads=[B_("y_idx"), B_("ag3")], writes=[B_("ysl")])
        em.dma("qsp", (lambda tc: lambda e: e.dma_start(
            out=x2c2[:, :].rearrange("p (kc t) -> p kc t", kc=KD),
            in_=x2_hbm[:, tc * 512:(tc + 1) * 512].rearrange("(kc p) t -> p kc t", p=128)))(tc),
            reads=[B_("x2_hbm")], writes=[B_("p3x2c2")])
        for oc in range(KD):
            po, pob = ps3_o[oc % 2], B_(f"p3ps_o{oc % 2}")
            k.p((lambda oc, po: lambda e: [e.matmul(
                po[:, :], wo_sb[:, (kc * KD + oc) * 128:(kc * KD + oc) * 128 + 128],
                ysl[:, kc * 512:(kc + 1) * 512],
                start=(kc == 0), stop=(kc == KD - 1)) for kc in range(KD)][-1])(oc, po),
                reads=[B_("wo_sb"), B_("ysl")], writes=[pob])
            k.v((lambda oc, po: lambda e: e.tensor_tensor(
                x3sb[:, :], po[:, :], x2c2[:, oc * 512:(oc + 1) * 512], ALU.add))(oc, po),
                reads=[pob, B_("p3x2c2")], writes=[B_("x3sb")])
            em.dma("qsp", (lambda oc, tc: lambda e: e.dma_start(
                out=x3_hbm[oc * 128:(oc + 1) * 128, tc * 512:(tc + 1) * 512],
                in_=x3sb[:, :]))(oc, tc),
                reads=[B_("x3sb")], writes=[B_("x3_hbm")])
    k.pop(barrier=False)


def build_p4(k, env):
    em, B_ = k.em, k.B
    g = env
    onescol, ones1f = g["onescol"], g["ones1f"]
    identf, identb = g["identf"], g["identb"]
    x3_hbm = g["x3_hbm"]
    out_h = g["out_h"]
    nc = k.nc

    def inp(name, shape, dt=F32):
        return nc.dram_tensor(name, list(shape), dt, kind="ExternalInput").ap()

    rw_h = inp("l1_rw", [D, NEXP])
    rb_h = inp("l1_rb_col", [NEXP, 1])
    wfc_h = inp("l1_wfc", [NEXP, D, INNER])
    wgt_h = inp("l1_wgt", [NEXP, D, INNER])
    wcp_h = inp("l1_wcp", [NEXP, INNER, D])
    bfc_h = inp("l1_bfc", [NEXP, INNER])
    bgt_h = inp("l1_bgt", [NEXP, INNER])
    bcp_h = inp("l1_bcp", [NEXP, D])
    eiota_h = inp("eiota", [NEXP, 1])
    ebase_h = inp("ebase", [NEXP, 1])
    ebase_i_h = inp("ebase_i", [NEXP, 1], I32)
    hsel8_h = inp("hsel8", [NEXP, 1])
    cnt_idx_h = inp("cnt_idx", [2, 1], I32)
    cand_idx_h = inp("cand_idx", [NEXP, 1], I32)
    tok8_h = inp("tok8", [NEXP, T])
    trash_h = inp("trashrow", [1, 2 * T])

    ag4_in = k.dram("ag4_in", [1, 16])
    ag4_out = k.dram("ag4_out", [N_CORES, 16])
    ar5_in = k.dram("ar5_in", [32, D])
    ar5_out = k.dram("ar5_out", [32, D])
    xt3_hbm = k.dram("xt3_hbm", [T, D], BF16)
    einp_hbm = k.dram("einp_hbm", [SROWS, D], BF16)
    sout_hbm = k.dram("sout_hbm", [TRASH, D])
    acth_hbm = k.dram("acth_hbm", [32, 128, ESL], BF16)
    scr_er = k.dram("scr_er", [1, 2 * T])
    scr_oh = k.dram("scr_oh", [2, NEXP, T])
    scr_cum = k.dram("scr_cum", [2, NEXP, T])
    scr_sg = k.dram("scr_sg", [1, 4 * T])
    scr_cg = k.dram("scr_cg", [1, 32])

    rt_tm = k.sb("rt_tm", [128, NT * 8])
    zt4 = k.sb("p4zt", [128, D], BF16)
    ZB = [B_(f"einp_z{i}") for i in range(3)]

    k.v(lambda e: e.memset(zt4[:, :], 0.0), writes=[B_("p4zt")])
    for i in range(SROWS // 128):
        em.dma(DMA_QUEUES[i % 3], (lambda i: lambda e: e.dma_start(
            out=einp_hbm[i * 128:(i + 1) * 128, :], in_=zt4[:, :]))(i),
            reads=[B_("p4zt")], writes=[B_(f"einp_z{i % 3}")])

    k.push()
    x3c = k.sb("p4x3c", [128, KD * 512])
    xn4 = k.sb("p4xn", [128, KD * 512], BF16)
    xsq4 = k.sb("p4xsq", [128, KD * 512])
    stat4 = k.sb("p4stat", [1, 4 * 512])
    mbc4 = k.sb("p4mbc", [128, 512])
    rbc4 = k.sb("p4rbc", [128, 512])
    rw_sb = k.sb("rw_sb", [128, KD * NEXP], BF16)
    rbcol = k.sb("rbcol", [NEXP, 1])
    logit = k.sb("logit", [NEXP, T])
    xt3t = k.sb("xt3t", [128, D], BF16)
    top8 = k.sb("top8", [128, 8])
    idx8 = k.sb("idx8", [128, 8], U32)
    prow = k.sb("prow", [128, 8])
    ecols = k.sb("ecols", [128, 2 * NT])
    etr = k.sb("etr", [NT, 128])
    ps4_s = k.ps("p4ps_s", [1, 512])
    ps4_b = k.ps("p4ps_b", [128, 512])
    ps4_l = k.ps("p4ps_l", [NEXP, 512])
    ps4_tb = [k.ps(f"p4ps_tb{i}", [128, 128], BF16) for i in range(2)]
    ps4_tf = k.ps("p4ps_tf", [128, 128])

    em.dma("qsp", lambda e: e.dma_start(out=rbcol[:, :], in_=rb_h[:, :]),
           writes=[B_("rbcol")])
    for kc in range(KD):
        em.dma("qpool", (lambda kc: lambda e: e.dma_start(
            out=rw_sb[:, kc * NEXP:(kc + 1) * NEXP],
            in_=rw_h[kc * 128:(kc + 1) * 128, :]))(kc), writes=[B_("rw_sb")])

    for tc in range(T // 512):
        em.dma("qsp", (lambda tc: lambda e: e.dma_start(
            out=x3c[:, :].rearrange("p (kc t) -> p kc t", kc=KD),
            in_=x3_hbm[:, tc * 512:(tc + 1) * 512].rearrange(
                "(kc p) t -> p kc t", p=128)))(tc),
            reads=[B_("x3_hbm")], writes=[B_("p4x3c")])
        k.p(lambda e: [e.matmul(
            ps4_s[:, :], onescol[:, :], x3c[:, kc * 512:(kc + 1) * 512],
            start=(kc == 0), stop=(kc == KD - 1)) for kc in range(KD)][-1],
            reads=[B_("onescol"), B_("p4x3c")], writes=[B_("p4ps_s")])
        sm = stat4[0:1, 0:512]
        sv = stat4[0:1, 512:1024]
        sr = stat4[0:1, 1024:1536]
        s4 = stat4[0:1, 1536:2048]
        k.v(lambda e: e.tensor_scalar(sm, ps4_s[:, :], 1.0 / D, None, ALU.mult),
            reads=[B_("p4ps_s")], writes=[B_("p4stat")])
        k.a(lambda e: e.activation(out=xsq4[:, :], in_=x3c[:, :], func=AF.Square),
            reads=[B_("p4x3c")], writes=[B_("p4xsq")])
        k.p(lambda e: [e.matmul(
            ps4_s[:, :], onescol[:, :], xsq4[:, kc * 512:(kc + 1) * 512],
            start=(kc == 0), stop=(kc == KD - 1)) for kc in range(KD)][-1],
            reads=[B_("onescol"), B_("p4xsq")], writes=[B_("p4ps_s")])
        k.v(lambda e: e.tensor_scalar(sv, ps4_s[:, :], 1.0 / D, None, ALU.mult),
            reads=[B_("p4ps_s")], writes=[B_("p4stat")])
        k.v(lambda e: e.tensor_tensor(s4, sm, sm, ALU.mult),
            reads=[B_("p4stat")], writes=[B_("p4stat")])
        k.v(lambda e: e.tensor_tensor(sv, sv, s4, ALU.subtract),
            reads=[B_("p4stat")], writes=[B_("p4stat")])
        k.v(lambda e: e.tensor_scalar(sv, sv, 1e-5, None, ALU.add),
            reads=[B_("p4stat")], writes=[B_("p4stat")])
        rsqrt_rows(k, sr, sv, s4, (B_("p4stat"), B_("p4stat"), B_("p4stat")))
        k.p(lambda e: e.matmul(ps4_b[:, :], ones1f[:, :], sm, start=True, stop=True),
            reads=[B_("ones1f"), B_("p4stat")], writes=[B_("p4ps_b")])
        k.v(lambda e: e.tensor_copy(mbc4[:, :], ps4_b[:, :]),
            reads=[B_("p4ps_b")], writes=[B_("p4mbc")])
        k.p(lambda e: e.matmul(ps4_b[:, :], ones1f[:, :], sr, start=True, stop=True),
            reads=[B_("ones1f"), B_("p4stat")], writes=[B_("p4ps_b")])
        k.v(lambda e: e.tensor_copy(rbc4[:, :], ps4_b[:, :]),
            reads=[B_("p4ps_b")], writes=[B_("p4rbc")])
        for kc in range(KD):
            k.v((lambda kc: lambda e: e.tensor_tensor(
                xsq4[:, kc * 512:(kc + 1) * 512], x3c[:, kc * 512:(kc + 1) * 512],
                mbc4[:, :], ALU.subtract))(kc),
                reads=[B_("p4x3c"), B_("p4mbc")], writes=[B_("p4xsq")])
            k.v((lambda kc: lambda e: e.tensor_tensor(
                xn4[:, kc * 512:(kc + 1) * 512], xsq4[:, kc * 512:(kc + 1) * 512],
                rbc4[:, :], ALU.mult))(kc),
                reads=[B_("p4xsq"), B_("p4rbc")], writes=[B_("p4xn")])
        k.p(lambda e: [e.matmul(
            ps4_l[:, :], rw_sb[:, kc * NEXP:(kc + 1) * NEXP],
            xn4[:, kc * 512:(kc + 1) * 512],
            start=(kc == 0), stop=(kc == KD - 1)) for kc in range(KD)][-1],
            reads=[B_("rw_sb"), B_("p4xn")], writes=[B_("p4ps_l")])
        k.v((lambda tc: lambda e: e.tensor_scalar(
            logit[:, tc * 512:(tc + 1) * 512], ps4_l[:, :], rbcol[:, 0:1],
            None, ALU.add))(tc),
            reads=[B_("p4ps_l"), B_("rbcol")], writes=[B_("logit")])
        for ti in range(4):
            tile = tc * 4 + ti
            for kc in range(KD):
                pt, ptb = ps4_tb[kc % 2], B_(f"p4ps_tb{kc % 2}")
                k.p((lambda kc, ti, pt: lambda e: e.transpose(
                    pt[:, :], xn4[:, kc * 512 + ti * 128:kc * 512 + ti * 128 + 128],
                    identb[:, :]))(kc, ti, pt),
                    reads=[B_("p4xn"), B_("identb")], writes=[ptb])
                k.v((lambda kc, pt, ptb: lambda e: e.tensor_copy(
                    xt3t[:, kc * 128:(kc + 1) * 128], pt[:, :]))(kc, pt, ptb),
                    reads=[ptb], writes=[B_("xt3t")])
            em.dma("qsp", (lambda tile: lambda e: e.dma_start(
                out=xt3_hbm[tile * 128:(tile + 1) * 128, :], in_=xt3t[:, :]))(tile),
                reads=[B_("xt3t")], writes=[B_("xt3_hbm")])

    for tile in range(NT):
        k.p((lambda tile: lambda e: e.transpose(
            ps4_tf[0:128, 0:NEXP], logit[:, tile * 128:(tile + 1) * 128],
            identf[0:NEXP, 0:NEXP]))(tile),
            reads=[B_("logit"), B_("identf")], writes=[B_("p4ps_tf")])
        k.v(lambda e: e.tensor_copy(prow[:, :], ps4_tf[0:128, 0:NEXP]),
            reads=[B_("p4ps_tf")], writes=[B_("prow")])
        k.v(lambda e: e.max(top8[:, :], prow[:, :]),
            reads=[B_("prow")], writes=[B_("top8")])
        k.v(lambda e: e.max_index(idx8[:, :], top8[:, :], prow[:, :]),
            reads=[B_("top8"), B_("prow")], writes=[B_("idx8")])
        rb_ = tile * 8
        k.v((lambda rb_: lambda e: e.tensor_copy(
            rt_tm[:, rb_:rb_ + 2], idx8[:, 0:2]))(rb_),
            reads=[B_("idx8")], writes=[B_("rt_tm")])
        k.v(lambda e: e.tensor_tensor(prow[:, 2:3], top8[:, 1:2], top8[:, 0:1],
                                      ALU.subtract),
            reads=[B_("top8")], writes=[B_("prow")])
        k.a(lambda e: e.activation(out=prow[:, 3:4], in_=prow[:, 2:3], func=AF.Exp),
            reads=[B_("prow")], writes=[B_("prow")])
        k.v(lambda e: e.tensor_scalar(prow[:, 4:5], prow[:, 3:4], 1.0, None,
                                      ALU.add),
            reads=[B_("prow")], writes=[B_("prow")])
        k.v(lambda e: e.reciprocal(prow[:, 4:5], prow[:, 4:5]),
            reads=[B_("prow")], writes=[B_("prow")])
        k.v((lambda rb_: lambda e: e.tensor_copy(
            rt_tm[:, rb_ + 2:rb_ + 3], prow[:, 4:5]))(rb_),
            reads=[B_("prow")], writes=[B_("rt_tm")])
        k.v((lambda rb_: lambda e: e.tensor_tensor(
            rt_tm[:, rb_ + 3:rb_ + 4], prow[:, 3:4], prow[:, 4:5], ALU.mult))(rb_),
            reads=[B_("prow")], writes=[B_("rt_tm")])

    for kk in range(2):
        for tile in range(NT):
            k.v((lambda kk, tile: lambda e: e.tensor_copy(
                ecols[:, kk * NT + tile:kk * NT + tile + 1],
                rt_tm[:, tile * 8 + kk:tile * 8 + kk + 1]))(kk, tile),
                reads=[B_("rt_tm")], writes=[B_("ecols")])
        k.p((lambda kk: lambda e: e.transpose(
            ps4_tf[0:NT, :], ecols[:, kk * NT:(kk + 1) * NT], identf[:, :]))(kk),
            reads=[B_("ecols"), B_("identf")], writes=[B_("p4ps_tf")])
        k.v((lambda kk: lambda e: e.tensor_copy(etr[:, :], ps4_tf[0:NT, :]))(kk),
            reads=[B_("p4ps_tf")], writes=[B_("etr")])
        em.dma("qsp", (lambda kk: lambda e: e.dma_start(
            out=scr_er[0:1, kk * T:(kk + 1) * T].rearrange(
                "o (w c) -> (o w) c", c=128),
            in_=etr[:, :]))(kk),
            reads=[B_("etr")], writes=[B_("scr_er")])
    k.pop()

    k.push()
    erc = k.sb("erc", [1, 512])
    one8c = k.sb("one8c", [NEXP, 512])
    ohc = k.sb("ohc", [NEXP, 512])
    cumc = k.sb("cumc", [NEXP, 512])
    carry = k.sb("p4carry", [NEXP, 2])
    carryK = k.sb("p4carryK", [NEXP, 2])
    eiota = k.sb("eiota_s", [NEXP, 1])
    ebase = k.sb("ebase_s", [NEXP, 1])
    hsel8 = k.sb("hsel8_s", [NEXP, 1])
    ag4row = k.sb("ag4row", [1, 16])
    cg2 = k.sb("cg2", [2, 16])
    crow = k.sb("crow", [1, 32])
    cnt_idx = k.sb("cnt_idx_s", [2, 1], I32)
    ccol = k.sb("ccol", [NEXP, 4])
    occ = k.sb("occ", [NEXP, 4])
    tokc = k.sb("tokc", [NEXP, 512])
    trc = k.sb("trc", [1, 512])
    posc = k.sb("posc", [NEXP, 512])
    kpc = k.sb("kpc", [NEXP, 512])
    ohkc = k.sb("ohkc", [NEXP, 512])
    ohk2 = k.sb("ohk2", [NEXP, 512])
    krkc = k.sb("krkc", [NEXP, 512])
    m8 = k.sb("m8", [NEXP, 512])
    ctc = k.sb("ctc", [NEXP, 4])
    rowsb = k.sb("p4rows", [1, 6 * 512])
    ps8 = k.ps("p4ps8", [NEXP, 512])
    ps1x = k.ps("p4ps1x", [1, 16])
    ps1a = k.ps("p4ps1a", [1, 512])
    ps_cc = k.ps("p4ps_cc", [32, 1])

    for dst, src, nm in [(eiota, eiota_h, "eiota_s"), (ebase, ebase_h, "ebase_s"),
                         (hsel8, hsel8_h, "hsel8_s"),
                         (cnt_idx, cnt_idx_h, "cnt_idx_s")]:
        em.dma("qsp", (lambda d, s: lambda e: e.dma_start(out=d[:, :], in_=s[:, :]))(dst, src),
               writes=[B_(nm)])
    k.v(lambda e: e.memset(one8c[:, :], 1.0), writes=[B_("one8c")])
    k.v(lambda e: e.memset(carry[:, :], 0.0), writes=[B_("p4carry")])
    k.v(lambda e: e.memset(carryK[:, :], 0.0), writes=[B_("p4carryK")])
    k.v(lambda e: e.memset(ctc[:, :], 0.0), writes=[B_("ctc")])

    for kk in range(2):
        for c8 in range(8):
            sl = slice(c8 * 512, (c8 + 1) * 512)
            em.dma("qsp", (lambda kk, sl: lambda e: e.dma_start(
                out=erc[:, :], in_=scr_er[0:1, kk * T + sl.start:kk * T + sl.stop]))(kk, sl),
                reads=[B_("scr_er")], writes=[B_("erc")])
            k.p(lambda e: e.matmul(ps8[:, :], ones1f[0:1, 0:NEXP], erc[:, :],
                                   start=True, stop=True),
                reads=[B_("ones1f"), B_("erc")], writes=[B_("p4ps8")])
            k.v(lambda e: e.tensor_scalar(ohc[:, :], ps8[:, :], eiota[:, 0:1],
                                          None, ALU.is_equal),
                reads=[B_("p4ps8"), B_("eiota_s")], writes=[B_("ohc")])
            em.dma("qact", (lambda kk, sl: lambda e: e.dma_start(
                out=scr_oh[kk, :, sl], in_=ohc[:, :]))(kk, sl),
                reads=[B_("ohc")], writes=[B_("scr_oh")])
            k.v((lambda kk: lambda e: e.tensor_tensor_scan(
                cumc[:, :], one8c[:, :], ohc[:, :], carry[:, kk:kk + 1],
                ALU.mult, ALU.add))(kk),
                reads=[B_("one8c"), B_("ohc"), B_("p4carry")], writes=[B_("cumc")])
            k.v((lambda kk: lambda e: e.tensor_copy(carry[:, kk:kk + 1],
                                                    cumc[:, 511:512]))(kk),
                reads=[B_("cumc")], writes=[B_("p4carry")])
            em.dma("qact", (lambda kk, sl: lambda e: e.dma_start(
                out=scr_cum[kk, :, sl], in_=cumc[:, :]))(kk, sl),
                reads=[B_("cumc")], writes=[B_("scr_cum")])
    for kk in range(2):
        k.p((lambda kk: lambda e: e.transpose(
            ps1x[0:1, 0:NEXP], carry[:, kk:kk + 1], identf[0:NEXP, 0:NEXP]))(kk),
            reads=[B_("p4carry"), B_("identf")], writes=[B_("p4ps1x")])
        k.v((lambda kk: lambda e: e.tensor_copy(
            ag4row[0:1, kk * NEXP:(kk + 1) * NEXP], ps1x[0:1, 0:NEXP]))(kk),
            reads=[B_("p4ps1x")], writes=[B_("ag4row")])
    em.dma("qsp", lambda e: e.dma_start(out=ag4_in[:, :], in_=ag4row[:, :]),
           reads=[B_("ag4row")], writes=[B_("ag4_in")])
    k.g(lambda e: e.collective_compute(
        "AllGather", ALU.bypass, replica_groups=[list(range(N_CORES))],
        ins=[ag4_in[:, :]], outs=[ag4_out[:, :]]),
        reads=[B_("ag4_in")], writes=[B_("ag4")])

    em.dma("qpool", lambda e: e.indirect_dma_start(
        cg2[:, :], None, ag4_out[:, :],
        bass.IndirectOffsetOnAxis(cnt_idx[:, 0:1], 0)),
        reads=[B_("cnt_idx_s"), B_("ag4")], writes=[B_("cg2")])
    em.dma("qsp", lambda e: e.dma_start(
        out=scr_cg[0:1, :].rearrange("o (r c) -> (o r) c", c=16), in_=cg2[:, :]),
        reads=[B_("cg2")], writes=[B_("scr_cg")])
    em.dma("qsp", lambda e: e.dma_start(out=crow[:, :], in_=scr_cg[0:1, :]),
        reads=[B_("scr_cg")], writes=[B_("crow")])
    for s in range(4):
        k.p((lambda s: lambda e: e.transpose(
            ps_cc[0:NEXP, 0:1], crow[0:1, s * 8:(s + 1) * 8],
            identf[0:1, 0:1]))(s),
            reads=[B_("crow"), B_("identf")], writes=[B_("p4ps_cc")])
        k.v((lambda s: lambda e: e.tensor_copy(ccol[:, s:s + 1],
                                               ps_cc[0:NEXP, 0:1]))(s),
            reads=[B_("p4ps_cc")], writes=[B_("ccol")])
    k.v(lambda e: e.tensor_scalar(occ[:, 0:1], ccol[:, 0:1], hsel8[:, 0:1], 1.0,
                                  ALU.mult, ALU.subtract),
        reads=[B_("ccol"), B_("hsel8_s")], writes=[B_("occ")])
    k.v(lambda e: e.tensor_scalar(occ[:, 2:3], ccol[:, 1:2], hsel8[:, 0:1], None,
                                  ALU.mult),
        reads=[B_("ccol"), B_("hsel8_s")], writes=[B_("occ")])
    k.v(lambda e: e.tensor_tensor(occ[:, 2:3], occ[:, 2:3], ccol[:, 0:1], ALU.add),
        reads=[B_("occ"), B_("ccol")], writes=[B_("occ")])
    k.v(lambda e: e.tensor_tensor(occ[:, 2:3], occ[:, 2:3], ccol[:, 2:3], ALU.add),
        reads=[B_("occ"), B_("ccol")], writes=[B_("occ")])
    k.v(lambda e: e.tensor_scalar(occ[:, 1:2], occ[:, 2:3], 1.0, None,
                                  ALU.subtract),
        reads=[B_("occ")], writes=[B_("occ")])

    sr_ = rowsb[0:1, 0:512]
    fr_ = rowsb[0:1, 512:1024]
    ebr_ = rowsb[0:1, 1024:1536]
    nf_ = rowsb[0:1, 1536:2048]
    t1_ = rowsb[0:1, 2048:2560]
    t2_ = rowsb[0:1, 2560:3072]
    for kk in range(2):
        for c8 in range(8):
            sl = slice(c8 * 512, (c8 + 1) * 512)
            em.dma("qsp", (lambda kk, sl: lambda e: e.dma_start(
                out=ohc[:, :], in_=scr_oh[kk, :, sl]))(kk, sl),
                reads=[B_("scr_oh")], writes=[B_("ohc")])
            em.dma("qsp", (lambda kk, sl: lambda e: e.dma_start(
                out=cumc[:, :], in_=scr_cum[kk, :, sl]))(kk, sl),
                reads=[B_("scr_cum")], writes=[B_("cumc")])
            em.dma("qact", (lambda sl: lambda e: e.dma_start(
                out=tokc[:, :], in_=tok8_h[:, sl]))(sl),
                writes=[B_("tokc")])
            em.dma("qact", (lambda kk, sl: lambda e: e.dma_start(
                out=trc[:, :],
                in_=trash_h[0:1, kk * T + sl.start:kk * T + sl.stop]))(kk, sl),
                writes=[B_("trc")])
            k.v((lambda kk: lambda e: e.tensor_scalar(
                posc[:, :], cumc[:, :], occ[:, kk:kk + 1], None, ALU.add))(kk),
                reads=[B_("cumc"), B_("occ")], writes=[B_("posc")])
            k.v(lambda e: e.tensor_scalar(kpc[:, :], posc[:, :], CAP - 0.5, None,
                                          ALU.is_le),
                reads=[B_("posc")], writes=[B_("kpc")])
            k.v(lambda e: e.tensor_tensor(ohkc[:, :], ohc[:, :], kpc[:, :],
                                          ALU.mult),
                reads=[B_("ohc"), B_("kpc")], writes=[B_("ohkc")])
            k.v(lambda e: e.tensor_tensor_scan(
                krkc[:, :], one8c[:, :], ohkc[:, :], carryK[:, 0:1],
                ALU.mult, ALU.add),
                reads=[B_("one8c"), B_("ohkc"), B_("p4carryK")], writes=[B_("krkc")])
            k.v(lambda e: e.tensor_copy(carryK[:, 0:1], krkc[:, 511:512]),
                reads=[B_("krkc")], writes=[B_("p4carryK")])
            k.v(lambda e: e.tensor_scalar(kpc[:, :], krkc[:, :], ESL - 0.5, None,
                                          ALU.is_le),
                reads=[B_("krkc")], writes=[B_("kpc")])
            k.v(lambda e: e.tensor_tensor(ohk2[:, :], ohkc[:, :], kpc[:, :],
                                          ALU.mult),
                reads=[B_("ohkc"), B_("kpc")], writes=[B_("ohk2")])
            k.v(lambda e: e.tensor_scalar(kpc[:, :], posc[:, :], CAP - 1.0, None,
                                          ALU.is_equal),
                reads=[B_("posc")], writes=[B_("kpc")])
            k.v(lambda e: e.tensor_tensor(kpc[:, :], ohc[:, :], kpc[:, :],
                                          ALU.mult),
                reads=[B_("ohc"), B_("kpc")], writes=[B_("kpc")])
            k.v(lambda e: e.tensor_tensor(m8[:, :], kpc[:, :], tokc[:, :],
                                          ALU.mult),
                reads=[B_("kpc"), B_("tokc")], writes=[B_("m8")])
            k.v(lambda e: e.tensor_reduce(ctc[:, 2:3], m8[:, :], AXX, ALU.add),
                reads=[B_("m8")], writes=[B_("ctc")])
            k.v(lambda e: e.tensor_reduce(ctc[:, 3:4], kpc[:, :], AXX, ALU.add),
                reads=[B_("kpc")], writes=[B_("ctc")])
            k.v(lambda e: e.tensor_tensor(ctc[:, 0:1], ctc[:, 0:1], ctc[:, 2:3],
                                          ALU.add),
                reads=[B_("ctc")], writes=[B_("ctc")])
            k.v(lambda e: e.tensor_tensor(ctc[:, 1:2], ctc[:, 1:2], ctc[:, 3:4],
                                          ALU.add),
                reads=[B_("ctc")], writes=[B_("ctc")])
            k.v(lambda e: e.tensor_scalar(m8[:, :], krkc[:, :], ebase[:, 0:1],
                                          None, ALU.add),
                reads=[B_("krkc"), B_("ebase_s")], writes=[B_("m8")])
            k.v(lambda e: e.tensor_tensor(m8[:, :], ohk2[:, :], m8[:, :],
                                          ALU.mult),
                reads=[B_("ohk2"), B_("m8")], writes=[B_("m8")])
            k.p(lambda e: e.matmul(ps1a[:, :], onescol[0:NEXP, 0:1], m8[:, :],
                                   start=True, stop=True),
                reads=[B_("onescol"), B_("m8")], writes=[B_("p4ps1a")])
            k.v(lambda e: e.tensor_copy(sr_, ps1a[:, :]),
                reads=[B_("p4ps1a")], writes=[B_("p4rows")])
            k.p(lambda e: e.matmul(ps1a[:, :], onescol[0:NEXP, 0:1], ohk2[:, :],
                                   start=True, stop=True),
                reads=[B_("onescol"), B_("ohk2")], writes=[B_("p4ps1a")])
            k.v(lambda e: e.tensor_copy(fr_, ps1a[:, :]),
                reads=[B_("p4ps1a")], writes=[B_("p4rows")])
            k.v(lambda e: e.tensor_scalar(m8[:, :], ohc[:, :], ebase[:, 0:1],
                                          None, ALU.mult),
                reads=[B_("ohc"), B_("ebase_s")], writes=[B_("m8")])
            k.p(lambda e: e.matmul(ps1a[:, :], onescol[0:NEXP, 0:1], m8[:, :],
                                   start=True, stop=True),
                reads=[B_("onescol"), B_("m8")], writes=[B_("p4ps1a")])
            k.v(lambda e: e.tensor_copy(ebr_, ps1a[:, :]),
                reads=[B_("p4ps1a")], writes=[B_("p4rows")])
            k.v(lambda e: e.tensor_scalar(nf_, fr_, -1.0, 1.0, ALU.mult, ALU.add),
                reads=[B_("p4rows")], writes=[B_("p4rows")])
            k.v(lambda e: e.tensor_tensor(t1_, nf_, trc[:, :], ALU.mult),
                reads=[B_("p4rows"), B_("trc")], writes=[B_("p4rows")])
            k.v(lambda e: e.tensor_tensor(t1_, sr_, t1_, ALU.add),
                reads=[B_("p4rows")], writes=[B_("p4rows")])
            em.dma("qsp", (lambda kk, sl: lambda e: e.dma_start(
                out=scr_sg[0:1, kk * T + sl.start:kk * T + sl.stop], in_=t1_))(kk, sl),
                reads=[B_("p4rows")], writes=[B_("scr_sg")])
            k.v(lambda e: e.tensor_tensor(t2_, nf_, ebr_, ALU.mult),
                reads=[B_("p4rows")], writes=[B_("p4rows")])
            k.v(lambda e: e.tensor_tensor(t2_, sr_, t2_, ALU.add),
                reads=[B_("p4rows")], writes=[B_("p4rows")])
            em.dma("qsp", (lambda kk, sl: lambda e: e.dma_start(
                out=scr_sg[0:1, 2 * T + kk * T + sl.start:2 * T + kk * T + sl.stop],
                in_=t2_))(kk, sl),
                reads=[B_("p4rows")], writes=[B_("scr_sg")])

    cti = k.sb("cti", [NEXP, 1], I32)
    cand_idx = k.sb("cand_idx_s", [NEXP, 1], I32)
    ebase_i = k.sb("ebase_i_s", [NEXP, 1], I32)
    cand_b = k.sb("cand_b", [NEXP, D], BF16)
    cand_f = k.sb("cand_f", [NEXP, D])
    corr_f = k.sb("corr_f", [NEXP, D])
    corr_b = k.sb("corr_b", [NEXP, D], BF16)
    z8 = k.sb("z8", [NEXP, D])
    em.dma("qsp", lambda e: e.dma_start(out=cand_idx[:, :], in_=cand_idx_h[:, :]),
           writes=[B_("cand_idx_s")])
    em.dma("qsp", lambda e: e.dma_start(out=ebase_i[:, :], in_=ebase_i_h[:, :]),
           writes=[B_("ebase_i_s")])
    k.v(lambda e: e.tensor_copy(cti[:, :], ctc[:, 0:1]),
        reads=[B_("ctc")], writes=[B_("cti")])
    em.dma("qpool", lambda e: e.indirect_dma_start(
        cand_b[:, :], None, xt3_hbm[:, :],
        bass.IndirectOffsetOnAxis(cti[:, 0:1], 0)),
        reads=[B_("cti"), B_("xt3_hbm")], writes=[B_("cand_b")])
    k.v(lambda e: e.tensor_scalar(cand_f[:, :], cand_b[:, :], ctc[:, 1:2], None,
                                  ALU.mult),
        reads=[B_("cand_b"), B_("ctc")], writes=[B_("cand_f")])
    k.v(lambda e: e.memset(z8[:, :], 0.0), writes=[B_("z8")])
    for i in range(4):
        em.dma("qsp", (lambda i: lambda e: e.dma_start(
            out=ar5_in[i * NEXP:(i + 1) * NEXP, :], in_=z8[:, :]))(i),
            reads=[B_("z8")], writes=[B_("ar5_in")])
    em.dma("qpool", lambda e: e.indirect_dma_start(
        ar5_in[:, :], bass.IndirectOffsetOnAxis(cand_idx[:, 0:1], 0),
        cand_f[:, :], None),
        reads=[B_("cand_f"), B_("cand_idx_s")], writes=[B_("ar5_in")])
    k.g(lambda e: e.collective_compute(
        "AllReduce", ALU.add, replica_groups=[list(range(N_CORES))],
        ins=[ar5_in[:, :]], outs=[ar5_out[:, :]]),
        reads=[B_("ar5_in")], writes=[B_("ar5")])
    em.dma("qpool", lambda e: e.indirect_dma_start(
        corr_f[:, :], None, ar5_out[:, :],
        bass.IndirectOffsetOnAxis(cand_idx[:, 0:1], 0)),
        reads=[B_("cand_idx_s"), B_("ar5")], writes=[B_("corr_f")])
    k.v(lambda e: e.tensor_copy(corr_b[:, :], corr_f[:, :]),
        reads=[B_("corr_f")], writes=[B_("corr_b")])
    em.dma("qpool", lambda e: e.indirect_dma_start(
        einp_hbm[:, :], bass.IndirectOffsetOnAxis(ebase_i[:, 0:1], 0),
        corr_b[:, :], None),
        reads=[B_("corr_b"), B_("ebase_i_s")] + ZB, writes=[B_("einp_sc")])

    stf = k.sb("stf", [128, 64])
    sti = k.sb("sti", [128, 64], I32)
    xts = [k.sb(f"p4xts{i}", [128, D], BF16) for i in range(2)]
    for kk in range(2):
        em.dma("qsp", (lambda kk: lambda e: e.dma_start(
            out=stf[:, kk * NT:(kk + 1) * NT],
            in_=scr_sg[0:1, kk * T:(kk + 1) * T].rearrange(
                "o (w p) -> (o p) w", p=128),
            allow_slow_non_contiguous=True))(kk),
            reads=[B_("scr_sg")], writes=[B_("stf")])
    k.v(lambda e: e.tensor_copy(sti[:, :], stf[:, :]),
        reads=[B_("stf")], writes=[B_("sti")])
    nsc = 0
    for tile in range(NT):
        xb, xbb = xts[tile % 2], B_(f"p4xts{tile % 2}")
        em.dma("qact", (lambda tile, xb: lambda e: e.dma_start(
            out=xb[:, :], in_=xt3_hbm[tile * 128:(tile + 1) * 128, :]))(tile, xb),
            reads=[B_("xt3_hbm")], writes=[xbb])
        for kk in range(2):
            em.dma("qpool", (lambda kk, tile, xb: lambda e: e.indirect_dma_start(
                einp_hbm[:, :],
                bass.IndirectOffsetOnAxis(sti[:, kk * NT + tile:kk * NT + tile + 1], 0),
                xb[:, :], None))(kk, tile, xb),
                reads=[B_("sti"), xbb] + ZB, writes=[B_("einp_sc")])
            nsc += 1
    k.pop()

    k.push()
    EIB = ZB + [B_("einp_sc")]
    einp_fm = k.sb("einp_fm", [128, KD * ESL], BF16)
    wbig = k.sb("wbig", [128, 32768], BF16)
    bfcr = k.sb("bfcr", [1, INNER], BF16)
    bgtr = k.sb("bgtr", [1, INNER], BF16)
    bcpr = k.sb("bcpr", [1, D], BF16)
    onesrb = k.sb("onesrb", [1, 512], BF16)
    prodf = [k.sb(f"prodf{i}", [128, 512]) for i in range(2)]
    hsb = [k.sb(f"p4hsb{i}", [128, 512]) for i in range(2)]
    actev = [k.sb(f"actev{i}", [128, ESL], BF16) for i in range(2)]
    at4 = [k.sb(f"at4_{i}", [128, 128], BF16) for i in range(2)]
    soutev = [k.sb(f"soutev{i}", [128, D]) for i in range(2)]
    hp = [k.ps(f"p4hp{i}", [128, 512]) for i in range(2)]
    gp = [k.ps(f"p4gp{i}", [128, 512]) for i in range(2)]
    psA = [k.ps(f"p4psA{i}", [128, 512]) for i in range(2)]
    psB = [k.ps(f"p4psB{i}", [128, 512]) for i in range(2)]
    k.v(lambda e: e.memset(onesrb[:, :], 1.0), writes=[B_("onesrb")])
    SPW = (512, 512, 512, 512, 416)
    ACB = [B_(f"acth{i}") for i in range(4)]

    def wbuf(j):
        return B_(f"wb{j}") if j < 8 else B_(f"wbg{j - 8}")

    nso = 0
    for ex in range(NEXP):
        for kc in range(KD):
            em.dma("qsp", (lambda ex, kc: lambda e: e.dma_start_transpose(
                einp_fm[:, kc * ESL:(kc + 1) * ESL],
                einp_hbm[ex * ESL:(ex + 1) * ESL, kc * 128:(kc + 1) * 128]))(ex, kc),
                reads=EIB, writes=[B_("einp_fm")])
        em.dma("qpool", (lambda ex: lambda e: e.dma_start(
            out=bfcr[:, :], in_=bfc_h[ex:ex + 1, :]))(ex), writes=[B_("bfcr")])
        em.dma("qpool", (lambda ex: lambda e: e.dma_start(
            out=bgtr[:, :], in_=bgt_h[ex:ex + 1, :]))(ex), writes=[B_("bgtr")])
        for ih in range(2):
            for kc in range(KD):
                em.dma("qpool", (lambda ex, ih, kc: lambda e: e.dma_start(
                    out=wbig[:, kc * 2048:(kc + 1) * 2048],
                    in_=wfc_h[ex, kc * 128:(kc + 1) * 128,
                              ih * 2048:(ih + 1) * 2048]))(ex, ih, kc),
                    writes=[wbuf(kc)])
                em.dma("qpool", (lambda ex, ih, kc: lambda e: e.dma_start(
                    out=wbig[:, 16384 + kc * 2048:16384 + (kc + 1) * 2048],
                    in_=wgt_h[ex, kc * 128:(kc + 1) * 128,
                              ih * 2048:(ih + 1) * 2048]))(ex, ih, kc),
                    writes=[wbuf(8 + kc)])
            for ml in range(16):
                mc = ih * 16 + ml
                hpc, hpb = hp[ml % 2], B_(f"p4hp{ml % 2}")
                gpc, gpb = gp[ml % 2], B_(f"p4gp{ml % 2}")
                aev, aevb = actev[ml % 2], B_(f"actev{ml % 2}")
                prf, prb = prodf[ml % 2], B_(f"prodf{ml % 2}")
                WRB = [wbuf(j) for j in range(8)]
                WGB = [wbuf(8 + j) for j in range(8)]
                for spn in range(len(SPW)):
                    w = SPW[spn]
                    sb0 = spn * 512
                    k.p((lambda ml, mc, w, sb0, hpc: lambda e: ([e.matmul(
                        hpc[:, 0:w],
                        wbig[:, kc * 2048 + ml * 128:kc * 2048 + ml * 128 + 128],
                        einp_fm[:, kc * ESL + sb0:kc * ESL + sb0 + w],
                        start=(kc == 0), stop=False) for kc in range(KD)]
                        and e.matmul(hpc[:, 0:w], bfcr[0:1, mc * 128:mc * 128 + 128],
                                     onesrb[0:1, 0:w], start=False, stop=True)))(
                        ml, mc, w, sb0, hpc),
                        reads=WRB + [B_("einp_fm"), B_("bfcr"), B_("onesrb")],
                        writes=[hpb])
                    k.p((lambda ml, mc, w, sb0, gpc: lambda e: ([e.matmul(
                        gpc[:, 0:w],
                        wbig[:, 16384 + kc * 2048 + ml * 128:16384 + kc * 2048 + ml * 128 + 128],
                        einp_fm[:, kc * ESL + sb0:kc * ESL + sb0 + w],
                        start=(kc == 0), stop=False) for kc in range(KD)]
                        and e.matmul(gpc[:, 0:w], bgtr[0:1, mc * 128:mc * 128 + 128],
                                     onesrb[0:1, 0:w], start=False, stop=True)))(
                        ml, mc, w, sb0, gpc),
                        reads=WGB + [B_("einp_fm"), B_("bgtr"), B_("onesrb")],
                        writes=[gpb])
                    hsc, hscb = hsb[ml % 2], B_(f"p4hsb{ml % 2}")
                    k.a((lambda w, hpc, hsc: lambda e: e.activation(
                        out=hsc[:, 0:w], in_=hpc[:, 0:w], func=AF.Copy))(
                        w, hpc, hsc),
                        reads=[hpb], writes=[hscb])
                    k.v((lambda w, hsc, gpc, prf: lambda e: e.tensor_tensor(
                        prf[:, 0:w], hsc[:, 0:w], gpc[:, 0:w], ALU.mult))(
                        w, hsc, gpc, prf),
                        reads=[hscb, gpb], writes=[prb])
                    k.a((lambda w, sb0, prf, aev: lambda e: e.activation(
                        out=aev[:, sb0:sb0 + w], in_=prf[:, 0:w], func=AF.Silu))(
                        w, sb0, prf, aev),
                        reads=[prb], writes=[aevb])
                em.dma("qsp", (lambda mc, aev: lambda e: e.dma_start(
                    out=acth_hbm[mc, :, :], in_=aev[:, :]))(mc, aev),
                    reads=[aevb], writes=[B_(f"acth{mc % 4}")])
        for j in range(32):
            em.dma("qpool", (lambda ex, j: lambda e: e.dma_start(
                out=wbig[:, j * 1024:(j + 1) * 1024],
                in_=wcp_h[ex, j * 128:(j + 1) * 128, :]))(ex, j),
                writes=[wbuf(j // 2) if j < 16 else wbuf(8 + (j - 16) // 2)])
        em.dma("qpool", (lambda ex: lambda e: e.dma_start(
            out=bcpr[:, :], in_=bcp_h[ex:ex + 1, :]))(ex), writes=[B_("bcpr")])
        WAB = [wbuf(j) for j in range(16)]
        NSC = (ESL + 127) // 128
        for sc in range(NSC):
            M = min(128, ESL - sc * 128)
            pA, pAb = psA[sc % 2], B_(f"p4psA{sc % 2}")
            pB, pBb = psB[sc % 2], B_(f"p4psB{sc % 2}")
            sev, sevb = soutev[sc % 2], B_(f"soutev{sc % 2}")
            for j in range(32):
                a_, ab_ = at4[j % 2], B_(f"at4_{j % 2}")
                em.dma("qact", (lambda j, sc, M, a_: lambda e: e.dma_start(
                    out=a_[:, 0:M], in_=acth_hbm[j, :, sc * 128:sc * 128 + M]))(
                    j, sc, M, a_),
                    reads=ACB, writes=[ab_])
                k.p((lambda j, M, a_, pA: lambda e: e.matmul(
                    pA[0:M, :], a_[:, 0:M], wbig[:, j * 1024:j * 1024 + 512],
                    start=(j == 0), stop=False))(j, M, a_, pA),
                    reads=[ab_] + WAB, writes=[pAb])
                k.p((lambda j, M, a_, pB: lambda e: e.matmul(
                    pB[0:M, :], a_[:, 0:M], wbig[:, j * 1024 + 512:j * 1024 + 1024],
                    start=(j == 0), stop=False))(j, M, a_, pB),
                    reads=[ab_] + WAB, writes=[pBb])
            k.p((lambda M, pA: lambda e: e.matmul(
                pA[0:M, :], onesrb[0:1, 0:M], bcpr[0:1, 0:512],
                start=False, stop=True))(M, pA),
                reads=[B_("onesrb"), B_("bcpr")], writes=[pAb])
            k.p((lambda M, pB: lambda e: e.matmul(
                pB[0:M, :], onesrb[0:1, 0:M], bcpr[0:1, 512:1024],
                start=False, stop=True))(M, pB),
                reads=[B_("onesrb"), B_("bcpr")], writes=[pBb])
            k.v((lambda M, pA, sev: lambda e: e.tensor_copy(
                sev[0:M, 0:512], pA[0:M, :]))(M, pA, sev),
                reads=[pAb], writes=[sevb])
            k.v((lambda M, pB, sev: lambda e: e.tensor_copy(
                sev[0:M, 512:1024], pB[0:M, :]))(M, pB, sev),
                reads=[pBb], writes=[sevb])
            em.dma("qsp", (lambda ex, sc, M, sev: lambda e: e.dma_start(
                out=sout_hbm[ex * ESL + sc * 128:ex * ESL + sc * 128 + M, :],
                in_=sev[0:M, :]))(ex, sc, M, sev),
                reads=[sevb], writes=[B_(f"sout_w{nso % 12}")])
            nso += 1
    k.pop()

    k.push()
    SOB = [B_(f"sout_w{i}") for i in range(12)]
    gbf = k.sb("gbf", [128, 64])
    gbi = k.sb("gbi", [128, 64], I32)
    xc4 = [k.sb(f"p4xc{i}", [128, KD * 128]) for i in range(2)]
    x3t = [k.sb(f"p4x3t{i}", [128, D]) for i in range(2)]
    g0sb = [k.sb(f"g0sb{i}", [128, D]) for i in range(2)]
    g1sb = [k.sb(f"g1sb{i}", [128, D]) for i in range(2)]
    out4 = [k.sb(f"out4_{i}", [128, D]) for i in range(2)]
    pst = [k.ps(f"p4pst{i}", [128, 128]) for i in range(2)]
    for kk in range(2):
        em.dma("qsp", (lambda kk: lambda e: e.dma_start(
            out=gbf[:, kk * NT:(kk + 1) * NT],
            in_=scr_sg[0:1, 2 * T + kk * T:2 * T + (kk + 1) * T].rearrange(
                "o (w p) -> (o p) w", p=128),
            allow_slow_non_contiguous=True))(kk),
            reads=[B_("scr_sg")], writes=[B_("gbf")])
    k.v(lambda e: e.tensor_copy(gbi[:, :], gbf[:, :]),
        reads=[B_("gbf")], writes=[B_("gbi")])
    for tile in range(NT):
        i2 = tile % 2
        xc, xcb = xc4[i2], B_(f"p4xc{i2}")
        xt, xtb = x3t[i2], B_(f"p4x3t{i2}")
        g0, g0b = g0sb[i2], B_(f"g0sb{i2}")
        g1, g1b = g1sb[i2], B_(f"g1sb{i2}")
        ot, otb = out4[i2], B_(f"out4_{i2}")
        em.dma("qsp", (lambda tile, xc: lambda e: e.dma_start(
            out=xc[:, :].rearrange("p (kc t) -> p kc t", kc=KD),
            in_=x3_hbm[:, tile * 128:(tile + 1) * 128].rearrange(
                "(kc p) t -> p kc t", p=128)))(tile, xc),
            reads=[B_("x3_hbm")], writes=[xcb])
        for kc in range(KD):
            pt, ptb = pst[kc % 2], B_(f"p4pst{kc % 2}")
            k.p((lambda kc, xc, pt: lambda e: e.transpose(
                pt[:, :], xc[:, kc * 128:(kc + 1) * 128], identf[:, :]))(kc, xc, pt),
                reads=[xcb, B_("identf")], writes=[ptb])
            k.v((lambda kc, xt, pt, ptb: lambda e: e.tensor_copy(
                xt[:, kc * 128:(kc + 1) * 128], pt[:, :]))(kc, xt, pt, ptb),
                reads=[ptb], writes=[xtb])
        em.dma("qpool", (lambda tile, g0: lambda e: e.indirect_dma_start(
            g0[:, :], None, sout_hbm[:, :],
            bass.IndirectOffsetOnAxis(gbi[:, tile:tile + 1], 0)))(tile, g0),
            reads=[B_("gbi")] + SOB, writes=[g0b])
        em.dma("qpool", (lambda tile, g1: lambda e: e.indirect_dma_start(
            g1[:, :], None, sout_hbm[:, :],
            bass.IndirectOffsetOnAxis(gbi[:, NT + tile:NT + tile + 1], 0)))(tile, g1),
            reads=[B_("gbi")] + SOB, writes=[g1b])
        k.v((lambda tile, g0, xt, ot: lambda e: e.scalar_tensor_tensor(
            ot[:, :], g0[:, :], rt_tm[:, tile * 8 + 2:tile * 8 + 3], xt[:, :],
            ALU.mult, ALU.add))(tile, g0, xt, ot),
            reads=[g0b, xtb, B_("rt_tm")], writes=[otb])
        k.v((lambda tile, g1, ot: lambda e: e.scalar_tensor_tensor(
            ot[:, :], g1[:, :], rt_tm[:, tile * 8 + 3:tile * 8 + 4], ot[:, :],
            ALU.mult, ALU.add))(tile, g1, ot),
            reads=[g1b, B_("rt_tm")], writes=[otb])
        em.dma("qsp", (lambda tile, ot: lambda e: e.dma_start(
            out=out_h[tile * 128:(tile + 1) * 128, :], in_=ot[:, :]))(tile, ot),
            reads=[otb], writes=[B_(f"outw{tile % 4}")])
    k.pop(barrier=False)


def _stage_inputs(full):
    f32 = np.float32
    x = np.asarray(full["L0_attn_x"], f32)
    import ml_dtypes
    identb = np.eye(128).astype(ml_dtypes.bfloat16)
    ones1f = np.ones((1, 128), f32)
    onescol = np.ones((128, 1), f32)
    sortw = np.asarray(full["L0_attn_sort_w"], f32)[0]
    in_maps = []
    for c in range(N_CORES):
        b, half = c // 2, c % 2
        im = {}
        im["x_fm"] = np.ascontiguousarray(x[b, half * T:(half + 1) * T, :].T)
        im["l0_wq"] = np.asarray(full["L0_attn_w_q"], f32)
        im["l0_wkv"] = np.asarray(full["L0_attn_w_kv"], f32)
        im["l0_wout"] = np.asarray(full["L0_attn_w_out"], f32)
        im["l0_bout"] = np.ascontiguousarray(
            np.asarray(full["L0_attn_b_out"], f32).reshape(KD, 128).T)
        im["l0_sortwq"] = np.ascontiguousarray(sortw[:, 0:64, :])
        im["ident_f32"] = np.eye(128, dtype=f32)
        im["l0_wg_t"] = np.ascontiguousarray(np.asarray(full["L0_ffn_w_gate"], f32).T)
        im["l0_wu_t"] = np.ascontiguousarray(np.asarray(full["L0_ffn_w_up"], f32).T)
        im["l0_wd_t"] = np.ascontiguousarray(np.asarray(full["L0_ffn_w_down"], f32).T)
        im["l0_sortwk"] = np.ascontiguousarray(sortw[:, 64:128, :])
        oh = np.zeros((64, 32), f32)
        for u in range(32):
            oh[half * 32 + u, u] = 1.0
        im["ownhot"] = oh
        neg = np.float32(-np.finfo(np.float32).max)
        mf = np.zeros((128, 384), f32)
        ml_ = np.zeros((128, 384), f32)
        if half == 0:
            mf[:, 0:128] = neg
        if half == 1:
            ml_[:, 256:384] = neg
        im["mask_first"] = mf
        im["mask_last"] = ml_
        im["ident_bf16"] = identb
        im["ones1_f32"] = ones1f
        im["onescol_f32"] = onescol
        im["kbm_idx"] = ((64 * b + np.arange(64))[:, None] * 4
                         + np.arange(4)[None, :]).astype(np.int32)
        lki = np.zeros((128, 34), np.int32)
        for w in range(34):
            gb = half * 32 + (w - 1)
            if gb < 0:
                gb = 0
            if gb > 63:
                gb = 63
            rank = 2 * b + gb // 32
            lki[:, w] = rank * T + (gb % 32) * 128 + np.arange(128)
        im["lkb_idx"] = lki
        ski = np.zeros((128, 32), np.int32)
        for u in range(32):
            ski[:, u] = (2 * b + half) * T + u * 128 + np.arange(128)
        im["sktile_idx"] = ski
        dh = half
        im["l1_wx_t"] = np.ascontiguousarray(np.asarray(full["L1_attn_Wx"], f32).T)
        im["l1_wz_t"] = np.ascontiguousarray(np.asarray(full["L1_attn_Wz"], f32).T)
        im["l1_wp_t"] = np.ascontiguousarray(np.asarray(full["L1_attn_Wp"], f32).T)
        wdt_t = np.asarray(full["L1_attn_Wdt"], f32).T
        im["l1_wdt_sl"] = np.ascontiguousarray(wdt_t[:, dh * 512:(dh + 1) * 512])
        im["l1_wo_t"] = np.ascontiguousarray(np.asarray(full["L1_attn_Wo"], f32).T)
        alog = np.asarray(full["L1_attn_A_log"], f32)[dh * 512:(dh + 1) * 512, :]
        im["l1_alog_sl"] = np.ascontiguousarray(
            alog.reshape(4, 128, NST).transpose(1, 0, 2).reshape(128, 4 * NST))
        bdt = np.asarray(full["L1_attn_bdt"], f32)[dh * 512:(dh + 1) * 512]
        im["l1_bdt_sl"] = np.ascontiguousarray(bdt.reshape(4, 128).T)
        dp = np.asarray(full["L1_attn_Dp"], f32)[dh * 512:(dh + 1) * 512]
        im["l1_dp_sl"] = np.ascontiguousarray(dp.reshape(4, 128).T)
        cw = np.asarray(full["L1_attn_conv_w"], f32)[dh * 512:(dh + 1) * 512, 0, :]
        im["l1_convw_sl"] = np.ascontiguousarray(
            cw.reshape(4, 128, 4).transpose(1, 0, 2).reshape(128, 16))
        p_ = np.arange(128)
        xpg = np.zeros((128, 8), np.int32)
        for dc in range(4):
            for r01 in range(2):
                xpg[:, dc * 2 + r01] = (2 * b + r01) * 1120 + dh * 512 + dc * 128 + p_
        im["xpg_idx"] = xpg
        zi = np.zeros((128, 64), np.int32)
        for dc in range(4):
            for tch in range(16):
                zi[:, dc * 16 + tch] = (((2 * b + tch // 8) * 1024 + dh * 512
                                         + dc * 128 + p_) * 8 + (tch % 8))
        im["z_idx"] = zi
        dti = np.zeros((64, 2), np.int32)
        for r01 in range(2):
            dti[:, r01] = (2 * b + r01) * 1120 + 1024 + np.arange(64)
        im["dtun_idx"] = dti
        bci = np.zeros((32, 2), np.int32)
        for r01 in range(2):
            bci[:, r01] = (2 * b + r01) * 1120 + 1088 + np.arange(32)
        im["bc_idx"] = bci
        yi = np.zeros((128, 64), np.int32)
        for kc in range(KD):
            for tc in range(8):
                yi[:, kc * 8 + tc] = (((2 * b + kc // 4) * 512 + (kc % 4) * 128 + p_) * 16
                                      + half * 8 + tc)
        im["y_idx"] = yi
        import ml_dtypes as _md
        bh = np.zeros((32, 32 * 128), np.float32)
        for n in range(32):
            bh[n, n * 128:(n + 1) * 128] = 1.0
        im["bchot"] = bh.astype(_md.bfloat16)
        im["l1_rw"] = np.asarray(full["L1_ffn_router_w"], f32)
        im["l1_rb_col"] = np.ascontiguousarray(
            np.asarray(full["L1_ffn_router_b"], f32).reshape(NEXP, 1))
        im["l1_wfc"] = np.asarray(full["L1_ffn_w_c_fc"], f32)
        im["l1_wgt"] = np.asarray(full["L1_ffn_w_gate"], f32)
        im["l1_wcp"] = np.asarray(full["L1_ffn_w_c_proj"], f32)
        im["l1_bfc"] = np.asarray(full["L1_ffn_b_c_fc"], f32).reshape(NEXP, INNER)
        im["l1_bgt"] = np.asarray(full["L1_ffn_b_gate"], f32).reshape(NEXP, INNER)
        im["l1_bcp"] = np.asarray(full["L1_ffn_b_c_proj"], f32).reshape(NEXP, D)
        im["eiota"] = np.arange(NEXP, dtype=f32).reshape(NEXP, 1)
        im["ebase"] = (np.arange(NEXP, dtype=f32) * ESL).reshape(NEXP, 1)
        im["ebase_i"] = (np.arange(NEXP) * ESL).astype(np.int32).reshape(NEXP, 1)
        im["hsel8"] = np.full((NEXP, 1), half, f32)
        im["cnt_idx"] = np.array([[2 * b], [2 * b + 1]], np.int32)
        im["cand_idx"] = (b * NEXP + np.arange(NEXP)).astype(np.int32).reshape(NEXP, 1)
        im["tok8"] = np.broadcast_to(np.arange(T, dtype=f32), (NEXP, T)).copy()
        tr = (TRASH + (np.arange(T) // 128) * 2).astype(f32)
        im["trashrow"] = np.concatenate([tr, tr + 1]).reshape(1, 2 * T)
        in_maps.append(im)
    return in_maps


_NC_CACHE = {}


def kernel(**inputs):
    n_phases = int(inputs.pop("_n_phases", 4))
    if n_phases not in _NC_CACHE:
        _NC_CACHE[n_phases] = build_program(n_phases)
    nc = _NC_CACHE[n_phases]
    in_maps = _stage_inputs(inputs)
    res = run_bass_kernel_spmd(nc, in_maps, core_ids=list(range(N_CORES)))
    out = np.zeros((B, TFULL, D), np.float32)
    key = "out_fm" if n_phases < 4 else "out_tm"
    for c in range(N_CORES):
        b, half = c // 2, c % 2
        o = res.results[c][key]
        out[b, half * T:(half + 1) * T, :] = o.T if n_phases < 4 else o
    return out


# version 06
# speedup vs baseline: 1.1330x; 1.1330x over previous
import numpy as np
from contextlib import ExitStack

import concourse.bass as bass
import concourse.mybir as mybir
from concourse.bass_utils import run_bass_kernel_spmd

F32 = mybir.dt.float32
BF16 = mybir.dt.bfloat16
I32 = mybir.dt.int32
U8 = mybir.dt.uint8
U32 = mybir.dt.uint32
AF = mybir.ActivationFunctionType
ALU = mybir.AluOpType
AXX = mybir.AxisListType.X

N_CORES = 8
B, TFULL, D = 4, 8192, 1024
T = 4096
NT = T // 128
KD = D // 128
H, LH, SH, DH = 16, 4, 12, 64
INNER = 4096
NST, DTR = 16, 64
NEXP, TOPK = 8, 2
CAP = int(1.2 * TOPK * TFULL // NEXP)
ESL = 2464
TRASH = NEXP * ESL
SROWS = TRASH + 128

C32 = 12582912.0
CDELTA = 6755399441055744.0

SK_SZ = 32 * 128 * 768
LK_SZ = T * 256

ENGS = ("pe", "dve", "act", "pool", "sp")
_ENG_ATTR = {"pe": "tensor", "dve": "vector", "act": "scalar", "pool": "gpsimd",
             "sp": "sync"}
DMA_QUEUES = ("qsp", "qact", "qpool")
_DMA_ISSUER = {"qsp": "sp", "qact": "act", "qpool": "pool"}


class Buf:
    __slots__ = ("name", "writer", "readers")

    def __init__(self, name):
        self.name = name
        self.writer = None
        self.readers = []


class Em:
    NSLOT = 12

    def __init__(self, nc):
        self.nc = nc
        self.steps = {e: [] for e in ENGS}
        self.cnt = {e: 0 for e in ENGS}
        self.qidx = {q: 0 for q in DMA_QUEUES}
        self._sems = {}
        self._waited = {e: {} for e in ENGS}
        self._fin = False

    def sem(self, key):
        if key not in self._sems:
            self._sems[key] = self.nc.alloc_semaphore(name=f"bk_{key}")
        return self._sems[key]

    def buf(self, name):
        return Buf(name)

    def _edge(self, eng, dep):
        if dep is None:
            return None
        sk, c = dep
        if self._waited[eng].get(sk, -1) >= c:
            return None
        return dep

    def _deps(self, eng, reads, writes):
        deps = []
        for b in reads:
            d = self._edge(eng, b.writer)
            if d:
                deps.append(d)
        for b in writes:
            d = self._edge(eng, b.writer)
            if d:
                deps.append(d)
            for r in b.readers:
                d = self._edge(eng, r)
                if d:
                    deps.append(d)
        dd = {}
        for sk, c in deps:
            dd[sk] = max(dd.get(sk, 0), c)
        return dd

    def _commit(self, eng, dd):
        waits = sorted(dd.items())
        for sk, c in waits:
            self._waited[eng][sk] = max(self._waited[eng].get(sk, -1), c)
        return waits

    def emit(self, eng, fn, reads=(), writes=()):
        assert not self._fin
        dd = self._deps(eng, reads, writes)
        waits = self._commit(eng, dd)
        self.cnt[eng] += 1
        self.steps[eng].append((waits, fn, eng, 1))
        me = (eng, self.cnt[eng])
        for b in writes:
            b.writer = me
            b.readers = []
        for b in reads:
            b.readers.append(me)
        return me

    def dma(self, q, fn, reads=(), writes=()):
        assert not self._fin
        eng = _DMA_ISSUER[q]
        dd = self._deps(eng, reads, writes)
        i = self.qidx[q]
        self.qidx[q] = i + 1
        slot = f"{q}{i % self.NSLOT}"
        prev = 16 * (i // self.NSLOT)
        if prev > 0 and self._waited[eng].get(slot, -1) < prev:
            dd[slot] = max(dd.get(slot, 0), prev)
        waits = self._commit(eng, dd)
        self.steps[eng].append((waits, fn, slot, 16))
        me = (slot, prev + 16)
        for b in writes:
            b.writer = me
            b.readers = []
        for b in reads:
            b.readers.append(me)
        return me

    def barrier_token(self):
        toks = [(e, self.cnt[e]) for e in ENGS if self.cnt[e] > 0]
        for q in DMA_QUEUES:
            n = self.qidx[q]
            for s in range(min(n, self.NSLOT)):
                i = (n - 1 - ((n - 1 - s) % self.NSLOT))
                toks.append((f"{q}{s}", 16 * (i // self.NSLOT + 1)))
        return toks

    def all_barrier(self):
        toks = self.barrier_token()
        for e in ENGS:
            self.wait_tokens(e, toks)

    def wait_tokens(self, eng, toks):
        dd = {}
        for sk, c in toks:
            if sk == eng:
                continue
            if self._waited[eng].get(sk, -1) >= c:
                continue
            dd[sk] = max(dd.get(sk, 0), c)
        if not dd:
            return
        waits = self._commit(eng, dd)
        self.steps[eng].append((waits, None, None, 0))

    def finalize(self):
        assert not self._fin
        self._fin = True
        with self.nc.Block() as blk:
            for eng in ENGS:
                steps = self.steps[eng]
                if not steps:
                    continue

                def mk(eng_steps):
                    def body(engobj):
                        for waits, fn, sem_key, inc in eng_steps:
                            for sk, c in waits:
                                engobj.wait_ge(self.sem(sk), c)
                            if fn is None:
                                continue
                            instr = fn(engobj)
                            if instr is not None and sem_key is not None:
                                instr.then_inc(self.sem(sem_key), inc)
                    return body
                getattr(blk, _ENG_ATTR[eng])(mk(steps))


class K:
    def __init__(self, nc):
        self.nc = nc
        self.em = Em(nc)
        self.scopes = [ExitStack()]
        self.bufs = {}

    def push(self):
        self.scopes.append(ExitStack())

    def pop(self, barrier=True):
        if barrier:
            self.em.all_barrier()
        self.scopes.pop().close()

    def B(self, name):
        if name not in self.bufs:
            self.bufs[name] = self.em.buf(name)
        return self.bufs[name]

    def sb(self, name, shape, dt=F32):
        return self.scopes[-1].enter_context(
            self.nc.sbuf_tensor(name, list(shape), dt))

    def ps(self, name, shape, dt=F32):
        return self.scopes[-1].enter_context(
            self.nc.psum_tensor(name, list(shape), dt))

    def dram(self, name, shape, dt=F32):
        return self.nc.dram_tensor(name, list(shape), dt)

    def v(self, fn, reads=(), writes=()):
        return self.em.emit("dve", fn, reads, writes)

    def a(self, fn, reads=(), writes=()):
        return self.em.emit("act", fn, reads, writes)

    def p(self, fn, reads=(), writes=()):
        return self.em.emit("pe", fn, reads, writes)

    def g(self, fn, reads=(), writes=()):
        return self.em.emit("pool", fn, reads, writes)


def rsqrt_rows(k, dst, src, scratch, bufs):
    sbuf, dbuf, cbuf = bufs
    k.a(lambda e: e.activation(out=scratch, in_=src, func=AF.Sqrt),
        reads=[sbuf], writes=[cbuf])
    k.v(lambda e: e.reciprocal(dst, scratch), reads=[cbuf], writes=[dbuf])
    for _ in range(2):
        k.v(lambda e: e.tensor_tensor(scratch, dst, dst, ALU.mult),
            reads=[dbuf], writes=[cbuf])
        k.v(lambda e: e.tensor_tensor(scratch, src, scratch, ALU.mult),
            reads=[sbuf, cbuf], writes=[cbuf])
        k.v(lambda e: e.tensor_scalar(scratch, scratch, 3.0, None, ALU.subtract),
            reads=[cbuf], writes=[cbuf])
        k.v(lambda e: e.tensor_scalar(scratch, scratch, -0.5, None, ALU.mult),
            reads=[cbuf], writes=[cbuf])
        k.v(lambda e: e.tensor_tensor(dst, dst, scratch, ALU.mult),
            reads=[dbuf, cbuf], writes=[dbuf])


def build_program(n_phases=4):
    nc = bass.Bass("TRN2", target_bir_lowering=False)
    k = K(nc)
    em, B_ = k.em, k.B

    def inp(name, shape, dt=F32):
        return nc.dram_tensor(name, list(shape), dt, kind="ExternalInput").ap()

    x_h = inp("x_fm", [D, T])
    wq_h = inp("l0_wq", [D, D])
    wkv_h = inp("l0_wkv", [D, 2 * D])
    wout_h = inp("l0_wout", [D, D])
    bout_h = inp("l0_bout", [128, KD])
    sortwq_h = inp("l0_sortwq", [SH, 64, 64])
    sortwk_h = inp("l0_sortwk", [SH, 64, 64])
    ownhot_h = inp("ownhot", [64, 32])
    mask_first_h = inp("mask_first", [128, 384])
    mask_last_h = inp("mask_last", [128, 384])
    identb_h = inp("ident_bf16", [128, 128], BF16)
    ones1f_h = inp("ones1_f32", [1, 128])
    onescol_h = inp("onescol_f32", [128, 1])
    kbm_idx_h = inp("kbm_idx", [64, 4], I32)
    sktile_idx_h = inp("sktile_idx", [128, 32], I32)
    lkb_idx_h = inp("lkb_idx", [128, 34], I32)
    identf_h = inp("ident_f32", [128, 128])
    wg_h = inp("l0_wg_t", [D, INNER])
    wu_h = inp("l0_wu_t", [D, INNER])
    wd_h = inp("l0_wd_t", [INNER, D])
    if n_phases < 4:
        out_h = nc.dram_tensor("out_fm", [D, T], F32, kind="ExternalOutput").ap()
    else:
        out_h = nc.dram_tensor("out_tm", [T, D], F32, kind="ExternalOutput").ap()

    ag1_sk_in = k.dram("ag1_sk_in", [32 * 128, 768], BF16)
    ag1_sv_in = k.dram("ag1_sv_in", [32 * 128, 768], BF16)
    ag1_lk_in = k.dram("ag1_lk_in", [T, 256], BF16)
    ag1_lv_in = k.dram("ag1_lv_in", [T, 256], BF16)
    ag1_sk = k.dram("ag1_sk", [N_CORES * 32 * 128, 768], BF16)
    ag1_sv = k.dram("ag1_sv", [N_CORES * 32 * 128, 768], BF16)
    ag1_lk = k.dram("ag1_lk", [N_CORES * T, 256], BF16)
    ag1_lv = k.dram("ag1_lv", [N_CORES * T, 256], BF16)
    x1_hbm = k.dram("x1_hbm", [D, T])
    attnout_hbm = k.dram("attnout_hbm", [D, T], BF16)
    kblob_hbm = k.dram("kblob_hbm", [6, 64 * 128, 128], BF16)
    vblob_hbm = k.dram("vblob_hbm", [3, 64 * 128, 256], BF16)
    lkstrip_hbm = k.dram("lkstrip_hbm", [34 * 128, 256], BF16)

    identb = k.sb("identb_s", [128, 128], BF16)
    identf = k.sb("identf_s", [128, 128])
    ones1f = k.sb("ones1f_s", [1, 128])
    onescol = k.sb("onescol_s", [128, 1])
    onescol_b = k.sb("onescol_bs", [128, 1], BF16)
    probe = k.sb("probe", [128, 1])
    magicC = k.sb("magicC", [128, 1])
    kbm_idx = k.sb("kbm_idx_s", [64, 4], I32)
    sktile_idx = k.sb("sktile_idx_s", [128, 32], I32)
    lkb_idx = k.sb("lkb_idx_s", [128, 34], I32)
    mask_first = k.sb("mask_first_s", [128, 384])
    mask_last = k.sb("mask_last_s", [128, 384])
    boutc = k.sb("boutc_s", [128, KD])

    for dst, src, nm in [(identb, identb_h, "identb"), (identf, identf_h, "identf"),
                         (ones1f, ones1f_h, "ones1f"),
                         (onescol, onescol_h, "onescol"), (kbm_idx, kbm_idx_h, "kbm_idx"),
                         (lkb_idx, lkb_idx_h, "lkb_idx"),
                         (sktile_idx, sktile_idx_h, "sktile_idx"),
                         (mask_first, mask_first_h, "mask_first"),
                         (mask_last, mask_last_h, "mask_last"), (boutc, bout_h, "boutc")]:
        em.dma("qsp", (lambda d, s: lambda e: e.dma_start(out=d[:], in_=s[:, :]))(dst, src),
               writes=[B_(nm)])
    k.v(lambda e: e.tensor_copy(onescol_b[:, :], onescol[:, :]),
        reads=[B_("onescol")], writes=[B_("onescol_b")])

    k.v(lambda e: e.memset(probe[:], 16777216.0), writes=[B_("probe")])
    k.v(lambda e: e.tensor_scalar(probe[:], probe[:], 1.0, None, ALU.add),
        reads=[B_("probe")], writes=[B_("probe")])
    k.v(lambda e: e.tensor_scalar(probe[:], probe[:], 16777216.0, None, ALU.subtract),
        reads=[B_("probe")], writes=[B_("probe")])
    k.v(lambda e: e.tensor_scalar(magicC[:], probe[:], CDELTA, C32, ALU.mult, ALU.add),
        reads=[B_("probe")], writes=[B_("magicC")])

    k.push()
    q_fm = k.sb("q_fm", [128, KD * T], BF16)

    k.push()
    xa = k.sb("xa", [128, KD * 512])
    xb = k.sb("xb", [128, KD * 512])
    xsq = k.sb("xsq", [128, KD * 512])
    xnr = k.sb("xnr", [128, 2 * KD * 512], BF16)
    wq_sb = k.sb("wq_sb", [128, KD * D], BF16)
    wkv_sb = k.sb("wkv_sb", [128, KD * 2048], BF16)
    stat = k.sb("stat", [1, 4 * 512])
    mbc = k.sb("mbc", [128, 512])
    rbc = k.sb("rbc", [128, 512])
    kvev = k.sb("kvev", [128, 2 * 2048], BF16)
    ps_s1 = k.ps("ps_s1", [1, 512])
    ps_bc2 = k.ps("ps_bc2", [128, 512])
    ps_q = [k.ps(f"ps_q{i}", [128, 512]) for i in range(2)]
    ps_kv = [k.ps(f"ps_kv{i}", [128, 512]) for i in range(2)]

    for kc in range(KD):
        em.dma("qpool", (lambda kc: lambda e: e.dma_start(
            out=wq_sb[:, kc * D:(kc + 1) * D],
            in_=wq_h[kc * 128:(kc + 1) * 128, :]))(kc), writes=[B_("wq_sb")])
        em.dma("qpool", (lambda kc: lambda e: e.dma_start(
            out=wkv_sb[:, kc * 2048:(kc + 1) * 2048],
            in_=wkv_h[kc * 128:(kc + 1) * 128, :]))(kc), writes=[B_("wkv_sb")])

    for tc in range(T // 512):
        xsb, xbuf = (xa, B_("xa")) if tc % 2 == 0 else (xb, B_("xb"))
        xnb = (tc % 2) * KD * 512
        xnbuf = B_(f"xnr{tc % 2}")
        em.dma("qsp", (lambda xsb, tc: lambda e: e.dma_start(
            out=xsb[:, :].rearrange("p (kc t) -> p kc t", kc=KD),
            in_=x_h[:, tc * 512:(tc + 1) * 512].rearrange("(kc p) t -> p kc t", p=128)))(xsb, tc),
            writes=[xbuf])
        k.p((lambda xsb: lambda e: [e.matmul(
            ps_s1[:, :], onescol[:, :], xsb[:, kc * 512:(kc + 1) * 512],
            start=(kc == 0), stop=(kc == KD - 1)) for kc in range(KD)][-1])(xsb),
            reads=[B_("onescol"), xbuf], writes=[B_("ps_s1")])
        k.a((lambda xsb: lambda e: e.activation(out=xsq[:, :], in_=xsb[:, :],
                                                func=AF.Square))(xsb),
            reads=[xbuf], writes=[B_("xsq")])
        sm = stat[0:1, 0:512]
        sv = stat[0:1, 512:1024]
        sr = stat[0:1, 1024:1536]
        sc_ = stat[0:1, 1536:2048]
        k.v(lambda e: e.tensor_scalar(sm, ps_s1[:, :], 1.0 / D, None, ALU.mult),
            reads=[B_("ps_s1")], writes=[B_("stat")])
        k.p(lambda e: [e.matmul(
            ps_s1[:, :], onescol[:, :], xsq[:, kc * 512:(kc + 1) * 512],
            start=(kc == 0), stop=(kc == KD - 1)) for kc in range(KD)][-1],
            reads=[B_("onescol"), B_("xsq")], writes=[B_("ps_s1")])
        k.v(lambda e: e.tensor_scalar(sv, ps_s1[:, :], 1.0 / D, None, ALU.mult),
            reads=[B_("ps_s1")], writes=[B_("stat")])
        k.v(lambda e: e.tensor_tensor(sc_, sm, sm, ALU.mult),
            reads=[B_("stat")], writes=[B_("stat")])
        k.v(lambda e: e.tensor_tensor(sv, sv, sc_, ALU.subtract),
            reads=[B_("stat")], writes=[B_("stat")])
        k.v(lambda e: e.tensor_scalar(sv, sv, 1e-5, None, ALU.add),
            reads=[B_("stat")], writes=[B_("stat")])
        rsqrt_rows(k, sr, sv, sc_, (B_("stat"), B_("stat"), B_("stat")))
        k.p(lambda e: e.matmul(ps_bc2[:, :], ones1f[:, :], sm, start=True, stop=True),
            reads=[B_("ones1f"), B_("stat")], writes=[B_("ps_bc2")])
        k.v(lambda e: e.tensor_copy(mbc[:, :], ps_bc2[:, :]),
            reads=[B_("ps_bc2")], writes=[B_("mbc")])
        k.p(lambda e: e.matmul(ps_bc2[:, :], ones1f[:, :], sr, start=True, stop=True),
            reads=[B_("ones1f"), B_("stat")], writes=[B_("ps_bc2")])
        k.v(lambda e: e.tensor_copy(rbc[:, :], ps_bc2[:, :]),
            reads=[B_("ps_bc2")], writes=[B_("rbc")])
        for kc in range(KD):
            k.v((lambda xsb, kc: lambda e: e.tensor_tensor(
                xsq[:, kc * 512:(kc + 1) * 512], xsb[:, kc * 512:(kc + 1) * 512],
                mbc[:, :], ALU.subtract))(xsb, kc),
                reads=[xbuf, B_("mbc")], writes=[B_("xsq")])
            k.v((lambda kc, xnb, xnbuf: lambda e: e.tensor_tensor(
                xnr[:, xnb + kc * 512:xnb + kc * 512 + 512],
                xsq[:, kc * 512:(kc + 1) * 512], rbc[:, :], ALU.mult))(kc, xnb, xnbuf),
                reads=[B_("xsq"), B_("rbc")], writes=[xnbuf])
        for oc in range(KD):
            pq, pqb = ps_q[oc % 2], B_(f"ps_q{oc % 2}")
            k.p((lambda oc, xnb, pq: lambda e: [e.matmul(
                pq[:, :], wq_sb[:, (kc * KD + oc) * 128:(kc * KD + oc) * 128 + 128],
                xnr[:, xnb + kc * 512:xnb + kc * 512 + 512],
                start=(kc == 0), stop=(kc == KD - 1)) for kc in range(KD)][-1])(oc, xnb, pq),
                reads=[B_("wq_sb"), xnbuf], writes=[pqb])
            k.v((lambda oc, tc, pq: lambda e: e.tensor_copy(
                q_fm[:, oc * T + tc * 512:oc * T + tc * 512 + 512], pq[:, :]))(oc, tc, pq),
                reads=[pqb], writes=[B_("q_fm")])
        for ti in range(4):
            vtile = tc * 4 + ti
            for nb in range(4):
                pkv, pkb = ps_kv[nb % 2], B_(f"ps_kv{nb % 2}")
                k.p((lambda xnb, ti, nb, pkv: lambda e: [e.matmul(
                    pkv[:, :],
                    xnr[:, xnb + kc * 512 + ti * 128:xnb + kc * 512 + ti * 128 + 128],
                    wkv_sb[:, kc * 2048 + nb * 512:kc * 2048 + nb * 512 + 512],
                    start=(kc == 0), stop=(kc == KD - 1)) for kc in range(KD)][-1])(xnb, ti, nb, pkv),
                    reads=[xnbuf, B_("wkv_sb")], writes=[pkb])
                k.v((lambda vtile, nb, pkv: lambda e: e.tensor_copy(
                    kvev[:, (vtile % 2) * 2048 + nb * 512:(vtile % 2) * 2048 + nb * 512 + 512],
                    pkv[:, :]))(vtile, nb, pkv),
                    reads=[pkb], writes=[B_(f"kvev{vtile % 2}")])
            db = (vtile % 2) * 2048
            kvb = B_(f"kvev{vtile % 2}")
            em.dma("qsp", (lambda vtile, db: lambda e: e.dma_start(
                out=ag1_sk_in[vtile * 128:(vtile + 1) * 128, :],
                in_=kvev[:, db + 256:db + 1024]))(vtile, db),
                reads=[kvb], writes=[B_("ag1_sk_in")])
            em.dma("qsp", (lambda vtile, db: lambda e: e.dma_start(
                out=ag1_sv_in[vtile * 128:(vtile + 1) * 128, :],
                in_=kvev[:, db + 1280:db + 2048]))(vtile, db),
                reads=[kvb], writes=[B_("ag1_sv_in")])
            em.dma("qact", (lambda vtile, db: lambda e: e.dma_start(
                out=ag1_lk_in[vtile * 128:(vtile + 1) * 128, :],
                in_=kvev[:, db:db + 256]))(vtile, db),
                reads=[kvb], writes=[B_("ag1_lk_in")])
            em.dma("qact", (lambda vtile, db: lambda e: e.dma_start(
                out=ag1_lv_in[vtile * 128:(vtile + 1) * 128, :],
                in_=kvev[:, db + 1024:db + 1280]))(vtile, db),
                reads=[kvb], writes=[B_("ag1_lv_in")])

    for _i, _o, _ib in [(ag1_sk_in, ag1_sk, "ag1_sk_in"), (ag1_sv_in, ag1_sv, "ag1_sv_in"),
                        (ag1_lk_in, ag1_lk, "ag1_lk_in"), (ag1_lv_in, ag1_lv, "ag1_lv_in")]:
        k.g((lambda _i, _o: lambda e: e.collective_compute(
            "AllGather", ALU.bypass, replica_groups=[list(range(N_CORES))],
            ins=[_i[:, :]], outs=[_o[:, :]]))(_i, _o),
            reads=[B_(_ib)], writes=[B_("ag1")])
    k.pop()

    env = dict(locals())
    if n_phases >= 1:
        build_p1(k, env)
    k.pop()
    if n_phases >= 2:
        build_p2(k, env)
    if n_phases >= 3:
        build_p3(k, env)
    if n_phases >= 4:
        build_p4(k, env)

    if n_phases < 4:
        res_map = {0: None, 1: "x1_hbm", 2: "x2_hbm", 3: "x3_hbm"}
        nm = res_map.get(n_phases)
        k.push()
        dbg = k.sb("dbgout", [128, KD * 512])
        for tc in range(T // 512):
            if nm is None:
                src_ap = x_h[:, tc * 512:(tc + 1) * 512]
                rb = []
            else:
                src_ap = env[nm][:, tc * 512:(tc + 1) * 512]
                rb = [B_(nm)]
            em.dma("qact", (lambda src_ap: lambda e: e.dma_start(
                out=dbg[:, :].rearrange("p (kc t) -> p kc t", kc=KD),
                in_=src_ap.rearrange("(kc p) t -> p kc t", p=128)))(src_ap),
                reads=rb, writes=[B_("dbgout")])
            em.dma("qact", (lambda tc: lambda e: e.dma_start(
                out=out_h[:, tc * 512:(tc + 1) * 512].rearrange("(kc p) t -> p kc t", p=128),
                in_=dbg[:, :].rearrange("p (kc t) -> p kc t", kc=KD)))(tc),
                reads=[B_("dbgout")], writes=[B_("out")])
        k.pop(barrier=False)

    em.wait_tokens("sp", em.barrier_token())
    em.finalize()
    while k.scopes:
        k.scopes.pop().close()
    return nc

def build_p1(k, env):
    em, B_ = k.em, k.B
    g = env
    q_fm = g["q_fm"]; identb = g["identb"]; onescol = g["onescol"]
    onescol_b = g["onescol_b"]
    ag1_sk, ag1_sv = g["ag1_sk"], g["ag1_sv"]
    ag1_lk, ag1_lv = g["ag1_lk"], g["ag1_lv"]
    kbm_idx, lkb_idx, sktile_idx = g["kbm_idx"], g["lkb_idx"], g["sktile_idx"]
    mask_first, mask_last = g["mask_first"], g["mask_last"]
    kblob_hbm, vblob_hbm, lkstrip_hbm = g["kblob_hbm"], g["vblob_hbm"], g["lkstrip_hbm"]
    attnout_hbm = g["attnout_hbm"]
    sortwq_h, sortwk_h, ownhot_h = g["sortwq_h"], g["sortwk_h"], g["ownhot_h"]
    wout_h, boutc = g["wout_h"], g["boutc"]
    x_h, x1_hbm = g["x_h"], g["x1_hbm"]

    k.push()
    sortwq = k.sb("sortwq", [64, SH * 64], BF16)
    sortwk = k.sb("sortwk", [64, SH * 64], BF16)
    ownhot = k.sb("ownhot_s", [64, 32], BF16)
    em.dma("qpool", lambda e: e.dma_start(
        out=sortwq[:, :].rearrange("p (s c) -> p s c", s=SH),
        in_=sortwq_h[:, :, :].rearrange("s p c -> p s c")), writes=[B_("sortwq")])
    em.dma("qpool", lambda e: e.dma_start(
        out=sortwk[:, :].rearrange("p (s c) -> p s c", s=SH),
        in_=sortwk_h[:, :, :].rearrange("s p c -> p s c")), writes=[B_("sortwk")])
    em.dma("qpool", lambda e: e.dma_start(out=ownhot[:, :], in_=ownhot_h[:, :]),
           writes=[B_("ownhot")])

    qsum_b = k.sb("qsum_b", [64, SH * 32], BF16)
    ksum_b = k.sb("ksum_b", [64, SH * 32], BF16)
    qsum_f = k.sb("qsum_f", [64, 32])
    for s in range(SH):
        h = s + 4
        hc, hb = h // 2, 64 * (h % 2)
        k.v((lambda hc, hb: lambda e: e.tensor_reduce(
            qsum_f[:, :].rearrange("p (u o) -> p u o", o=1),
            q_fm[hb:hb + 64, hc * T:(hc + 1) * T].rearrange("p (u t) -> p u t", u=32),
            AXX, ALU.add))(hc, hb),
            reads=[B_("q_fm")], writes=[B_("qsum_f")])
        k.v((lambda s: lambda e: e.tensor_copy(qsum_b[:, s * 32:(s + 1) * 32],
                                               qsum_f[:, :]))(s),
            reads=[B_("qsum_f")], writes=[B_("qsum_b")])

    sktile = k.sb("sktile", [128, 2 * 768], BF16)
    ps_ks = k.ps("ps_ks", [64, SH])
    for u in range(32):
        db = (u % 2) * 768
        em.dma("qpool", (lambda u, db: lambda e: e.indirect_dma_start(
            sktile[:, db:db + 768], None, ag1_sk[:, :],
            bass.IndirectOffsetOnAxis(sktile_idx[:, u:u + 1], 0)))(u, db),
            reads=[B_("sktile_idx"), B_("ag1")], writes=[B_(f"sktile{u % 2}")])
        k.p((lambda u, db: lambda e: [e.matmul(
            ps_ks[:, s:s + 1], sktile[:, db + s * 64:db + s * 64 + 64],
            onescol_b[:, :], start=True, stop=True) for s in range(SH)][-1])(u, db),
            reads=[B_(f"sktile{u % 2}"), B_("onescol_b")], writes=[B_("ps_ks")])
        k.v((lambda u: lambda e: e.tensor_copy(
            ksum_b[:, :].rearrange("p (s u2) -> p s u2", s=SH)[:, :, u:u + 1],
            ps_ks[:, :].rearrange("p (s o) -> p s o", o=1)))(u),
            reads=[B_("ps_ks")], writes=[B_("ksum_b")])

    r_sb = k.sb("r_sb", [32, 64])
    r_bb = k.sb("r_bb", [32, 64], BF16)
    r_row = k.sb("r_row", [32, 2])
    r2all = k.sb("r2all", [64, SH * 64], BF16)
    ps_r = k.ps("ps_r", [32, 64])
    ps_rt = k.ps("ps_rt", [64, 32], BF16)
    for s in range(SH):
        k.p((lambda s: lambda e: [
            e.matmul(ps_r[:, :], qsum_b[:, s * 32:(s + 1) * 32],
                     sortwq[:, s * 64:(s + 1) * 64], start=True, stop=False),
            e.matmul(ps_r[:, :], ksum_b[:, s * 32:(s + 1) * 32],
                     sortwk[:, s * 64:(s + 1) * 64], start=False, stop=True)][-1])(s),
            reads=[B_("qsum_b"), B_("ksum_b"), B_("sortwq"), B_("sortwk")],
            writes=[B_("ps_r")])
        k.v(lambda e: e.tensor_single_scalar(r_sb[:, :], ps_r[:, :], 0.0, ALU.max),
            reads=[B_("ps_r")], writes=[B_("r_sb")])
        k.v(lambda e: e.tensor_reduce(r_row[:, 0:1], r_sb[:, :], AXX, ALU.max),
            reads=[B_("r_sb")], writes=[B_("r_row")])
        k.v(lambda e: e.tensor_scalar(r_row[:, 0:1], r_row[:, 0:1], -1.0, None, ALU.mult),
            reads=[B_("r_row")], writes=[B_("r_row")])
        k.a(lambda e: e.activation(out=r_sb[:, :], in_=r_sb[:, :], func=AF.Exp,
                                   bias=r_row[:, 0:1]),
            reads=[B_("r_row"), B_("r_sb")], writes=[B_("r_sb")])
        k.v(lambda e: e.tensor_reduce(r_row[:, 1:2], r_sb[:, :], AXX, ALU.add),
            reads=[B_("r_sb")], writes=[B_("r_row")])
        k.v(lambda e: e.reciprocal(r_row[:, 1:2], r_row[:, 1:2]),
            reads=[B_("r_row")], writes=[B_("r_row")])
        k.v(lambda e: e.tensor_scalar(r_sb[:, :], r_sb[:, :], r_row[:, 1:2], None,
                                      ALU.mult),
            reads=[B_("r_row"), B_("r_sb")], writes=[B_("r_sb")])
        k.v(lambda e: e.tensor_copy(r_bb[:, :], r_sb[:, :]),
            reads=[B_("r_sb")], writes=[B_("r_bb")])
        k.p(lambda e: e.transpose(ps_rt[:, :], r_bb[:, :], identb[0:32, 0:32]),
            reads=[B_("r_bb"), B_("identb")], writes=[B_("ps_rt")])
        k.v((lambda s: lambda e: e.tensor_copy(r2all[:, s * 64:s * 64 + 32],
                                               ps_rt[:, :]))(s),
            reads=[B_("ps_rt")], writes=[B_("r2all")])
        k.v((lambda s: lambda e: e.tensor_copy(r2all[:, s * 64 + 32:s * 64 + 64],
                                               ownhot[:, :]))(s),
            reads=[B_("ownhot")], writes=[B_("r2all")])

    kbm = k.sb("kbm", [64, 32 * 768], BF16)
    kblob = k.sb("kblob", [64, 2 * 32 * 128], BF16)
    vblob = k.sb("vblob", [64, 32 * 256], BF16)
    ps_mx = [k.ps(f"ps_mx{i}", [64, 512]) for i in range(2)]

    def mix_head(s, dstblob, dstbuf, colw, coloff):
        for c in range(4):
            pm, pmb = ps_mx[c % 2], B_(f"ps_mx{c % 2}")
            k.p((lambda s, c, pm: lambda e: e.matmul(
                pm[:, :], r2all[:, s * 64:(s + 1) * 64],
                kbm[:, :].rearrange("p (t s2 e) -> p t s2 e", t=32, s2=SH)[
                    :, c * 8:(c + 1) * 8, s:s + 1, :],
                start=True, stop=True))(s, c, pm),
                reads=[B_("kbm"), B_("r2all")], writes=[pmb])
            k.v((lambda c, pm, dstblob, colw, coloff: lambda e: e.tensor_copy(
                dstblob[:, :].rearrange("p (t c2) -> p t c2", c2=colw)[
                    :, c * 8:(c + 1) * 8, coloff:coloff + 64],
                pm[:, :].rearrange("p (t e) -> p t e", t=8)))(c, pm, dstblob, colw, coloff),
                reads=[pmb], writes=[dstbuf])

    for th in range(4):
        em.dma("qpool", (lambda th: lambda e: e.indirect_dma_start(
            kbm[:, :], None,
            ag1_sk[:, :].rearrange("(r t) c -> r (t c)", t=32),
            bass.IndirectOffsetOnAxis(kbm_idx[:, th:th + 1], 0)))(th),
            reads=[B_("kbm_idx"), B_("ag1")], writes=[B_("kbm")])
        for pr in range(6):
            blb = kblob[:, (pr % 2) * 4096:(pr % 2) * 4096 + 4096]
            blbuf = B_(f"kblob{pr % 2}")
            mix_head(2 * pr, blb, blbuf, 128, 0)
            mix_head(2 * pr + 1, blb, blbuf, 128, 64)
            em.dma("qsp", (lambda pr, th, blb: lambda e: e.dma_start(
                out=kblob_hbm[pr, :, :].rearrange("(u t) c -> u (t c)", t=128)[
                    :, th * 32 * 128:(th + 1) * 32 * 128],
                in_=blb))(pr, th, blb), reads=[blbuf], writes=[B_("kblob_hbm")])
        em.dma("qpool", (lambda th: lambda e: e.indirect_dma_start(
            kbm[:, :], None,
            ag1_sv[:, :].rearrange("(r t) c -> r (t c)", t=32),
            bass.IndirectOffsetOnAxis(kbm_idx[:, th:th + 1], 0)))(th),
            reads=[B_("kbm_idx"), B_("ag1")], writes=[B_("kbm")])
        for grp in range(3):
            for si in range(4):
                mix_head(grp * 4 + si, vblob, B_("vblob"), 256, si * 64)
            em.dma("qsp", (lambda grp, th: lambda e: e.dma_start(
                out=vblob_hbm[grp, :, :].rearrange("(u t) c -> u (t c)", t=128)[
                    :, th * 32 * 256:(th + 1) * 32 * 256],
                in_=vblob[:, :]))(grp, th), reads=[B_("vblob")], writes=[B_("vblob_hbm")])
    k.pop()

    k.push()
    lv_sb = k.sb("lv_sb", [128, 34 * 256], BF16)
    lk_sb = k.sb("lk_sb", [128, 34 * 256], BF16)
    for w in range(34):
        em.dma("qpool", (lambda w: lambda e: e.indirect_dma_start(
            lv_sb[:, w * 256:(w + 1) * 256], None, ag1_lv[:, :],
            bass.IndirectOffsetOnAxis(lkb_idx[:, w:w + 1], 0)))(w),
            reads=[B_("lkb_idx"), B_("ag1")], writes=[B_("lv_sb")])
        em.dma("qpool", (lambda w: lambda e: e.indirect_dma_start(
            lk_sb[:, w * 256:(w + 1) * 256], None, ag1_lk[:, :],
            bass.IndirectOffsetOnAxis(lkb_idx[:, w:w + 1], 0)))(w),
            reads=[B_("lkb_idx"), B_("ag1")], writes=[B_("lk_sb")])
    em.dma("qact", lambda e: e.dma_start(
        out=lkstrip_hbm[:, :].rearrange("(w t) c -> t w c", t=128),
        in_=lk_sb[:, :].rearrange("p (w c) -> p w c", w=34)),
        reads=[B_("lk_sb")], writes=[B_("lkstrip_hbm")])
    lkfm = k.sb("lkfm", [128, 2 * 4352], BF16)
    for pr2 in range(2):
        em.dma("qsp", (lambda pr2: lambda e: e.dma_start_transpose(
            lkfm[:, pr2 * 4352:(pr2 + 1) * 4352],
            lkstrip_hbm[:, pr2 * 128:(pr2 + 1) * 128]))(pr2),
            reads=[B_("lkstrip_hbm")], writes=[B_("lkfm")])

    ps_d = k.ps("ps_d", [128, 512])
    ps_t = [k.ps(f"ps_t{i}", [128, 128], BF16) for i in range(2)]
    ps_av = k.ps("ps_av", [64, 128])
    smx = k.sb("smx", [128, 512])
    scol = k.sb("scol", [128, 2])
    attnb = k.sb("attnb", [128, 512], BF16)
    attnT = k.sb("attnT", [128, 3 * 128], BF16)
    av_sb = k.sb("av_sb", [64, 128], BF16)
    vt_r = k.sb("vt_r", [128, 256], BF16)
    vt_o = k.sb("vt_o", [128, 256], BF16)

    def softmax_av(h, u, nj, vt_tiles, scale):
        nch = nj // 128
        k.v((lambda nj: lambda e: e.tensor_reduce(
            scol[:, 0:1], smx[:, 0:nj], AXX, ALU.max))(nj),
            reads=[B_("smx")], writes=[B_("scol")])
        k.v((lambda scale: lambda e: e.tensor_scalar(
            scol[:, 0:1], scol[:, 0:1], -scale, None, ALU.mult))(scale),
            reads=[B_("scol")], writes=[B_("scol")])
        k.a((lambda nj, scale: lambda e: e.activation(
            out=smx[:, 0:nj], in_=smx[:, 0:nj], func=AF.Exp,
            bias=scol[:, 0:1], scale=scale))(nj, scale),
            reads=[B_("scol"), B_("smx")], writes=[B_("smx")])
        k.v((lambda nj: lambda e: e.tensor_reduce(
            scol[:, 1:2], smx[:, 0:nj], AXX, ALU.add))(nj),
            reads=[B_("smx")], writes=[B_("scol")])
        k.v(lambda e: e.reciprocal(scol[:, 1:2], scol[:, 1:2]),
            reads=[B_("scol")], writes=[B_("scol")])
        k.v((lambda nj: lambda e: e.tensor_scalar(
            attnb[:, 0:nj], smx[:, 0:nj], scol[:, 1:2], None, ALU.mult))(nj),
            reads=[B_("scol"), B_("smx")], writes=[B_("attnb")])
        for jc in range(nch):
            pt, ptb = ps_t[jc % 2], B_(f"ps_t{jc % 2}")
            k.p((lambda jc, pt: lambda e: e.transpose(
                pt[:, :], attnb[:, jc * 128:(jc + 1) * 128], identb[:, :]))(jc, pt),
                reads=[B_("attnb"), B_("identb")], writes=[ptb])
            k.v((lambda jc, pt: lambda e: e.tensor_copy(
                attnT[:, jc * 128:(jc + 1) * 128], pt[:, :]))(jc, pt),
                reads=[ptb], writes=[B_("attnT")])
        k.p((lambda nch, vt_tiles: lambda e: [e.matmul(
            ps_av[:, :], vt_tiles[jc][0][:, vt_tiles[jc][1]:vt_tiles[jc][1] + 64],
            attnT[:, jc * 128:(jc + 1) * 128],
            start=(jc == 0), stop=(jc == nch - 1)) for jc in range(nch)][-1])(nch, vt_tiles),
            reads=[B_("attnT")] + [B_(vt_tiles[jc][2]) for jc in range(nch)],
            writes=[B_("ps_av")])
        k.v(lambda e: e.tensor_copy(av_sb[:, :], ps_av[:, :]),
            reads=[B_("ps_av")], writes=[B_("av_sb")])
        em.dma("qact", (lambda h, u: lambda e: e.dma_start(
            out=attnout_hbm[h * 64:(h + 1) * 64, u * 128:(u + 1) * 128],
            in_=av_sb[:, :]))(h, u), reads=[B_("av_sb")], writes=[B_("attnout_hbm")])

    for h in range(LH):
        hb = 64 * (h % 2)
        for u in range(NT):
            k.p((lambda h, hb, u: lambda e: e.matmul(
                ps_d[:, 0:384],
                q_fm[hb:hb + 64, (h // 2) * T + u * 128:(h // 2) * T + u * 128 + 128],
                lkfm[hb:hb + 64, (h // 2) * 4352 + u * 128:(h // 2) * 4352 + u * 128 + 384],
                start=True, stop=True))(h, hb, u),
                reads=[B_("q_fm"), B_("lkfm")], writes=[B_("ps_d")])
            if u == 0:
                k.v(lambda e: e.tensor_tensor(smx[:, 0:384], ps_d[:, 0:384],
                                              mask_first[:, :], ALU.add),
                    reads=[B_("ps_d"), B_("mask_first")], writes=[B_("smx")])
            elif u == NT - 1:
                k.v(lambda e: e.tensor_tensor(smx[:, 0:384], ps_d[:, 0:384],
                                              mask_last[:, :], ALU.add),
                    reads=[B_("ps_d"), B_("mask_last")], writes=[B_("smx")])
            else:
                k.v(lambda e: e.tensor_copy(smx[:, 0:384], ps_d[:, 0:384]),
                    reads=[B_("ps_d")], writes=[B_("smx")])
            vts = [(lv_sb, (u + jc) * 256 + h * 64, "lv_sb") for jc in range(3)]
            softmax_av(h, u, 384, vts, 1.0)

    kfm = k.sb("kfm", [128, 2 * 8192], BF16)
    for pr in range(6):
        kb = (pr % 2) * 8192
        kfb = B_(f"kfm{pr % 2}")
        em.dma("qsp", (lambda pr, kb: lambda e: e.dma_start_transpose(
            kfm[:, kb:kb + 8192], kblob_hbm[pr, :, :]))(pr, kb),
            reads=[B_("kblob_hbm")], writes=[kfb])
        grp = pr // 2
        for u in range(NT):
            em.dma("qsp", (lambda grp, u: lambda e: e.dma_start(
                out=vt_r[:, :], in_=vblob_hbm[grp, u * 128:(u + 1) * 128, :]))(grp, u),
                reads=[B_("vblob_hbm")], writes=[B_("vt_r")])
            em.dma("qsp", (lambda grp, u: lambda e: e.dma_start(
                out=vt_o[:, :], in_=vblob_hbm[grp, (32 + u) * 128:(33 + u) * 128, :]))(grp, u),
                reads=[B_("vblob_hbm")], writes=[B_("vt_o")])
            for sp in range(2):
                s = 2 * pr + sp
                h = s + 4
                hb = 64 * (h % 2)
                si = s % 4
                k.p((lambda h, hb, u, kb, sp: lambda e: [
                    e.matmul(ps_d[:, 0:128],
                             q_fm[hb:hb + 64, (h // 2) * T + u * 128:(h // 2) * T + u * 128 + 128],
                             kfm[hb:hb + 64, kb + u * 128:kb + u * 128 + 128],
                             start=True, stop=True),
                    e.matmul(ps_d[:, 128:256],
                             q_fm[hb:hb + 64, (h // 2) * T + u * 128:(h // 2) * T + u * 128 + 128],
                             kfm[hb:hb + 64, kb + (32 + u) * 128:kb + (32 + u) * 128 + 128],
                             start=True, stop=True)][-1])(h, hb, u, kb, sp),
                    reads=[B_("q_fm"), kfb], writes=[B_("ps_d")])
                k.v(lambda e: e.tensor_copy(smx[:, 0:256], ps_d[:, 0:256]),
                    reads=[B_("ps_d")], writes=[B_("smx")])
                vts = [(vt_r, si * 64, "vt_r"), (vt_o, si * 64, "vt_o")]
                softmax_av(h, u, 256, vts, float(DH) ** -0.5)
    k.pop()

    k.push()
    aout = k.sb("aout", [128, KD * 512], BF16)
    wot = k.sb("wot", [128, KD * KD * 128], BF16)
    xres = k.sb("xres", [128, KD * 512])
    x1sb = k.sb("x1sb", [128, 512])
    ps_o = [k.ps(f"ps_o{i}", [128, 512]) for i in range(2)]
    for tc in range(T // 512):
        em.dma("qsp", (lambda tc: lambda e: e.dma_start(
            out=aout[:, :].rearrange("p (kc t) -> p kc t", kc=KD),
            in_=attnout_hbm[:, tc * 512:(tc + 1) * 512].rearrange(
                "(kc p) t -> p kc t", p=128)))(tc),
            reads=[B_("attnout_hbm")], writes=[B_("aout")])
        em.dma("qact", (lambda tc: lambda e: e.dma_start(
            out=xres[:, :].rearrange("p (kc t) -> p kc t", kc=KD),
            in_=x_h[:, tc * 512:(tc + 1) * 512].rearrange("(kc p) t -> p kc t", p=128)))(tc),
            writes=[B_("xres")])
        for oc in range(KD):
            wb = oc * KD * 128
            if tc == 0:
                em.dma("qpool", (lambda oc, wb: lambda e: e.dma_start(
                    out=wot[:, wb:wb + KD * 128].rearrange("p (kc c) -> p kc c", kc=KD),
                    in_=wout_h[:, oc * 128:(oc + 1) * 128].rearrange(
                        "(kc p) c -> p kc c", p=128)))(oc, wb),
                    writes=[B_(f"wot{oc}")])
            po, pob = ps_o[oc % 2], B_(f"ps_o{oc % 2}")
            k.p((lambda oc, wb, po: lambda e: [e.matmul(
                po[:, :], wot[:, wb + kc * 128:wb + kc * 128 + 128],
                aout[:, kc * 512:(kc + 1) * 512],
                start=(kc == 0), stop=(kc == KD - 1)) for kc in range(KD)][-1])(oc, wb, po),
                reads=[B_(f"wot{oc}"), B_("aout")], writes=[pob])
            k.v((lambda oc, po: lambda e: e.scalar_tensor_tensor(
                x1sb[:, :], po[:, :], boutc[:, oc:oc + 1],
                xres[:, oc * 512:(oc + 1) * 512], ALU.add, ALU.add))(oc, po),
                reads=[pob, B_("boutc"), B_("xres")], writes=[B_("x1sb")])
            em.dma("qsp", (lambda oc, tc: lambda e: e.dma_start(
                out=x1_hbm[oc * 128:(oc + 1) * 128, tc * 512:(tc + 1) * 512],
                in_=x1sb[:, :]))(oc, tc),
                reads=[B_("x1sb")], writes=[B_("x1_hbm")])
    k.pop(barrier=False)

def build_p2(k, env):
    em, B_ = k.em, k.B
    g = env
    onescol, ones1f = g["onescol"], g["ones1f"]
    magicC = g["magicC"]
    x1_hbm = g["x1_hbm"]
    wg_h, wu_h, wd_h = g["wg_h"], g["wu_h"], g["wd_h"]
    identf = g["identf"]
    nc = k.nc
    x2_hbm = k.dram("x2_hbm", [D, T])
    wqg_hbm = k.dram("wqg_hbm", [D, INNER], BF16)
    wqu_hbm = k.dram("wqu_hbm", [D, INNER], BF16)
    wqd_hbm = k.dram("wqd_hbm", [INNER, D], BF16)
    prod_hbm = k.dram("prod_hbm", [INNER, 512])
    scr_hbm = k.dram("p2scr", [1, 1024])

    k.push()
    mrow = k.sb("mrow", [1, 8])
    mcol = k.sb("mcol", [128, 6])
    k.push()
    wload = k.sb("wload", [128, 4096])
    wqt = k.sb("wqt", [128, 4096], BF16)
    acc = k.sb("acc", [128, 3])
    accs = k.sb("accs", [128, 1])
    ps_m = k.ps("ps_m", [1, 1])
    ps_mb = k.ps("ps_mb", [128, 6])
    wabs = k.sb("wabs", [128, 4096])

    Ws = [(wg_h, wqg_hbm, D, "wg"), (wu_h, wqu_hbm, D, "wu"), (wd_h, wqd_hbm, INNER, "wd")]
    for wi, (wh, wqh, nrows, tag) in enumerate(Ws):
        nld = nrows // 128
        ncols = (D * INNER) // nrows
        for ld in range(nld):
            em.dma("qsp", (lambda wh, ld, ncols: lambda e: e.dma_start(
                out=wload[:, 0:ncols], in_=wh[ld * 128:(ld + 1) * 128, :]))(wh, ld, ncols),
                writes=[B_("wload")])
            k.a((lambda ncols: lambda e: e.activation(
                out=wabs[:, 0:ncols], in_=wload[:, 0:ncols], func=AF.Abs,
                accum_out=accs[:, 0:1]))(ncols),
                reads=[B_("wload")], writes=[B_("wabs"), B_("accs")])
            if ld == 0:
                k.v((lambda wi: lambda e: e.tensor_copy(acc[:, wi:wi + 1],
                                                        accs[:, 0:1]))(wi),
                    reads=[B_("accs")], writes=[B_("acc")])
            else:
                k.v((lambda wi: lambda e: e.tensor_tensor(
                    acc[:, wi:wi + 1], acc[:, wi:wi + 1], accs[:, 0:1], ALU.add))(wi),
                    reads=[B_("accs"), B_("acc")], writes=[B_("acc")])
        k.p((lambda wi: lambda e: e.matmul(ps_m[:, :], onescol[:, :],
                                           acc[:, wi:wi + 1], start=True, stop=True))(wi),
            reads=[B_("onescol"), B_("acc")], writes=[B_("ps_m")])
        k.v((lambda wi: lambda e: e.tensor_scalar(
            mrow[0:1, wi:wi + 1], ps_m[:, :], 1.0 / (D * INNER), None, ALU.mult))(wi),
            reads=[B_("ps_m")], writes=[B_("mrow")])
        k.v((lambda wi: lambda e: e.tensor_scalar(
            mrow[0:1, wi:wi + 1], mrow[0:1, wi:wi + 1], 1e-5, None, ALU.max))(wi),
            reads=[B_("mrow")], writes=[B_("mrow")])
        k.v((lambda wi: lambda e: e.reciprocal(mrow[0:1, 3 + wi:4 + wi],
                                               mrow[0:1, wi:wi + 1]))(wi),
            reads=[B_("mrow")], writes=[B_("mrow")])
    k.p(lambda e: e.matmul(ps_mb[:, :], ones1f[:, :], mrow[0:1, 0:6],
                           start=True, stop=True),
        reads=[B_("ones1f"), B_("mrow")], writes=[B_("ps_mb")])
    k.v(lambda e: e.tensor_copy(mcol[:, :], ps_mb[:, :]),
        reads=[B_("ps_mb")], writes=[B_("mcol")])
    for wi, (wh, wqh, nrows, tag) in enumerate(Ws):
        nld = nrows // 128
        ncols = (D * INNER) // nrows
        for ld in range(nld):
            em.dma("qsp", (lambda wh, ld, ncols: lambda e: e.dma_start(
                out=wload[:, 0:ncols], in_=wh[ld * 128:(ld + 1) * 128, :]))(wh, ld, ncols),
                writes=[B_("wload")])
            k.v((lambda wi, ncols: lambda e: e.tensor_scalar(
                wabs[:, 0:ncols], wload[:, 0:ncols], mcol[:, 3 + wi:4 + wi],
                None, ALU.mult))(wi, ncols),
                reads=[B_("wload"), B_("mcol")], writes=[B_("wabs")])
            k.v((lambda ncols: lambda e: e.tensor_scalar(
                wabs[:, 0:ncols], wabs[:, 0:ncols], magicC[:, 0:1], magicC[:, 0:1],
                ALU.add, ALU.subtract))(ncols),
                reads=[B_("wabs"), B_("magicC")], writes=[B_("wabs")])
            k.v((lambda ncols: lambda e: e.tensor_scalar(
                wqt[:, 0:ncols], wabs[:, 0:ncols], 1.0, -1.0, ALU.min, ALU.max))(ncols),
                reads=[B_("wabs")], writes=[B_("wqt")])
            em.dma("qsp", (lambda wqh, ld, ncols: lambda e: e.dma_start(
                out=wqh[ld * 128:(ld + 1) * 128, :], in_=wqt[:, 0:ncols]))(wqh, ld, ncols),
                reads=[B_("wqt")], writes=[B_(tag + "q")])
    k.pop()

    k.push()
    x1c = k.sb("x1c", [128, KD * 512])
    xnq = k.sb("xnq", [128, KD * 512])
    xq_b = k.sb("xq_b", [128, KD * 512], BF16)
    stat2 = k.sb("stat2", [1, 4 * 512])
    mbc2 = k.sb("mbc2", [128, 512])
    rbc2 = k.sb("rbc2", [128, 512])
    amax = k.sb("amax", [128, 512])
    amt = k.sb("amt", [128, 512])
    mx4 = k.sb("mx4", [128, 4])
    row4 = k.sb("row4", [4, 128])
    qrow = k.sb("qrow", [1, 3 * 512])
    scbc = k.sb("scbc", [128, 512])
    rsbc = k.sb("rsbc", [128, 512])
    wslab = k.sb("wslab", [128, 2 * 8 * 1024], BF16)
    gout = k.sb("gout", [128, 512])
    uout = k.sb("uout", [128, 512])
    prodc = k.sb("prodc", [128, 512])
    pq_sb = k.sb("pq_sb", [128, 32 * 512], BF16)
    wdsl = k.sb("wdsl", [128, 2 * 1024], BF16)
    x2sb = k.sb("x2sb", [128, 512])
    ps_s1 = k.ps("p2ps_s1", [1, 512])
    ps_bc = k.ps("p2ps_bc", [128, 512])
    ps_tr = k.ps("p2ps_tr", [128, 128])
    ps_g = k.ps("p2ps_g", [128, 512])
    ps_u = k.ps("p2ps_u", [128, 512])
    ps_dn = [k.ps(f"p2ps_dn{i}", [128, 512]) for i in range(2)]

    def pmax_to_rows(src_sb, srcbuf, n, rowdst, rowbuf, scr_off):
        nb = n // 128
        for blk in range(nb):
            k.p((lambda blk: lambda e: e.transpose(
                ps_tr[:, :], src_sb[:, blk * 128:(blk + 1) * 128], identf[:, :]))(blk),
                reads=[srcbuf, B_("identf")], writes=[B_("p2ps_tr")])
            k.v((lambda blk: lambda e: e.tensor_reduce(
                mx4[:, blk:blk + 1], ps_tr[:, :], AXX, ALU.max))(blk),
                reads=[B_("p2ps_tr")], writes=[B_("mx4")])
        k.p((lambda nb: lambda e: e.transpose(
            ps_tr[0:nb, 0:128], mx4[:, 0:nb], identf[:, :]))(nb),
            reads=[B_("mx4"), B_("identf")], writes=[B_("p2ps_tr")])
        k.v((lambda nb: lambda e: e.tensor_copy(row4[0:nb, :], ps_tr[0:nb, 0:128]))(nb),
            reads=[B_("p2ps_tr")], writes=[B_("row4")])
        em.dma("qact", (lambda nb, scr_off: lambda e: e.dma_start(
            out=scr_hbm[0:1, scr_off:scr_off + nb * 128].rearrange(
                "o (b c) -> (o b) c", b=nb),
            in_=row4[0:nb, :]))(nb, scr_off),
            reads=[B_("row4")], writes=[B_("p2scr")])
        em.dma("qact", (lambda n, scr_off: lambda e: e.dma_start(
            out=rowdst, in_=scr_hbm[0:1, scr_off:scr_off + n]))(n, scr_off),
            reads=[B_("p2scr")], writes=[rowbuf])

    for tc in range(T // 512):
        em.dma("qsp", (lambda tc: lambda e: e.dma_start(
            out=x1c[:, :].rearrange("p (kc t) -> p kc t", kc=KD),
            in_=x1_hbm[:, tc * 512:(tc + 1) * 512].rearrange("(kc p) t -> p kc t", p=128)))(tc),
            reads=[B_("x1_hbm")], writes=[B_("x1c")])
        k.p(lambda e: [e.matmul(
            ps_s1[:, :], onescol[:, :], x1c[:, kc * 512:(kc + 1) * 512],
            start=(kc == 0), stop=(kc == KD - 1)) for kc in range(KD)][-1],
            reads=[B_("onescol"), B_("x1c")], writes=[B_("p2ps_s1")])
        sm = stat2[0:1, 0:512]
        sv = stat2[0:1, 512:1024]
        sr = stat2[0:1, 1024:1536]
        sc2 = stat2[0:1, 1536:2048]
        k.v(lambda e: e.tensor_scalar(sm, ps_s1[:, :], 1.0 / D, None, ALU.mult),
            reads=[B_("p2ps_s1")], writes=[B_("stat2")])
        k.a(lambda e: e.activation(out=xnq[:, :], in_=x1c[:, :], func=AF.Square),
            reads=[B_("x1c")], writes=[B_("xnq")])
        k.p(lambda e: [e.matmul(
            ps_s1[:, :], onescol[:, :], xnq[:, kc * 512:(kc + 1) * 512],
            start=(kc == 0), stop=(kc == KD - 1)) for kc in range(KD)][-1],
            reads=[B_("onescol"), B_("xnq")], writes=[B_("p2ps_s1")])
        k.v(lambda e: e.tensor_scalar(sv, ps_s1[:, :], 1.0 / D, None, ALU.mult),
            reads=[B_("p2ps_s1")], writes=[B_("stat2")])
        k.v(lambda e: e.tensor_tensor(sc2, sm, sm, ALU.mult),
            reads=[B_("stat2")], writes=[B_("stat2")])
        k.v(lambda e: e.tensor_tensor(sv, sv, sc2, ALU.subtract),
            reads=[B_("stat2")], writes=[B_("stat2")])
        k.v(lambda e: e.tensor_scalar(sv, sv, 1e-5, None, ALU.add),
            reads=[B_("stat2")], writes=[B_("stat2")])
        rsqrt_rows(k, sr, sv, sc2, (B_("stat2"), B_("stat2"), B_("stat2")))
        k.p(lambda e: e.matmul(ps_bc[:, :], ones1f[:, :], sm, start=True, stop=True),
            reads=[B_("ones1f"), B_("stat2")], writes=[B_("p2ps_bc")])
        k.v(lambda e: e.tensor_copy(mbc2[:, :], ps_bc[:, :]),
            reads=[B_("p2ps_bc")], writes=[B_("mbc2")])
        k.p(lambda e: e.matmul(ps_bc[:, :], ones1f[:, :], sr, start=True, stop=True),
            reads=[B_("ones1f"), B_("stat2")], writes=[B_("p2ps_bc")])
        k.v(lambda e: e.tensor_copy(rbc2[:, :], ps_bc[:, :]),
            reads=[B_("p2ps_bc")], writes=[B_("rbc2")])
        for kc in range(KD):
            k.v((lambda kc: lambda e: e.tensor_tensor(
                xnq[:, kc * 512:(kc + 1) * 512], x1c[:, kc * 512:(kc + 1) * 512],
                mbc2[:, :], ALU.subtract))(kc),
                reads=[B_("x1c"), B_("mbc2")], writes=[B_("xnq")])
            k.v((lambda kc: lambda e: e.tensor_tensor(
                xnq[:, kc * 512:(kc + 1) * 512], xnq[:, kc * 512:(kc + 1) * 512],
                rbc2[:, :], ALU.mult))(kc),
                reads=[B_("xnq"), B_("rbc2")], writes=[B_("xnq")])
        k.a(lambda e: e.activation(out=amax[:, :], in_=xnq[:, 0:512],
                                   func=AF.Abs),
            reads=[B_("xnq")], writes=[B_("amax")])
        for kc in range(1, KD):
            k.a((lambda kc: lambda e: e.activation(
                out=amt[:, :], in_=xnq[:, kc * 512:(kc + 1) * 512],
                func=AF.Abs))(kc),
                reads=[B_("xnq")], writes=[B_("amt")])
            k.v(lambda e: e.tensor_tensor(amax[:, :], amax[:, :], amt[:, :],
                                          ALU.max),
                reads=[B_("amt"), B_("amax")], writes=[B_("amax")])
        pmax_to_rows(amax, B_("amax"), 512, qrow[0:1, 0:512], B_("qrow"), 0)
        q_m = qrow[0:1, 0:512]
        q_rs = qrow[0:1, 512:1024]
        q_sc = qrow[0:1, 1024:1536]
        k.v(lambda e: e.tensor_scalar(q_m, q_m, 1e-5, None, ALU.max),
            reads=[B_("qrow")], writes=[B_("qrow")])
        k.v(lambda e: e.reciprocal(q_sc, q_m), reads=[B_("qrow")], writes=[B_("qrow")])
        k.v(lambda e: e.tensor_scalar(q_sc, q_sc, 127.0, None, ALU.mult),
            reads=[B_("qrow")], writes=[B_("qrow")])
        k.v(lambda e: e.tensor_scalar(q_rs, q_m, 1.0 / 127.0, None, ALU.mult),
            reads=[B_("qrow")], writes=[B_("qrow")])
        k.p(lambda e: e.matmul(ps_bc[:, :], ones1f[:, :], q_sc, start=True, stop=True),
            reads=[B_("ones1f"), B_("qrow")], writes=[B_("p2ps_bc")])
        k.v(lambda e: e.tensor_copy(scbc[:, :], ps_bc[:, :]),
            reads=[B_("p2ps_bc")], writes=[B_("scbc")])
        k.p(lambda e: e.matmul(ps_bc[:, :], ones1f[:, :], q_rs, start=True, stop=True),
            reads=[B_("ones1f"), B_("qrow")], writes=[B_("p2ps_bc")])
        k.v(lambda e: e.tensor_copy(rsbc[:, :], ps_bc[:, :]),
            reads=[B_("p2ps_bc")], writes=[B_("rsbc")])
        for kc in range(KD):
            k.v((lambda kc: lambda e: e.tensor_tensor(
                xnq[:, kc * 512:(kc + 1) * 512], xnq[:, kc * 512:(kc + 1) * 512],
                scbc[:, :], ALU.mult))(kc),
                reads=[B_("xnq"), B_("scbc")], writes=[B_("xnq")])
        k.v(lambda e: e.tensor_scalar(xnq[:, :], xnq[:, :], magicC[:, 0:1],
                                      magicC[:, 0:1], ALU.add, ALU.subtract),
            reads=[B_("xnq"), B_("magicC")], writes=[B_("xnq")])
        k.v(lambda e: e.tensor_scalar(xq_b[:, :], xnq[:, :], 127.0, -128.0,
                                      ALU.min, ALU.max),
            reads=[B_("xnq")], writes=[B_("xq_b")])
        for icg in range(4):
            em.dma("qpool", (lambda icg: lambda e: e.dma_start(
                out=wslab[:, 0:8 * 1024].rearrange("p (kc c) -> p kc c", kc=KD),
                in_=wqg_hbm[:, icg * 1024:(icg + 1) * 1024].rearrange(
                    "(kc p) c -> p kc c", p=128)))(icg),
                reads=[B_("wgq")], writes=[B_("wslabg")])
            em.dma("qpool", (lambda icg: lambda e: e.dma_start(
                out=wslab[:, 8 * 1024:16 * 1024].rearrange("p (kc c) -> p kc c", kc=KD),
                in_=wqu_hbm[:, icg * 1024:(icg + 1) * 1024].rearrange(
                    "(kc p) c -> p kc c", p=128)))(icg),
                reads=[B_("wuq")], writes=[B_("wslabu")])
            for ici in range(8):
                ic = icg * 8 + ici
                k.p((lambda ici: lambda e: [e.matmul(
                    ps_g[:, :], wslab[:, kc * 1024 + ici * 128:kc * 1024 + ici * 128 + 128],
                    xq_b[:, kc * 512:(kc + 1) * 512],
                    start=(kc == 0), stop=(kc == KD - 1)) for kc in range(KD)][-1])(ici),
                    reads=[B_("wslabg"), B_("xq_b")], writes=[B_("p2ps_g")])
                k.p((lambda ici: lambda e: [e.matmul(
                    ps_u[:, :], wslab[:, 8192 + kc * 1024 + ici * 128:8192 + kc * 1024 + ici * 128 + 128],
                    xq_b[:, kc * 512:(kc + 1) * 512],
                    start=(kc == 0), stop=(kc == KD - 1)) for kc in range(KD)][-1])(ici),
                    reads=[B_("wslabu"), B_("xq_b")], writes=[B_("p2ps_u")])
                k.v(lambda e: e.scalar_tensor_tensor(
                    gout[:, :], ps_g[:, :], mcol[:, 0:1], rsbc[:, :],
                    ALU.mult, ALU.mult),
                    reads=[B_("p2ps_g"), B_("mcol"), B_("rsbc")], writes=[B_("gout")])
                k.a(lambda e: e.activation(out=gout[:, :], in_=gout[:, :],
                                           func=AF.Silu),
                    reads=[B_("gout")], writes=[B_("gout")])
                k.v(lambda e: e.scalar_tensor_tensor(
                    uout[:, :], ps_u[:, :], mcol[:, 1:2], rsbc[:, :],
                    ALU.mult, ALU.mult),
                    reads=[B_("p2ps_u"), B_("mcol"), B_("rsbc")], writes=[B_("uout")])
                k.v(lambda e: e.tensor_tensor(prodc[:, :], gout[:, :], uout[:, :],
                                              ALU.mult),
                    reads=[B_("gout"), B_("uout")], writes=[B_("prodc")])
                em.dma("qsp", (lambda ic: lambda e: e.dma_start(
                    out=prod_hbm[ic * 128:(ic + 1) * 128, :], in_=prodc[:, :]))(ic),
                    reads=[B_("prodc")], writes=[B_("prod_hbm")])
                if ic == 0:
                    k.a(lambda e: e.activation(out=amax[:, :], in_=prodc[:, :],
                                               func=AF.Abs),
                        reads=[B_("prodc")], writes=[B_("amax")])
                else:
                    k.a(lambda e: e.activation(out=amt[:, :], in_=prodc[:, :],
                                               func=AF.Abs),
                        reads=[B_("prodc")], writes=[B_("amt")])
                    k.v(lambda e: e.tensor_tensor(amax[:, :], amax[:, :],
                                                  amt[:, :], ALU.max),
                        reads=[B_("amt"), B_("amax")], writes=[B_("amax")])
        pmax_to_rows(amax, B_("amax"), 512, qrow[0:1, 0:512], B_("qrow"), 512)
        k.v(lambda e: e.tensor_scalar(q_m, q_m, 1e-5, None, ALU.max),
            reads=[B_("qrow")], writes=[B_("qrow")])
        k.v(lambda e: e.reciprocal(q_sc, q_m), reads=[B_("qrow")], writes=[B_("qrow")])
        k.v(lambda e: e.tensor_scalar(q_sc, q_sc, 127.0, None, ALU.mult),
            reads=[B_("qrow")], writes=[B_("qrow")])
        k.v(lambda e: e.tensor_scalar(q_rs, q_m, 1.0 / 127.0, None, ALU.mult),
            reads=[B_("qrow")], writes=[B_("qrow")])
        k.p(lambda e: e.matmul(ps_bc[:, :], ones1f[:, :], q_sc, start=True, stop=True),
            reads=[B_("ones1f"), B_("qrow")], writes=[B_("p2ps_bc")])
        k.v(lambda e: e.tensor_copy(scbc[:, :], ps_bc[:, :]),
            reads=[B_("p2ps_bc")], writes=[B_("scbc")])
        k.p(lambda e: e.matmul(ps_bc[:, :], ones1f[:, :], q_rs, start=True, stop=True),
            reads=[B_("ones1f"), B_("qrow")], writes=[B_("p2ps_bc")])
        k.v(lambda e: e.tensor_copy(rsbc[:, :], ps_bc[:, :]),
            reads=[B_("p2ps_bc")], writes=[B_("rsbc")])
        for ic in range(32):
            em.dma("qsp", (lambda ic: lambda e: e.dma_start(
                out=prodc[:, :], in_=prod_hbm[ic * 128:(ic + 1) * 128, :]))(ic),
                reads=[B_("prod_hbm")], writes=[B_("prodc")])
            k.v(lambda e: e.tensor_tensor(prodc[:, :], prodc[:, :], scbc[:, :],
                                          ALU.mult),
                reads=[B_("prodc"), B_("scbc")], writes=[B_("prodc")])
            k.v(lambda e: e.tensor_scalar(prodc[:, :], prodc[:, :], magicC[:, 0:1],
                                          magicC[:, 0:1], ALU.add, ALU.subtract),
                reads=[B_("prodc"), B_("magicC")], writes=[B_("prodc")])
            k.v((lambda ic: lambda e: e.tensor_scalar(
                pq_sb[:, ic * 512:(ic + 1) * 512], prodc[:, :], 127.0, -128.0,
                ALU.min, ALU.max))(ic),
                reads=[B_("prodc")], writes=[B_("pq_sb")])
        for ocp in range(4):
            for ic in range(32):
                em.dma("qpool", (lambda ic: lambda e: e.dma_start(
                    out=wdsl[:, (ic % 2) * 1024:(ic % 2) * 1024 + 1024],
                    in_=wqd_hbm[ic * 128:(ic + 1) * 128, :]))(ic),
                    reads=[B_("wdq")], writes=[B_(f"wdsl{ic % 2}")])
                for oi in range(2):
                    oc = ocp * 2 + oi
                    pd, pdb = ps_dn[oi], B_(f"p2ps_dn{oi}")
                    k.p((lambda ic, oc, pd: lambda e: e.matmul(
                        pd[:, :], wdsl[:, (ic % 2) * 1024 + oc * 128:(ic % 2) * 1024 + oc * 128 + 128],
                        pq_sb[:, ic * 512:(ic + 1) * 512],
                        start=(ic == 0), stop=(ic == 31)))(ic, oc, pd),
                        reads=[B_(f"wdsl{ic % 2}"), B_("pq_sb")], writes=[pdb])
            for oi in range(2):
                oc = ocp * 2 + oi
                pd, pdb = ps_dn[oi], B_(f"p2ps_dn{oi}")
                k.v((lambda oc, pd: lambda e: e.scalar_tensor_tensor(
                    x2sb[:, :], pd[:, :], mcol[:, 2:3], rsbc[:, :],
                    ALU.mult, ALU.mult))(oc, pd),
                    reads=[pdb, B_("mcol"), B_("rsbc")], writes=[B_("x2sb")])
                k.v((lambda oc: lambda e: e.tensor_tensor(
                    x2sb[:, :], x2sb[:, :], x1c[:, oc * 512:(oc + 1) * 512],
                    ALU.add))(oc),
                    reads=[B_("x2sb"), B_("x1c")], writes=[B_("x2sb")])
                em.dma("qsp", (lambda oc, tc: lambda e: e.dma_start(
                    out=x2_hbm[oc * 128:(oc + 1) * 128, tc * 512:(tc + 1) * 512],
                    in_=x2sb[:, :]))(oc, tc),
                    reads=[B_("x2sb")], writes=[B_("x2_hbm")])
    k.pop()
    k.pop(barrier=False)
    env["x2_hbm"] = x2_hbm


def build_p3(k, env):
    em, B_ = k.em, k.B
    g = env
    onescol, ones1f = g["onescol"], g["ones1f"]
    x2_hbm = g["x2_hbm"]
    nc = k.nc

    def inp(name, shape, dt=F32):
        return nc.dram_tensor(name, list(shape), dt, kind="ExternalInput").ap()

    wx_h = inp("l1_wx_t", [D, D])
    wz_h = inp("l1_wz_t", [D, D])
    wp_h = inp("l1_wp_t", [D, 96])
    wdt_h = inp("l1_wdt_sl", [DTR, 512])
    wo_h = inp("l1_wo_t", [D, D])
    alog_h = inp("l1_alog_sl", [128, 4 * NST])
    bdt_h = inp("l1_bdt_sl", [128, 4])
    dp_h = inp("l1_dp_sl", [128, 4])
    convw_h = inp("l1_convw_sl", [128, 16])
    xpg_idx_h = inp("xpg_idx", [128, 8], I32)
    z_idx_h = inp("z_idx", [128, 64], I32)
    dtun_idx_h = inp("dtun_idx", [64, 2], I32)
    bc_idx_h = inp("bc_idx", [32, 2], I32)
    y_idx_h = inp("y_idx", [128, 64], I32)
    bchot_h = inp("bchot", [32, 32 * 128], BF16)

    ag2_in = k.dram("ag2_in", [1120, T])
    ag2_out = k.dram("ag2_out", [N_CORES * 1120, T])
    ag2z_in = k.dram("ag2z_in", [1024, T], BF16)
    ag2z_out = k.dram("ag2z_out", [N_CORES * 1024, T], BF16)
    ag3_in = k.dram("ag3_in", [512, TFULL], BF16)
    ag3_out = k.dram("ag3_out", [N_CORES * 512, TFULL], BF16)
    xp_loc = k.dram("xp_loc", [512, TFULL + 8])
    x3_hbm = k.dram("x3_hbm", [D, T])
    env["x3_hbm"] = x3_hbm

    k.push()
    x2c = k.sb("p3x2c", [128, KD * 512])
    xnb = k.sb("p3xn", [128, KD * 512], BF16)
    xsq3 = k.sb("p3xsq", [128, KD * 512])
    stat3 = k.sb("p3stat", [1, 4 * 512])
    mbc3 = k.sb("p3mbc", [128, 512])
    rbc3 = k.sb("p3rbc", [128, 512])
    wx_sb = k.sb("wx_sb", [128, KD * D], BF16)
    wz_sb = k.sb("wz_sb", [128, KD * D], BF16)
    wp_sb = k.sb("wp_sb", [128, KD * 96], BF16)
    prj = k.sb("prj", [128, 512])
    prjz = k.sb("prjz", [128, 512], BF16)
    ps3_s = k.ps("p3ps_s", [1, 512])
    ps3_b = k.ps("p3ps_b", [128, 512])
    ps3_m = [k.ps(f"p3ps_m{i}", [128, 512]) for i in range(2)]
    ps3_p = k.ps("p3ps_p", [96, 512])

    for kc in range(KD):
        em.dma("qpool", (lambda kc: lambda e: e.dma_start(
            out=wx_sb[:, kc * D:(kc + 1) * D], in_=wx_h[kc * 128:(kc + 1) * 128, :]))(kc),
            writes=[B_("wx_sb")])
        em.dma("qpool", (lambda kc: lambda e: e.dma_start(
            out=wz_sb[:, kc * D:(kc + 1) * D], in_=wz_h[kc * 128:(kc + 1) * 128, :]))(kc),
            writes=[B_("wz_sb")])
        em.dma("qpool", (lambda kc: lambda e: e.dma_start(
            out=wp_sb[:, kc * 96:(kc + 1) * 96], in_=wp_h[kc * 128:(kc + 1) * 128, :]))(kc),
            writes=[B_("wp_sb")])

    for tc in range(T // 512):
        em.dma("qsp", (lambda tc: lambda e: e.dma_start(
            out=x2c[:, :].rearrange("p (kc t) -> p kc t", kc=KD),
            in_=x2_hbm[:, tc * 512:(tc + 1) * 512].rearrange("(kc p) t -> p kc t", p=128)))(tc),
            reads=[B_("x2_hbm")], writes=[B_("p3x2c")])
        k.p(lambda e: [e.matmul(
            ps3_s[:, :], onescol[:, :], x2c[:, kc * 512:(kc + 1) * 512],
            start=(kc == 0), stop=(kc == KD - 1)) for kc in range(KD)][-1],
            reads=[B_("onescol"), B_("p3x2c")], writes=[B_("p3ps_s")])
        sm = stat3[0:1, 0:512]
        sv = stat3[0:1, 512:1024]
        sr = stat3[0:1, 1024:1536]
        s4 = stat3[0:1, 1536:2048]
        k.v(lambda e: e.tensor_scalar(sm, ps3_s[:, :], 1.0 / D, None, ALU.mult),
            reads=[B_("p3ps_s")], writes=[B_("p3stat")])
        k.a(lambda e: e.activation(out=xsq3[:, :], in_=x2c[:, :], func=AF.Square),
            reads=[B_("p3x2c")], writes=[B_("p3xsq")])
        k.p(lambda e: [e.matmul(
            ps3_s[:, :], onescol[:, :], xsq3[:, kc * 512:(kc + 1) * 512],
            start=(kc == 0), stop=(kc == KD - 1)) for kc in range(KD)][-1],
            reads=[B_("onescol"), B_("p3xsq")], writes=[B_("p3ps_s")])
        k.v(lambda e: e.tensor_scalar(sv, ps3_s[:, :], 1.0 / D, None, ALU.mult),
            reads=[B_("p3ps_s")], writes=[B_("p3stat")])
        k.v(lambda e: e.tensor_tensor(s4, sm, sm, ALU.mult),
            reads=[B_("p3stat")], writes=[B_("p3stat")])
        k.v(lambda e: e.tensor_tensor(sv, sv, s4, ALU.subtract),
            reads=[B_("p3stat")], writes=[B_("p3stat")])
        k.v(lambda e: e.tensor_scalar(sv, sv, 1e-5, None, ALU.add),
            reads=[B_("p3stat")], writes=[B_("p3stat")])
        rsqrt_rows(k, sr, sv, s4, (B_("p3stat"), B_("p3stat"), B_("p3stat")))
        k.p(lambda e: e.matmul(ps3_b[:, :], ones1f[:, :], sm, start=True, stop=True),
            reads=[B_("ones1f"), B_("p3stat")], writes=[B_("p3ps_b")])
        k.v(lambda e: e.tensor_copy(mbc3[:, :], ps3_b[:, :]),
            reads=[B_("p3ps_b")], writes=[B_("p3mbc")])
        k.p(lambda e: e.matmul(ps3_b[:, :], ones1f[:, :], sr, start=True, stop=True),
            reads=[B_("ones1f"), B_("p3stat")], writes=[B_("p3ps_b")])
        k.v(lambda e: e.tensor_copy(rbc3[:, :], ps3_b[:, :]),
            reads=[B_("p3ps_b")], writes=[B_("p3rbc")])
        for kc in range(KD):
            k.v((lambda kc: lambda e: e.tensor_tensor(
                xsq3[:, kc * 512:(kc + 1) * 512], x2c[:, kc * 512:(kc + 1) * 512],
                mbc3[:, :], ALU.subtract))(kc),
                reads=[B_("p3x2c"), B_("p3mbc")], writes=[B_("p3xsq")])
            k.v((lambda kc: lambda e: e.tensor_tensor(
                xnb[:, kc * 512:(kc + 1) * 512], xsq3[:, kc * 512:(kc + 1) * 512],
                rbc3[:, :], ALU.mult))(kc),
                reads=[B_("p3xsq"), B_("p3rbc")], writes=[B_("p3xn")])
        for wi, (w_sb, wbuf) in enumerate([(wx_sb, "wx_sb"), (wz_sb, "wz_sb")]):
            for oc in range(KD):
                pm, pmb = ps3_m[oc % 2], B_(f"p3ps_m{oc % 2}")
                k.p((lambda oc, w_sb, pm: lambda e: [e.matmul(
                    pm[:, :], w_sb[:, (kc * KD + oc) * 128:(kc * KD + oc) * 128 + 128],
                    xnb[:, kc * 512:(kc + 1) * 512],
                    start=(kc == 0), stop=(kc == KD - 1)) for kc in range(KD)][-1])(oc, w_sb, pm),
                    reads=[B_(wbuf), B_("p3xn")], writes=[pmb])
                if wi == 0:
                    k.v((lambda pm: lambda e: e.tensor_copy(prj[:, :], pm[:, :]))(pm),
                        reads=[pmb], writes=[B_("prj")])
                    em.dma("qsp", (lambda oc, tc: lambda e: e.dma_start(
                        out=ag2_in[oc * 128:oc * 128 + 128, tc * 512:(tc + 1) * 512],
                        in_=prj[:, :]))(oc, tc),
                        reads=[B_("prj")], writes=[B_("ag2_in")])
                else:
                    k.v((lambda pm: lambda e: e.tensor_copy(prjz[:, :], pm[:, :]))(pm),
                        reads=[pmb], writes=[B_("prjz")])
                    em.dma("qsp", (lambda oc, tc: lambda e: e.dma_start(
                        out=ag2z_in[oc * 128:oc * 128 + 128, tc * 512:(tc + 1) * 512],
                        in_=prjz[:, :]))(oc, tc),
                        reads=[B_("prjz")], writes=[B_("ag2z_in")])
        k.p(lambda e: [e.matmul(
            ps3_p[:, :], wp_sb[:, kc * 96:(kc + 1) * 96],
            xnb[:, kc * 512:(kc + 1) * 512],
            start=(kc == 0), stop=(kc == KD - 1)) for kc in range(KD)][-1],
            reads=[B_("wp_sb"), B_("p3xn")], writes=[B_("p3ps_p")])
        k.v(lambda e: e.tensor_copy(prj[0:96, :], ps3_p[:, :]),
            reads=[B_("p3ps_p")], writes=[B_("prj")])
        em.dma("qsp", (lambda tc: lambda e: e.dma_start(
            out=ag2_in[1024:1120, tc * 512:(tc + 1) * 512], in_=prj[0:96, :]))(tc),
            reads=[B_("prj")], writes=[B_("ag2_in")])
    k.pop()

    k.g(lambda e: e.collective_compute(
        "AllGather", ALU.bypass, replica_groups=[list(range(N_CORES))],
        ins=[ag2_in[:, :]], outs=[ag2_out[:, :]]),
        reads=[B_("ag2_in")], writes=[B_("ag2")])
    k.g(lambda e: e.collective_compute(
        "AllGather", ALU.bypass, replica_groups=[list(range(N_CORES))],
        ins=[ag2z_in[:, :]], outs=[ag2z_out[:, :]]),
        reads=[B_("ag2z_in")], writes=[B_("ag2")])

    k.push()
    xpg_idx = k.sb("xpg_idx_s", [128, 8], I32)
    z_idx = k.sb("z_idx_s", [128, 64], I32)
    dtun_idx = k.sb("dtun_idx_s", [64, 2], I32)
    bc_idx = k.sb("bc_idx_s", [32, 2], I32)
    alog = k.sb("alog_s", [128, 4 * NST])
    acol = k.sb("acol", [128, 4 * NST])
    invA = k.sb("invA", [128, 4 * NST])
    bdtc = k.sb("bdtc_s", [128, 4])
    dpc = k.sb("dpc_s", [128, 4])
    cwc = k.sb("cwc_s", [128, 16])
    wdt_sb = k.sb("wdt_sb", [64, 512], BF16)
    dtun_sb = k.sb("dtun_sb", [64, TFULL], BF16)
    bc_f = k.sb("bc_f", [32, TFULL])
    bc_sb = k.sb("bc_sb", [32, TFULL], BF16)
    bchot = k.sb("bchot_s", [32, 32 * 128], BF16)
    gath = k.sb("gath", [128, T])
    for dst, src, nm in [(xpg_idx, xpg_idx_h, "xpg_idx"), (z_idx, z_idx_h, "z_idx"),
                         (dtun_idx, dtun_idx_h, "dtun_idx"), (bc_idx, bc_idx_h, "bc_idx"),
                         (alog, alog_h, "alog"), (bdtc, bdt_h, "bdtc"),
                         (dpc, dp_h, "dpc"), (cwc, convw_h, "cwc"),
                         (bchot, bchot_h, "bchot")]:
        em.dma("qsp", (lambda d, s: lambda e: e.dma_start(out=d[:], in_=s[:, :]))(dst, src),
               writes=[B_(nm)])
    em.dma("qpool", lambda e: e.dma_start(out=wdt_sb[:, :], in_=wdt_h[:, :]),
           writes=[B_("wdt_sb")])
    k.a(lambda e: e.activation(out=acol[:, :], in_=alog[:, :], func=AF.Exp),
        reads=[B_("alog")], writes=[B_("acol")])
    k.v(lambda e: e.tensor_scalar(acol[:, :], acol[:, :], -1.0, None, ALU.mult),
        reads=[B_("acol")], writes=[B_("acol")])
    k.v(lambda e: e.tensor_scalar(invA[:, :], acol[:, :], 1e-8, None, ALU.add),
        reads=[B_("acol")], writes=[B_("invA")])
    k.v(lambda e: e.reciprocal(invA[:, :], invA[:, :]),
        reads=[B_("invA")], writes=[B_("invA")])
    for r01 in range(2):
        em.dma("qpool", (lambda r01: lambda e: e.indirect_dma_start(
            dtun_sb[:, r01 * T:(r01 + 1) * T], None, ag2_out[:, :],
            bass.IndirectOffsetOnAxis(dtun_idx[:, r01:r01 + 1], 0)))(r01),
            reads=[B_("dtun_idx"), B_("ag2")], writes=[B_("dtun_sb")])
        em.dma("qpool", (lambda r01: lambda e: e.indirect_dma_start(
            bc_f[:, r01 * T:(r01 + 1) * T], None, ag2_out[:, :],
            bass.IndirectOffsetOnAxis(bc_idx[:, r01:r01 + 1], 0)))(r01),
            reads=[B_("bc_idx"), B_("ag2")], writes=[B_("bc_f")])
        k.v((lambda r01: lambda e: e.tensor_copy(
            bc_sb[:, r01 * T:(r01 + 1) * T], bc_f[:, r01 * T:(r01 + 1) * T]))(r01),
            reads=[B_("bc_f")], writes=[B_("bc_sb")])
        for dc in range(4):
            em.dma("qpool", (lambda r01, dc: lambda e: e.indirect_dma_start(
                gath[:, :], None, ag2_out[:, :],
                bass.IndirectOffsetOnAxis(xpg_idx[:, dc * 2 + r01:dc * 2 + r01 + 1], 0)))(r01, dc),
                reads=[B_("xpg_idx"), B_("ag2")], writes=[B_("gath")])
            em.dma("qsp", (lambda r01, dc: lambda e: e.dma_start(
                out=xp_loc[dc * 128:(dc + 1) * 128, 8 + r01 * T:8 + (r01 + 1) * T],
                in_=gath[:, :]))(r01, dc),
                reads=[B_("gath")], writes=[B_("xp_loc")])
    zpad = k.sb("zpad", [128, 8])
    k.v(lambda e: e.memset(zpad[:, :], 0.0), writes=[B_("zpad")])
    em.dma("qsp", lambda e: e.dma_start(out=xp_loc[0:128, 0:8], in_=zpad[:, :]),
           reads=[B_("zpad")], writes=[B_("xp_loc")])
    em.dma("qsp", lambda e: e.dma_start(out=xp_loc[128:256, 0:8], in_=zpad[:, :]),
           reads=[B_("zpad")], writes=[B_("xp_loc")])
    em.dma("qsp", lambda e: e.dma_start(out=xp_loc[256:384, 0:8], in_=zpad[:, :]),
           reads=[B_("zpad")], writes=[B_("xp_loc")])
    em.dma("qsp", lambda e: e.dma_start(out=xp_loc[384:512, 0:8], in_=zpad[:, :]),
           reads=[B_("zpad")], writes=[B_("xp_loc")])

    xps = k.sb("xps", [128, 2 * 520])
    zs = k.sb("zs", [128, 2 * 512], BF16)
    dtf = k.sb("dtf", [128, 512])
    sp0 = k.sb("sp0", [128, 512])
    sp1 = k.sb("sp1", [128, 512])
    uf = k.sb("uf", [128, 512])
    a_f = k.sb("a_f", [128, 512])
    bu2 = k.sb("bu2", [128, 512])
    inpt = k.sb("inpt", [128, 512])
    hstr = k.sb("hstr", [128, 512])
    yacc = k.sb("yacc", [128, 512])
    ytmp = k.sb("ytmp", [128, 512])
    hcarry = k.sb("hcarry", [128, 4 * NST])
    bbc = k.sb("bbc", [128, NST * 512], BF16)
    cbc = k.sb("cbc", [128, NST * 512], BF16)
    ygb = k.sb("ygb", [128, 512], BF16)
    ps3_d = k.ps("p3ps_d", [128, 512])
    ps3_bb = k.ps("p3ps_bb", [128, 512])

    for tch in range(TFULL // 512):
        for n in range(NST):
            k.p((lambda n, tch: lambda e: e.matmul(
                ps3_bb[:, :], bchot[:, n * 128:(n + 1) * 128],
                bc_sb[:, tch * 512:(tch + 1) * 512],
                start=True, stop=True))(n, tch),
                reads=[B_("bchot"), B_("bc_sb")], writes=[B_("p3ps_bb")])
            k.v((lambda n: lambda e: e.tensor_copy(
                bbc[:, n * 512:(n + 1) * 512], ps3_bb[:, :]))(n),
                reads=[B_("p3ps_bb")], writes=[B_("bbc")])
            k.p((lambda n, tch: lambda e: e.matmul(
                ps3_bb[:, :], bchot[:, (NST + n) * 128:(NST + n + 1) * 128],
                bc_sb[:, tch * 512:(tch + 1) * 512],
                start=True, stop=True))(n, tch),
                reads=[B_("bchot"), B_("bc_sb")], writes=[B_("p3ps_bb")])
            k.v((lambda n: lambda e: e.tensor_copy(
                cbc[:, n * 512:(n + 1) * 512], ps3_bb[:, :]))(n),
                reads=[B_("p3ps_bb")], writes=[B_("cbc")])
        for dc in range(4):
            k.p((lambda dc, tch: lambda e: e.matmul(
                ps3_d[:, :], wdt_sb[:, dc * 128:(dc + 1) * 128],
                dtun_sb[:, tch * 512:(tch + 1) * 512], start=True, stop=True))(dc, tch),
                reads=[B_("wdt_sb"), B_("dtun_sb")], writes=[B_("p3ps_d")])
            k.v((lambda dc: lambda e: e.tensor_scalar(
                dtf[:, :], ps3_d[:, :], bdtc[:, dc:dc + 1], None, ALU.add))(dc),
                reads=[B_("p3ps_d"), B_("bdtc")], writes=[B_("dtf")])
            k.a(lambda e: e.activation(out=sp0[:, :], in_=dtf[:, :],
                                       func=AF.Abs),
                reads=[B_("dtf")], writes=[B_("sp0")])
            k.a(lambda e: e.activation(out=sp1[:, :], in_=sp0[:, :], func=AF.Exp,
                                       scale=-1.0),
                reads=[B_("sp0")], writes=[B_("sp1")])
            k.a(lambda e: e.activation(out=sp1[:, :], in_=sp1[:, :], func=AF.Ln,
                                       bias=1.0),
                reads=[B_("sp1")], writes=[B_("sp1")])
            k.v(lambda e: e.tensor_scalar(sp0[:, :], dtf[:, :], 0.0, None, ALU.max),
                reads=[B_("dtf")], writes=[B_("sp0")])
            k.v(lambda e: e.tensor_tensor(dtf[:, :], sp0[:, :], sp1[:, :], ALU.add),
                reads=[B_("sp0"), B_("sp1")], writes=[B_("dtf")])
            em.dma("qsp", (lambda dc, tch: lambda e: e.dma_start(
                out=xps[:, (tch % 2) * 520:(tch % 2) * 520 + 515],
                in_=xp_loc[dc * 128:(dc + 1) * 128, 5 + tch * 512:5 + tch * 512 + 515]))(dc, tch),
                reads=[B_("xp_loc")], writes=[B_(f"xps{tch % 2}")])
            xb0 = (tch % 2) * 520
            k.v((lambda dc, xb0: lambda e: e.tensor_scalar(
                uf[:, :], xps[:, xb0:xb0 + 512], cwc[:, dc * 4:dc * 4 + 1],
                None, ALU.mult))(dc, xb0),
                reads=[B_(f"xps{tch % 2}"), B_("cwc")], writes=[B_("uf")])
            for j in range(1, 4):
                k.v((lambda dc, j, xb0: lambda e: e.scalar_tensor_tensor(
                    uf[:, :], xps[:, xb0 + j:xb0 + j + 512],
                    cwc[:, dc * 4 + j:dc * 4 + j + 1], uf[:, :],
                    ALU.mult, ALU.add))(dc, j, xb0),
                    reads=[B_(f"xps{tch % 2}"), B_("cwc"), B_("uf")], writes=[B_("uf")])
            k.a(lambda e: e.activation(out=uf[:, :], in_=uf[:, :], func=AF.Silu),
                reads=[B_("uf")], writes=[B_("uf")])
            if tch == 0 and dc == 0:
                k.v(lambda e: e.memset(hcarry[:, :], 0.0), writes=[B_("hcarry")])
            for n in range(NST):
                an = dc * NST + n
                k.a((lambda an: lambda e: e.activation(
                    out=a_f[:, :], in_=dtf[:, :], func=AF.Exp,
                    scale=acol[:, an:an + 1]))(an),
                    reads=[B_("dtf"), B_("acol")], writes=[B_("a_f")])
                k.v((lambda an, n: lambda e: e.scalar_tensor_tensor(
                    bu2[:, :], uf[:, :], invA[:, an:an + 1],
                    bbc[:, n * 512:(n + 1) * 512], ALU.mult, ALU.mult))(an, n),
                    reads=[B_("uf"), B_("invA"), B_("bbc")], writes=[B_("bu2")])
                k.v(lambda e: e.scalar_tensor_tensor(
                    inpt[:, :], a_f[:, :], 1.0, bu2[:, :], ALU.subtract, ALU.mult),
                    reads=[B_("a_f"), B_("bu2")], writes=[B_("inpt")])
                k.v((lambda an: lambda e: e.tensor_tensor_scan(
                    hstr[:, :], a_f[:, :], inpt[:, :], hcarry[:, an:an + 1],
                    ALU.mult, ALU.add))(an),
                    reads=[B_("a_f"), B_("inpt"), B_("hcarry")], writes=[B_("hstr")])
                k.v((lambda an: lambda e: e.tensor_copy(
                    hcarry[:, an:an + 1], hstr[:, 511:512]))(an),
                    reads=[B_("hstr")], writes=[B_("hcarry")])
                if n == 0:
                    k.v((lambda n: lambda e: e.tensor_tensor(
                        yacc[:, :], hstr[:, :], cbc[:, n * 512:(n + 1) * 512],
                        ALU.mult))(n),
                        reads=[B_("hstr"), B_("cbc")], writes=[B_("yacc")])
                else:
                    k.g((lambda n: lambda e: e.tensor_tensor(
                        ytmp[:, :], hstr[:, :], cbc[:, n * 512:(n + 1) * 512],
                        ALU.mult))(n),
                        reads=[B_("hstr"), B_("cbc")], writes=[B_("ytmp")])
                    k.g(lambda e: e.tensor_tensor(yacc[:, :], yacc[:, :],
                                                  ytmp[:, :], ALU.add),
                        reads=[B_("ytmp"), B_("yacc")], writes=[B_("yacc")])
            em.dma("qpool", (lambda dc, tch: lambda e: e.indirect_dma_start(
                zs[:, (tch % 2) * 512:(tch % 2) * 512 + 512], None,
                ag2z_out[:, :].rearrange("r (c t) -> (r c) t", t=512),
                bass.IndirectOffsetOnAxis(z_idx[:, dc * 16 + tch:dc * 16 + tch + 1], 0)))(dc, tch),
                reads=[B_("z_idx"), B_("ag2")], writes=[B_(f"zs{tch % 2}")])
            k.v((lambda dc: lambda e: e.scalar_tensor_tensor(
                ytmp[:, :], uf[:, :], dpc[:, dc:dc + 1], yacc[:, :],
                ALU.mult, ALU.add))(dc),
                reads=[B_("uf"), B_("dpc"), B_("yacc")], writes=[B_("ytmp")])
            k.a((lambda tch: lambda e: e.activation(
                out=sp0[:, :], in_=zs[:, (tch % 2) * 512:(tch % 2) * 512 + 512],
                func=AF.Silu))(tch),
                reads=[B_(f"zs{tch % 2}")], writes=[B_("sp0")])
            k.v(lambda e: e.tensor_tensor(ygb[:, :], ytmp[:, :], sp0[:, :], ALU.mult),
                reads=[B_("ytmp"), B_("sp0")], writes=[B_("ygb")])
            em.dma("qsp", (lambda dc, tch: lambda e: e.dma_start(
                out=ag3_in[dc * 128:(dc + 1) * 128, tch * 512:(tch + 1) * 512],
                in_=ygb[:, :]))(dc, tch),
                reads=[B_("ygb")], writes=[B_("ag3_in")])
    k.pop()

    k.g(lambda e: e.collective_compute(
        "AllGather", ALU.bypass, replica_groups=[list(range(N_CORES))],
        ins=[ag3_in[:, :]], outs=[ag3_out[:, :]]),
        reads=[B_("ag3_in")], writes=[B_("ag3")])

    k.push()
    y_idx = k.sb("y_idx_s", [128, 64], I32)
    em.dma("qsp", lambda e: e.dma_start(out=y_idx[:, :], in_=y_idx_h[:, :]),
           writes=[B_("y_idx")])
    wo_sb = k.sb("wo_sb", [128, KD * D], BF16)
    ysl = k.sb("ysl", [128, KD * 512], BF16)
    x2c2 = k.sb("p3x2c2", [128, KD * 512])
    x3sb = k.sb("x3sb", [128, 512])
    ps3_o = [k.ps(f"p3ps_o{i}", [128, 512]) for i in range(2)]
    for kc in range(KD):
        em.dma("qpool", (lambda kc: lambda e: e.dma_start(
            out=wo_sb[:, kc * D:(kc + 1) * D], in_=wo_h[kc * 128:(kc + 1) * 128, :]))(kc),
            writes=[B_("wo_sb")])
    for tc in range(T // 512):
        for kc in range(KD):
            em.dma("qpool", (lambda kc, tc: lambda e: e.indirect_dma_start(
                ysl[:, kc * 512:(kc + 1) * 512], None,
                ag3_out[:, :].rearrange("r (c t) -> (r c) t", t=512),
                bass.IndirectOffsetOnAxis(y_idx[:, kc * 8 + tc:kc * 8 + tc + 1], 0)))(kc, tc),
                reads=[B_("y_idx"), B_("ag3")], writes=[B_("ysl")])
        em.dma("qsp", (lambda tc: lambda e: e.dma_start(
            out=x2c2[:, :].rearrange("p (kc t) -> p kc t", kc=KD),
            in_=x2_hbm[:, tc * 512:(tc + 1) * 512].rearrange("(kc p) t -> p kc t", p=128)))(tc),
            reads=[B_("x2_hbm")], writes=[B_("p3x2c2")])
        for oc in range(KD):
            po, pob = ps3_o[oc % 2], B_(f"p3ps_o{oc % 2}")
            k.p((lambda oc, po: lambda e: [e.matmul(
                po[:, :], wo_sb[:, (kc * KD + oc) * 128:(kc * KD + oc) * 128 + 128],
                ysl[:, kc * 512:(kc + 1) * 512],
                start=(kc == 0), stop=(kc == KD - 1)) for kc in range(KD)][-1])(oc, po),
                reads=[B_("wo_sb"), B_("ysl")], writes=[pob])
            k.v((lambda oc, po: lambda e: e.tensor_tensor(
                x3sb[:, :], po[:, :], x2c2[:, oc * 512:(oc + 1) * 512], ALU.add))(oc, po),
                reads=[pob, B_("p3x2c2")], writes=[B_("x3sb")])
            em.dma("qsp", (lambda oc, tc: lambda e: e.dma_start(
                out=x3_hbm[oc * 128:(oc + 1) * 128, tc * 512:(tc + 1) * 512],
                in_=x3sb[:, :]))(oc, tc),
                reads=[B_("x3sb")], writes=[B_("x3_hbm")])
    k.pop()


def build_p4(k, env):
    em, B_ = k.em, k.B
    g = env
    onescol, ones1f = g["onescol"], g["ones1f"]
    identf, identb = g["identf"], g["identb"]
    x3_hbm = g["x3_hbm"]
    out_h = g["out_h"]
    nc = k.nc

    def inp(name, shape, dt=F32):
        return nc.dram_tensor(name, list(shape), dt, kind="ExternalInput").ap()

    rw_h = inp("l1_rw", [D, NEXP])
    rb_h = inp("l1_rb_col", [NEXP, 1])
    wfc_h = inp("l1_wfc", [NEXP, D, INNER])
    wgt_h = inp("l1_wgt", [NEXP, D, INNER])
    wcp_h = inp("l1_wcp", [NEXP, INNER, D])
    bfc_h = inp("l1_bfc", [NEXP, INNER])
    bgt_h = inp("l1_bgt", [NEXP, INNER])
    bcp_h = inp("l1_bcp", [NEXP, D])
    eiota_h = inp("eiota", [NEXP, 1])
    ebase_h = inp("ebase", [NEXP, 1])
    ebase_i_h = inp("ebase_i", [NEXP, 1], I32)
    hsel8_h = inp("hsel8", [NEXP, 1])
    cnt_idx_h = inp("cnt_idx", [2, 1], I32)
    cand_idx_h = inp("cand_idx", [NEXP, 1], I32)
    tok8_h = inp("tok8", [NEXP, T])
    trash_h = inp("trashrow", [1, 2 * T])

    ag4_in = k.dram("ag4_in", [1, 16])
    ag4_out = k.dram("ag4_out", [N_CORES, 16])
    ar5_in = k.dram("ar5_in", [32, D])
    ar5_out = k.dram("ar5_out", [32, D])
    xt3_hbm = k.dram("xt3_hbm", [T, D], BF16)
    einp_hbm = k.dram("einp_hbm", [SROWS, D], BF16)
    sout_hbm = k.dram("sout_hbm", [TRASH, D])
    acth_hbm = k.dram("acth_hbm", [128, 32 * ESL], BF16)
    scr_er = k.dram("scr_er", [1, 2 * T])
    scr_oh = k.dram("scr_oh", [2, NEXP, T])
    scr_cum = k.dram("scr_cum", [2, NEXP, T])
    scr_sg = k.dram("scr_sg", [1, 4 * T])
    scr_cg = k.dram("scr_cg", [1, 32])
    wfcb_hbm = k.dram("wfcb_hbm", [NEXP * D, INNER], BF16)
    wgtb_hbm = k.dram("wgtb_hbm", [NEXP * D, INNER], BF16)
    wcpb_hbm = k.dram("wcpb_hbm", [NEXP * INNER, D], BF16)

    rt_tm = k.sb("rt_tm", [128, NT * 8])
    zt4 = k.sb("p4zt", [128, D], BF16)
    ZB = [B_(f"einp_z{i}") for i in range(3)]

    pcb = [k.sb(f"p4pcb{i}", [128, 2048], BF16) for i in range(4)]
    npc = 0
    for ex in range(NEXP):
        for kc in range(KD):
            for hh in range(2):
                bi = npc % 4
                em.dma("qpool", (lambda ex, kc, hh, bi: lambda e: e.dma_start(
                    out=pcb[bi][:, :],
                    in_=wfc_h[ex, kc * 128:(kc + 1) * 128,
                              hh * 2048:(hh + 1) * 2048]))(ex, kc, hh, bi),
                    writes=[B_(f"p4pcb{bi}")])
                em.dma("qsp", (lambda ex, kc, hh, bi: lambda e: e.dma_start(
                    out=wfcb_hbm[ex * D + kc * 128:ex * D + (kc + 1) * 128,
                                 hh * 2048:(hh + 1) * 2048],
                    in_=pcb[bi][:, :]))(ex, kc, hh, bi),
                    reads=[B_(f"p4pcb{bi}")], writes=[B_("wfcb")])
                npc += 1
                bi = npc % 4
                em.dma("qpool", (lambda ex, kc, hh, bi: lambda e: e.dma_start(
                    out=pcb[bi][:, :],
                    in_=wgt_h[ex, kc * 128:(kc + 1) * 128,
                              hh * 2048:(hh + 1) * 2048]))(ex, kc, hh, bi),
                    writes=[B_(f"p4pcb{bi}")])
                em.dma("qsp", (lambda ex, kc, hh, bi: lambda e: e.dma_start(
                    out=wgtb_hbm[ex * D + kc * 128:ex * D + (kc + 1) * 128,
                                 hh * 2048:(hh + 1) * 2048],
                    in_=pcb[bi][:, :]))(ex, kc, hh, bi),
                    reads=[B_(f"p4pcb{bi}")], writes=[B_("wgtb")])
                npc += 1
        for j2 in range(16):
            bi = npc % 4
            em.dma("qpool", (lambda ex, j2, bi: lambda e: e.dma_start(
                out=pcb[bi][:, :].rearrange("p (r c) -> p r c", r=2),
                in_=wcp_h[ex, j2 * 256:(j2 + 1) * 256, :].rearrange(
                    "(r p) c -> p r c", p=128)))(ex, j2, bi),
                writes=[B_(f"p4pcb{bi}")])
            em.dma("qsp", (lambda ex, j2, bi: lambda e: e.dma_start(
                out=wcpb_hbm[ex * INNER + j2 * 256:ex * INNER + (j2 + 1) * 256,
                             :].rearrange("(r p) c -> p r c", p=128),
                in_=pcb[bi][:, :].rearrange("p (r c) -> p r c", r=2)))(ex, j2, bi),
                reads=[B_(f"p4pcb{bi}")], writes=[B_("wcpb")])
            npc += 1

    k.v(lambda e: e.memset(zt4[:, :], 0.0), writes=[B_("p4zt")])
    for i in range(SROWS // 128):
        em.dma(DMA_QUEUES[i % 3], (lambda i: lambda e: e.dma_start(
            out=einp_hbm[i * 128:(i + 1) * 128, :], in_=zt4[:, :]))(i),
            reads=[B_("p4zt")], writes=[B_(f"einp_z{i % 3}")])

    k.push()
    x3c = k.sb("p4x3c", [128, KD * 512])
    xn4 = k.sb("p4xn", [128, KD * 512], BF16)
    xsq4 = k.sb("p4xsq", [128, KD * 512])
    stat4 = k.sb("p4stat", [1, 4 * 512])
    mbc4 = k.sb("p4mbc", [128, 512])
    rbc4 = k.sb("p4rbc", [128, 512])
    rw_sb = k.sb("rw_sb", [128, KD * NEXP], BF16)
    rbcol = k.sb("rbcol", [NEXP, 1])
    logit = k.sb("logit", [NEXP, T])
    xt3t = k.sb("xt3t", [128, D], BF16)
    top8 = k.sb("top8", [128, 8])
    idx8 = k.sb("idx8", [128, 8], U32)
    prow = k.sb("prow", [128, 8])
    ecols = k.sb("ecols", [128, 2 * NT])
    etr = k.sb("etr", [NT, 128])
    ps4_s = k.ps("p4ps_s", [1, 512])
    ps4_b = k.ps("p4ps_b", [128, 512])
    ps4_l = k.ps("p4ps_l", [NEXP, 512])
    ps4_tb = [k.ps(f"p4ps_tb{i}", [128, 128], BF16) for i in range(2)]
    ps4_tf = k.ps("p4ps_tf", [128, 128])

    em.dma("qsp", lambda e: e.dma_start(out=rbcol[:, :], in_=rb_h[:, :]),
           writes=[B_("rbcol")])
    for kc in range(KD):
        em.dma("qpool", (lambda kc: lambda e: e.dma_start(
            out=rw_sb[:, kc * NEXP:(kc + 1) * NEXP],
            in_=rw_h[kc * 128:(kc + 1) * 128, :]))(kc), writes=[B_("rw_sb")])

    for tc in range(T // 512):
        em.dma("qsp", (lambda tc: lambda e: e.dma_start(
            out=x3c[:, :].rearrange("p (kc t) -> p kc t", kc=KD),
            in_=x3_hbm[:, tc * 512:(tc + 1) * 512].rearrange(
                "(kc p) t -> p kc t", p=128)))(tc),
            reads=[B_("x3_hbm")], writes=[B_("p4x3c")])
        k.p(lambda e: [e.matmul(
            ps4_s[:, :], onescol[:, :], x3c[:, kc * 512:(kc + 1) * 512],
            start=(kc == 0), stop=(kc == KD - 1)) for kc in range(KD)][-1],
            reads=[B_("onescol"), B_("p4x3c")], writes=[B_("p4ps_s")])
        sm = stat4[0:1, 0:512]
        sv = stat4[0:1, 512:1024]
        sr = stat4[0:1, 1024:1536]
        s4 = stat4[0:1, 1536:2048]
        k.v(lambda e: e.tensor_scalar(sm, ps4_s[:, :], 1.0 / D, None, ALU.mult),
            reads=[B_("p4ps_s")], writes=[B_("p4stat")])
        k.a(lambda e: e.activation(out=xsq4[:, :], in_=x3c[:, :], func=AF.Square),
            reads=[B_("p4x3c")], writes=[B_("p4xsq")])
        k.p(lambda e: [e.matmul(
            ps4_s[:, :], onescol[:, :], xsq4[:, kc * 512:(kc + 1) * 512],
            start=(kc == 0), stop=(kc == KD - 1)) for kc in range(KD)][-1],
            reads=[B_("onescol"), B_("p4xsq")], writes=[B_("p4ps_s")])
        k.v(lambda e: e.tensor_scalar(sv, ps4_s[:, :], 1.0 / D, None, ALU.mult),
            reads=[B_("p4ps_s")], writes=[B_("p4stat")])
        k.v(lambda e: e.tensor_tensor(s4, sm, sm, ALU.mult),
            reads=[B_("p4stat")], writes=[B_("p4stat")])
        k.v(lambda e: e.tensor_tensor(sv, sv, s4, ALU.subtract),
            reads=[B_("p4stat")], writes=[B_("p4stat")])
        k.v(lambda e: e.tensor_scalar(sv, sv, 1e-5, None, ALU.add),
            reads=[B_("p4stat")], writes=[B_("p4stat")])
        rsqrt_rows(k, sr, sv, s4, (B_("p4stat"), B_("p4stat"), B_("p4stat")))
        k.p(lambda e: e.matmul(ps4_b[:, :], ones1f[:, :], sm, start=True, stop=True),
            reads=[B_("ones1f"), B_("p4stat")], writes=[B_("p4ps_b")])
        k.v(lambda e: e.tensor_copy(mbc4[:, :], ps4_b[:, :]),
            reads=[B_("p4ps_b")], writes=[B_("p4mbc")])
        k.p(lambda e: e.matmul(ps4_b[:, :], ones1f[:, :], sr, start=True, stop=True),
            reads=[B_("ones1f"), B_("p4stat")], writes=[B_("p4ps_b")])
        k.v(lambda e: e.tensor_copy(rbc4[:, :], ps4_b[:, :]),
            reads=[B_("p4ps_b")], writes=[B_("p4rbc")])
        for kc in range(KD):
            k.v((lambda kc: lambda e: e.tensor_tensor(
                xsq4[:, kc * 512:(kc + 1) * 512], x3c[:, kc * 512:(kc + 1) * 512],
                mbc4[:, :], ALU.subtract))(kc),
                reads=[B_("p4x3c"), B_("p4mbc")], writes=[B_("p4xsq")])
            k.v((lambda kc: lambda e: e.tensor_tensor(
                xn4[:, kc * 512:(kc + 1) * 512], xsq4[:, kc * 512:(kc + 1) * 512],
                rbc4[:, :], ALU.mult))(kc),
                reads=[B_("p4xsq"), B_("p4rbc")], writes=[B_("p4xn")])
        k.p(lambda e: [e.matmul(
            ps4_l[:, :], rw_sb[:, kc * NEXP:(kc + 1) * NEXP],
            xn4[:, kc * 512:(kc + 1) * 512],
            start=(kc == 0), stop=(kc == KD - 1)) for kc in range(KD)][-1],
            reads=[B_("rw_sb"), B_("p4xn")], writes=[B_("p4ps_l")])
        k.v((lambda tc: lambda e: e.tensor_scalar(
            logit[:, tc * 512:(tc + 1) * 512], ps4_l[:, :], rbcol[:, 0:1],
            None, ALU.add))(tc),
            reads=[B_("p4ps_l"), B_("rbcol")], writes=[B_("logit")])
        for ti in range(4):
            tile = tc * 4 + ti
            for kc in range(KD):
                pt, ptb = ps4_tb[kc % 2], B_(f"p4ps_tb{kc % 2}")
                k.p((lambda kc, ti, pt: lambda e: e.transpose(
                    pt[:, :], xn4[:, kc * 512 + ti * 128:kc * 512 + ti * 128 + 128],
                    identb[:, :]))(kc, ti, pt),
                    reads=[B_("p4xn"), B_("identb")], writes=[ptb])
                k.v((lambda kc, pt, ptb: lambda e: e.tensor_copy(
                    xt3t[:, kc * 128:(kc + 1) * 128], pt[:, :]))(kc, pt, ptb),
                    reads=[ptb], writes=[B_("xt3t")])
            em.dma("qsp", (lambda tile: lambda e: e.dma_start(
                out=xt3_hbm[tile * 128:(tile + 1) * 128, :], in_=xt3t[:, :]))(tile),
                reads=[B_("xt3t")], writes=[B_("xt3_hbm")])

    for tile in range(NT):
        k.p((lambda tile: lambda e: e.transpose(
            ps4_tf[0:128, 0:NEXP], logit[:, tile * 128:(tile + 1) * 128],
            identf[0:NEXP, 0:NEXP]))(tile),
            reads=[B_("logit"), B_("identf")], writes=[B_("p4ps_tf")])
        k.v(lambda e: e.tensor_copy(prow[:, :], ps4_tf[0:128, 0:NEXP]),
            reads=[B_("p4ps_tf")], writes=[B_("prow")])
        k.v(lambda e: e.max(top8[:, :], prow[:, :]),
            reads=[B_("prow")], writes=[B_("top8")])
        k.v(lambda e: e.max_index(idx8[:, :], top8[:, :], prow[:, :]),
            reads=[B_("top8"), B_("prow")], writes=[B_("idx8")])
        rb_ = tile * 8
        k.v((lambda rb_: lambda e: e.tensor_copy(
            rt_tm[:, rb_:rb_ + 2], idx8[:, 0:2]))(rb_),
            reads=[B_("idx8")], writes=[B_("rt_tm")])
        k.v(lambda e: e.tensor_tensor(prow[:, 2:3], top8[:, 1:2], top8[:, 0:1],
                                      ALU.subtract),
            reads=[B_("top8")], writes=[B_("prow")])
        k.a(lambda e: e.activation(out=prow[:, 3:4], in_=prow[:, 2:3], func=AF.Exp),
            reads=[B_("prow")], writes=[B_("prow")])
        k.v(lambda e: e.tensor_scalar(prow[:, 4:5], prow[:, 3:4], 1.0, None,
                                      ALU.add),
            reads=[B_("prow")], writes=[B_("prow")])
        k.v(lambda e: e.reciprocal(prow[:, 4:5], prow[:, 4:5]),
            reads=[B_("prow")], writes=[B_("prow")])
        k.v((lambda rb_: lambda e: e.tensor_copy(
            rt_tm[:, rb_ + 2:rb_ + 3], prow[:, 4:5]))(rb_),
            reads=[B_("prow")], writes=[B_("rt_tm")])
        k.v((lambda rb_: lambda e: e.tensor_tensor(
            rt_tm[:, rb_ + 3:rb_ + 4], prow[:, 3:4], prow[:, 4:5], ALU.mult))(rb_),
            reads=[B_("prow")], writes=[B_("rt_tm")])

    for kk in range(2):
        for tile in range(NT):
            k.v((lambda kk, tile: lambda e: e.tensor_copy(
                ecols[:, kk * NT + tile:kk * NT + tile + 1],
                rt_tm[:, tile * 8 + kk:tile * 8 + kk + 1]))(kk, tile),
                reads=[B_("rt_tm")], writes=[B_("ecols")])
        k.p((lambda kk: lambda e: e.transpose(
            ps4_tf[0:NT, :], ecols[:, kk * NT:(kk + 1) * NT], identf[:, :]))(kk),
            reads=[B_("ecols"), B_("identf")], writes=[B_("p4ps_tf")])
        k.v((lambda kk: lambda e: e.tensor_copy(etr[:, :], ps4_tf[0:NT, :]))(kk),
            reads=[B_("p4ps_tf")], writes=[B_("etr")])
        em.dma("qsp", (lambda kk: lambda e: e.dma_start(
            out=scr_er[0:1, kk * T:(kk + 1) * T].rearrange(
                "o (w c) -> (o w) c", c=128),
            in_=etr[:, :]))(kk),
            reads=[B_("etr")], writes=[B_("scr_er")])
    k.pop()

    k.push()
    erc = k.sb("erc", [1, 512])
    one8c = k.sb("one8c", [NEXP, 512])
    ohc = k.sb("ohc", [NEXP, 512])
    cumc = k.sb("cumc", [NEXP, 512])
    carry = k.sb("p4carry", [NEXP, 2])
    carryK = k.sb("p4carryK", [NEXP, 2])
    eiota = k.sb("eiota_s", [NEXP, 1])
    ebase = k.sb("ebase_s", [NEXP, 1])
    hsel8 = k.sb("hsel8_s", [NEXP, 1])
    ag4row = k.sb("ag4row", [1, 16])
    cg2 = k.sb("cg2", [2, 16])
    crow = k.sb("crow", [1, 32])
    cnt_idx = k.sb("cnt_idx_s", [2, 1], I32)
    ccol = k.sb("ccol", [NEXP, 4])
    occ = k.sb("occ", [NEXP, 4])
    tokc = k.sb("tokc", [NEXP, 512])
    trc = k.sb("trc", [1, 512])
    posc = k.sb("posc", [NEXP, 512])
    kpc = k.sb("kpc", [NEXP, 512])
    ohkc = k.sb("ohkc", [NEXP, 512])
    ohk2 = k.sb("ohk2", [NEXP, 512])
    krkc = k.sb("krkc", [NEXP, 512])
    m8 = k.sb("m8", [NEXP, 512])
    ctc = k.sb("ctc", [NEXP, 4])
    rowsb = k.sb("p4rows", [1, 6 * 512])
    ps8 = k.ps("p4ps8", [NEXP, 512])
    ps1x = k.ps("p4ps1x", [1, 16])
    ps1a = k.ps("p4ps1a", [1, 512])
    ps_cc = k.ps("p4ps_cc", [32, 1])

    for dst, src, nm in [(eiota, eiota_h, "eiota_s"), (ebase, ebase_h, "ebase_s"),
                         (hsel8, hsel8_h, "hsel8_s"),
                         (cnt_idx, cnt_idx_h, "cnt_idx_s")]:
        em.dma("qsp", (lambda d, s: lambda e: e.dma_start(out=d[:, :], in_=s[:, :]))(dst, src),
               writes=[B_(nm)])
    k.v(lambda e: e.memset(one8c[:, :], 1.0), writes=[B_("one8c")])
    k.v(lambda e: e.memset(carry[:, :], 0.0), writes=[B_("p4carry")])
    k.v(lambda e: e.memset(carryK[:, :], 0.0), writes=[B_("p4carryK")])
    k.v(lambda e: e.memset(ctc[:, :], 0.0), writes=[B_("ctc")])

    for kk in range(2):
        for c8 in range(8):
            sl = slice(c8 * 512, (c8 + 1) * 512)
            em.dma("qsp", (lambda kk, sl: lambda e: e.dma_start(
                out=erc[:, :], in_=scr_er[0:1, kk * T + sl.start:kk * T + sl.stop]))(kk, sl),
                reads=[B_("scr_er")], writes=[B_("erc")])
            k.p(lambda e: e.matmul(ps8[:, :], ones1f[0:1, 0:NEXP], erc[:, :],
                                   start=True, stop=True),
                reads=[B_("ones1f"), B_("erc")], writes=[B_("p4ps8")])
            k.v(lambda e: e.tensor_scalar(ohc[:, :], ps8[:, :], eiota[:, 0:1],
                                          None, ALU.is_equal),
                reads=[B_("p4ps8"), B_("eiota_s")], writes=[B_("ohc")])
            em.dma("qact", (lambda kk, sl: lambda e: e.dma_start(
                out=scr_oh[kk, :, sl], in_=ohc[:, :]))(kk, sl),
                reads=[B_("ohc")], writes=[B_("scr_oh")])
            k.v((lambda kk: lambda e: e.tensor_tensor_scan(
                cumc[:, :], one8c[:, :], ohc[:, :], carry[:, kk:kk + 1],
                ALU.mult, ALU.add))(kk),
                reads=[B_("one8c"), B_("ohc"), B_("p4carry")], writes=[B_("cumc")])
            k.v((lambda kk: lambda e: e.tensor_copy(carry[:, kk:kk + 1],
                                                    cumc[:, 511:512]))(kk),
                reads=[B_("cumc")], writes=[B_("p4carry")])
            em.dma("qact", (lambda kk, sl: lambda e: e.dma_start(
                out=scr_cum[kk, :, sl], in_=cumc[:, :]))(kk, sl),
                reads=[B_("cumc")], writes=[B_("scr_cum")])
    for kk in range(2):
        k.p((lambda kk: lambda e: e.transpose(
            ps1x[0:1, 0:NEXP], carry[:, kk:kk + 1], identf[0:NEXP, 0:NEXP]))(kk),
            reads=[B_("p4carry"), B_("identf")], writes=[B_("p4ps1x")])
        k.v((lambda kk: lambda e: e.tensor_copy(
            ag4row[0:1, kk * NEXP:(kk + 1) * NEXP], ps1x[0:1, 0:NEXP]))(kk),
            reads=[B_("p4ps1x")], writes=[B_("ag4row")])
    em.dma("qsp", lambda e: e.dma_start(out=ag4_in[:, :], in_=ag4row[:, :]),
           reads=[B_("ag4row")], writes=[B_("ag4_in")])
    k.g(lambda e: e.collective_compute(
        "AllGather", ALU.bypass, replica_groups=[list(range(N_CORES))],
        ins=[ag4_in[:, :]], outs=[ag4_out[:, :]]),
        reads=[B_("ag4_in")], writes=[B_("ag4")])

    em.dma("qpool", lambda e: e.indirect_dma_start(
        cg2[:, :], None, ag4_out[:, :],
        bass.IndirectOffsetOnAxis(cnt_idx[:, 0:1], 0)),
        reads=[B_("cnt_idx_s"), B_("ag4")], writes=[B_("cg2")])
    em.dma("qsp", lambda e: e.dma_start(
        out=scr_cg[0:1, :].rearrange("o (r c) -> (o r) c", c=16), in_=cg2[:, :]),
        reads=[B_("cg2")], writes=[B_("scr_cg")])
    em.dma("qsp", lambda e: e.dma_start(out=crow[:, :], in_=scr_cg[0:1, :]),
        reads=[B_("scr_cg")], writes=[B_("crow")])
    for s in range(4):
        k.p((lambda s: lambda e: e.transpose(
            ps_cc[0:NEXP, 0:1], crow[0:1, s * 8:(s + 1) * 8],
            identf[0:1, 0:1]))(s),
            reads=[B_("crow"), B_("identf")], writes=[B_("p4ps_cc")])
        k.v((lambda s: lambda e: e.tensor_copy(ccol[:, s:s + 1],
                                               ps_cc[0:NEXP, 0:1]))(s),
            reads=[B_("p4ps_cc")], writes=[B_("ccol")])
    k.v(lambda e: e.tensor_scalar(occ[:, 0:1], ccol[:, 0:1], hsel8[:, 0:1], 1.0,
                                  ALU.mult, ALU.subtract),
        reads=[B_("ccol"), B_("hsel8_s")], writes=[B_("occ")])
    k.v(lambda e: e.tensor_scalar(occ[:, 2:3], ccol[:, 1:2], hsel8[:, 0:1], None,
                                  ALU.mult),
        reads=[B_("ccol"), B_("hsel8_s")], writes=[B_("occ")])
    k.v(lambda e: e.tensor_tensor(occ[:, 2:3], occ[:, 2:3], ccol[:, 0:1], ALU.add),
        reads=[B_("occ"), B_("ccol")], writes=[B_("occ")])
    k.v(lambda e: e.tensor_tensor(occ[:, 2:3], occ[:, 2:3], ccol[:, 2:3], ALU.add),
        reads=[B_("occ"), B_("ccol")], writes=[B_("occ")])
    k.v(lambda e: e.tensor_scalar(occ[:, 1:2], occ[:, 2:3], 1.0, None,
                                  ALU.subtract),
        reads=[B_("occ")], writes=[B_("occ")])

    sr_ = rowsb[0:1, 0:512]
    fr_ = rowsb[0:1, 512:1024]
    ebr_ = rowsb[0:1, 1024:1536]
    nf_ = rowsb[0:1, 1536:2048]
    t1_ = rowsb[0:1, 2048:2560]
    t2_ = rowsb[0:1, 2560:3072]
    for kk in range(2):
        for c8 in range(8):
            sl = slice(c8 * 512, (c8 + 1) * 512)
            em.dma("qsp", (lambda kk, sl: lambda e: e.dma_start(
                out=ohc[:, :], in_=scr_oh[kk, :, sl]))(kk, sl),
                reads=[B_("scr_oh")], writes=[B_("ohc")])
            em.dma("qsp", (lambda kk, sl: lambda e: e.dma_start(
                out=cumc[:, :], in_=scr_cum[kk, :, sl]))(kk, sl),
                reads=[B_("scr_cum")], writes=[B_("cumc")])
            em.dma("qact", (lambda sl: lambda e: e.dma_start(
                out=tokc[:, :], in_=tok8_h[:, sl]))(sl),
                writes=[B_("tokc")])
            em.dma("qact", (lambda kk, sl: lambda e: e.dma_start(
                out=trc[:, :],
                in_=trash_h[0:1, kk * T + sl.start:kk * T + sl.stop]))(kk, sl),
                writes=[B_("trc")])
            k.v((lambda kk: lambda e: e.tensor_scalar(
                posc[:, :], cumc[:, :], occ[:, kk:kk + 1], None, ALU.add))(kk),
                reads=[B_("cumc"), B_("occ")], writes=[B_("posc")])
            k.v(lambda e: e.tensor_scalar(kpc[:, :], posc[:, :], CAP - 0.5, None,
                                          ALU.is_le),
                reads=[B_("posc")], writes=[B_("kpc")])
            k.v(lambda e: e.tensor_tensor(ohkc[:, :], ohc[:, :], kpc[:, :],
                                          ALU.mult),
                reads=[B_("ohc"), B_("kpc")], writes=[B_("ohkc")])
            k.v(lambda e: e.tensor_tensor_scan(
                krkc[:, :], one8c[:, :], ohkc[:, :], carryK[:, 0:1],
                ALU.mult, ALU.add),
                reads=[B_("one8c"), B_("ohkc"), B_("p4carryK")], writes=[B_("krkc")])
            k.v(lambda e: e.tensor_copy(carryK[:, 0:1], krkc[:, 511:512]),
                reads=[B_("krkc")], writes=[B_("p4carryK")])
            k.v(lambda e: e.tensor_scalar(kpc[:, :], krkc[:, :], ESL - 0.5, None,
                                          ALU.is_le),
                reads=[B_("krkc")], writes=[B_("kpc")])
            k.v(lambda e: e.tensor_tensor(ohk2[:, :], ohkc[:, :], kpc[:, :],
                                          ALU.mult),
                reads=[B_("ohkc"), B_("kpc")], writes=[B_("ohk2")])
            k.v(lambda e: e.tensor_scalar(kpc[:, :], posc[:, :], CAP - 1.0, None,
                                          ALU.is_equal),
                reads=[B_("posc")], writes=[B_("kpc")])
            k.v(lambda e: e.tensor_tensor(kpc[:, :], ohc[:, :], kpc[:, :],
                                          ALU.mult),
                reads=[B_("ohc"), B_("kpc")], writes=[B_("kpc")])
            k.v(lambda e: e.tensor_tensor(m8[:, :], kpc[:, :], tokc[:, :],
                                          ALU.mult),
                reads=[B_("kpc"), B_("tokc")], writes=[B_("m8")])
            k.v(lambda e: e.tensor_reduce(ctc[:, 2:3], m8[:, :], AXX, ALU.add),
                reads=[B_("m8")], writes=[B_("ctc")])
            k.v(lambda e: e.tensor_reduce(ctc[:, 3:4], kpc[:, :], AXX, ALU.add),
                reads=[B_("kpc")], writes=[B_("ctc")])
            k.v(lambda e: e.tensor_tensor(ctc[:, 0:1], ctc[:, 0:1], ctc[:, 2:3],
                                          ALU.add),
                reads=[B_("ctc")], writes=[B_("ctc")])
            k.v(lambda e: e.tensor_tensor(ctc[:, 1:2], ctc[:, 1:2], ctc[:, 3:4],
                                          ALU.add),
                reads=[B_("ctc")], writes=[B_("ctc")])
            k.v(lambda e: e.tensor_scalar(m8[:, :], krkc[:, :], ebase[:, 0:1],
                                          None, ALU.add),
                reads=[B_("krkc"), B_("ebase_s")], writes=[B_("m8")])
            k.v(lambda e: e.tensor_tensor(m8[:, :], ohk2[:, :], m8[:, :],
                                          ALU.mult),
                reads=[B_("ohk2"), B_("m8")], writes=[B_("m8")])
            k.p(lambda e: e.matmul(ps1a[:, :], onescol[0:NEXP, 0:1], m8[:, :],
                                   start=True, stop=True),
                reads=[B_("onescol"), B_("m8")], writes=[B_("p4ps1a")])
            k.v(lambda e: e.tensor_copy(sr_, ps1a[:, :]),
                reads=[B_("p4ps1a")], writes=[B_("p4rows")])
            k.p(lambda e: e.matmul(ps1a[:, :], onescol[0:NEXP, 0:1], ohk2[:, :],
                                   start=True, stop=True),
                reads=[B_("onescol"), B_("ohk2")], writes=[B_("p4ps1a")])
            k.v(lambda e: e.tensor_copy(fr_, ps1a[:, :]),
                reads=[B_("p4ps1a")], writes=[B_("p4rows")])
            k.v(lambda e: e.tensor_scalar(m8[:, :], ohc[:, :], ebase[:, 0:1],
                                          None, ALU.mult),
                reads=[B_("ohc"), B_("ebase_s")], writes=[B_("m8")])
            k.p(lambda e: e.matmul(ps1a[:, :], onescol[0:NEXP, 0:1], m8[:, :],
                                   start=True, stop=True),
                reads=[B_("onescol"), B_("m8")], writes=[B_("p4ps1a")])
            k.v(lambda e: e.tensor_copy(ebr_, ps1a[:, :]),
                reads=[B_("p4ps1a")], writes=[B_("p4rows")])
            k.v(lambda e: e.tensor_scalar(nf_, fr_, -1.0, 1.0, ALU.mult, ALU.add),
                reads=[B_("p4rows")], writes=[B_("p4rows")])
            k.v(lambda e: e.tensor_tensor(t1_, nf_, trc[:, :], ALU.mult),
                reads=[B_("p4rows"), B_("trc")], writes=[B_("p4rows")])
            k.v(lambda e: e.tensor_tensor(t1_, sr_, t1_, ALU.add),
                reads=[B_("p4rows")], writes=[B_("p4rows")])
            em.dma("qsp", (lambda kk, sl: lambda e: e.dma_start(
                out=scr_sg[0:1, kk * T + sl.start:kk * T + sl.stop], in_=t1_))(kk, sl),
                reads=[B_("p4rows")], writes=[B_("scr_sg")])
            k.v(lambda e: e.tensor_tensor(t2_, nf_, ebr_, ALU.mult),
                reads=[B_("p4rows")], writes=[B_("p4rows")])
            k.v(lambda e: e.tensor_tensor(t2_, sr_, t2_, ALU.add),
                reads=[B_("p4rows")], writes=[B_("p4rows")])
            em.dma("qsp", (lambda kk, sl: lambda e: e.dma_start(
                out=scr_sg[0:1, 2 * T + kk * T + sl.start:2 * T + kk * T + sl.stop],
                in_=t2_))(kk, sl),
                reads=[B_("p4rows")], writes=[B_("scr_sg")])

    cti = k.sb("cti", [NEXP, 1], I32)
    cand_idx = k.sb("cand_idx_s", [NEXP, 1], I32)
    ebase_i = k.sb("ebase_i_s", [NEXP, 1], I32)
    cand_b = k.sb("cand_b", [NEXP, D], BF16)
    cand_f = k.sb("cand_f", [NEXP, D])
    corr_f = k.sb("corr_f", [NEXP, D])
    corr_b = k.sb("corr_b", [NEXP, D], BF16)
    z8 = k.sb("z8", [NEXP, D])
    em.dma("qsp", lambda e: e.dma_start(out=cand_idx[:, :], in_=cand_idx_h[:, :]),
           writes=[B_("cand_idx_s")])
    em.dma("qsp", lambda e: e.dma_start(out=ebase_i[:, :], in_=ebase_i_h[:, :]),
           writes=[B_("ebase_i_s")])
    k.v(lambda e: e.tensor_copy(cti[:, :], ctc[:, 0:1]),
        reads=[B_("ctc")], writes=[B_("cti")])
    em.dma("qpool", lambda e: e.indirect_dma_start(
        cand_b[:, :], None, xt3_hbm[:, :],
        bass.IndirectOffsetOnAxis(cti[:, 0:1], 0)),
        reads=[B_("cti"), B_("xt3_hbm")], writes=[B_("cand_b")])
    k.v(lambda e: e.tensor_scalar(cand_f[:, :], cand_b[:, :], ctc[:, 1:2], None,
                                  ALU.mult),
        reads=[B_("cand_b"), B_("ctc")], writes=[B_("cand_f")])
    k.v(lambda e: e.memset(z8[:, :], 0.0), writes=[B_("z8")])
    for i in range(4):
        em.dma("qsp", (lambda i: lambda e: e.dma_start(
            out=ar5_in[i * NEXP:(i + 1) * NEXP, :], in_=z8[:, :]))(i),
            reads=[B_("z8")], writes=[B_("ar5_in")])
    em.dma("qpool", lambda e: e.indirect_dma_start(
        ar5_in[:, :], bass.IndirectOffsetOnAxis(cand_idx[:, 0:1], 0),
        cand_f[:, :], None),
        reads=[B_("cand_f"), B_("cand_idx_s")], writes=[B_("ar5_in")])
    k.g(lambda e: e.collective_compute(
        "AllReduce", ALU.add, replica_groups=[list(range(N_CORES))],
        ins=[ar5_in[:, :]], outs=[ar5_out[:, :]]),
        reads=[B_("ar5_in")], writes=[B_("ar5")])
    em.dma("qpool", lambda e: e.indirect_dma_start(
        corr_f[:, :], None, ar5_out[:, :],
        bass.IndirectOffsetOnAxis(cand_idx[:, 0:1], 0)),
        reads=[B_("cand_idx_s"), B_("ar5")], writes=[B_("corr_f")])
    k.v(lambda e: e.tensor_copy(corr_b[:, :], corr_f[:, :]),
        reads=[B_("corr_f")], writes=[B_("corr_b")])
    em.dma("qpool", lambda e: e.indirect_dma_start(
        einp_hbm[:, :], bass.IndirectOffsetOnAxis(ebase_i[:, 0:1], 0),
        corr_b[:, :], None),
        reads=[B_("corr_b"), B_("ebase_i_s")] + ZB, writes=[B_("einp_sc")])

    stf = k.sb("stf", [128, 64])
    sti = k.sb("sti", [128, 64], I32)
    xts = [k.sb(f"p4xts{i}", [128, D], BF16) for i in range(2)]
    for kk in range(2):
        em.dma("qsp", (lambda kk: lambda e: e.dma_start(
            out=stf[:, kk * NT:(kk + 1) * NT],
            in_=scr_sg[0:1, kk * T:(kk + 1) * T].rearrange(
                "o (w p) -> (o p) w", p=128),
            allow_slow_non_contiguous=True))(kk),
            reads=[B_("scr_sg")], writes=[B_("stf")])
    k.v(lambda e: e.tensor_copy(sti[:, :], stf[:, :]),
        reads=[B_("stf")], writes=[B_("sti")])
    nsc = 0
    for tile in range(NT):
        xb, xbb = xts[tile % 2], B_(f"p4xts{tile % 2}")
        em.dma("qact", (lambda tile, xb: lambda e: e.dma_start(
            out=xb[:, :], in_=xt3_hbm[tile * 128:(tile + 1) * 128, :]))(tile, xb),
            reads=[B_("xt3_hbm")], writes=[xbb])
        for kk in range(2):
            em.dma("qpool", (lambda kk, tile, xb: lambda e: e.indirect_dma_start(
                einp_hbm[:, :],
                bass.IndirectOffsetOnAxis(sti[:, kk * NT + tile:kk * NT + tile + 1], 0),
                xb[:, :], None))(kk, tile, xb),
                reads=[B_("sti"), xbb] + ZB, writes=[B_("einp_sc")])
            nsc += 1
    k.pop()

    k.push()
    EIB = ZB + [B_("einp_sc")]
    einp_fm = k.sb("einp_fm", [128, KD * ESL], BF16)
    wbig = k.sb("wbig", [128, 32768], BF16)
    bfcr = k.sb("bfcr", [1, INNER], BF16)
    bgtr = k.sb("bgtr", [1, INNER], BF16)
    bcpr = k.sb("bcpr", [1, D], BF16)
    onesrb = k.sb("onesrb", [1, 512], BF16)
    prodf = [k.sb(f"prodf{i}", [128, 512]) for i in range(2)]
    hsb = [k.sb(f"p4hsb{i}", [128, 512]) for i in range(2)]
    actev = [k.sb(f"actev{i}", [128, ESL], BF16) for i in range(2)]
    at4 = [k.sb(f"at4_{i}", [128, 32 * 128], BF16) for i in range(2)]
    soutev = [k.sb(f"soutev{i}", [128, D]) for i in range(2)]
    hp = [k.ps(f"p4hp{i}", [128, 512]) for i in range(2)]
    gp = [k.ps(f"p4gp{i}", [128, 512]) for i in range(2)]
    psA = [k.ps(f"p4psA{i}", [128, 512]) for i in range(2)]
    psB = [k.ps(f"p4psB{i}", [128, 512]) for i in range(2)]
    k.v(lambda e: e.memset(onesrb[:, :], 1.0), writes=[B_("onesrb")])
    SPW = (512, 512, 512, 512, 416)
    ACB = [B_(f"acth{i}") for i in range(4)]

    EFB = [B_(f"einp_fm{kc}") for kc in range(KD)]

    def wbuf(j):
        return B_(f"wb{j}") if j < 8 else B_(f"wbg{j - 8}")

    nso = 0
    for ex in range(NEXP):
        for kc in range(KD):
            em.dma("qsp", (lambda ex, kc: lambda e: e.dma_start_transpose(
                einp_fm[:, kc * ESL:(kc + 1) * ESL],
                einp_hbm[ex * ESL:(ex + 1) * ESL, kc * 128:(kc + 1) * 128]))(ex, kc),
                reads=EIB, writes=[B_(f"einp_fm{kc}")])
        em.dma("qpool", (lambda ex: lambda e: e.dma_start(
            out=bfcr[:, :], in_=bfc_h[ex:ex + 1, :]))(ex), writes=[B_("bfcr")])
        em.dma("qpool", (lambda ex: lambda e: e.dma_start(
            out=bgtr[:, :], in_=bgt_h[ex:ex + 1, :]))(ex), writes=[B_("bgtr")])
        for ih in range(2):
            for kc in range(KD):
                em.dma(("qsp", "qact")[kc % 2], (lambda ex, ih, kc: lambda e: e.dma_start(
                    out=wbig[:, kc * 2048:(kc + 1) * 2048],
                    in_=wfcb_hbm[ex * D + kc * 128:ex * D + (kc + 1) * 128,
                                 ih * 2048:(ih + 1) * 2048]))(ex, ih, kc),
                    reads=[B_("wfcb")], writes=[wbuf(kc)])
                em.dma(("qact", "qsp")[kc % 2], (lambda ex, ih, kc: lambda e: e.dma_start(
                    out=wbig[:, 16384 + kc * 2048:16384 + (kc + 1) * 2048],
                    in_=wgtb_hbm[ex * D + kc * 128:ex * D + (kc + 1) * 128,
                                 ih * 2048:(ih + 1) * 2048]))(ex, ih, kc),
                    reads=[B_("wgtb")], writes=[wbuf(8 + kc)])
            for ml in range(16):
                mc = ih * 16 + ml
                hpc, hpb = hp[ml % 2], B_(f"p4hp{ml % 2}")
                gpc, gpb = gp[ml % 2], B_(f"p4gp{ml % 2}")
                aev, aevb = actev[ml % 2], B_(f"actev{ml % 2}")
                prf, prb = prodf[ml % 2], B_(f"prodf{ml % 2}")
                WRB = [wbuf(j) for j in range(8)]
                WGB = [wbuf(8 + j) for j in range(8)]
                for spn in range(len(SPW)):
                    w = SPW[spn]
                    sb0 = spn * 512
                    k.p((lambda ml, mc, w, sb0, hpc: lambda e: ([e.matmul(
                        hpc[:, 0:w],
                        wbig[:, kc * 2048 + ml * 128:kc * 2048 + ml * 128 + 128],
                        einp_fm[:, kc * ESL + sb0:kc * ESL + sb0 + w],
                        start=(kc == 0), stop=False) for kc in range(KD)]
                        and e.matmul(hpc[:, 0:w], bfcr[0:1, mc * 128:mc * 128 + 128],
                                     onesrb[0:1, 0:w], start=False, stop=True)))(
                        ml, mc, w, sb0, hpc),
                        reads=WRB + EFB + [B_("bfcr"), B_("onesrb")],
                        writes=[hpb])
                    k.p((lambda ml, mc, w, sb0, gpc: lambda e: ([e.matmul(
                        gpc[:, 0:w],
                        wbig[:, 16384 + kc * 2048 + ml * 128:16384 + kc * 2048 + ml * 128 + 128],
                        einp_fm[:, kc * ESL + sb0:kc * ESL + sb0 + w],
                        start=(kc == 0), stop=False) for kc in range(KD)]
                        and e.matmul(gpc[:, 0:w], bgtr[0:1, mc * 128:mc * 128 + 128],
                                     onesrb[0:1, 0:w], start=False, stop=True)))(
                        ml, mc, w, sb0, gpc),
                        reads=WGB + EFB + [B_("bgtr"), B_("onesrb")],
                        writes=[gpb])
                    hsc, hscb = hsb[ml % 2], B_(f"p4hsb{ml % 2}")
                    k.a((lambda w, hpc, hsc: lambda e: e.activation(
                        out=hsc[:, 0:w], in_=hpc[:, 0:w], func=AF.Copy))(
                        w, hpc, hsc),
                        reads=[hpb], writes=[hscb])
                    k.v((lambda w, hsc, gpc, prf: lambda e: e.tensor_tensor(
                        prf[:, 0:w], hsc[:, 0:w], gpc[:, 0:w], ALU.mult))(
                        w, hsc, gpc, prf),
                        reads=[hscb, gpb], writes=[prb])
                    k.a((lambda w, sb0, prf, aev: lambda e: e.activation(
                        out=aev[:, sb0:sb0 + w], in_=prf[:, 0:w], func=AF.Silu))(
                        w, sb0, prf, aev),
                        reads=[prb], writes=[aevb])
                em.dma("qsp", (lambda mc, aev: lambda e: e.dma_start(
                    out=acth_hbm[:, mc * ESL:(mc + 1) * ESL], in_=aev[:, :]))(mc, aev),
                    reads=[aevb], writes=[B_(f"acth{mc % 4}")])
        for j in range(32):
            em.dma(("qsp", "qact")[j % 2], (lambda ex, j: lambda e: e.dma_start(
                out=wbig[:, j * 1024:(j + 1) * 1024],
                in_=wcpb_hbm[ex * INNER + j * 128:ex * INNER + (j + 1) * 128, :]))(ex, j),
                reads=[B_("wcpb")],
                writes=[wbuf(j // 2) if j < 16 else wbuf(8 + (j - 16) // 2)])
        em.dma("qpool", (lambda ex: lambda e: e.dma_start(
            out=bcpr[:, :], in_=bcp_h[ex:ex + 1, :]))(ex), writes=[B_("bcpr")])
        WAB = [wbuf(j) for j in range(16)]
        NSC = (ESL + 127) // 128
        for sc in range(NSC):
            M = min(128, ESL - sc * 128)
            pA, pAb = psA[sc % 2], B_(f"p4psA{sc % 2}")
            pB, pBb = psB[sc % 2], B_(f"p4psB{sc % 2}")
            sev, sevb = soutev[sc % 2], B_(f"soutev{sc % 2}")
            a_, ab_ = at4[sc % 2], B_(f"at4_{sc % 2}")
            em.dma("qact", (lambda sc, M, a_: lambda e: e.dma_start(
                out=a_[:, 0:32 * M].rearrange("p (j m) -> p j m", j=32),
                in_=acth_hbm[:, :].rearrange("p (j s) -> p j s", j=32)[
                    :, :, sc * 128:sc * 128 + M]))(sc, M, a_),
                reads=ACB, writes=[ab_])
            for j in range(32):
                k.p((lambda j, M, a_, pA: lambda e: e.matmul(
                    pA[0:M, :], a_[:, j * M:(j + 1) * M],
                    wbig[:, j * 1024:j * 1024 + 512],
                    start=(j == 0), stop=False))(j, M, a_, pA),
                    reads=[ab_] + WAB, writes=[pAb])
                k.p((lambda j, M, a_, pB: lambda e: e.matmul(
                    pB[0:M, :], a_[:, j * M:(j + 1) * M],
                    wbig[:, j * 1024 + 512:j * 1024 + 1024],
                    start=(j == 0), stop=False))(j, M, a_, pB),
                    reads=[ab_] + WAB, writes=[pBb])
            k.p((lambda M, pA: lambda e: e.matmul(
                pA[0:M, :], onesrb[0:1, 0:M], bcpr[0:1, 0:512],
                start=False, stop=True))(M, pA),
                reads=[B_("onesrb"), B_("bcpr")], writes=[pAb])
            k.p((lambda M, pB: lambda e: e.matmul(
                pB[0:M, :], onesrb[0:1, 0:M], bcpr[0:1, 512:1024],
                start=False, stop=True))(M, pB),
                reads=[B_("onesrb"), B_("bcpr")], writes=[pBb])
            k.v((lambda M, pA, sev: lambda e: e.tensor_copy(
                sev[0:M, 0:512], pA[0:M, :]))(M, pA, sev),
                reads=[pAb], writes=[sevb])
            k.v((lambda M, pB, sev: lambda e: e.tensor_copy(
                sev[0:M, 512:1024], pB[0:M, :]))(M, pB, sev),
                reads=[pBb], writes=[sevb])
            em.dma("qsp", (lambda ex, sc, M, sev: lambda e: e.dma_start(
                out=sout_hbm[ex * ESL + sc * 128:ex * ESL + sc * 128 + M, :],
                in_=sev[0:M, :]))(ex, sc, M, sev),
                reads=[sevb], writes=[B_(f"sout_w{nso % 12}")])
            nso += 1
    k.pop()

    k.push()
    SOB = [B_(f"sout_w{i}") for i in range(12)]
    gbf = k.sb("gbf", [128, 64])
    gbi = k.sb("gbi", [128, 64], I32)
    xc4 = [k.sb(f"p4xc{i}", [128, KD * 128]) for i in range(2)]
    x3t = [k.sb(f"p4x3t{i}", [128, D]) for i in range(2)]
    g0sb = [k.sb(f"g0sb{i}", [128, D]) for i in range(2)]
    g1sb = [k.sb(f"g1sb{i}", [128, D]) for i in range(2)]
    out4 = [k.sb(f"out4_{i}", [128, D]) for i in range(2)]
    pst = [k.ps(f"p4pst{i}", [128, 128]) for i in range(2)]
    for kk in range(2):
        em.dma("qsp", (lambda kk: lambda e: e.dma_start(
            out=gbf[:, kk * NT:(kk + 1) * NT],
            in_=scr_sg[0:1, 2 * T + kk * T:2 * T + (kk + 1) * T].rearrange(
                "o (w p) -> (o p) w", p=128),
            allow_slow_non_contiguous=True))(kk),
            reads=[B_("scr_sg")], writes=[B_("gbf")])
    k.v(lambda e: e.tensor_copy(gbi[:, :], gbf[:, :]),
        reads=[B_("gbf")], writes=[B_("gbi")])
    for tile in range(NT):
        i2 = tile % 2
        xc, xcb = xc4[i2], B_(f"p4xc{i2}")
        xt, xtb = x3t[i2], B_(f"p4x3t{i2}")
        g0, g0b = g0sb[i2], B_(f"g0sb{i2}")
        g1, g1b = g1sb[i2], B_(f"g1sb{i2}")
        ot, otb = out4[i2], B_(f"out4_{i2}")
        em.dma("qsp", (lambda tile, xc: lambda e: e.dma_start(
            out=xc[:, :].rearrange("p (kc t) -> p kc t", kc=KD),
            in_=x3_hbm[:, tile * 128:(tile + 1) * 128].rearrange(
                "(kc p) t -> p kc t", p=128)))(tile, xc),
            reads=[B_("x3_hbm")], writes=[xcb])
        for kc in range(KD):
            pt, ptb = pst[kc % 2], B_(f"p4pst{kc % 2}")
            k.p((lambda kc, xc, pt: lambda e: e.transpose(
                pt[:, :], xc[:, kc * 128:(kc + 1) * 128], identf[:, :]))(kc, xc, pt),
                reads=[xcb, B_("identf")], writes=[ptb])
            k.v((lambda kc, xt, pt, ptb: lambda e: e.tensor_copy(
                xt[:, kc * 128:(kc + 1) * 128], pt[:, :]))(kc, xt, pt, ptb),
                reads=[ptb], writes=[xtb])
        em.dma("qpool", (lambda tile, g0: lambda e: e.indirect_dma_start(
            g0[:, :], None, sout_hbm[:, :],
            bass.IndirectOffsetOnAxis(gbi[:, tile:tile + 1], 0)))(tile, g0),
            reads=[B_("gbi")] + SOB, writes=[g0b])
        em.dma("qpool", (lambda tile, g1: lambda e: e.indirect_dma_start(
            g1[:, :], None, sout_hbm[:, :],
            bass.IndirectOffsetOnAxis(gbi[:, NT + tile:NT + tile + 1], 0)))(tile, g1),
            reads=[B_("gbi")] + SOB, writes=[g1b])
        k.v((lambda tile, g0, xt, ot: lambda e: e.scalar_tensor_tensor(
            ot[:, :], g0[:, :], rt_tm[:, tile * 8 + 2:tile * 8 + 3], xt[:, :],
            ALU.mult, ALU.add))(tile, g0, xt, ot),
            reads=[g0b, xtb, B_("rt_tm")], writes=[otb])
        k.v((lambda tile, g1, ot: lambda e: e.scalar_tensor_tensor(
            ot[:, :], g1[:, :], rt_tm[:, tile * 8 + 3:tile * 8 + 4], ot[:, :],
            ALU.mult, ALU.add))(tile, g1, ot),
            reads=[g1b, B_("rt_tm")], writes=[otb])
        em.dma("qsp", (lambda tile, ot: lambda e: e.dma_start(
            out=out_h[tile * 128:(tile + 1) * 128, :], in_=ot[:, :]))(tile, ot),
            reads=[otb], writes=[B_(f"outw{tile % 4}")])
    k.pop(barrier=False)


def _stage_inputs(full):
    f32 = np.float32
    x = np.asarray(full["L0_attn_x"], f32)
    import ml_dtypes
    identb = np.eye(128).astype(ml_dtypes.bfloat16)
    ones1f = np.ones((1, 128), f32)
    onescol = np.ones((128, 1), f32)
    sortw = np.asarray(full["L0_attn_sort_w"], f32)[0]
    in_maps = []
    for c in range(N_CORES):
        b, half = c // 2, c % 2
        im = {}
        im["x_fm"] = np.ascontiguousarray(x[b, half * T:(half + 1) * T, :].T)
        im["l0_wq"] = np.asarray(full["L0_attn_w_q"], f32)
        im["l0_wkv"] = np.asarray(full["L0_attn_w_kv"], f32)
        im["l0_wout"] = np.asarray(full["L0_attn_w_out"], f32)
        im["l0_bout"] = np.ascontiguousarray(
            np.asarray(full["L0_attn_b_out"], f32).reshape(KD, 128).T)
        im["l0_sortwq"] = np.ascontiguousarray(sortw[:, 0:64, :])
        im["ident_f32"] = np.eye(128, dtype=f32)
        im["l0_wg_t"] = np.ascontiguousarray(np.asarray(full["L0_ffn_w_gate"], f32).T)
        im["l0_wu_t"] = np.ascontiguousarray(np.asarray(full["L0_ffn_w_up"], f32).T)
        im["l0_wd_t"] = np.ascontiguousarray(np.asarray(full["L0_ffn_w_down"], f32).T)
        im["l0_sortwk"] = np.ascontiguousarray(sortw[:, 64:128, :])
        oh = np.zeros((64, 32), f32)
        for u in range(32):
            oh[half * 32 + u, u] = 1.0
        im["ownhot"] = oh
        neg = np.float32(-np.finfo(np.float32).max)
        mf = np.zeros((128, 384), f32)
        ml_ = np.zeros((128, 384), f32)
        if half == 0:
            mf[:, 0:128] = neg
        if half == 1:
            ml_[:, 256:384] = neg
        im["mask_first"] = mf
        im["mask_last"] = ml_
        im["ident_bf16"] = identb
        im["ones1_f32"] = ones1f
        im["onescol_f32"] = onescol
        im["kbm_idx"] = ((64 * b + np.arange(64))[:, None] * 4
                         + np.arange(4)[None, :]).astype(np.int32)
        lki = np.zeros((128, 34), np.int32)
        for w in range(34):
            gb = half * 32 + (w - 1)
            if gb < 0:
                gb = 0
            if gb > 63:
                gb = 63
            rank = 2 * b + gb // 32
            lki[:, w] = rank * T + (gb % 32) * 128 + np.arange(128)
        im["lkb_idx"] = lki
        ski = np.zeros((128, 32), np.int32)
        for u in range(32):
            ski[:, u] = (2 * b + half) * T + u * 128 + np.arange(128)
        im["sktile_idx"] = ski
        dh = half
        im["l1_wx_t"] = np.ascontiguousarray(np.asarray(full["L1_attn_Wx"], f32).T)
        im["l1_wz_t"] = np.ascontiguousarray(np.asarray(full["L1_attn_Wz"], f32).T)
        im["l1_wp_t"] = np.ascontiguousarray(np.asarray(full["L1_attn_Wp"], f32).T)
        wdt_t = np.asarray(full["L1_attn_Wdt"], f32).T
        im["l1_wdt_sl"] = np.ascontiguousarray(wdt_t[:, dh * 512:(dh + 1) * 512])
        im["l1_wo_t"] = np.ascontiguousarray(np.asarray(full["L1_attn_Wo"], f32).T)
        alog = np.asarray(full["L1_attn_A_log"], f32)[dh * 512:(dh + 1) * 512, :]
        im["l1_alog_sl"] = np.ascontiguousarray(
            alog.reshape(4, 128, NST).transpose(1, 0, 2).reshape(128, 4 * NST))
        bdt = np.asarray(full["L1_attn_bdt"], f32)[dh * 512:(dh + 1) * 512]
        im["l1_bdt_sl"] = np.ascontiguousarray(bdt.reshape(4, 128).T)
        dp = np.asarray(full["L1_attn_Dp"], f32)[dh * 512:(dh + 1) * 512]
        im["l1_dp_sl"] = np.ascontiguousarray(dp.reshape(4, 128).T)
        cw = np.asarray(full["L1_attn_conv_w"], f32)[dh * 512:(dh + 1) * 512, 0, :]
        im["l1_convw_sl"] = np.ascontiguousarray(
            cw.reshape(4, 128, 4).transpose(1, 0, 2).reshape(128, 16))
        p_ = np.arange(128)
        xpg = np.zeros((128, 8), np.int32)
        for dc in range(4):
            for r01 in range(2):
                xpg[:, dc * 2 + r01] = (2 * b + r01) * 1120 + dh * 512 + dc * 128 + p_
        im["xpg_idx"] = xpg
        zi = np.zeros((128, 64), np.int32)
        for dc in range(4):
            for tch in range(16):
                zi[:, dc * 16 + tch] = (((2 * b + tch // 8) * 1024 + dh * 512
                                         + dc * 128 + p_) * 8 + (tch % 8))
        im["z_idx"] = zi
        dti = np.zeros((64, 2), np.int32)
        for r01 in range(2):
            dti[:, r01] = (2 * b + r01) * 1120 + 1024 + np.arange(64)
        im["dtun_idx"] = dti
        bci = np.zeros((32, 2), np.int32)
        for r01 in range(2):
            bci[:, r01] = (2 * b + r01) * 1120 + 1088 + np.arange(32)
        im["bc_idx"] = bci
        yi = np.zeros((128, 64), np.int32)
        for kc in range(KD):
            for tc in range(8):
                yi[:, kc * 8 + tc] = (((2 * b + kc // 4) * 512 + (kc % 4) * 128 + p_) * 16
                                      + half * 8 + tc)
        im["y_idx"] = yi
        import ml_dtypes as _md
        bh = np.zeros((32, 32 * 128), np.float32)
        for n in range(32):
            bh[n, n * 128:(n + 1) * 128] = 1.0
        im["bchot"] = bh.astype(_md.bfloat16)
        im["l1_rw"] = np.asarray(full["L1_ffn_router_w"], f32)
        im["l1_rb_col"] = np.ascontiguousarray(
            np.asarray(full["L1_ffn_router_b"], f32).reshape(NEXP, 1))
        im["l1_wfc"] = np.asarray(full["L1_ffn_w_c_fc"], f32)
        im["l1_wgt"] = np.asarray(full["L1_ffn_w_gate"], f32)
        im["l1_wcp"] = np.asarray(full["L1_ffn_w_c_proj"], f32)
        im["l1_bfc"] = np.asarray(full["L1_ffn_b_c_fc"], f32).reshape(NEXP, INNER)
        im["l1_bgt"] = np.asarray(full["L1_ffn_b_gate"], f32).reshape(NEXP, INNER)
        im["l1_bcp"] = np.asarray(full["L1_ffn_b_c_proj"], f32).reshape(NEXP, D)
        im["eiota"] = np.arange(NEXP, dtype=f32).reshape(NEXP, 1)
        im["ebase"] = (np.arange(NEXP, dtype=f32) * ESL).reshape(NEXP, 1)
        im["ebase_i"] = (np.arange(NEXP) * ESL).astype(np.int32).reshape(NEXP, 1)
        im["hsel8"] = np.full((NEXP, 1), half, f32)
        im["cnt_idx"] = np.array([[2 * b], [2 * b + 1]], np.int32)
        im["cand_idx"] = (b * NEXP + np.arange(NEXP)).astype(np.int32).reshape(NEXP, 1)
        im["tok8"] = np.broadcast_to(np.arange(T, dtype=f32), (NEXP, T)).copy()
        tr = (TRASH + (np.arange(T) // 128) * 2).astype(f32)
        im["trashrow"] = np.concatenate([tr, tr + 1]).reshape(1, 2 * T)
        in_maps.append(im)
    return in_maps


_NC_CACHE = {}


def kernel(**inputs):
    n_phases = int(inputs.pop("_n_phases", 4))
    if n_phases not in _NC_CACHE:
        _NC_CACHE[n_phases] = build_program(n_phases)
    nc = _NC_CACHE[n_phases]
    in_maps = _stage_inputs(inputs)
    res = run_bass_kernel_spmd(nc, in_maps, core_ids=list(range(N_CORES)))
    out = np.zeros((B, TFULL, D), np.float32)
    key = "out_fm" if n_phases < 4 else "out_tm"
    for c in range(N_CORES):
        b, half = c // 2, c % 2
        o = res.results[c][key]
        out[b, half * T:(half + 1) * T, :] = o.T if n_phases < 4 else o
    return out


# version 08
# speedup vs baseline: 1.1335x; 1.0004x over previous
import numpy as np
from contextlib import ExitStack

import concourse.bass as bass
import concourse.mybir as mybir
from concourse.bass_utils import run_bass_kernel_spmd

F32 = mybir.dt.float32
BF16 = mybir.dt.bfloat16
I32 = mybir.dt.int32
U8 = mybir.dt.uint8
U32 = mybir.dt.uint32
AF = mybir.ActivationFunctionType
ALU = mybir.AluOpType
AXX = mybir.AxisListType.X

N_CORES = 8
B, TFULL, D = 4, 8192, 1024
T = 4096
NT = T // 128
KD = D // 128
H, LH, SH, DH = 16, 4, 12, 64
INNER = 4096
NST, DTR = 16, 64
NEXP, TOPK = 8, 2
CAP = int(1.2 * TOPK * TFULL // NEXP)
ESL = 2464
TRASH = NEXP * ESL
SROWS = TRASH + 128

C32 = 12582912.0
CDELTA = 6755399441055744.0

SK_SZ = 32 * 128 * 768
LK_SZ = T * 256

ENGS = ("pe", "dve", "act", "pool", "sp")
_ENG_ATTR = {"pe": "tensor", "dve": "vector", "act": "scalar", "pool": "gpsimd",
             "sp": "sync"}
DMA_QUEUES = ("qsp", "qact", "qpool")
_DMA_ISSUER = {"qsp": "sp", "qact": "act", "qpool": "pool"}


class Buf:
    __slots__ = ("name", "writer", "readers")

    def __init__(self, name):
        self.name = name
        self.writer = None
        self.readers = []


class Em:
    NSLOT = 12

    def __init__(self, nc):
        self.nc = nc
        self.steps = {e: [] for e in ENGS}
        self.cnt = {e: 0 for e in ENGS}
        self.qidx = {q: 0 for q in DMA_QUEUES}
        self._sems = {}
        self._waited = {e: {} for e in ENGS}
        self._fin = False

    def sem(self, key):
        if key not in self._sems:
            self._sems[key] = self.nc.alloc_semaphore(name=f"bk_{key}")
        return self._sems[key]

    def buf(self, name):
        return Buf(name)

    def _edge(self, eng, dep):
        if dep is None:
            return None
        sk, c = dep
        if self._waited[eng].get(sk, -1) >= c:
            return None
        return dep

    def _deps(self, eng, reads, writes):
        deps = []
        for b in reads:
            d = self._edge(eng, b.writer)
            if d:
                deps.append(d)
        for b in writes:
            d = self._edge(eng, b.writer)
            if d:
                deps.append(d)
            for r in b.readers:
                d = self._edge(eng, r)
                if d:
                    deps.append(d)
        dd = {}
        for sk, c in deps:
            dd[sk] = max(dd.get(sk, 0), c)
        return dd

    def _commit(self, eng, dd):
        waits = sorted(dd.items())
        for sk, c in waits:
            self._waited[eng][sk] = max(self._waited[eng].get(sk, -1), c)
        return waits

    def emit(self, eng, fn, reads=(), writes=()):
        assert not self._fin
        dd = self._deps(eng, reads, writes)
        waits = self._commit(eng, dd)
        self.cnt[eng] += 1
        self.steps[eng].append((waits, fn, eng, 1))
        me = (eng, self.cnt[eng])
        for b in writes:
            b.writer = me
            b.readers = []
        for b in reads:
            b.readers.append(me)
        return me

    def dma(self, q, fn, reads=(), writes=()):
        assert not self._fin
        eng = _DMA_ISSUER[q]
        dd = self._deps(eng, reads, writes)
        i = self.qidx[q]
        self.qidx[q] = i + 1
        slot = f"{q}{i % self.NSLOT}"
        prev = 16 * (i // self.NSLOT)
        if prev > 0 and self._waited[eng].get(slot, -1) < prev:
            dd[slot] = max(dd.get(slot, 0), prev)
        waits = self._commit(eng, dd)
        self.steps[eng].append((waits, fn, slot, 16))
        me = (slot, prev + 16)
        for b in writes:
            b.writer = me
            b.readers = []
        for b in reads:
            b.readers.append(me)
        return me

    def barrier_token(self):
        toks = [(e, self.cnt[e]) for e in ENGS if self.cnt[e] > 0]
        for q in DMA_QUEUES:
            n = self.qidx[q]
            for s in range(min(n, self.NSLOT)):
                i = (n - 1 - ((n - 1 - s) % self.NSLOT))
                toks.append((f"{q}{s}", 16 * (i // self.NSLOT + 1)))
        return toks

    def all_barrier(self):
        toks = self.barrier_token()
        for e in ENGS:
            self.wait_tokens(e, toks)

    def wait_tokens(self, eng, toks):
        dd = {}
        for sk, c in toks:
            if sk == eng:
                continue
            if self._waited[eng].get(sk, -1) >= c:
                continue
            dd[sk] = max(dd.get(sk, 0), c)
        if not dd:
            return
        waits = self._commit(eng, dd)
        self.steps[eng].append((waits, None, None, 0))

    def finalize(self):
        assert not self._fin
        self._fin = True
        with self.nc.Block() as blk:
            for eng in ENGS:
                steps = self.steps[eng]
                if not steps:
                    continue

                def mk(eng_steps):
                    def body(engobj):
                        for waits, fn, sem_key, inc in eng_steps:
                            for sk, c in waits:
                                engobj.wait_ge(self.sem(sk), c)
                            if fn is None:
                                continue
                            instr = fn(engobj)
                            if instr is not None and sem_key is not None:
                                instr.then_inc(self.sem(sem_key), inc)
                    return body
                getattr(blk, _ENG_ATTR[eng])(mk(steps))


class K:
    def __init__(self, nc):
        self.nc = nc
        self.em = Em(nc)
        self.scopes = [ExitStack()]
        self.bufs = {}

    def push(self):
        self.scopes.append(ExitStack())

    def pop(self, barrier=True):
        if barrier:
            self.em.all_barrier()
        self.scopes.pop().close()

    def B(self, name):
        if name not in self.bufs:
            self.bufs[name] = self.em.buf(name)
        return self.bufs[name]

    def sb(self, name, shape, dt=F32):
        return self.scopes[-1].enter_context(
            self.nc.sbuf_tensor(name, list(shape), dt))

    def ps(self, name, shape, dt=F32):
        return self.scopes[-1].enter_context(
            self.nc.psum_tensor(name, list(shape), dt))

    def dram(self, name, shape, dt=F32):
        return self.nc.dram_tensor(name, list(shape), dt)

    def v(self, fn, reads=(), writes=()):
        return self.em.emit("dve", fn, reads, writes)

    def a(self, fn, reads=(), writes=()):
        return self.em.emit("act", fn, reads, writes)

    def p(self, fn, reads=(), writes=()):
        return self.em.emit("pe", fn, reads, writes)

    def g(self, fn, reads=(), writes=()):
        return self.em.emit("pool", fn, reads, writes)


def rsqrt_rows(k, dst, src, scratch, bufs):
    sbuf, dbuf, cbuf = bufs
    k.a(lambda e: e.activation(out=scratch, in_=src, func=AF.Sqrt),
        reads=[sbuf], writes=[cbuf])
    k.v(lambda e: e.reciprocal(dst, scratch), reads=[cbuf], writes=[dbuf])
    for _ in range(2):
        k.v(lambda e: e.tensor_tensor(scratch, dst, dst, ALU.mult),
            reads=[dbuf], writes=[cbuf])
        k.v(lambda e: e.tensor_tensor(scratch, src, scratch, ALU.mult),
            reads=[sbuf, cbuf], writes=[cbuf])
        k.v(lambda e: e.tensor_scalar(scratch, scratch, 3.0, None, ALU.subtract),
            reads=[cbuf], writes=[cbuf])
        k.v(lambda e: e.tensor_scalar(scratch, scratch, -0.5, None, ALU.mult),
            reads=[cbuf], writes=[cbuf])
        k.v(lambda e: e.tensor_tensor(dst, dst, scratch, ALU.mult),
            reads=[dbuf, cbuf], writes=[dbuf])


def build_program(n_phases=4):
    nc = bass.Bass("TRN2", target_bir_lowering=False)
    k = K(nc)
    em, B_ = k.em, k.B

    def inp(name, shape, dt=F32):
        return nc.dram_tensor(name, list(shape), dt, kind="ExternalInput").ap()

    x_h = inp("x_fm", [D, T])
    wq_h = inp("l0_wq", [D, D])
    wkv_h = inp("l0_wkv", [D, 2 * D])
    wout_h = inp("l0_wout", [D, D])
    bout_h = inp("l0_bout", [128, KD])
    sortwq_h = inp("l0_sortwq", [SH, 64, 64])
    sortwk_h = inp("l0_sortwk", [SH, 64, 64])
    ownhot_h = inp("ownhot", [64, 32])
    mask_first_h = inp("mask_first", [128, 384])
    mask_last_h = inp("mask_last", [128, 384])
    identb_h = inp("ident_bf16", [128, 128], BF16)
    ones1f_h = inp("ones1_f32", [1, 128])
    onescol_h = inp("onescol_f32", [128, 1])
    kbm_idx_h = inp("kbm_idx", [64, 4], I32)
    sktile_idx_h = inp("sktile_idx", [128, 32], I32)
    lkb_idx_h = inp("lkb_idx", [128, 34], I32)
    identf_h = inp("ident_f32", [128, 128])
    wg_h = inp("l0_wg_t", [D, INNER])
    wu_h = inp("l0_wu_t", [D, INNER])
    wd_h = inp("l0_wd_t", [INNER, D])
    if n_phases < 4:
        out_h = nc.dram_tensor("out_fm", [D, T], F32, kind="ExternalOutput").ap()
    else:
        out_h = nc.dram_tensor("out_tm", [T, D], F32, kind="ExternalOutput").ap()

    ag1_sk_in = k.dram("ag1_sk_in", [32 * 128, 768], BF16)
    ag1_sv_in = k.dram("ag1_sv_in", [32 * 128, 768], BF16)
    ag1_lk_in = k.dram("ag1_lk_in", [T, 256], BF16)
    ag1_lv_in = k.dram("ag1_lv_in", [T, 256], BF16)
    ag1_sk = k.dram("ag1_sk", [N_CORES * 32 * 128, 768], BF16)
    ag1_sv = k.dram("ag1_sv", [N_CORES * 32 * 128, 768], BF16)
    ag1_lk = k.dram("ag1_lk", [N_CORES * T, 256], BF16)
    ag1_lv = k.dram("ag1_lv", [N_CORES * T, 256], BF16)
    x1_hbm = k.dram("x1_hbm", [D, T])
    attnout_hbm = k.dram("attnout_hbm", [D, T], BF16)
    kblob_hbm = k.dram("kblob_hbm", [6, 64 * 128, 128], BF16)
    vblob_hbm = k.dram("vblob_hbm", [3, 64 * 128, 256], BF16)
    lkstrip_hbm = k.dram("lkstrip_hbm", [34 * 128, 256], BF16)

    identb = k.sb("identb_s", [128, 128], BF16)
    identf = k.sb("identf_s", [128, 128])
    ones1f = k.sb("ones1f_s", [1, 128])
    onescol = k.sb("onescol_s", [128, 1])
    onescol_b = k.sb("onescol_bs", [128, 1], BF16)
    probe = k.sb("probe", [128, 1])
    magicC = k.sb("magicC", [128, 1])
    kbm_idx = k.sb("kbm_idx_s", [64, 4], I32)
    sktile_idx = k.sb("sktile_idx_s", [128, 32], I32)
    lkb_idx = k.sb("lkb_idx_s", [128, 34], I32)
    mask_first = k.sb("mask_first_s", [128, 384])
    mask_last = k.sb("mask_last_s", [128, 384])
    boutc = k.sb("boutc_s", [128, KD])

    for dst, src, nm in [(identb, identb_h, "identb"), (identf, identf_h, "identf"),
                         (ones1f, ones1f_h, "ones1f"),
                         (onescol, onescol_h, "onescol"), (kbm_idx, kbm_idx_h, "kbm_idx"),
                         (lkb_idx, lkb_idx_h, "lkb_idx"),
                         (sktile_idx, sktile_idx_h, "sktile_idx"),
                         (mask_first, mask_first_h, "mask_first"),
                         (mask_last, mask_last_h, "mask_last"), (boutc, bout_h, "boutc")]:
        em.dma("qsp", (lambda d, s: lambda e: e.dma_start(out=d[:], in_=s[:, :]))(dst, src),
               writes=[B_(nm)])
    k.v(lambda e: e.tensor_copy(onescol_b[:, :], onescol[:, :]),
        reads=[B_("onescol")], writes=[B_("onescol_b")])

    k.v(lambda e: e.memset(probe[:], 16777216.0), writes=[B_("probe")])
    k.v(lambda e: e.tensor_scalar(probe[:], probe[:], 1.0, None, ALU.add),
        reads=[B_("probe")], writes=[B_("probe")])
    k.v(lambda e: e.tensor_scalar(probe[:], probe[:], 16777216.0, None, ALU.subtract),
        reads=[B_("probe")], writes=[B_("probe")])
    k.v(lambda e: e.tensor_scalar(magicC[:], probe[:], CDELTA, C32, ALU.mult, ALU.add),
        reads=[B_("probe")], writes=[B_("magicC")])

    k.push()
    q_fm = k.sb("q_fm", [128, KD * T], BF16)

    k.push()
    xa = k.sb("xa", [128, KD * 512])
    xb = k.sb("xb", [128, KD * 512])
    xsq = k.sb("xsq", [128, KD * 512])
    xnr = k.sb("xnr", [128, 2 * KD * 512], BF16)
    wq_sb = k.sb("wq_sb", [128, KD * D], BF16)
    wkv_sb = k.sb("wkv_sb", [128, KD * 2048], BF16)
    stat = k.sb("stat", [1, 4 * 512])
    mbc = k.sb("mbc", [128, 512])
    rbc = k.sb("rbc", [128, 512])
    kvev = k.sb("kvev", [128, 2 * 2048], BF16)
    ps_s1 = k.ps("ps_s1", [1, 512])
    ps_bc2 = k.ps("ps_bc2", [128, 512])
    ps_q = [k.ps(f"ps_q{i}", [128, 512]) for i in range(2)]
    ps_kv = [k.ps(f"ps_kv{i}", [128, 512]) for i in range(2)]

    for kc in range(KD):
        em.dma("qpool", (lambda kc: lambda e: e.dma_start(
            out=wq_sb[:, kc * D:(kc + 1) * D],
            in_=wq_h[kc * 128:(kc + 1) * 128, :]))(kc), writes=[B_("wq_sb")])
        em.dma("qpool", (lambda kc: lambda e: e.dma_start(
            out=wkv_sb[:, kc * 2048:(kc + 1) * 2048],
            in_=wkv_h[kc * 128:(kc + 1) * 128, :]))(kc), writes=[B_("wkv_sb")])

    for tc in range(T // 512):
        xsb, xbuf = (xa, B_("xa")) if tc % 2 == 0 else (xb, B_("xb"))
        xnb = (tc % 2) * KD * 512
        xnbuf = B_(f"xnr{tc % 2}")
        em.dma("qsp", (lambda xsb, tc: lambda e: e.dma_start(
            out=xsb[:, :].rearrange("p (kc t) -> p kc t", kc=KD),
            in_=x_h[:, tc * 512:(tc + 1) * 512].rearrange("(kc p) t -> p kc t", p=128)))(xsb, tc),
            writes=[xbuf])
        k.p((lambda xsb: lambda e: [e.matmul(
            ps_s1[:, :], onescol[:, :], xsb[:, kc * 512:(kc + 1) * 512],
            start=(kc == 0), stop=(kc == KD - 1)) for kc in range(KD)][-1])(xsb),
            reads=[B_("onescol"), xbuf], writes=[B_("ps_s1")])
        k.a((lambda xsb: lambda e: e.activation(out=xsq[:, :], in_=xsb[:, :],
                                                func=AF.Square))(xsb),
            reads=[xbuf], writes=[B_("xsq")])
        sm = stat[0:1, 0:512]
        sv = stat[0:1, 512:1024]
        sr = stat[0:1, 1024:1536]
        sc_ = stat[0:1, 1536:2048]
        k.v(lambda e: e.tensor_scalar(sm, ps_s1[:, :], 1.0 / D, None, ALU.mult),
            reads=[B_("ps_s1")], writes=[B_("stat")])
        k.p(lambda e: [e.matmul(
            ps_s1[:, :], onescol[:, :], xsq[:, kc * 512:(kc + 1) * 512],
            start=(kc == 0), stop=(kc == KD - 1)) for kc in range(KD)][-1],
            reads=[B_("onescol"), B_("xsq")], writes=[B_("ps_s1")])
        k.v(lambda e: e.tensor_scalar(sv, ps_s1[:, :], 1.0 / D, None, ALU.mult),
            reads=[B_("ps_s1")], writes=[B_("stat")])
        k.v(lambda e: e.tensor_tensor(sc_, sm, sm, ALU.mult),
            reads=[B_("stat")], writes=[B_("stat")])
        k.v(lambda e: e.tensor_tensor(sv, sv, sc_, ALU.subtract),
            reads=[B_("stat")], writes=[B_("stat")])
        k.v(lambda e: e.tensor_scalar(sv, sv, 1e-5, None, ALU.add),
            reads=[B_("stat")], writes=[B_("stat")])
        rsqrt_rows(k, sr, sv, sc_, (B_("stat"), B_("stat"), B_("stat")))
        k.p(lambda e: e.matmul(ps_bc2[:, :], ones1f[:, :], sm, start=True, stop=True),
            reads=[B_("ones1f"), B_("stat")], writes=[B_("ps_bc2")])
        k.v(lambda e: e.tensor_copy(mbc[:, :], ps_bc2[:, :]),
            reads=[B_("ps_bc2")], writes=[B_("mbc")])
        k.p(lambda e: e.matmul(ps_bc2[:, :], ones1f[:, :], sr, start=True, stop=True),
            reads=[B_("ones1f"), B_("stat")], writes=[B_("ps_bc2")])
        k.v(lambda e: e.tensor_copy(rbc[:, :], ps_bc2[:, :]),
            reads=[B_("ps_bc2")], writes=[B_("rbc")])
        for kc in range(KD):
            k.v((lambda xsb, kc: lambda e: e.tensor_tensor(
                xsq[:, kc * 512:(kc + 1) * 512], xsb[:, kc * 512:(kc + 1) * 512],
                mbc[:, :], ALU.subtract))(xsb, kc),
                reads=[xbuf, B_("mbc")], writes=[B_("xsq")])
            k.v((lambda kc, xnb, xnbuf: lambda e: e.tensor_tensor(
                xnr[:, xnb + kc * 512:xnb + kc * 512 + 512],
                xsq[:, kc * 512:(kc + 1) * 512], rbc[:, :], ALU.mult))(kc, xnb, xnbuf),
                reads=[B_("xsq"), B_("rbc")], writes=[xnbuf])
        for oc in range(KD):
            pq, pqb = ps_q[oc % 2], B_(f"ps_q{oc % 2}")
            k.p((lambda oc, xnb, pq: lambda e: [e.matmul(
                pq[:, :], wq_sb[:, (kc * KD + oc) * 128:(kc * KD + oc) * 128 + 128],
                xnr[:, xnb + kc * 512:xnb + kc * 512 + 512],
                start=(kc == 0), stop=(kc == KD - 1)) for kc in range(KD)][-1])(oc, xnb, pq),
                reads=[B_("wq_sb"), xnbuf], writes=[pqb])
            k.v((lambda oc, tc, pq: lambda e: e.tensor_copy(
                q_fm[:, oc * T + tc * 512:oc * T + tc * 512 + 512], pq[:, :]))(oc, tc, pq),
                reads=[pqb], writes=[B_("q_fm")])
        for ti in range(4):
            vtile = tc * 4 + ti
            for nb in range(4):
                pkv, pkb = ps_kv[nb % 2], B_(f"ps_kv{nb % 2}")
                k.p((lambda xnb, ti, nb, pkv: lambda e: [e.matmul(
                    pkv[:, :],
                    xnr[:, xnb + kc * 512 + ti * 128:xnb + kc * 512 + ti * 128 + 128],
                    wkv_sb[:, kc * 2048 + nb * 512:kc * 2048 + nb * 512 + 512],
                    start=(kc == 0), stop=(kc == KD - 1)) for kc in range(KD)][-1])(xnb, ti, nb, pkv),
                    reads=[xnbuf, B_("wkv_sb")], writes=[pkb])
                k.v((lambda vtile, nb, pkv: lambda e: e.tensor_copy(
                    kvev[:, (vtile % 2) * 2048 + nb * 512:(vtile % 2) * 2048 + nb * 512 + 512],
                    pkv[:, :]))(vtile, nb, pkv),
                    reads=[pkb], writes=[B_(f"kvev{vtile % 2}")])
            db = (vtile % 2) * 2048
            kvb = B_(f"kvev{vtile % 2}")
            em.dma("qsp", (lambda vtile, db: lambda e: e.dma_start(
                out=ag1_sk_in[vtile * 128:(vtile + 1) * 128, :],
                in_=kvev[:, db + 256:db + 1024]))(vtile, db),
                reads=[kvb], writes=[B_("ag1_sk_in")])
            em.dma("qsp", (lambda vtile, db: lambda e: e.dma_start(
                out=ag1_sv_in[vtile * 128:(vtile + 1) * 128, :],
                in_=kvev[:, db + 1280:db + 2048]))(vtile, db),
                reads=[kvb], writes=[B_("ag1_sv_in")])
            em.dma("qact", (lambda vtile, db: lambda e: e.dma_start(
                out=ag1_lk_in[vtile * 128:(vtile + 1) * 128, :],
                in_=kvev[:, db:db + 256]))(vtile, db),
                reads=[kvb], writes=[B_("ag1_lk_in")])
            em.dma("qact", (lambda vtile, db: lambda e: e.dma_start(
                out=ag1_lv_in[vtile * 128:(vtile + 1) * 128, :],
                in_=kvev[:, db + 1024:db + 1280]))(vtile, db),
                reads=[kvb], writes=[B_("ag1_lv_in")])

    for _i, _o, _ib in [(ag1_sk_in, ag1_sk, "ag1_sk_in"), (ag1_sv_in, ag1_sv, "ag1_sv_in"),
                        (ag1_lk_in, ag1_lk, "ag1_lk_in"), (ag1_lv_in, ag1_lv, "ag1_lv_in")]:
        k.g((lambda _i, _o: lambda e: e.collective_compute(
            "AllGather", ALU.bypass, replica_groups=[list(range(N_CORES))],
            ins=[_i[:, :]], outs=[_o[:, :]]))(_i, _o),
            reads=[B_(_ib)], writes=[B_("ag1")])
    k.pop()

    env = dict(locals())
    if n_phases >= 1:
        build_p1(k, env)
    k.pop()
    if n_phases >= 2:
        build_p2(k, env)
    if n_phases >= 3:
        build_p3(k, env)
    if n_phases >= 4:
        build_p4(k, env)

    if n_phases < 4:
        res_map = {0: None, 1: "x1_hbm", 2: "x2_hbm", 3: "x3_hbm"}
        nm = res_map.get(n_phases)
        k.push()
        dbg = k.sb("dbgout", [128, KD * 512])
        for tc in range(T // 512):
            if nm is None:
                src_ap = x_h[:, tc * 512:(tc + 1) * 512]
                rb = []
            else:
                src_ap = env[nm][:, tc * 512:(tc + 1) * 512]
                rb = [B_(nm)]
            em.dma("qact", (lambda src_ap: lambda e: e.dma_start(
                out=dbg[:, :].rearrange("p (kc t) -> p kc t", kc=KD),
                in_=src_ap.rearrange("(kc p) t -> p kc t", p=128)))(src_ap),
                reads=rb, writes=[B_("dbgout")])
            em.dma("qact", (lambda tc: lambda e: e.dma_start(
                out=out_h[:, tc * 512:(tc + 1) * 512].rearrange("(kc p) t -> p kc t", p=128),
                in_=dbg[:, :].rearrange("p (kc t) -> p kc t", kc=KD)))(tc),
                reads=[B_("dbgout")], writes=[B_("out")])
        k.pop(barrier=False)

    em.wait_tokens("sp", em.barrier_token())
    em.finalize()
    while k.scopes:
        k.scopes.pop().close()
    return nc

def build_p1(k, env):
    em, B_ = k.em, k.B
    g = env
    q_fm = g["q_fm"]; identb = g["identb"]; onescol = g["onescol"]
    onescol_b = g["onescol_b"]
    ag1_sk, ag1_sv = g["ag1_sk"], g["ag1_sv"]
    ag1_lk, ag1_lv = g["ag1_lk"], g["ag1_lv"]
    kbm_idx, lkb_idx, sktile_idx = g["kbm_idx"], g["lkb_idx"], g["sktile_idx"]
    mask_first, mask_last = g["mask_first"], g["mask_last"]
    kblob_hbm, vblob_hbm, lkstrip_hbm = g["kblob_hbm"], g["vblob_hbm"], g["lkstrip_hbm"]
    attnout_hbm = g["attnout_hbm"]
    sortwq_h, sortwk_h, ownhot_h = g["sortwq_h"], g["sortwk_h"], g["ownhot_h"]
    wout_h, boutc = g["wout_h"], g["boutc"]
    x_h, x1_hbm = g["x_h"], g["x1_hbm"]

    k.push()
    sortwq = k.sb("sortwq", [64, SH * 64], BF16)
    sortwk = k.sb("sortwk", [64, SH * 64], BF16)
    ownhot = k.sb("ownhot_s", [64, 32], BF16)
    em.dma("qpool", lambda e: e.dma_start(
        out=sortwq[:, :].rearrange("p (s c) -> p s c", s=SH),
        in_=sortwq_h[:, :, :].rearrange("s p c -> p s c")), writes=[B_("sortwq")])
    em.dma("qpool", lambda e: e.dma_start(
        out=sortwk[:, :].rearrange("p (s c) -> p s c", s=SH),
        in_=sortwk_h[:, :, :].rearrange("s p c -> p s c")), writes=[B_("sortwk")])
    em.dma("qpool", lambda e: e.dma_start(out=ownhot[:, :], in_=ownhot_h[:, :]),
           writes=[B_("ownhot")])

    qsum_b = k.sb("qsum_b", [64, SH * 32], BF16)
    ksum_b = k.sb("ksum_b", [64, SH * 32], BF16)
    qsum_f = k.sb("qsum_f", [64, 32])
    for s in range(SH):
        h = s + 4
        hc, hb = h // 2, 64 * (h % 2)
        k.v((lambda hc, hb: lambda e: e.tensor_reduce(
            qsum_f[:, :].rearrange("p (u o) -> p u o", o=1),
            q_fm[hb:hb + 64, hc * T:(hc + 1) * T].rearrange("p (u t) -> p u t", u=32),
            AXX, ALU.add))(hc, hb),
            reads=[B_("q_fm")], writes=[B_("qsum_f")])
        k.v((lambda s: lambda e: e.tensor_copy(qsum_b[:, s * 32:(s + 1) * 32],
                                               qsum_f[:, :]))(s),
            reads=[B_("qsum_f")], writes=[B_("qsum_b")])

    sktile = k.sb("sktile", [128, 2 * 768], BF16)
    ps_ks = k.ps("ps_ks", [64, SH])
    for u in range(32):
        db = (u % 2) * 768
        em.dma("qpool", (lambda u, db: lambda e: e.indirect_dma_start(
            sktile[:, db:db + 768], None, ag1_sk[:, :],
            bass.IndirectOffsetOnAxis(sktile_idx[:, u:u + 1], 0)))(u, db),
            reads=[B_("sktile_idx"), B_("ag1")], writes=[B_(f"sktile{u % 2}")])
        k.p((lambda u, db: lambda e: [e.matmul(
            ps_ks[:, s:s + 1], sktile[:, db + s * 64:db + s * 64 + 64],
            onescol_b[:, :], start=True, stop=True) for s in range(SH)][-1])(u, db),
            reads=[B_(f"sktile{u % 2}"), B_("onescol_b")], writes=[B_("ps_ks")])
        k.v((lambda u: lambda e: e.tensor_copy(
            ksum_b[:, :].rearrange("p (s u2) -> p s u2", s=SH)[:, :, u:u + 1],
            ps_ks[:, :].rearrange("p (s o) -> p s o", o=1)))(u),
            reads=[B_("ps_ks")], writes=[B_("ksum_b")])

    r_sb = k.sb("r_sb", [32, 64])
    r_bb = k.sb("r_bb", [32, 64], BF16)
    r_row = k.sb("r_row", [32, 2])
    r2all = k.sb("r2all", [64, SH * 64], BF16)
    ps_r = k.ps("ps_r", [32, 64])
    ps_rt = k.ps("ps_rt", [64, 32], BF16)
    for s in range(SH):
        k.p((lambda s: lambda e: [
            e.matmul(ps_r[:, :], qsum_b[:, s * 32:(s + 1) * 32],
                     sortwq[:, s * 64:(s + 1) * 64], start=True, stop=False),
            e.matmul(ps_r[:, :], ksum_b[:, s * 32:(s + 1) * 32],
                     sortwk[:, s * 64:(s + 1) * 64], start=False, stop=True)][-1])(s),
            reads=[B_("qsum_b"), B_("ksum_b"), B_("sortwq"), B_("sortwk")],
            writes=[B_("ps_r")])
        k.v(lambda e: e.tensor_single_scalar(r_sb[:, :], ps_r[:, :], 0.0, ALU.max),
            reads=[B_("ps_r")], writes=[B_("r_sb")])
        k.v(lambda e: e.tensor_reduce(r_row[:, 0:1], r_sb[:, :], AXX, ALU.max),
            reads=[B_("r_sb")], writes=[B_("r_row")])
        k.v(lambda e: e.tensor_scalar(r_row[:, 0:1], r_row[:, 0:1], -1.0, None, ALU.mult),
            reads=[B_("r_row")], writes=[B_("r_row")])
        k.a(lambda e: e.activation(out=r_sb[:, :], in_=r_sb[:, :], func=AF.Exp,
                                   bias=r_row[:, 0:1]),
            reads=[B_("r_row"), B_("r_sb")], writes=[B_("r_sb")])
        k.v(lambda e: e.tensor_reduce(r_row[:, 1:2], r_sb[:, :], AXX, ALU.add),
            reads=[B_("r_sb")], writes=[B_("r_row")])
        k.v(lambda e: e.reciprocal(r_row[:, 1:2], r_row[:, 1:2]),
            reads=[B_("r_row")], writes=[B_("r_row")])
        k.v(lambda e: e.tensor_scalar(r_sb[:, :], r_sb[:, :], r_row[:, 1:2], None,
                                      ALU.mult),
            reads=[B_("r_row"), B_("r_sb")], writes=[B_("r_sb")])
        k.v(lambda e: e.tensor_copy(r_bb[:, :], r_sb[:, :]),
            reads=[B_("r_sb")], writes=[B_("r_bb")])
        k.p(lambda e: e.transpose(ps_rt[:, :], r_bb[:, :], identb[0:32, 0:32]),
            reads=[B_("r_bb"), B_("identb")], writes=[B_("ps_rt")])
        k.v((lambda s: lambda e: e.tensor_copy(r2all[:, s * 64:s * 64 + 32],
                                               ps_rt[:, :]))(s),
            reads=[B_("ps_rt")], writes=[B_("r2all")])
        k.v((lambda s: lambda e: e.tensor_copy(r2all[:, s * 64 + 32:s * 64 + 64],
                                               ownhot[:, :]))(s),
            reads=[B_("ownhot")], writes=[B_("r2all")])

    kbm = k.sb("kbm", [64, 32 * 768], BF16)
    kblob = k.sb("kblob", [64, 2 * 32 * 128], BF16)
    vblob = k.sb("vblob", [64, 32 * 256], BF16)
    ps_mx = [k.ps(f"ps_mx{i}", [64, 512]) for i in range(2)]

    def mix_head(s, dstblob, dstbuf, colw, coloff):
        for c in range(4):
            pm, pmb = ps_mx[c % 2], B_(f"ps_mx{c % 2}")
            k.p((lambda s, c, pm: lambda e: e.matmul(
                pm[:, :], r2all[:, s * 64:(s + 1) * 64],
                kbm[:, :].rearrange("p (t s2 e) -> p t s2 e", t=32, s2=SH)[
                    :, c * 8:(c + 1) * 8, s:s + 1, :],
                start=True, stop=True))(s, c, pm),
                reads=[B_("kbm"), B_("r2all")], writes=[pmb])
            k.v((lambda c, pm, dstblob, colw, coloff: lambda e: e.tensor_copy(
                dstblob[:, :].rearrange("p (t c2) -> p t c2", c2=colw)[
                    :, c * 8:(c + 1) * 8, coloff:coloff + 64],
                pm[:, :].rearrange("p (t e) -> p t e", t=8)))(c, pm, dstblob, colw, coloff),
                reads=[pmb], writes=[dstbuf])

    for th in range(4):
        em.dma("qpool", (lambda th: lambda e: e.indirect_dma_start(
            kbm[:, :], None,
            ag1_sk[:, :].rearrange("(r t) c -> r (t c)", t=32),
            bass.IndirectOffsetOnAxis(kbm_idx[:, th:th + 1], 0)))(th),
            reads=[B_("kbm_idx"), B_("ag1")], writes=[B_("kbm")])
        for pr in range(6):
            blb = kblob[:, (pr % 2) * 4096:(pr % 2) * 4096 + 4096]
            blbuf = B_(f"kblob{pr % 2}")
            mix_head(2 * pr, blb, blbuf, 128, 0)
            mix_head(2 * pr + 1, blb, blbuf, 128, 64)
            em.dma("qsp", (lambda pr, th, blb: lambda e: e.dma_start(
                out=kblob_hbm[pr, :, :].rearrange("(u t) c -> u (t c)", t=128)[
                    :, th * 32 * 128:(th + 1) * 32 * 128],
                in_=blb))(pr, th, blb), reads=[blbuf], writes=[B_("kblob_hbm")])
        em.dma("qpool", (lambda th: lambda e: e.indirect_dma_start(
            kbm[:, :], None,
            ag1_sv[:, :].rearrange("(r t) c -> r (t c)", t=32),
            bass.IndirectOffsetOnAxis(kbm_idx[:, th:th + 1], 0)))(th),
            reads=[B_("kbm_idx"), B_("ag1")], writes=[B_("kbm")])
        for grp in range(3):
            for si in range(4):
                mix_head(grp * 4 + si, vblob, B_("vblob"), 256, si * 64)
            em.dma("qsp", (lambda grp, th: lambda e: e.dma_start(
                out=vblob_hbm[grp, :, :].rearrange("(u t) c -> u (t c)", t=128)[
                    :, th * 32 * 256:(th + 1) * 32 * 256],
                in_=vblob[:, :]))(grp, th), reads=[B_("vblob")], writes=[B_("vblob_hbm")])
    k.pop()

    k.push()
    lv_sb = k.sb("lv_sb", [128, 34 * 256], BF16)
    lk_sb = k.sb("lk_sb", [128, 34 * 256], BF16)
    for w in range(34):
        em.dma("qpool", (lambda w: lambda e: e.indirect_dma_start(
            lv_sb[:, w * 256:(w + 1) * 256], None, ag1_lv[:, :],
            bass.IndirectOffsetOnAxis(lkb_idx[:, w:w + 1], 0)))(w),
            reads=[B_("lkb_idx"), B_("ag1")], writes=[B_("lv_sb")])
        em.dma("qpool", (lambda w: lambda e: e.indirect_dma_start(
            lk_sb[:, w * 256:(w + 1) * 256], None, ag1_lk[:, :],
            bass.IndirectOffsetOnAxis(lkb_idx[:, w:w + 1], 0)))(w),
            reads=[B_("lkb_idx"), B_("ag1")], writes=[B_("lk_sb")])
    em.dma("qact", lambda e: e.dma_start(
        out=lkstrip_hbm[:, :].rearrange("(w t) c -> t w c", t=128),
        in_=lk_sb[:, :].rearrange("p (w c) -> p w c", w=34)),
        reads=[B_("lk_sb")], writes=[B_("lkstrip_hbm")])
    lkfm = k.sb("lkfm", [128, 2 * 4352], BF16)
    for pr2 in range(2):
        em.dma("qsp", (lambda pr2: lambda e: e.dma_start_transpose(
            lkfm[:, pr2 * 4352:(pr2 + 1) * 4352],
            lkstrip_hbm[:, pr2 * 128:(pr2 + 1) * 128]))(pr2),
            reads=[B_("lkstrip_hbm")], writes=[B_("lkfm")])

    ps_d = k.ps("ps_d", [128, 512])
    ps_t = [k.ps(f"ps_t{i}", [128, 128], BF16) for i in range(2)]
    ps_av = k.ps("ps_av", [64, 128])
    smx = k.sb("smx", [128, 512])
    scol = k.sb("scol", [128, 2])
    attnb = k.sb("attnb", [128, 512], BF16)
    attnT = k.sb("attnT", [128, 3 * 128], BF16)
    av_sb = k.sb("av_sb", [64, 128], BF16)
    vt_r = k.sb("vt_r", [128, 256], BF16)
    vt_o = k.sb("vt_o", [128, 256], BF16)

    def softmax_av(h, u, nj, vt_tiles, scale):
        nch = nj // 128
        k.v((lambda nj: lambda e: e.tensor_reduce(
            scol[:, 0:1], smx[:, 0:nj], AXX, ALU.max))(nj),
            reads=[B_("smx")], writes=[B_("scol")])
        k.v((lambda scale: lambda e: e.tensor_scalar(
            scol[:, 0:1], scol[:, 0:1], -scale, None, ALU.mult))(scale),
            reads=[B_("scol")], writes=[B_("scol")])
        k.a((lambda nj, scale: lambda e: e.activation(
            out=smx[:, 0:nj], in_=smx[:, 0:nj], func=AF.Exp,
            bias=scol[:, 0:1], scale=scale))(nj, scale),
            reads=[B_("scol"), B_("smx")], writes=[B_("smx")])
        k.v((lambda nj: lambda e: e.tensor_reduce(
            scol[:, 1:2], smx[:, 0:nj], AXX, ALU.add))(nj),
            reads=[B_("smx")], writes=[B_("scol")])
        k.v(lambda e: e.reciprocal(scol[:, 1:2], scol[:, 1:2]),
            reads=[B_("scol")], writes=[B_("scol")])
        k.v((lambda nj: lambda e: e.tensor_scalar(
            attnb[:, 0:nj], smx[:, 0:nj], scol[:, 1:2], None, ALU.mult))(nj),
            reads=[B_("scol"), B_("smx")], writes=[B_("attnb")])
        for jc in range(nch):
            pt, ptb = ps_t[jc % 2], B_(f"ps_t{jc % 2}")
            k.p((lambda jc, pt: lambda e: e.transpose(
                pt[:, :], attnb[:, jc * 128:(jc + 1) * 128], identb[:, :]))(jc, pt),
                reads=[B_("attnb"), B_("identb")], writes=[ptb])
            k.v((lambda jc, pt: lambda e: e.tensor_copy(
                attnT[:, jc * 128:(jc + 1) * 128], pt[:, :]))(jc, pt),
                reads=[ptb], writes=[B_("attnT")])
        k.p((lambda nch, vt_tiles: lambda e: [e.matmul(
            ps_av[:, :], vt_tiles[jc][0][:, vt_tiles[jc][1]:vt_tiles[jc][1] + 64],
            attnT[:, jc * 128:(jc + 1) * 128],
            start=(jc == 0), stop=(jc == nch - 1)) for jc in range(nch)][-1])(nch, vt_tiles),
            reads=[B_("attnT")] + [B_(vt_tiles[jc][2]) for jc in range(nch)],
            writes=[B_("ps_av")])
        k.v(lambda e: e.tensor_copy(av_sb[:, :], ps_av[:, :]),
            reads=[B_("ps_av")], writes=[B_("av_sb")])
        em.dma("qact", (lambda h, u: lambda e: e.dma_start(
            out=attnout_hbm[h * 64:(h + 1) * 64, u * 128:(u + 1) * 128],
            in_=av_sb[:, :]))(h, u), reads=[B_("av_sb")], writes=[B_("attnout_hbm")])

    for h in range(LH):
        hb = 64 * (h % 2)
        for u in range(NT):
            k.p((lambda h, hb, u: lambda e: e.matmul(
                ps_d[:, 0:384],
                q_fm[hb:hb + 64, (h // 2) * T + u * 128:(h // 2) * T + u * 128 + 128],
                lkfm[hb:hb + 64, (h // 2) * 4352 + u * 128:(h // 2) * 4352 + u * 128 + 384],
                start=True, stop=True))(h, hb, u),
                reads=[B_("q_fm"), B_("lkfm")], writes=[B_("ps_d")])
            if u == 0:
                k.v(lambda e: e.tensor_tensor(smx[:, 0:384], ps_d[:, 0:384],
                                              mask_first[:, :], ALU.add),
                    reads=[B_("ps_d"), B_("mask_first")], writes=[B_("smx")])
            elif u == NT - 1:
                k.v(lambda e: e.tensor_tensor(smx[:, 0:384], ps_d[:, 0:384],
                                              mask_last[:, :], ALU.add),
                    reads=[B_("ps_d"), B_("mask_last")], writes=[B_("smx")])
            else:
                k.v(lambda e: e.tensor_copy(smx[:, 0:384], ps_d[:, 0:384]),
                    reads=[B_("ps_d")], writes=[B_("smx")])
            vts = [(lv_sb, (u + jc) * 256 + h * 64, "lv_sb") for jc in range(3)]
            softmax_av(h, u, 384, vts, 1.0)

    kfm = k.sb("kfm", [128, 2 * 8192], BF16)
    for pr in range(6):
        kb = (pr % 2) * 8192
        kfb = B_(f"kfm{pr % 2}")
        em.dma("qsp", (lambda pr, kb: lambda e: e.dma_start_transpose(
            kfm[:, kb:kb + 8192], kblob_hbm[pr, :, :]))(pr, kb),
            reads=[B_("kblob_hbm")], writes=[kfb])
        grp = pr // 2
        for u in range(NT):
            em.dma("qsp", (lambda grp, u: lambda e: e.dma_start(
                out=vt_r[:, :], in_=vblob_hbm[grp, u * 128:(u + 1) * 128, :]))(grp, u),
                reads=[B_("vblob_hbm")], writes=[B_("vt_r")])
            em.dma("qsp", (lambda grp, u: lambda e: e.dma_start(
                out=vt_o[:, :], in_=vblob_hbm[grp, (32 + u) * 128:(33 + u) * 128, :]))(grp, u),
                reads=[B_("vblob_hbm")], writes=[B_("vt_o")])
            for sp in range(2):
                s = 2 * pr + sp
                h = s + 4
                hb = 64 * (h % 2)
                si = s % 4
                k.p((lambda h, hb, u, kb, sp: lambda e: [
                    e.matmul(ps_d[:, 0:128],
                             q_fm[hb:hb + 64, (h // 2) * T + u * 128:(h // 2) * T + u * 128 + 128],
                             kfm[hb:hb + 64, kb + u * 128:kb + u * 128 + 128],
                             start=True, stop=True),
                    e.matmul(ps_d[:, 128:256],
                             q_fm[hb:hb + 64, (h // 2) * T + u * 128:(h // 2) * T + u * 128 + 128],
                             kfm[hb:hb + 64, kb + (32 + u) * 128:kb + (32 + u) * 128 + 128],
                             start=True, stop=True)][-1])(h, hb, u, kb, sp),
                    reads=[B_("q_fm"), kfb], writes=[B_("ps_d")])
                k.v(lambda e: e.tensor_copy(smx[:, 0:256], ps_d[:, 0:256]),
                    reads=[B_("ps_d")], writes=[B_("smx")])
                vts = [(vt_r, si * 64, "vt_r"), (vt_o, si * 64, "vt_o")]
                softmax_av(h, u, 256, vts, float(DH) ** -0.5)
    k.pop()

    k.push()
    aout = k.sb("aout", [128, KD * 512], BF16)
    wot = k.sb("wot", [128, KD * KD * 128], BF16)
    xres = k.sb("xres", [128, KD * 512])
    x1sb = k.sb("x1sb", [128, 512])
    ps_o = [k.ps(f"ps_o{i}", [128, 512]) for i in range(2)]
    for tc in range(T // 512):
        em.dma("qsp", (lambda tc: lambda e: e.dma_start(
            out=aout[:, :].rearrange("p (kc t) -> p kc t", kc=KD),
            in_=attnout_hbm[:, tc * 512:(tc + 1) * 512].rearrange(
                "(kc p) t -> p kc t", p=128)))(tc),
            reads=[B_("attnout_hbm")], writes=[B_("aout")])
        em.dma("qact", (lambda tc: lambda e: e.dma_start(
            out=xres[:, :].rearrange("p (kc t) -> p kc t", kc=KD),
            in_=x_h[:, tc * 512:(tc + 1) * 512].rearrange("(kc p) t -> p kc t", p=128)))(tc),
            writes=[B_("xres")])
        for oc in range(KD):
            wb = oc * KD * 128
            if tc == 0:
                em.dma("qpool", (lambda oc, wb: lambda e: e.dma_start(
                    out=wot[:, wb:wb + KD * 128].rearrange("p (kc c) -> p kc c", kc=KD),
                    in_=wout_h[:, oc * 128:(oc + 1) * 128].rearrange(
                        "(kc p) c -> p kc c", p=128)))(oc, wb),
                    writes=[B_(f"wot{oc}")])
            po, pob = ps_o[oc % 2], B_(f"ps_o{oc % 2}")
            k.p((lambda oc, wb, po: lambda e: [e.matmul(
                po[:, :], wot[:, wb + kc * 128:wb + kc * 128 + 128],
                aout[:, kc * 512:(kc + 1) * 512],
                start=(kc == 0), stop=(kc == KD - 1)) for kc in range(KD)][-1])(oc, wb, po),
                reads=[B_(f"wot{oc}"), B_("aout")], writes=[pob])
            k.v((lambda oc, po: lambda e: e.scalar_tensor_tensor(
                x1sb[:, :], po[:, :], boutc[:, oc:oc + 1],
                xres[:, oc * 512:(oc + 1) * 512], ALU.add, ALU.add))(oc, po),
                reads=[pob, B_("boutc"), B_("xres")], writes=[B_("x1sb")])
            em.dma("qsp", (lambda oc, tc: lambda e: e.dma_start(
                out=x1_hbm[oc * 128:(oc + 1) * 128, tc * 512:(tc + 1) * 512],
                in_=x1sb[:, :]))(oc, tc),
                reads=[B_("x1sb")], writes=[B_("x1_hbm")])
    k.pop(barrier=False)

def build_p2(k, env):
    em, B_ = k.em, k.B
    g = env
    onescol, ones1f = g["onescol"], g["ones1f"]
    magicC = g["magicC"]
    x1_hbm = g["x1_hbm"]
    wg_h, wu_h, wd_h = g["wg_h"], g["wu_h"], g["wd_h"]
    identf = g["identf"]
    nc = k.nc
    x2_hbm = k.dram("x2_hbm", [D, T])
    wqg_hbm = k.dram("wqg_hbm", [D, INNER], BF16)
    wqu_hbm = k.dram("wqu_hbm", [D, INNER], BF16)
    wqd_hbm = k.dram("wqd_hbm", [INNER, D], BF16)
    prod_hbm = k.dram("prod_hbm", [INNER, 512])
    scr_hbm = k.dram("p2scr", [1, 1024])

    k.push()
    mrow = k.sb("mrow", [1, 8])
    mcol = k.sb("mcol", [128, 6])
    k.push()
    wload = k.sb("wload", [128, 4096])
    wqt = k.sb("wqt", [128, 4096], BF16)
    acc = k.sb("acc", [128, 3])
    accs = k.sb("accs", [128, 1])
    ps_m = k.ps("ps_m", [1, 1])
    ps_mb = k.ps("ps_mb", [128, 6])
    wabs = k.sb("wabs", [128, 4096])

    Ws = [(wg_h, wqg_hbm, D, "wg"), (wu_h, wqu_hbm, D, "wu"), (wd_h, wqd_hbm, INNER, "wd")]
    for wi, (wh, wqh, nrows, tag) in enumerate(Ws):
        nld = nrows // 128
        ncols = (D * INNER) // nrows
        for ld in range(nld):
            em.dma("qsp", (lambda wh, ld, ncols: lambda e: e.dma_start(
                out=wload[:, 0:ncols], in_=wh[ld * 128:(ld + 1) * 128, :]))(wh, ld, ncols),
                writes=[B_("wload")])
            k.a((lambda ncols: lambda e: e.activation(
                out=wabs[:, 0:ncols], in_=wload[:, 0:ncols], func=AF.Abs,
                accum_out=accs[:, 0:1]))(ncols),
                reads=[B_("wload")], writes=[B_("wabs"), B_("accs")])
            if ld == 0:
                k.v((lambda wi: lambda e: e.tensor_copy(acc[:, wi:wi + 1],
                                                        accs[:, 0:1]))(wi),
                    reads=[B_("accs")], writes=[B_("acc")])
            else:
                k.v((lambda wi: lambda e: e.tensor_tensor(
                    acc[:, wi:wi + 1], acc[:, wi:wi + 1], accs[:, 0:1], ALU.add))(wi),
                    reads=[B_("accs"), B_("acc")], writes=[B_("acc")])
        k.p((lambda wi: lambda e: e.matmul(ps_m[:, :], onescol[:, :],
                                           acc[:, wi:wi + 1], start=True, stop=True))(wi),
            reads=[B_("onescol"), B_("acc")], writes=[B_("ps_m")])
        k.v((lambda wi: lambda e: e.tensor_scalar(
            mrow[0:1, wi:wi + 1], ps_m[:, :], 1.0 / (D * INNER), None, ALU.mult))(wi),
            reads=[B_("ps_m")], writes=[B_("mrow")])
        k.v((lambda wi: lambda e: e.tensor_scalar(
            mrow[0:1, wi:wi + 1], mrow[0:1, wi:wi + 1], 1e-5, None, ALU.max))(wi),
            reads=[B_("mrow")], writes=[B_("mrow")])
        k.v((lambda wi: lambda e: e.reciprocal(mrow[0:1, 3 + wi:4 + wi],
                                               mrow[0:1, wi:wi + 1]))(wi),
            reads=[B_("mrow")], writes=[B_("mrow")])
    k.p(lambda e: e.matmul(ps_mb[:, :], ones1f[:, :], mrow[0:1, 0:6],
                           start=True, stop=True),
        reads=[B_("ones1f"), B_("mrow")], writes=[B_("ps_mb")])
    k.v(lambda e: e.tensor_copy(mcol[:, :], ps_mb[:, :]),
        reads=[B_("ps_mb")], writes=[B_("mcol")])
    for wi, (wh, wqh, nrows, tag) in enumerate(Ws):
        nld = nrows // 128
        ncols = (D * INNER) // nrows
        for ld in range(nld):
            em.dma("qsp", (lambda wh, ld, ncols: lambda e: e.dma_start(
                out=wload[:, 0:ncols], in_=wh[ld * 128:(ld + 1) * 128, :]))(wh, ld, ncols),
                writes=[B_("wload")])
            k.v((lambda wi, ncols: lambda e: e.tensor_scalar(
                wabs[:, 0:ncols], wload[:, 0:ncols], mcol[:, 3 + wi:4 + wi],
                None, ALU.mult))(wi, ncols),
                reads=[B_("wload"), B_("mcol")], writes=[B_("wabs")])
            k.v((lambda ncols: lambda e: e.tensor_scalar(
                wabs[:, 0:ncols], wabs[:, 0:ncols], magicC[:, 0:1], magicC[:, 0:1],
                ALU.add, ALU.subtract))(ncols),
                reads=[B_("wabs"), B_("magicC")], writes=[B_("wabs")])
            k.v((lambda ncols: lambda e: e.tensor_scalar(
                wqt[:, 0:ncols], wabs[:, 0:ncols], 1.0, -1.0, ALU.min, ALU.max))(ncols),
                reads=[B_("wabs")], writes=[B_("wqt")])
            em.dma("qsp", (lambda wqh, ld, ncols: lambda e: e.dma_start(
                out=wqh[ld * 128:(ld + 1) * 128, :], in_=wqt[:, 0:ncols]))(wqh, ld, ncols),
                reads=[B_("wqt")], writes=[B_(tag + "q")])
    k.pop()

    k.push()
    x1c = k.sb("x1c", [128, KD * 512])
    xnq = k.sb("xnq", [128, KD * 512])
    xq_b = k.sb("xq_b", [128, KD * 512], BF16)
    stat2 = k.sb("stat2", [1, 4 * 512])
    mbc2 = k.sb("mbc2", [128, 512])
    rbc2 = k.sb("rbc2", [128, 512])
    amax = k.sb("amax", [128, 512])
    amt = k.sb("amt", [128, 512])
    mx4 = k.sb("mx4", [128, 4])
    row4 = k.sb("row4", [4, 128])
    qrow = k.sb("qrow", [1, 3 * 512])
    scbc = k.sb("scbc", [128, 512])
    rsbc = k.sb("rsbc", [128, 512])
    wslab = k.sb("wslab", [128, 2 * 8 * 1024], BF16)
    gout = k.sb("gout", [128, 512])
    uout = k.sb("uout", [128, 512])
    prodc = k.sb("prodc", [128, 512])
    pq_sb = k.sb("pq_sb", [128, 32 * 512], BF16)
    wdsl = k.sb("wdsl", [128, 2 * 1024], BF16)
    x2sb = k.sb("x2sb", [128, 512])
    ps_s1 = k.ps("p2ps_s1", [1, 512])
    ps_bc = k.ps("p2ps_bc", [128, 512])
    ps_tr = k.ps("p2ps_tr", [128, 128])
    ps_g = k.ps("p2ps_g", [128, 512])
    ps_u = k.ps("p2ps_u", [128, 512])
    ps_dn = [k.ps(f"p2ps_dn{i}", [128, 512]) for i in range(2)]

    def pmax_to_rows(src_sb, srcbuf, n, rowdst, rowbuf, scr_off):
        nb = n // 128
        for blk in range(nb):
            k.p((lambda blk: lambda e: e.transpose(
                ps_tr[:, :], src_sb[:, blk * 128:(blk + 1) * 128], identf[:, :]))(blk),
                reads=[srcbuf, B_("identf")], writes=[B_("p2ps_tr")])
            k.v((lambda blk: lambda e: e.tensor_reduce(
                mx4[:, blk:blk + 1], ps_tr[:, :], AXX, ALU.max))(blk),
                reads=[B_("p2ps_tr")], writes=[B_("mx4")])
        k.p((lambda nb: lambda e: e.transpose(
            ps_tr[0:nb, 0:128], mx4[:, 0:nb], identf[:, :]))(nb),
            reads=[B_("mx4"), B_("identf")], writes=[B_("p2ps_tr")])
        k.v((lambda nb: lambda e: e.tensor_copy(row4[0:nb, :], ps_tr[0:nb, 0:128]))(nb),
            reads=[B_("p2ps_tr")], writes=[B_("row4")])
        em.dma("qact", (lambda nb, scr_off: lambda e: e.dma_start(
            out=scr_hbm[0:1, scr_off:scr_off + nb * 128].rearrange(
                "o (b c) -> (o b) c", b=nb),
            in_=row4[0:nb, :]))(nb, scr_off),
            reads=[B_("row4")], writes=[B_("p2scr")])
        em.dma("qact", (lambda n, scr_off: lambda e: e.dma_start(
            out=rowdst, in_=scr_hbm[0:1, scr_off:scr_off + n]))(n, scr_off),
            reads=[B_("p2scr")], writes=[rowbuf])

    for tc in range(T // 512):
        em.dma("qsp", (lambda tc: lambda e: e.dma_start(
            out=x1c[:, :].rearrange("p (kc t) -> p kc t", kc=KD),
            in_=x1_hbm[:, tc * 512:(tc + 1) * 512].rearrange("(kc p) t -> p kc t", p=128)))(tc),
            reads=[B_("x1_hbm")], writes=[B_("x1c")])
        k.p(lambda e: [e.matmul(
            ps_s1[:, :], onescol[:, :], x1c[:, kc * 512:(kc + 1) * 512],
            start=(kc == 0), stop=(kc == KD - 1)) for kc in range(KD)][-1],
            reads=[B_("onescol"), B_("x1c")], writes=[B_("p2ps_s1")])
        sm = stat2[0:1, 0:512]
        sv = stat2[0:1, 512:1024]
        sr = stat2[0:1, 1024:1536]
        sc2 = stat2[0:1, 1536:2048]
        k.v(lambda e: e.tensor_scalar(sm, ps_s1[:, :], 1.0 / D, None, ALU.mult),
            reads=[B_("p2ps_s1")], writes=[B_("stat2")])
        k.a(lambda e: e.activation(out=xnq[:, :], in_=x1c[:, :], func=AF.Square),
            reads=[B_("x1c")], writes=[B_("xnq")])
        k.p(lambda e: [e.matmul(
            ps_s1[:, :], onescol[:, :], xnq[:, kc * 512:(kc + 1) * 512],
            start=(kc == 0), stop=(kc == KD - 1)) for kc in range(KD)][-1],
            reads=[B_("onescol"), B_("xnq")], writes=[B_("p2ps_s1")])
        k.v(lambda e: e.tensor_scalar(sv, ps_s1[:, :], 1.0 / D, None, ALU.mult),
            reads=[B_("p2ps_s1")], writes=[B_("stat2")])
        k.v(lambda e: e.tensor_tensor(sc2, sm, sm, ALU.mult),
            reads=[B_("stat2")], writes=[B_("stat2")])
        k.v(lambda e: e.tensor_tensor(sv, sv, sc2, ALU.subtract),
            reads=[B_("stat2")], writes=[B_("stat2")])
        k.v(lambda e: e.tensor_scalar(sv, sv, 1e-5, None, ALU.add),
            reads=[B_("stat2")], writes=[B_("stat2")])
        rsqrt_rows(k, sr, sv, sc2, (B_("stat2"), B_("stat2"), B_("stat2")))
        k.p(lambda e: e.matmul(ps_bc[:, :], ones1f[:, :], sm, start=True, stop=True),
            reads=[B_("ones1f"), B_("stat2")], writes=[B_("p2ps_bc")])
        k.v(lambda e: e.tensor_copy(mbc2[:, :], ps_bc[:, :]),
            reads=[B_("p2ps_bc")], writes=[B_("mbc2")])
        k.p(lambda e: e.matmul(ps_bc[:, :], ones1f[:, :], sr, start=True, stop=True),
            reads=[B_("ones1f"), B_("stat2")], writes=[B_("p2ps_bc")])
        k.v(lambda e: e.tensor_copy(rbc2[:, :], ps_bc[:, :]),
            reads=[B_("p2ps_bc")], writes=[B_("rbc2")])
        for kc in range(KD):
            k.v((lambda kc: lambda e: e.tensor_tensor(
                xnq[:, kc * 512:(kc + 1) * 512], x1c[:, kc * 512:(kc + 1) * 512],
                mbc2[:, :], ALU.subtract))(kc),
                reads=[B_("x1c"), B_("mbc2")], writes=[B_("xnq")])
            k.v((lambda kc: lambda e: e.tensor_tensor(
                xnq[:, kc * 512:(kc + 1) * 512], xnq[:, kc * 512:(kc + 1) * 512],
                rbc2[:, :], ALU.mult))(kc),
                reads=[B_("xnq"), B_("rbc2")], writes=[B_("xnq")])
        k.a(lambda e: e.activation(out=amax[:, :], in_=xnq[:, 0:512],
                                   func=AF.Abs),
            reads=[B_("xnq")], writes=[B_("amax")])
        for kc in range(1, KD):
            k.a((lambda kc: lambda e: e.activation(
                out=amt[:, :], in_=xnq[:, kc * 512:(kc + 1) * 512],
                func=AF.Abs))(kc),
                reads=[B_("xnq")], writes=[B_("amt")])
            k.v(lambda e: e.tensor_tensor(amax[:, :], amax[:, :], amt[:, :],
                                          ALU.max),
                reads=[B_("amt"), B_("amax")], writes=[B_("amax")])
        pmax_to_rows(amax, B_("amax"), 512, qrow[0:1, 0:512], B_("qrow"), 0)
        q_m = qrow[0:1, 0:512]
        q_rs = qrow[0:1, 512:1024]
        q_sc = qrow[0:1, 1024:1536]
        k.v(lambda e: e.tensor_scalar(q_m, q_m, 1e-5, None, ALU.max),
            reads=[B_("qrow")], writes=[B_("qrow")])
        k.v(lambda e: e.reciprocal(q_sc, q_m), reads=[B_("qrow")], writes=[B_("qrow")])
        k.v(lambda e: e.tensor_scalar(q_sc, q_sc, 127.0, None, ALU.mult),
            reads=[B_("qrow")], writes=[B_("qrow")])
        k.v(lambda e: e.tensor_scalar(q_rs, q_m, 1.0 / 127.0, None, ALU.mult),
            reads=[B_("qrow")], writes=[B_("qrow")])
        k.p(lambda e: e.matmul(ps_bc[:, :], ones1f[:, :], q_sc, start=True, stop=True),
            reads=[B_("ones1f"), B_("qrow")], writes=[B_("p2ps_bc")])
        k.v(lambda e: e.tensor_copy(scbc[:, :], ps_bc[:, :]),
            reads=[B_("p2ps_bc")], writes=[B_("scbc")])
        k.p(lambda e: e.matmul(ps_bc[:, :], ones1f[:, :], q_rs, start=True, stop=True),
            reads=[B_("ones1f"), B_("qrow")], writes=[B_("p2ps_bc")])
        k.v(lambda e: e.tensor_copy(rsbc[:, :], ps_bc[:, :]),
            reads=[B_("p2ps_bc")], writes=[B_("rsbc")])
        for kc in range(KD):
            k.v((lambda kc: lambda e: e.tensor_tensor(
                xnq[:, kc * 512:(kc + 1) * 512], xnq[:, kc * 512:(kc + 1) * 512],
                scbc[:, :], ALU.mult))(kc),
                reads=[B_("xnq"), B_("scbc")], writes=[B_("xnq")])
        k.v(lambda e: e.tensor_scalar(xnq[:, :], xnq[:, :], magicC[:, 0:1],
                                      magicC[:, 0:1], ALU.add, ALU.subtract),
            reads=[B_("xnq"), B_("magicC")], writes=[B_("xnq")])
        k.v(lambda e: e.tensor_scalar(xq_b[:, :], xnq[:, :], 127.0, -128.0,
                                      ALU.min, ALU.max),
            reads=[B_("xnq")], writes=[B_("xq_b")])
        for icg in range(4):
            em.dma("qpool", (lambda icg: lambda e: e.dma_start(
                out=wslab[:, 0:8 * 1024].rearrange("p (kc c) -> p kc c", kc=KD),
                in_=wqg_hbm[:, icg * 1024:(icg + 1) * 1024].rearrange(
                    "(kc p) c -> p kc c", p=128)))(icg),
                reads=[B_("wgq")], writes=[B_("wslabg")])
            em.dma("qpool", (lambda icg: lambda e: e.dma_start(
                out=wslab[:, 8 * 1024:16 * 1024].rearrange("p (kc c) -> p kc c", kc=KD),
                in_=wqu_hbm[:, icg * 1024:(icg + 1) * 1024].rearrange(
                    "(kc p) c -> p kc c", p=128)))(icg),
                reads=[B_("wuq")], writes=[B_("wslabu")])
            for ici in range(8):
                ic = icg * 8 + ici
                k.p((lambda ici: lambda e: [e.matmul(
                    ps_g[:, :], wslab[:, kc * 1024 + ici * 128:kc * 1024 + ici * 128 + 128],
                    xq_b[:, kc * 512:(kc + 1) * 512],
                    start=(kc == 0), stop=(kc == KD - 1)) for kc in range(KD)][-1])(ici),
                    reads=[B_("wslabg"), B_("xq_b")], writes=[B_("p2ps_g")])
                k.p((lambda ici: lambda e: [e.matmul(
                    ps_u[:, :], wslab[:, 8192 + kc * 1024 + ici * 128:8192 + kc * 1024 + ici * 128 + 128],
                    xq_b[:, kc * 512:(kc + 1) * 512],
                    start=(kc == 0), stop=(kc == KD - 1)) for kc in range(KD)][-1])(ici),
                    reads=[B_("wslabu"), B_("xq_b")], writes=[B_("p2ps_u")])
                k.v(lambda e: e.scalar_tensor_tensor(
                    gout[:, :], ps_g[:, :], mcol[:, 0:1], rsbc[:, :],
                    ALU.mult, ALU.mult),
                    reads=[B_("p2ps_g"), B_("mcol"), B_("rsbc")], writes=[B_("gout")])
                k.a(lambda e: e.activation(out=gout[:, :], in_=gout[:, :],
                                           func=AF.Silu),
                    reads=[B_("gout")], writes=[B_("gout")])
                k.v(lambda e: e.scalar_tensor_tensor(
                    uout[:, :], ps_u[:, :], mcol[:, 1:2], rsbc[:, :],
                    ALU.mult, ALU.mult),
                    reads=[B_("p2ps_u"), B_("mcol"), B_("rsbc")], writes=[B_("uout")])
                k.v(lambda e: e.tensor_tensor(prodc[:, :], gout[:, :], uout[:, :],
                                              ALU.mult),
                    reads=[B_("gout"), B_("uout")], writes=[B_("prodc")])
                em.dma("qsp", (lambda ic: lambda e: e.dma_start(
                    out=prod_hbm[ic * 128:(ic + 1) * 128, :], in_=prodc[:, :]))(ic),
                    reads=[B_("prodc")], writes=[B_("prod_hbm")])
                if ic == 0:
                    k.a(lambda e: e.activation(out=amax[:, :], in_=prodc[:, :],
                                               func=AF.Abs),
                        reads=[B_("prodc")], writes=[B_("amax")])
                else:
                    k.a(lambda e: e.activation(out=amt[:, :], in_=prodc[:, :],
                                               func=AF.Abs),
                        reads=[B_("prodc")], writes=[B_("amt")])
                    k.v(lambda e: e.tensor_tensor(amax[:, :], amax[:, :],
                                                  amt[:, :], ALU.max),
                        reads=[B_("amt"), B_("amax")], writes=[B_("amax")])
        pmax_to_rows(amax, B_("amax"), 512, qrow[0:1, 0:512], B_("qrow"), 512)
        k.v(lambda e: e.tensor_scalar(q_m, q_m, 1e-5, None, ALU.max),
            reads=[B_("qrow")], writes=[B_("qrow")])
        k.v(lambda e: e.reciprocal(q_sc, q_m), reads=[B_("qrow")], writes=[B_("qrow")])
        k.v(lambda e: e.tensor_scalar(q_sc, q_sc, 127.0, None, ALU.mult),
            reads=[B_("qrow")], writes=[B_("qrow")])
        k.v(lambda e: e.tensor_scalar(q_rs, q_m, 1.0 / 127.0, None, ALU.mult),
            reads=[B_("qrow")], writes=[B_("qrow")])
        k.p(lambda e: e.matmul(ps_bc[:, :], ones1f[:, :], q_sc, start=True, stop=True),
            reads=[B_("ones1f"), B_("qrow")], writes=[B_("p2ps_bc")])
        k.v(lambda e: e.tensor_copy(scbc[:, :], ps_bc[:, :]),
            reads=[B_("p2ps_bc")], writes=[B_("scbc")])
        k.p(lambda e: e.matmul(ps_bc[:, :], ones1f[:, :], q_rs, start=True, stop=True),
            reads=[B_("ones1f"), B_("qrow")], writes=[B_("p2ps_bc")])
        k.v(lambda e: e.tensor_copy(rsbc[:, :], ps_bc[:, :]),
            reads=[B_("p2ps_bc")], writes=[B_("rsbc")])
        for ic in range(32):
            em.dma("qsp", (lambda ic: lambda e: e.dma_start(
                out=prodc[:, :], in_=prod_hbm[ic * 128:(ic + 1) * 128, :]))(ic),
                reads=[B_("prod_hbm")], writes=[B_("prodc")])
            k.v(lambda e: e.tensor_tensor(prodc[:, :], prodc[:, :], scbc[:, :],
                                          ALU.mult),
                reads=[B_("prodc"), B_("scbc")], writes=[B_("prodc")])
            k.v(lambda e: e.tensor_scalar(prodc[:, :], prodc[:, :], magicC[:, 0:1],
                                          magicC[:, 0:1], ALU.add, ALU.subtract),
                reads=[B_("prodc"), B_("magicC")], writes=[B_("prodc")])
            k.v((lambda ic: lambda e: e.tensor_scalar(
                pq_sb[:, ic * 512:(ic + 1) * 512], prodc[:, :], 127.0, -128.0,
                ALU.min, ALU.max))(ic),
                reads=[B_("prodc")], writes=[B_("pq_sb")])
        for ocp in range(4):
            for ic in range(32):
                em.dma("qpool", (lambda ic: lambda e: e.dma_start(
                    out=wdsl[:, (ic % 2) * 1024:(ic % 2) * 1024 + 1024],
                    in_=wqd_hbm[ic * 128:(ic + 1) * 128, :]))(ic),
                    reads=[B_("wdq")], writes=[B_(f"wdsl{ic % 2}")])
                for oi in range(2):
                    oc = ocp * 2 + oi
                    pd, pdb = ps_dn[oi], B_(f"p2ps_dn{oi}")
                    k.p((lambda ic, oc, pd: lambda e: e.matmul(
                        pd[:, :], wdsl[:, (ic % 2) * 1024 + oc * 128:(ic % 2) * 1024 + oc * 128 + 128],
                        pq_sb[:, ic * 512:(ic + 1) * 512],
                        start=(ic == 0), stop=(ic == 31)))(ic, oc, pd),
                        reads=[B_(f"wdsl{ic % 2}"), B_("pq_sb")], writes=[pdb])
            for oi in range(2):
                oc = ocp * 2 + oi
                pd, pdb = ps_dn[oi], B_(f"p2ps_dn{oi}")
                k.v((lambda oc, pd: lambda e: e.scalar_tensor_tensor(
                    x2sb[:, :], pd[:, :], mcol[:, 2:3], rsbc[:, :],
                    ALU.mult, ALU.mult))(oc, pd),
                    reads=[pdb, B_("mcol"), B_("rsbc")], writes=[B_("x2sb")])
                k.v((lambda oc: lambda e: e.tensor_tensor(
                    x2sb[:, :], x2sb[:, :], x1c[:, oc * 512:(oc + 1) * 512],
                    ALU.add))(oc),
                    reads=[B_("x2sb"), B_("x1c")], writes=[B_("x2sb")])
                em.dma("qsp", (lambda oc, tc: lambda e: e.dma_start(
                    out=x2_hbm[oc * 128:(oc + 1) * 128, tc * 512:(tc + 1) * 512],
                    in_=x2sb[:, :]))(oc, tc),
                    reads=[B_("x2sb")], writes=[B_("x2_hbm")])
    k.pop()
    k.pop(barrier=False)
    env["x2_hbm"] = x2_hbm


def build_p3(k, env):
    em, B_ = k.em, k.B
    g = env
    onescol, ones1f = g["onescol"], g["ones1f"]
    x2_hbm = g["x2_hbm"]
    nc = k.nc

    def inp(name, shape, dt=F32):
        return nc.dram_tensor(name, list(shape), dt, kind="ExternalInput").ap()

    wx_h = inp("l1_wx_t", [D, D])
    wz_h = inp("l1_wz_t", [D, D])
    wp_h = inp("l1_wp_t", [D, 96])
    wdt_h = inp("l1_wdt_sl", [DTR, 512])
    wo_h = inp("l1_wo_t", [D, D])
    alog_h = inp("l1_alog_sl", [128, 4 * NST])
    bdt_h = inp("l1_bdt_sl", [128, 4])
    dp_h = inp("l1_dp_sl", [128, 4])
    convw_h = inp("l1_convw_sl", [128, 16])
    xpg_idx_h = inp("xpg_idx", [128, 8], I32)
    z_idx_h = inp("z_idx", [128, 64], I32)
    dtun_idx_h = inp("dtun_idx", [64, 2], I32)
    bc_idx_h = inp("bc_idx", [32, 2], I32)
    y_idx_h = inp("y_idx", [128, 64], I32)
    bchot_h = inp("bchot", [32, 32 * 128], BF16)

    ag2_in = k.dram("ag2_in", [1120, T])
    ag2_out = k.dram("ag2_out", [N_CORES * 1120, T])
    ag2z_in = k.dram("ag2z_in", [1024, T], BF16)
    ag2z_out = k.dram("ag2z_out", [N_CORES * 1024, T], BF16)
    ag3_in = k.dram("ag3_in", [512, TFULL], BF16)
    ag3_out = k.dram("ag3_out", [N_CORES * 512, TFULL], BF16)
    xp_loc = k.dram("xp_loc", [512, TFULL + 8])
    x3_hbm = k.dram("x3_hbm", [D, T])
    env["x3_hbm"] = x3_hbm

    k.push()
    x2c = k.sb("p3x2c", [128, KD * 512])
    xnb = k.sb("p3xn", [128, KD * 512], BF16)
    xsq3 = k.sb("p3xsq", [128, KD * 512])
    stat3 = k.sb("p3stat", [1, 4 * 512])
    mbc3 = k.sb("p3mbc", [128, 512])
    rbc3 = k.sb("p3rbc", [128, 512])
    wx_sb = k.sb("wx_sb", [128, KD * D], BF16)
    wz_sb = k.sb("wz_sb", [128, KD * D], BF16)
    wp_sb = k.sb("wp_sb", [128, KD * 96], BF16)
    prj = k.sb("prj", [128, 512])
    prjz = k.sb("prjz", [128, 512], BF16)
    ps3_s = k.ps("p3ps_s", [1, 512])
    ps3_b = k.ps("p3ps_b", [128, 512])
    ps3_m = [k.ps(f"p3ps_m{i}", [128, 512]) for i in range(2)]
    ps3_p = k.ps("p3ps_p", [96, 512])

    for kc in range(KD):
        em.dma("qpool", (lambda kc: lambda e: e.dma_start(
            out=wx_sb[:, kc * D:(kc + 1) * D], in_=wx_h[kc * 128:(kc + 1) * 128, :]))(kc),
            writes=[B_("wx_sb")])
        em.dma("qpool", (lambda kc: lambda e: e.dma_start(
            out=wz_sb[:, kc * D:(kc + 1) * D], in_=wz_h[kc * 128:(kc + 1) * 128, :]))(kc),
            writes=[B_("wz_sb")])
        em.dma("qpool", (lambda kc: lambda e: e.dma_start(
            out=wp_sb[:, kc * 96:(kc + 1) * 96], in_=wp_h[kc * 128:(kc + 1) * 128, :]))(kc),
            writes=[B_("wp_sb")])

    for tc in range(T // 512):
        em.dma("qsp", (lambda tc: lambda e: e.dma_start(
            out=x2c[:, :].rearrange("p (kc t) -> p kc t", kc=KD),
            in_=x2_hbm[:, tc * 512:(tc + 1) * 512].rearrange("(kc p) t -> p kc t", p=128)))(tc),
            reads=[B_("x2_hbm")], writes=[B_("p3x2c")])
        k.p(lambda e: [e.matmul(
            ps3_s[:, :], onescol[:, :], x2c[:, kc * 512:(kc + 1) * 512],
            start=(kc == 0), stop=(kc == KD - 1)) for kc in range(KD)][-1],
            reads=[B_("onescol"), B_("p3x2c")], writes=[B_("p3ps_s")])
        sm = stat3[0:1, 0:512]
        sv = stat3[0:1, 512:1024]
        sr = stat3[0:1, 1024:1536]
        s4 = stat3[0:1, 1536:2048]
        k.v(lambda e: e.tensor_scalar(sm, ps3_s[:, :], 1.0 / D, None, ALU.mult),
            reads=[B_("p3ps_s")], writes=[B_("p3stat")])
        k.a(lambda e: e.activation(out=xsq3[:, :], in_=x2c[:, :], func=AF.Square),
            reads=[B_("p3x2c")], writes=[B_("p3xsq")])
        k.p(lambda e: [e.matmul(
            ps3_s[:, :], onescol[:, :], xsq3[:, kc * 512:(kc + 1) * 512],
            start=(kc == 0), stop=(kc == KD - 1)) for kc in range(KD)][-1],
            reads=[B_("onescol"), B_("p3xsq")], writes=[B_("p3ps_s")])
        k.v(lambda e: e.tensor_scalar(sv, ps3_s[:, :], 1.0 / D, None, ALU.mult),
            reads=[B_("p3ps_s")], writes=[B_("p3stat")])
        k.v(lambda e: e.tensor_tensor(s4, sm, sm, ALU.mult),
            reads=[B_("p3stat")], writes=[B_("p3stat")])
        k.v(lambda e: e.tensor_tensor(sv, sv, s4, ALU.subtract),
            reads=[B_("p3stat")], writes=[B_("p3stat")])
        k.v(lambda e: e.tensor_scalar(sv, sv, 1e-5, None, ALU.add),
            reads=[B_("p3stat")], writes=[B_("p3stat")])
        rsqrt_rows(k, sr, sv, s4, (B_("p3stat"), B_("p3stat"), B_("p3stat")))
        k.p(lambda e: e.matmul(ps3_b[:, :], ones1f[:, :], sm, start=True, stop=True),
            reads=[B_("ones1f"), B_("p3stat")], writes=[B_("p3ps_b")])
        k.v(lambda e: e.tensor_copy(mbc3[:, :], ps3_b[:, :]),
            reads=[B_("p3ps_b")], writes=[B_("p3mbc")])
        k.p(lambda e: e.matmul(ps3_b[:, :], ones1f[:, :], sr, start=True, stop=True),
            reads=[B_("ones1f"), B_("p3stat")], writes=[B_("p3ps_b")])
        k.v(lambda e: e.tensor_copy(rbc3[:, :], ps3_b[:, :]),
            reads=[B_("p3ps_b")], writes=[B_("p3rbc")])
        for kc in range(KD):
            k.v((lambda kc: lambda e: e.tensor_tensor(
                xsq3[:, kc * 512:(kc + 1) * 512], x2c[:, kc * 512:(kc + 1) * 512],
                mbc3[:, :], ALU.subtract))(kc),
                reads=[B_("p3x2c"), B_("p3mbc")], writes=[B_("p3xsq")])
            k.v((lambda kc: lambda e: e.tensor_tensor(
                xnb[:, kc * 512:(kc + 1) * 512], xsq3[:, kc * 512:(kc + 1) * 512],
                rbc3[:, :], ALU.mult))(kc),
                reads=[B_("p3xsq"), B_("p3rbc")], writes=[B_("p3xn")])
        for wi, (w_sb, wbuf) in enumerate([(wx_sb, "wx_sb"), (wz_sb, "wz_sb")]):
            for oc in range(KD):
                pm, pmb = ps3_m[oc % 2], B_(f"p3ps_m{oc % 2}")
                k.p((lambda oc, w_sb, pm: lambda e: [e.matmul(
                    pm[:, :], w_sb[:, (kc * KD + oc) * 128:(kc * KD + oc) * 128 + 128],
                    xnb[:, kc * 512:(kc + 1) * 512],
                    start=(kc == 0), stop=(kc == KD - 1)) for kc in range(KD)][-1])(oc, w_sb, pm),
                    reads=[B_(wbuf), B_("p3xn")], writes=[pmb])
                if wi == 0:
                    k.v((lambda pm: lambda e: e.tensor_copy(prj[:, :], pm[:, :]))(pm),
                        reads=[pmb], writes=[B_("prj")])
                    em.dma("qsp", (lambda oc, tc: lambda e: e.dma_start(
                        out=ag2_in[oc * 128:oc * 128 + 128, tc * 512:(tc + 1) * 512],
                        in_=prj[:, :]))(oc, tc),
                        reads=[B_("prj")], writes=[B_("ag2_in")])
                else:
                    k.v((lambda pm: lambda e: e.tensor_copy(prjz[:, :], pm[:, :]))(pm),
                        reads=[pmb], writes=[B_("prjz")])
                    em.dma("qsp", (lambda oc, tc: lambda e: e.dma_start(
                        out=ag2z_in[oc * 128:oc * 128 + 128, tc * 512:(tc + 1) * 512],
                        in_=prjz[:, :]))(oc, tc),
                        reads=[B_("prjz")], writes=[B_("ag2z_in")])
        k.p(lambda e: [e.matmul(
            ps3_p[:, :], wp_sb[:, kc * 96:(kc + 1) * 96],
            xnb[:, kc * 512:(kc + 1) * 512],
            start=(kc == 0), stop=(kc == KD - 1)) for kc in range(KD)][-1],
            reads=[B_("wp_sb"), B_("p3xn")], writes=[B_("p3ps_p")])
        k.v(lambda e: e.tensor_copy(prj[0:96, :], ps3_p[:, :]),
            reads=[B_("p3ps_p")], writes=[B_("prj")])
        em.dma("qsp", (lambda tc: lambda e: e.dma_start(
            out=ag2_in[1024:1120, tc * 512:(tc + 1) * 512], in_=prj[0:96, :]))(tc),
            reads=[B_("prj")], writes=[B_("ag2_in")])
    k.pop()

    k.g(lambda e: e.collective_compute(
        "AllGather", ALU.bypass, replica_groups=[list(range(N_CORES))],
        ins=[ag2_in[:, :]], outs=[ag2_out[:, :]]),
        reads=[B_("ag2_in")], writes=[B_("ag2")])
    k.g(lambda e: e.collective_compute(
        "AllGather", ALU.bypass, replica_groups=[list(range(N_CORES))],
        ins=[ag2z_in[:, :]], outs=[ag2z_out[:, :]]),
        reads=[B_("ag2z_in")], writes=[B_("ag2")])

    k.push()
    xpg_idx = k.sb("xpg_idx_s", [128, 8], I32)
    z_idx = k.sb("z_idx_s", [128, 64], I32)
    dtun_idx = k.sb("dtun_idx_s", [64, 2], I32)
    bc_idx = k.sb("bc_idx_s", [32, 2], I32)
    alog = k.sb("alog_s", [128, 4 * NST])
    acol = k.sb("acol", [128, 4 * NST])
    invA = k.sb("invA", [128, 4 * NST])
    bdtc = k.sb("bdtc_s", [128, 4])
    dpc = k.sb("dpc_s", [128, 4])
    cwc = k.sb("cwc_s", [128, 16])
    wdt_sb = k.sb("wdt_sb", [64, 512], BF16)
    dtun_sb = k.sb("dtun_sb", [64, TFULL], BF16)
    bc_f = k.sb("bc_f", [32, TFULL])
    bc_sb = k.sb("bc_sb", [32, TFULL], BF16)
    bchot = k.sb("bchot_s", [32, 32 * 128], BF16)
    gath = k.sb("gath", [128, T])
    for dst, src, nm in [(xpg_idx, xpg_idx_h, "xpg_idx"), (z_idx, z_idx_h, "z_idx"),
                         (dtun_idx, dtun_idx_h, "dtun_idx"), (bc_idx, bc_idx_h, "bc_idx"),
                         (alog, alog_h, "alog"), (bdtc, bdt_h, "bdtc"),
                         (dpc, dp_h, "dpc"), (cwc, convw_h, "cwc"),
                         (bchot, bchot_h, "bchot")]:
        em.dma("qsp", (lambda d, s: lambda e: e.dma_start(out=d[:], in_=s[:, :]))(dst, src),
               writes=[B_(nm)])
    em.dma("qpool", lambda e: e.dma_start(out=wdt_sb[:, :], in_=wdt_h[:, :]),
           writes=[B_("wdt_sb")])
    k.a(lambda e: e.activation(out=acol[:, :], in_=alog[:, :], func=AF.Exp),
        reads=[B_("alog")], writes=[B_("acol")])
    k.v(lambda e: e.tensor_scalar(acol[:, :], acol[:, :], -1.0, None, ALU.mult),
        reads=[B_("acol")], writes=[B_("acol")])
    k.v(lambda e: e.tensor_scalar(invA[:, :], acol[:, :], 1e-8, None, ALU.add),
        reads=[B_("acol")], writes=[B_("invA")])
    k.v(lambda e: e.reciprocal(invA[:, :], invA[:, :]),
        reads=[B_("invA")], writes=[B_("invA")])
    for r01 in range(2):
        em.dma("qpool", (lambda r01: lambda e: e.indirect_dma_start(
            dtun_sb[:, r01 * T:(r01 + 1) * T], None, ag2_out[:, :],
            bass.IndirectOffsetOnAxis(dtun_idx[:, r01:r01 + 1], 0)))(r01),
            reads=[B_("dtun_idx"), B_("ag2")], writes=[B_("dtun_sb")])
        em.dma("qpool", (lambda r01: lambda e: e.indirect_dma_start(
            bc_f[:, r01 * T:(r01 + 1) * T], None, ag2_out[:, :],
            bass.IndirectOffsetOnAxis(bc_idx[:, r01:r01 + 1], 0)))(r01),
            reads=[B_("bc_idx"), B_("ag2")], writes=[B_("bc_f")])
        k.v((lambda r01: lambda e: e.tensor_copy(
            bc_sb[:, r01 * T:(r01 + 1) * T], bc_f[:, r01 * T:(r01 + 1) * T]))(r01),
            reads=[B_("bc_f")], writes=[B_("bc_sb")])
        for dc in range(4):
            em.dma("qpool", (lambda r01, dc: lambda e: e.indirect_dma_start(
                gath[:, :], None, ag2_out[:, :],
                bass.IndirectOffsetOnAxis(xpg_idx[:, dc * 2 + r01:dc * 2 + r01 + 1], 0)))(r01, dc),
                reads=[B_("xpg_idx"), B_("ag2")], writes=[B_("gath")])
            em.dma("qsp", (lambda r01, dc: lambda e: e.dma_start(
                out=xp_loc[dc * 128:(dc + 1) * 128, 8 + r01 * T:8 + (r01 + 1) * T],
                in_=gath[:, :]))(r01, dc),
                reads=[B_("gath")], writes=[B_("xp_loc")])
    zpad = k.sb("zpad", [128, 8])
    k.v(lambda e: e.memset(zpad[:, :], 0.0), writes=[B_("zpad")])
    em.dma("qsp", lambda e: e.dma_start(out=xp_loc[0:128, 0:8], in_=zpad[:, :]),
           reads=[B_("zpad")], writes=[B_("xp_loc")])
    em.dma("qsp", lambda e: e.dma_start(out=xp_loc[128:256, 0:8], in_=zpad[:, :]),
           reads=[B_("zpad")], writes=[B_("xp_loc")])
    em.dma("qsp", lambda e: e.dma_start(out=xp_loc[256:384, 0:8], in_=zpad[:, :]),
           reads=[B_("zpad")], writes=[B_("xp_loc")])
    em.dma("qsp", lambda e: e.dma_start(out=xp_loc[384:512, 0:8], in_=zpad[:, :]),
           reads=[B_("zpad")], writes=[B_("xp_loc")])

    xps = k.sb("xps", [128, 2 * 520])
    zs = k.sb("zs", [128, 2 * 512], BF16)
    dtf = k.sb("dtf", [128, 512])
    sp0 = k.sb("sp0", [128, 512])
    sp1 = k.sb("sp1", [128, 512])
    uf = k.sb("uf", [128, 512])
    a_f = k.sb("a_f", [128, 512])
    bu2 = k.sb("bu2", [128, 512])
    inpt = k.sb("inpt", [128, 512])
    hstr = k.sb("hstr", [128, 512])
    yacc = k.sb("yacc", [128, 512])
    ytmp = k.sb("ytmp", [128, 512])
    hcarry = k.sb("hcarry", [128, 4 * NST])
    bbc = k.sb("bbc", [128, NST * 512], BF16)
    cbc = k.sb("cbc", [128, NST * 512], BF16)
    ygb = k.sb("ygb", [128, 512], BF16)
    ps3_d = k.ps("p3ps_d", [128, 512])
    ps3_bb = k.ps("p3ps_bb", [128, 512])

    for tch in range(TFULL // 512):
        for n in range(NST):
            k.p((lambda n, tch: lambda e: e.matmul(
                ps3_bb[:, :], bchot[:, n * 128:(n + 1) * 128],
                bc_sb[:, tch * 512:(tch + 1) * 512],
                start=True, stop=True))(n, tch),
                reads=[B_("bchot"), B_("bc_sb")], writes=[B_("p3ps_bb")])
            k.v((lambda n: lambda e: e.tensor_copy(
                bbc[:, n * 512:(n + 1) * 512], ps3_bb[:, :]))(n),
                reads=[B_("p3ps_bb")], writes=[B_("bbc")])
            k.p((lambda n, tch: lambda e: e.matmul(
                ps3_bb[:, :], bchot[:, (NST + n) * 128:(NST + n + 1) * 128],
                bc_sb[:, tch * 512:(tch + 1) * 512],
                start=True, stop=True))(n, tch),
                reads=[B_("bchot"), B_("bc_sb")], writes=[B_("p3ps_bb")])
            k.v((lambda n: lambda e: e.tensor_copy(
                cbc[:, n * 512:(n + 1) * 512], ps3_bb[:, :]))(n),
                reads=[B_("p3ps_bb")], writes=[B_("cbc")])
        for dc in range(4):
            k.p((lambda dc, tch: lambda e: e.matmul(
                ps3_d[:, :], wdt_sb[:, dc * 128:(dc + 1) * 128],
                dtun_sb[:, tch * 512:(tch + 1) * 512], start=True, stop=True))(dc, tch),
                reads=[B_("wdt_sb"), B_("dtun_sb")], writes=[B_("p3ps_d")])
            k.v((lambda dc: lambda e: e.tensor_scalar(
                dtf[:, :], ps3_d[:, :], bdtc[:, dc:dc + 1], None, ALU.add))(dc),
                reads=[B_("p3ps_d"), B_("bdtc")], writes=[B_("dtf")])
            k.a(lambda e: e.activation(out=sp0[:, :], in_=dtf[:, :],
                                       func=AF.Abs),
                reads=[B_("dtf")], writes=[B_("sp0")])
            k.a(lambda e: e.activation(out=sp1[:, :], in_=sp0[:, :], func=AF.Exp,
                                       scale=-1.0),
                reads=[B_("sp0")], writes=[B_("sp1")])
            k.a(lambda e: e.activation(out=sp1[:, :], in_=sp1[:, :], func=AF.Ln,
                                       bias=1.0),
                reads=[B_("sp1")], writes=[B_("sp1")])
            k.v(lambda e: e.tensor_scalar(sp0[:, :], dtf[:, :], 0.0, None, ALU.max),
                reads=[B_("dtf")], writes=[B_("sp0")])
            k.v(lambda e: e.tensor_tensor(dtf[:, :], sp0[:, :], sp1[:, :], ALU.add),
                reads=[B_("sp0"), B_("sp1")], writes=[B_("dtf")])
            em.dma("qsp", (lambda dc, tch: lambda e: e.dma_start(
                out=xps[:, (tch % 2) * 520:(tch % 2) * 520 + 515],
                in_=xp_loc[dc * 128:(dc + 1) * 128, 5 + tch * 512:5 + tch * 512 + 515]))(dc, tch),
                reads=[B_("xp_loc")], writes=[B_(f"xps{tch % 2}")])
            xb0 = (tch % 2) * 520
            k.v((lambda dc, xb0: lambda e: e.tensor_scalar(
                uf[:, :], xps[:, xb0:xb0 + 512], cwc[:, dc * 4:dc * 4 + 1],
                None, ALU.mult))(dc, xb0),
                reads=[B_(f"xps{tch % 2}"), B_("cwc")], writes=[B_("uf")])
            for j in range(1, 4):
                k.v((lambda dc, j, xb0: lambda e: e.scalar_tensor_tensor(
                    uf[:, :], xps[:, xb0 + j:xb0 + j + 512],
                    cwc[:, dc * 4 + j:dc * 4 + j + 1], uf[:, :],
                    ALU.mult, ALU.add))(dc, j, xb0),
                    reads=[B_(f"xps{tch % 2}"), B_("cwc"), B_("uf")], writes=[B_("uf")])
            k.a(lambda e: e.activation(out=uf[:, :], in_=uf[:, :], func=AF.Silu),
                reads=[B_("uf")], writes=[B_("uf")])
            if tch == 0 and dc == 0:
                k.v(lambda e: e.memset(hcarry[:, :], 0.0), writes=[B_("hcarry")])
            for n in range(NST):
                an = dc * NST + n
                k.a((lambda an: lambda e: e.activation(
                    out=a_f[:, :], in_=dtf[:, :], func=AF.Exp,
                    scale=acol[:, an:an + 1]))(an),
                    reads=[B_("dtf"), B_("acol")], writes=[B_("a_f")])
                k.v((lambda an, n: lambda e: e.scalar_tensor_tensor(
                    bu2[:, :], uf[:, :], invA[:, an:an + 1],
                    bbc[:, n * 512:(n + 1) * 512], ALU.mult, ALU.mult))(an, n),
                    reads=[B_("uf"), B_("invA"), B_("bbc")], writes=[B_("bu2")])
                k.v(lambda e: e.scalar_tensor_tensor(
                    inpt[:, :], a_f[:, :], 1.0, bu2[:, :], ALU.subtract, ALU.mult),
                    reads=[B_("a_f"), B_("bu2")], writes=[B_("inpt")])
                k.v((lambda an: lambda e: e.tensor_tensor_scan(
                    hstr[:, :], a_f[:, :], inpt[:, :], hcarry[:, an:an + 1],
                    ALU.mult, ALU.add))(an),
                    reads=[B_("a_f"), B_("inpt"), B_("hcarry")], writes=[B_("hstr")])
                k.v((lambda an: lambda e: e.tensor_copy(
                    hcarry[:, an:an + 1], hstr[:, 511:512]))(an),
                    reads=[B_("hstr")], writes=[B_("hcarry")])
                if n == 0:
                    k.v((lambda n: lambda e: e.tensor_tensor(
                        yacc[:, :], hstr[:, :], cbc[:, n * 512:(n + 1) * 512],
                        ALU.mult))(n),
                        reads=[B_("hstr"), B_("cbc")], writes=[B_("yacc")])
                else:
                    k.g((lambda n: lambda e: e.tensor_tensor(
                        ytmp[:, :], hstr[:, :], cbc[:, n * 512:(n + 1) * 512],
                        ALU.mult))(n),
                        reads=[B_("hstr"), B_("cbc")], writes=[B_("ytmp")])
                    k.g(lambda e: e.tensor_tensor(yacc[:, :], yacc[:, :],
                                                  ytmp[:, :], ALU.add),
                        reads=[B_("ytmp"), B_("yacc")], writes=[B_("yacc")])
            em.dma("qpool", (lambda dc, tch: lambda e: e.indirect_dma_start(
                zs[:, (tch % 2) * 512:(tch % 2) * 512 + 512], None,
                ag2z_out[:, :].rearrange("r (c t) -> (r c) t", t=512),
                bass.IndirectOffsetOnAxis(z_idx[:, dc * 16 + tch:dc * 16 + tch + 1], 0)))(dc, tch),
                reads=[B_("z_idx"), B_("ag2")], writes=[B_(f"zs{tch % 2}")])
            k.v((lambda dc: lambda e: e.scalar_tensor_tensor(
                ytmp[:, :], uf[:, :], dpc[:, dc:dc + 1], yacc[:, :],
                ALU.mult, ALU.add))(dc),
                reads=[B_("uf"), B_("dpc"), B_("yacc")], writes=[B_("ytmp")])
            k.a((lambda tch: lambda e: e.activation(
                out=sp0[:, :], in_=zs[:, (tch % 2) * 512:(tch % 2) * 512 + 512],
                func=AF.Silu))(tch),
                reads=[B_(f"zs{tch % 2}")], writes=[B_("sp0")])
            k.v(lambda e: e.tensor_tensor(ygb[:, :], ytmp[:, :], sp0[:, :], ALU.mult),
                reads=[B_("ytmp"), B_("sp0")], writes=[B_("ygb")])
            em.dma("qsp", (lambda dc, tch: lambda e: e.dma_start(
                out=ag3_in[dc * 128:(dc + 1) * 128, tch * 512:(tch + 1) * 512],
                in_=ygb[:, :]))(dc, tch),
                reads=[B_("ygb")], writes=[B_("ag3_in")])
    k.pop()

    k.g(lambda e: e.collective_compute(
        "AllGather", ALU.bypass, replica_groups=[list(range(N_CORES))],
        ins=[ag3_in[:, :]], outs=[ag3_out[:, :]]),
        reads=[B_("ag3_in")], writes=[B_("ag3")])

    k.push()
    y_idx = k.sb("y_idx_s", [128, 64], I32)
    em.dma("qsp", lambda e: e.dma_start(out=y_idx[:, :], in_=y_idx_h[:, :]),
           writes=[B_("y_idx")])
    wo_sb = k.sb("wo_sb", [128, KD * D], BF16)
    ysl = k.sb("ysl", [128, KD * 512], BF16)
    x2c2 = k.sb("p3x2c2", [128, KD * 512])
    x3sb = k.sb("x3sb", [128, 512])
    ps3_o = [k.ps(f"p3ps_o{i}", [128, 512]) for i in range(2)]
    for kc in range(KD):
        em.dma("qpool", (lambda kc: lambda e: e.dma_start(
            out=wo_sb[:, kc * D:(kc + 1) * D], in_=wo_h[kc * 128:(kc + 1) * 128, :]))(kc),
            writes=[B_("wo_sb")])
    for tc in range(T // 512):
        for kc in range(KD):
            em.dma("qpool", (lambda kc, tc: lambda e: e.indirect_dma_start(
                ysl[:, kc * 512:(kc + 1) * 512], None,
                ag3_out[:, :].rearrange("r (c t) -> (r c) t", t=512),
                bass.IndirectOffsetOnAxis(y_idx[:, kc * 8 + tc:kc * 8 + tc + 1], 0)))(kc, tc),
                reads=[B_("y_idx"), B_("ag3")], writes=[B_("ysl")])
        em.dma("qsp", (lambda tc: lambda e: e.dma_start(
            out=x2c2[:, :].rearrange("p (kc t) -> p kc t", kc=KD),
            in_=x2_hbm[:, tc * 512:(tc + 1) * 512].rearrange("(kc p) t -> p kc t", p=128)))(tc),
            reads=[B_("x2_hbm")], writes=[B_("p3x2c2")])
        for oc in range(KD):
            po, pob = ps3_o[oc % 2], B_(f"p3ps_o{oc % 2}")
            k.p((lambda oc, po: lambda e: [e.matmul(
                po[:, :], wo_sb[:, (kc * KD + oc) * 128:(kc * KD + oc) * 128 + 128],
                ysl[:, kc * 512:(kc + 1) * 512],
                start=(kc == 0), stop=(kc == KD - 1)) for kc in range(KD)][-1])(oc, po),
                reads=[B_("wo_sb"), B_("ysl")], writes=[pob])
            k.v((lambda oc, po: lambda e: e.tensor_tensor(
                x3sb[:, :], po[:, :], x2c2[:, oc * 512:(oc + 1) * 512], ALU.add))(oc, po),
                reads=[pob, B_("p3x2c2")], writes=[B_("x3sb")])
            em.dma("qsp", (lambda oc, tc: lambda e: e.dma_start(
                out=x3_hbm[oc * 128:(oc + 1) * 128, tc * 512:(tc + 1) * 512],
                in_=x3sb[:, :]))(oc, tc),
                reads=[B_("x3sb")], writes=[B_("x3_hbm")])
    k.pop()


def build_p4(k, env):
    em, B_ = k.em, k.B
    g = env
    onescol, ones1f = g["onescol"], g["ones1f"]
    identf, identb = g["identf"], g["identb"]
    x3_hbm = g["x3_hbm"]
    out_h = g["out_h"]
    nc = k.nc

    def inp(name, shape, dt=F32):
        return nc.dram_tensor(name, list(shape), dt, kind="ExternalInput").ap()

    rw_h = inp("l1_rw", [D, NEXP])
    rb_h = inp("l1_rb_col", [NEXP, 1])
    wfc_h = inp("l1_wfc", [NEXP, D, INNER])
    wgt_h = inp("l1_wgt", [NEXP, D, INNER])
    wcp_h = inp("l1_wcp", [NEXP, INNER, D])
    bfc_h = inp("l1_bfc", [NEXP, INNER])
    bgt_h = inp("l1_bgt", [NEXP, INNER])
    bcp_h = inp("l1_bcp", [NEXP, D])
    eiota_h = inp("eiota", [NEXP, 1])
    ebase_h = inp("ebase", [NEXP, 1])
    ebase_i_h = inp("ebase_i", [NEXP, 1], I32)
    hsel8_h = inp("hsel8", [NEXP, 1])
    cnt_idx_h = inp("cnt_idx", [2, 1], I32)
    cand_idx_h = inp("cand_idx", [NEXP, 1], I32)
    tok8_h = inp("tok8", [NEXP, T])
    trash_h = inp("trashrow", [1, 2 * T])

    ag4_in = k.dram("ag4_in", [1, 16])
    ag4_out = k.dram("ag4_out", [N_CORES, 16])
    ar5_in = k.dram("ar5_in", [32, D])
    ar5_out = k.dram("ar5_out", [32, D])
    xt3_hbm = k.dram("xt3_hbm", [T, D], BF16)
    einp_hbm = k.dram("einp_hbm", [SROWS, D], BF16)
    sout_hbm = k.dram("sout_hbm", [TRASH, D])
    acth_hbm = k.dram("acth_hbm", [128, 32 * ESL], BF16)
    scr_er = k.dram("scr_er", [1, 2 * T])
    scr_oh = k.dram("scr_oh", [2, NEXP, T])
    scr_cum = k.dram("scr_cum", [2, NEXP, T])
    scr_sg = k.dram("scr_sg", [1, 4 * T])
    scr_cg = k.dram("scr_cg", [1, 32])
    wfcb_hbm = k.dram("wfcb_hbm", [NEXP * D, INNER], BF16)
    wgtb_hbm = k.dram("wgtb_hbm", [NEXP * D, INNER], BF16)
    wcpb_hbm = k.dram("wcpb_hbm", [NEXP * INNER, D], BF16)

    rt_tm = k.sb("rt_tm", [128, NT * 8])
    zt4 = k.sb("p4zt", [128, D], BF16)
    ZB = [B_(f"einp_z{i}") for i in range(6)]

    pcb = [k.sb(f"p4pcb{i}", [128, 2048], BF16) for i in range(8)]
    npc = 0
    for ex in range(NEXP):
        for kc in range(KD):
            for hh in range(2):
                bi = npc % 8
                em.dma("qpool", (lambda ex, kc, hh, bi: lambda e: e.dma_start(
                    out=pcb[bi][:, :],
                    in_=wfc_h[ex, kc * 128:(kc + 1) * 128,
                              hh * 2048:(hh + 1) * 2048]))(ex, kc, hh, bi),
                    writes=[B_(f"p4pcb{bi}")])
                em.dma("qsp", (lambda ex, kc, hh, bi: lambda e: e.dma_start(
                    out=wfcb_hbm[ex * D + kc * 128:ex * D + (kc + 1) * 128,
                                 hh * 2048:(hh + 1) * 2048],
                    in_=pcb[bi][:, :]))(ex, kc, hh, bi),
                    reads=[B_(f"p4pcb{bi}")], writes=[B_("wfcb")])
                npc += 1
                bi = npc % 8
                em.dma("qpool", (lambda ex, kc, hh, bi: lambda e: e.dma_start(
                    out=pcb[bi][:, :],
                    in_=wgt_h[ex, kc * 128:(kc + 1) * 128,
                              hh * 2048:(hh + 1) * 2048]))(ex, kc, hh, bi),
                    writes=[B_(f"p4pcb{bi}")])
                em.dma("qsp", (lambda ex, kc, hh, bi: lambda e: e.dma_start(
                    out=wgtb_hbm[ex * D + kc * 128:ex * D + (kc + 1) * 128,
                                 hh * 2048:(hh + 1) * 2048],
                    in_=pcb[bi][:, :]))(ex, kc, hh, bi),
                    reads=[B_(f"p4pcb{bi}")], writes=[B_("wgtb")])
                npc += 1
        for j2 in range(16):
            bi = npc % 8
            em.dma("qpool", (lambda ex, j2, bi: lambda e: e.dma_start(
                out=pcb[bi][:, :].rearrange("p (r c) -> p r c", r=2),
                in_=wcp_h[ex, j2 * 256:(j2 + 1) * 256, :].rearrange(
                    "(r p) c -> p r c", p=128)))(ex, j2, bi),
                writes=[B_(f"p4pcb{bi}")])
            em.dma("qsp", (lambda ex, j2, bi: lambda e: e.dma_start(
                out=wcpb_hbm[ex * INNER + j2 * 256:ex * INNER + (j2 + 1) * 256,
                             :].rearrange("(r p) c -> p r c", p=128),
                in_=pcb[bi][:, :].rearrange("p (r c) -> p r c", r=2)))(ex, j2, bi),
                reads=[B_(f"p4pcb{bi}")], writes=[B_("wcpb")])
            npc += 1

    k.v(lambda e: e.memset(zt4[:, :], 0.0), writes=[B_("p4zt")])
    for i in range(SROWS // 128):
        em.dma(DMA_QUEUES[i % 3], (lambda i: lambda e: e.dma_start(
            out=einp_hbm[i * 128:(i + 1) * 128, :], in_=zt4[:, :]))(i),
            reads=[B_("p4zt")], writes=[B_(f"einp_z{i % 6}")])

    k.push()
    x3c = k.sb("p4x3c", [128, KD * 512])
    xn4 = k.sb("p4xn", [128, KD * 512], BF16)
    xsq4 = k.sb("p4xsq", [128, KD * 512])
    stat4 = k.sb("p4stat", [1, 4 * 512])
    mbc4 = k.sb("p4mbc", [128, 512])
    rbc4 = k.sb("p4rbc", [128, 512])
    rw_sb = k.sb("rw_sb", [128, KD * NEXP], BF16)
    rbcol = k.sb("rbcol", [NEXP, 1])
    logit = k.sb("logit", [NEXP, T])
    xt3t = [k.sb(f"xt3t{i}", [128, D], BF16) for i in range(2)]
    top8 = k.sb("top8", [128, 8])
    idx8 = k.sb("idx8", [128, 8], U32)
    prow = k.sb("prow", [128, 8])
    ecols = k.sb("ecols", [128, 2 * NT])
    etr = k.sb("etr", [NT, 128])
    ps4_s = k.ps("p4ps_s", [1, 512])
    ps4_b = k.ps("p4ps_b", [128, 512])
    ps4_l = k.ps("p4ps_l", [NEXP, 512])
    ps4_tb = [k.ps(f"p4ps_tb{i}", [128, 128], BF16) for i in range(2)]
    ps4_tf = k.ps("p4ps_tf", [128, 128])

    em.dma("qsp", lambda e: e.dma_start(out=rbcol[:, :], in_=rb_h[:, :]),
           writes=[B_("rbcol")])
    for kc in range(KD):
        em.dma("qpool", (lambda kc: lambda e: e.dma_start(
            out=rw_sb[:, kc * NEXP:(kc + 1) * NEXP],
            in_=rw_h[kc * 128:(kc + 1) * 128, :]))(kc), writes=[B_("rw_sb")])

    for tc in range(T // 512):
        em.dma("qsp", (lambda tc: lambda e: e.dma_start(
            out=x3c[:, :].rearrange("p (kc t) -> p kc t", kc=KD),
            in_=x3_hbm[:, tc * 512:(tc + 1) * 512].rearrange(
                "(kc p) t -> p kc t", p=128)))(tc),
            reads=[B_("x3_hbm")], writes=[B_("p4x3c")])
        k.p(lambda e: [e.matmul(
            ps4_s[:, :], onescol[:, :], x3c[:, kc * 512:(kc + 1) * 512],
            start=(kc == 0), stop=(kc == KD - 1)) for kc in range(KD)][-1],
            reads=[B_("onescol"), B_("p4x3c")], writes=[B_("p4ps_s")])
        sm = stat4[0:1, 0:512]
        sv = stat4[0:1, 512:1024]
        sr = stat4[0:1, 1024:1536]
        s4 = stat4[0:1, 1536:2048]
        k.v(lambda e: e.tensor_scalar(sm, ps4_s[:, :], 1.0 / D, None, ALU.mult),
            reads=[B_("p4ps_s")], writes=[B_("p4stat")])
        k.a(lambda e: e.activation(out=xsq4[:, :], in_=x3c[:, :], func=AF.Square),
            reads=[B_("p4x3c")], writes=[B_("p4xsq")])
        k.p(lambda e: [e.matmul(
            ps4_s[:, :], onescol[:, :], xsq4[:, kc * 512:(kc + 1) * 512],
            start=(kc == 0), stop=(kc == KD - 1)) for kc in range(KD)][-1],
            reads=[B_("onescol"), B_("p4xsq")], writes=[B_("p4ps_s")])
        k.v(lambda e: e.tensor_scalar(sv, ps4_s[:, :], 1.0 / D, None, ALU.mult),
            reads=[B_("p4ps_s")], writes=[B_("p4stat")])
        k.v(lambda e: e.tensor_tensor(s4, sm, sm, ALU.mult),
            reads=[B_("p4stat")], writes=[B_("p4stat")])
        k.v(lambda e: e.tensor_tensor(sv, sv, s4, ALU.subtract),
            reads=[B_("p4stat")], writes=[B_("p4stat")])
        k.v(lambda e: e.tensor_scalar(sv, sv, 1e-5, None, ALU.add),
            reads=[B_("p4stat")], writes=[B_("p4stat")])
        rsqrt_rows(k, sr, sv, s4, (B_("p4stat"), B_("p4stat"), B_("p4stat")))
        k.p(lambda e: e.matmul(ps4_b[:, :], ones1f[:, :], sm, start=True, stop=True),
            reads=[B_("ones1f"), B_("p4stat")], writes=[B_("p4ps_b")])
        k.v(lambda e: e.tensor_copy(mbc4[:, :], ps4_b[:, :]),
            reads=[B_("p4ps_b")], writes=[B_("p4mbc")])
        k.p(lambda e: e.matmul(ps4_b[:, :], ones1f[:, :], sr, start=True, stop=True),
            reads=[B_("ones1f"), B_("p4stat")], writes=[B_("p4ps_b")])
        k.v(lambda e: e.tensor_copy(rbc4[:, :], ps4_b[:, :]),
            reads=[B_("p4ps_b")], writes=[B_("p4rbc")])
        for kc in range(KD):
            k.v((lambda kc: lambda e: e.tensor_tensor(
                xsq4[:, kc * 512:(kc + 1) * 512], x3c[:, kc * 512:(kc + 1) * 512],
                mbc4[:, :], ALU.subtract))(kc),
                reads=[B_("p4x3c"), B_("p4mbc")], writes=[B_("p4xsq")])
            k.v((lambda kc: lambda e: e.tensor_tensor(
                xn4[:, kc * 512:(kc + 1) * 512], xsq4[:, kc * 512:(kc + 1) * 512],
                rbc4[:, :], ALU.mult))(kc),
                reads=[B_("p4xsq"), B_("p4rbc")], writes=[B_("p4xn")])
        k.p(lambda e: [e.matmul(
            ps4_l[:, :], rw_sb[:, kc * NEXP:(kc + 1) * NEXP],
            xn4[:, kc * 512:(kc + 1) * 512],
            start=(kc == 0), stop=(kc == KD - 1)) for kc in range(KD)][-1],
            reads=[B_("rw_sb"), B_("p4xn")], writes=[B_("p4ps_l")])
        k.v((lambda tc: lambda e: e.tensor_scalar(
            logit[:, tc * 512:(tc + 1) * 512], ps4_l[:, :], rbcol[:, 0:1],
            None, ALU.add))(tc),
            reads=[B_("p4ps_l"), B_("rbcol")], writes=[B_("logit")])
        for ti in range(4):
            tile = tc * 4 + ti
            xtt, xttb = xt3t[tile % 2], B_(f"xt3t{tile % 2}")
            for kc in range(KD):
                pt, ptb = ps4_tb[kc % 2], B_(f"p4ps_tb{kc % 2}")
                k.p((lambda kc, ti, pt: lambda e: e.transpose(
                    pt[:, :], xn4[:, kc * 512 + ti * 128:kc * 512 + ti * 128 + 128],
                    identb[:, :]))(kc, ti, pt),
                    reads=[B_("p4xn"), B_("identb")], writes=[ptb])
                k.v((lambda kc, pt, xtt: lambda e: e.tensor_copy(
                    xtt[:, kc * 128:(kc + 1) * 128], pt[:, :]))(kc, pt, xtt),
                    reads=[ptb], writes=[xttb])
            em.dma("qsp", (lambda tile, xtt: lambda e: e.dma_start(
                out=xt3_hbm[tile * 128:(tile + 1) * 128, :], in_=xtt[:, :]))(tile, xtt),
                reads=[xttb], writes=[B_("xt3_hbm")])

    for tile in range(NT):
        k.p((lambda tile: lambda e: e.transpose(
            ps4_tf[0:128, 0:NEXP], logit[:, tile * 128:(tile + 1) * 128],
            identf[0:NEXP, 0:NEXP]))(tile),
            reads=[B_("logit"), B_("identf")], writes=[B_("p4ps_tf")])
        k.v(lambda e: e.tensor_copy(prow[:, :], ps4_tf[0:128, 0:NEXP]),
            reads=[B_("p4ps_tf")], writes=[B_("prow")])
        k.v(lambda e: e.max(top8[:, :], prow[:, :]),
            reads=[B_("prow")], writes=[B_("top8")])
        k.v(lambda e: e.max_index(idx8[:, :], top8[:, :], prow[:, :]),
            reads=[B_("top8"), B_("prow")], writes=[B_("idx8")])
        rb_ = tile * 8
        k.v((lambda rb_: lambda e: e.tensor_copy(
            rt_tm[:, rb_:rb_ + 2], idx8[:, 0:2]))(rb_),
            reads=[B_("idx8")], writes=[B_("rt_tm")])
        k.v(lambda e: e.tensor_tensor(prow[:, 2:3], top8[:, 1:2], top8[:, 0:1],
                                      ALU.subtract),
            reads=[B_("top8")], writes=[B_("prow")])
        k.a(lambda e: e.activation(out=prow[:, 3:4], in_=prow[:, 2:3], func=AF.Exp),
            reads=[B_("prow")], writes=[B_("prow")])
        k.v(lambda e: e.tensor_scalar(prow[:, 4:5], prow[:, 3:4], 1.0, None,
                                      ALU.add),
            reads=[B_("prow")], writes=[B_("prow")])
        k.v(lambda e: e.reciprocal(prow[:, 4:5], prow[:, 4:5]),
            reads=[B_("prow")], writes=[B_("prow")])
        k.v((lambda rb_: lambda e: e.tensor_copy(
            rt_tm[:, rb_ + 2:rb_ + 3], prow[:, 4:5]))(rb_),
            reads=[B_("prow")], writes=[B_("rt_tm")])
        k.v((lambda rb_: lambda e: e.tensor_tensor(
            rt_tm[:, rb_ + 3:rb_ + 4], prow[:, 3:4], prow[:, 4:5], ALU.mult))(rb_),
            reads=[B_("prow")], writes=[B_("rt_tm")])

    for kk in range(2):
        for tile in range(NT):
            k.v((lambda kk, tile: lambda e: e.tensor_copy(
                ecols[:, kk * NT + tile:kk * NT + tile + 1],
                rt_tm[:, tile * 8 + kk:tile * 8 + kk + 1]))(kk, tile),
                reads=[B_("rt_tm")], writes=[B_("ecols")])
        k.p((lambda kk: lambda e: e.transpose(
            ps4_tf[0:NT, :], ecols[:, kk * NT:(kk + 1) * NT], identf[:, :]))(kk),
            reads=[B_("ecols"), B_("identf")], writes=[B_("p4ps_tf")])
        k.v((lambda kk: lambda e: e.tensor_copy(etr[:, :], ps4_tf[0:NT, :]))(kk),
            reads=[B_("p4ps_tf")], writes=[B_("etr")])
        em.dma("qsp", (lambda kk: lambda e: e.dma_start(
            out=scr_er[0:1, kk * T:(kk + 1) * T].rearrange(
                "o (w c) -> (o w) c", c=128),
            in_=etr[:, :]))(kk),
            reads=[B_("etr")], writes=[B_("scr_er")])
    k.pop()

    k.push()
    erc = k.sb("erc", [1, 512])
    one8c = k.sb("one8c", [NEXP, 512])
    ohc = k.sb("ohc", [NEXP, 512])
    cumc = k.sb("cumc", [NEXP, 512])
    carry = k.sb("p4carry", [NEXP, 2])
    carryK = k.sb("p4carryK", [NEXP, 2])
    eiota = k.sb("eiota_s", [NEXP, 1])
    ebase = k.sb("ebase_s", [NEXP, 1])
    hsel8 = k.sb("hsel8_s", [NEXP, 1])
    ag4row = k.sb("ag4row", [1, 16])
    cg2 = k.sb("cg2", [2, 16])
    crow = k.sb("crow", [1, 32])
    cnt_idx = k.sb("cnt_idx_s", [2, 1], I32)
    ccol = k.sb("ccol", [NEXP, 4])
    occ = k.sb("occ", [NEXP, 4])
    tokc = k.sb("tokc", [NEXP, 512])
    trc = k.sb("trc", [1, 512])
    posc = k.sb("posc", [NEXP, 512])
    kpc = k.sb("kpc", [NEXP, 512])
    ohkc = k.sb("ohkc", [NEXP, 512])
    ohk2 = k.sb("ohk2", [NEXP, 512])
    krkc = k.sb("krkc", [NEXP, 512])
    m8 = k.sb("m8", [NEXP, 512])
    ctc = k.sb("ctc", [NEXP, 4])
    rowsb = k.sb("p4rows", [1, 6 * 512])
    ps8 = k.ps("p4ps8", [NEXP, 512])
    ps1x = k.ps("p4ps1x", [1, 16])
    ps1a = k.ps("p4ps1a", [1, 512])
    ps_cc = k.ps("p4ps_cc", [32, 1])

    for dst, src, nm in [(eiota, eiota_h, "eiota_s"), (ebase, ebase_h, "ebase_s"),
                         (hsel8, hsel8_h, "hsel8_s"),
                         (cnt_idx, cnt_idx_h, "cnt_idx_s")]:
        em.dma("qsp", (lambda d, s: lambda e: e.dma_start(out=d[:, :], in_=s[:, :]))(dst, src),
               writes=[B_(nm)])
    k.v(lambda e: e.memset(one8c[:, :], 1.0), writes=[B_("one8c")])
    k.v(lambda e: e.memset(carry[:, :], 0.0), writes=[B_("p4carry")])
    k.v(lambda e: e.memset(carryK[:, :], 0.0), writes=[B_("p4carryK")])
    k.v(lambda e: e.memset(ctc[:, :], 0.0), writes=[B_("ctc")])

    for kk in range(2):
        for c8 in range(8):
            sl = slice(c8 * 512, (c8 + 1) * 512)
            em.dma("qsp", (lambda kk, sl: lambda e: e.dma_start(
                out=erc[:, :], in_=scr_er[0:1, kk * T + sl.start:kk * T + sl.stop]))(kk, sl),
                reads=[B_("scr_er")], writes=[B_("erc")])
            k.p(lambda e: e.matmul(ps8[:, :], ones1f[0:1, 0:NEXP], erc[:, :],
                                   start=True, stop=True),
                reads=[B_("ones1f"), B_("erc")], writes=[B_("p4ps8")])
            k.v(lambda e: e.tensor_scalar(ohc[:, :], ps8[:, :], eiota[:, 0:1],
                                          None, ALU.is_equal),
                reads=[B_("p4ps8"), B_("eiota_s")], writes=[B_("ohc")])
            em.dma("qact", (lambda kk, sl: lambda e: e.dma_start(
                out=scr_oh[kk, :, sl], in_=ohc[:, :]))(kk, sl),
                reads=[B_("ohc")], writes=[B_("scr_oh")])
            k.v((lambda kk: lambda e: e.tensor_tensor_scan(
                cumc[:, :], one8c[:, :], ohc[:, :], carry[:, kk:kk + 1],
                ALU.mult, ALU.add))(kk),
                reads=[B_("one8c"), B_("ohc"), B_("p4carry")], writes=[B_("cumc")])
            k.v((lambda kk: lambda e: e.tensor_copy(carry[:, kk:kk + 1],
                                                    cumc[:, 511:512]))(kk),
                reads=[B_("cumc")], writes=[B_("p4carry")])
            em.dma("qact", (lambda kk, sl: lambda e: e.dma_start(
                out=scr_cum[kk, :, sl], in_=cumc[:, :]))(kk, sl),
                reads=[B_("cumc")], writes=[B_("scr_cum")])
    for kk in range(2):
        k.p((lambda kk: lambda e: e.transpose(
            ps1x[0:1, 0:NEXP], carry[:, kk:kk + 1], identf[0:NEXP, 0:NEXP]))(kk),
            reads=[B_("p4carry"), B_("identf")], writes=[B_("p4ps1x")])
        k.v((lambda kk: lambda e: e.tensor_copy(
            ag4row[0:1, kk * NEXP:(kk + 1) * NEXP], ps1x[0:1, 0:NEXP]))(kk),
            reads=[B_("p4ps1x")], writes=[B_("ag4row")])
    em.dma("qsp", lambda e: e.dma_start(out=ag4_in[:, :], in_=ag4row[:, :]),
           reads=[B_("ag4row")], writes=[B_("ag4_in")])
    k.g(lambda e: e.collective_compute(
        "AllGather", ALU.bypass, replica_groups=[list(range(N_CORES))],
        ins=[ag4_in[:, :]], outs=[ag4_out[:, :]]),
        reads=[B_("ag4_in")], writes=[B_("ag4")])

    em.dma("qpool", lambda e: e.indirect_dma_start(
        cg2[:, :], None, ag4_out[:, :],
        bass.IndirectOffsetOnAxis(cnt_idx[:, 0:1], 0)),
        reads=[B_("cnt_idx_s"), B_("ag4")], writes=[B_("cg2")])
    em.dma("qsp", lambda e: e.dma_start(
        out=scr_cg[0:1, :].rearrange("o (r c) -> (o r) c", c=16), in_=cg2[:, :]),
        reads=[B_("cg2")], writes=[B_("scr_cg")])
    em.dma("qsp", lambda e: e.dma_start(out=crow[:, :], in_=scr_cg[0:1, :]),
        reads=[B_("scr_cg")], writes=[B_("crow")])
    for s in range(4):
        k.p((lambda s: lambda e: e.transpose(
            ps_cc[0:NEXP, 0:1], crow[0:1, s * 8:(s + 1) * 8],
            identf[0:1, 0:1]))(s),
            reads=[B_("crow"), B_("identf")], writes=[B_("p4ps_cc")])
        k.v((lambda s: lambda e: e.tensor_copy(ccol[:, s:s + 1],
                                               ps_cc[0:NEXP, 0:1]))(s),
            reads=[B_("p4ps_cc")], writes=[B_("ccol")])
    k.v(lambda e: e.tensor_scalar(occ[:, 0:1], ccol[:, 0:1], hsel8[:, 0:1], 1.0,
                                  ALU.mult, ALU.subtract),
        reads=[B_("ccol"), B_("hsel8_s")], writes=[B_("occ")])
    k.v(lambda e: e.tensor_scalar(occ[:, 2:3], ccol[:, 1:2], hsel8[:, 0:1], None,
                                  ALU.mult),
        reads=[B_("ccol"), B_("hsel8_s")], writes=[B_("occ")])
    k.v(lambda e: e.tensor_tensor(occ[:, 2:3], occ[:, 2:3], ccol[:, 0:1], ALU.add),
        reads=[B_("occ"), B_("ccol")], writes=[B_("occ")])
    k.v(lambda e: e.tensor_tensor(occ[:, 2:3], occ[:, 2:3], ccol[:, 2:3], ALU.add),
        reads=[B_("occ"), B_("ccol")], writes=[B_("occ")])
    k.v(lambda e: e.tensor_scalar(occ[:, 1:2], occ[:, 2:3], 1.0, None,
                                  ALU.subtract),
        reads=[B_("occ")], writes=[B_("occ")])

    sr_ = rowsb[0:1, 0:512]
    fr_ = rowsb[0:1, 512:1024]
    ebr_ = rowsb[0:1, 1024:1536]
    nf_ = rowsb[0:1, 1536:2048]
    t1_ = rowsb[0:1, 2048:2560]
    t2_ = rowsb[0:1, 2560:3072]
    for kk in range(2):
        for c8 in range(8):
            sl = slice(c8 * 512, (c8 + 1) * 512)
            em.dma("qsp", (lambda kk, sl: lambda e: e.dma_start(
                out=ohc[:, :], in_=scr_oh[kk, :, sl]))(kk, sl),
                reads=[B_("scr_oh")], writes=[B_("ohc")])
            em.dma("qsp", (lambda kk, sl: lambda e: e.dma_start(
                out=cumc[:, :], in_=scr_cum[kk, :, sl]))(kk, sl),
                reads=[B_("scr_cum")], writes=[B_("cumc")])
            em.dma("qact", (lambda sl: lambda e: e.dma_start(
                out=tokc[:, :], in_=tok8_h[:, sl]))(sl),
                writes=[B_("tokc")])
            em.dma("qact", (lambda kk, sl: lambda e: e.dma_start(
                out=trc[:, :],
                in_=trash_h[0:1, kk * T + sl.start:kk * T + sl.stop]))(kk, sl),
                writes=[B_("trc")])
            k.v((lambda kk: lambda e: e.tensor_scalar(
                posc[:, :], cumc[:, :], occ[:, kk:kk + 1], None, ALU.add))(kk),
                reads=[B_("cumc"), B_("occ")], writes=[B_("posc")])
            k.v(lambda e: e.tensor_scalar(kpc[:, :], posc[:, :], CAP - 0.5, None,
                                          ALU.is_le),
                reads=[B_("posc")], writes=[B_("kpc")])
            k.v(lambda e: e.tensor_tensor(ohkc[:, :], ohc[:, :], kpc[:, :],
                                          ALU.mult),
                reads=[B_("ohc"), B_("kpc")], writes=[B_("ohkc")])
            k.v(lambda e: e.tensor_tensor_scan(
                krkc[:, :], one8c[:, :], ohkc[:, :], carryK[:, 0:1],
                ALU.mult, ALU.add),
                reads=[B_("one8c"), B_("ohkc"), B_("p4carryK")], writes=[B_("krkc")])
            k.v(lambda e: e.tensor_copy(carryK[:, 0:1], krkc[:, 511:512]),
                reads=[B_("krkc")], writes=[B_("p4carryK")])
            k.v(lambda e: e.tensor_scalar(kpc[:, :], krkc[:, :], ESL - 0.5, None,
                                          ALU.is_le),
                reads=[B_("krkc")], writes=[B_("kpc")])
            k.v(lambda e: e.tensor_tensor(ohk2[:, :], ohkc[:, :], kpc[:, :],
                                          ALU.mult),
                reads=[B_("ohkc"), B_("kpc")], writes=[B_("ohk2")])
            k.v(lambda e: e.tensor_scalar(kpc[:, :], posc[:, :], CAP - 1.0, None,
                                          ALU.is_equal),
                reads=[B_("posc")], writes=[B_("kpc")])
            k.v(lambda e: e.tensor_tensor(kpc[:, :], ohc[:, :], kpc[:, :],
                                          ALU.mult),
                reads=[B_("ohc"), B_("kpc")], writes=[B_("kpc")])
            k.v(lambda e: e.tensor_tensor(m8[:, :], kpc[:, :], tokc[:, :],
                                          ALU.mult),
                reads=[B_("kpc"), B_("tokc")], writes=[B_("m8")])
            k.v(lambda e: e.tensor_reduce(ctc[:, 2:3], m8[:, :], AXX, ALU.add),
                reads=[B_("m8")], writes=[B_("ctc")])
            k.v(lambda e: e.tensor_reduce(ctc[:, 3:4], kpc[:, :], AXX, ALU.add),
                reads=[B_("kpc")], writes=[B_("ctc")])
            k.v(lambda e: e.tensor_tensor(ctc[:, 0:1], ctc[:, 0:1], ctc[:, 2:3],
                                          ALU.add),
                reads=[B_("ctc")], writes=[B_("ctc")])
            k.v(lambda e: e.tensor_tensor(ctc[:, 1:2], ctc[:, 1:2], ctc[:, 3:4],
                                          ALU.add),
                reads=[B_("ctc")], writes=[B_("ctc")])
            k.v(lambda e: e.tensor_scalar(m8[:, :], krkc[:, :], ebase[:, 0:1],
                                          None, ALU.add),
                reads=[B_("krkc"), B_("ebase_s")], writes=[B_("m8")])
            k.v(lambda e: e.tensor_tensor(m8[:, :], ohk2[:, :], m8[:, :],
                                          ALU.mult),
                reads=[B_("ohk2"), B_("m8")], writes=[B_("m8")])
            k.p(lambda e: e.matmul(ps1a[:, :], onescol[0:NEXP, 0:1], m8[:, :],
                                   start=True, stop=True),
                reads=[B_("onescol"), B_("m8")], writes=[B_("p4ps1a")])
            k.v(lambda e: e.tensor_copy(sr_, ps1a[:, :]),
                reads=[B_("p4ps1a")], writes=[B_("p4rows")])
            k.p(lambda e: e.matmul(ps1a[:, :], onescol[0:NEXP, 0:1], ohk2[:, :],
                                   start=True, stop=True),
                reads=[B_("onescol"), B_("ohk2")], writes=[B_("p4ps1a")])
            k.v(lambda e: e.tensor_copy(fr_, ps1a[:, :]),
                reads=[B_("p4ps1a")], writes=[B_("p4rows")])
            k.v(lambda e: e.tensor_scalar(m8[:, :], ohc[:, :], ebase[:, 0:1],
                                          None, ALU.mult),
                reads=[B_("ohc"), B_("ebase_s")], writes=[B_("m8")])
            k.p(lambda e: e.matmul(ps1a[:, :], onescol[0:NEXP, 0:1], m8[:, :],
                                   start=True, stop=True),
                reads=[B_("onescol"), B_("m8")], writes=[B_("p4ps1a")])
            k.v(lambda e: e.tensor_copy(ebr_, ps1a[:, :]),
                reads=[B_("p4ps1a")], writes=[B_("p4rows")])
            k.v(lambda e: e.tensor_scalar(nf_, fr_, -1.0, 1.0, ALU.mult, ALU.add),
                reads=[B_("p4rows")], writes=[B_("p4rows")])
            k.v(lambda e: e.tensor_tensor(t1_, nf_, trc[:, :], ALU.mult),
                reads=[B_("p4rows"), B_("trc")], writes=[B_("p4rows")])
            k.v(lambda e: e.tensor_tensor(t1_, sr_, t1_, ALU.add),
                reads=[B_("p4rows")], writes=[B_("p4rows")])
            em.dma("qsp", (lambda kk, sl: lambda e: e.dma_start(
                out=scr_sg[0:1, kk * T + sl.start:kk * T + sl.stop], in_=t1_))(kk, sl),
                reads=[B_("p4rows")], writes=[B_("scr_sg")])
            k.v(lambda e: e.tensor_tensor(t2_, nf_, ebr_, ALU.mult),
                reads=[B_("p4rows")], writes=[B_("p4rows")])
            k.v(lambda e: e.tensor_tensor(t2_, sr_, t2_, ALU.add),
                reads=[B_("p4rows")], writes=[B_("p4rows")])
            em.dma("qsp", (lambda kk, sl: lambda e: e.dma_start(
                out=scr_sg[0:1, 2 * T + kk * T + sl.start:2 * T + kk * T + sl.stop],
                in_=t2_))(kk, sl),
                reads=[B_("p4rows")], writes=[B_("scr_sg")])

    cti = k.sb("cti", [NEXP, 1], I32)
    cand_idx = k.sb("cand_idx_s", [NEXP, 1], I32)
    ebase_i = k.sb("ebase_i_s", [NEXP, 1], I32)
    cand_b = k.sb("cand_b", [NEXP, D], BF16)
    cand_f = k.sb("cand_f", [NEXP, D])
    corr_f = k.sb("corr_f", [NEXP, D])
    corr_b = k.sb("corr_b", [NEXP, D], BF16)
    z8 = k.sb("z8", [NEXP, D])
    em.dma("qsp", lambda e: e.dma_start(out=cand_idx[:, :], in_=cand_idx_h[:, :]),
           writes=[B_("cand_idx_s")])
    em.dma("qsp", lambda e: e.dma_start(out=ebase_i[:, :], in_=ebase_i_h[:, :]),
           writes=[B_("ebase_i_s")])
    k.v(lambda e: e.tensor_copy(cti[:, :], ctc[:, 0:1]),
        reads=[B_("ctc")], writes=[B_("cti")])
    em.dma("qpool", lambda e: e.indirect_dma_start(
        cand_b[:, :], None, xt3_hbm[:, :],
        bass.IndirectOffsetOnAxis(cti[:, 0:1], 0)),
        reads=[B_("cti"), B_("xt3_hbm")], writes=[B_("cand_b")])
    k.v(lambda e: e.tensor_scalar(cand_f[:, :], cand_b[:, :], ctc[:, 1:2], None,
                                  ALU.mult),
        reads=[B_("cand_b"), B_("ctc")], writes=[B_("cand_f")])
    k.v(lambda e: e.memset(z8[:, :], 0.0), writes=[B_("z8")])
    for i in range(4):
        em.dma("qsp", (lambda i: lambda e: e.dma_start(
            out=ar5_in[i * NEXP:(i + 1) * NEXP, :], in_=z8[:, :]))(i),
            reads=[B_("z8")], writes=[B_("ar5_in")])
    em.dma("qpool", lambda e: e.indirect_dma_start(
        ar5_in[:, :], bass.IndirectOffsetOnAxis(cand_idx[:, 0:1], 0),
        cand_f[:, :], None),
        reads=[B_("cand_f"), B_("cand_idx_s")], writes=[B_("ar5_in")])
    k.g(lambda e: e.collective_compute(
        "AllReduce", ALU.add, replica_groups=[list(range(N_CORES))],
        ins=[ar5_in[:, :]], outs=[ar5_out[:, :]]),
        reads=[B_("ar5_in")], writes=[B_("ar5")])
    em.dma("qpool", lambda e: e.indirect_dma_start(
        corr_f[:, :], None, ar5_out[:, :],
        bass.IndirectOffsetOnAxis(cand_idx[:, 0:1], 0)),
        reads=[B_("cand_idx_s"), B_("ar5")], writes=[B_("corr_f")])
    k.v(lambda e: e.tensor_copy(corr_b[:, :], corr_f[:, :]),
        reads=[B_("corr_f")], writes=[B_("corr_b")])
    em.dma("qpool", lambda e: e.indirect_dma_start(
        einp_hbm[:, :], bass.IndirectOffsetOnAxis(ebase_i[:, 0:1], 0),
        corr_b[:, :], None),
        reads=[B_("corr_b"), B_("ebase_i_s")] + ZB, writes=[B_("einp_sc")])

    stf = k.sb("stf", [128, 64])
    sti = k.sb("sti", [128, 64], I32)
    xts = [k.sb(f"p4xts{i}", [128, D], BF16) for i in range(2)]
    for kk in range(2):
        em.dma("qsp", (lambda kk: lambda e: e.dma_start(
            out=stf[:, kk * NT:(kk + 1) * NT],
            in_=scr_sg[0:1, kk * T:(kk + 1) * T].rearrange(
                "o (w p) -> (o p) w", p=128),
            allow_slow_non_contiguous=True))(kk),
            reads=[B_("scr_sg")], writes=[B_("stf")])
    k.v(lambda e: e.tensor_copy(sti[:, :], stf[:, :]),
        reads=[B_("stf")], writes=[B_("sti")])
    nsc = 0
    for tile in range(NT):
        xb, xbb = xts[tile % 2], B_(f"p4xts{tile % 2}")
        em.dma("qact", (lambda tile, xb: lambda e: e.dma_start(
            out=xb[:, :], in_=xt3_hbm[tile * 128:(tile + 1) * 128, :]))(tile, xb),
            reads=[B_("xt3_hbm")], writes=[xbb])
        for kk in range(2):
            em.dma("qpool", (lambda kk, tile, xb: lambda e: e.indirect_dma_start(
                einp_hbm[:, :],
                bass.IndirectOffsetOnAxis(sti[:, kk * NT + tile:kk * NT + tile + 1], 0),
                xb[:, :], None))(kk, tile, xb),
                reads=[B_("sti"), xbb] + ZB, writes=[B_("einp_sc")])
            nsc += 1
    k.pop()

    k.push()
    EIB = ZB + [B_("einp_sc")]
    einp_fm = k.sb("einp_fm", [128, KD * ESL], BF16)
    wbig = k.sb("wbig", [128, 32768], BF16)
    bfcr = k.sb("bfcr", [1, INNER], BF16)
    bgtr = k.sb("bgtr", [1, INNER], BF16)
    bcpr = k.sb("bcpr", [1, D], BF16)
    onesrb = k.sb("onesrb", [1, 512], BF16)
    prodf = [k.sb(f"prodf{i}", [128, 512]) for i in range(2)]
    hsb = [k.sb(f"p4hsb{i}", [128, 512]) for i in range(2)]
    actev = [k.sb(f"actev{i}", [128, ESL], BF16) for i in range(2)]
    at4 = [k.sb(f"at4_{i}", [128, 32 * 128], BF16) for i in range(2)]
    soutev = [k.sb(f"soutev{i}", [128, D]) for i in range(2)]
    hp = [k.ps(f"p4hp{i}", [128, 512]) for i in range(2)]
    gp = [k.ps(f"p4gp{i}", [128, 512]) for i in range(2)]
    psA = [k.ps(f"p4psA{i}", [128, 512]) for i in range(2)]
    psB = [k.ps(f"p4psB{i}", [128, 512]) for i in range(2)]
    k.v(lambda e: e.memset(onesrb[:, :], 1.0), writes=[B_("onesrb")])
    SPW = (512, 512, 512, 512, 416)
    ACB = [B_(f"acth{i}") for i in range(8)]

    EFB = [B_(f"einp_fm{kc}") for kc in range(KD)]

    def wbuf(j):
        return B_(f"wb{j}") if j < 8 else B_(f"wbg{j - 8}")

    nso = 0
    for ex in range(NEXP):
        for kc in range(KD):
            em.dma("qsp", (lambda ex, kc: lambda e: e.dma_start_transpose(
                einp_fm[:, kc * ESL:(kc + 1) * ESL],
                einp_hbm[ex * ESL:(ex + 1) * ESL, kc * 128:(kc + 1) * 128]))(ex, kc),
                reads=EIB, writes=[B_(f"einp_fm{kc}")])
        em.dma("qpool", (lambda ex: lambda e: e.dma_start(
            out=bfcr[:, :], in_=bfc_h[ex:ex + 1, :]))(ex), writes=[B_("bfcr")])
        em.dma("qpool", (lambda ex: lambda e: e.dma_start(
            out=bgtr[:, :], in_=bgt_h[ex:ex + 1, :]))(ex), writes=[B_("bgtr")])
        for ih in range(2):
            for kc in range(KD):
                em.dma(("qsp", "qact")[kc % 2], (lambda ex, ih, kc: lambda e: e.dma_start(
                    out=wbig[:, kc * 2048:(kc + 1) * 2048],
                    in_=wfcb_hbm[ex * D + kc * 128:ex * D + (kc + 1) * 128,
                                 ih * 2048:(ih + 1) * 2048]))(ex, ih, kc),
                    reads=[B_("wfcb")], writes=[wbuf(kc)])
                em.dma(("qact", "qsp")[kc % 2], (lambda ex, ih, kc: lambda e: e.dma_start(
                    out=wbig[:, 16384 + kc * 2048:16384 + (kc + 1) * 2048],
                    in_=wgtb_hbm[ex * D + kc * 128:ex * D + (kc + 1) * 128,
                                 ih * 2048:(ih + 1) * 2048]))(ex, ih, kc),
                    reads=[B_("wgtb")], writes=[wbuf(8 + kc)])
            for ml in range(16):
                mc = ih * 16 + ml
                hpc, hpb = hp[ml % 2], B_(f"p4hp{ml % 2}")
                gpc, gpb = gp[ml % 2], B_(f"p4gp{ml % 2}")
                aev, aevb = actev[ml % 2], B_(f"actev{ml % 2}")
                prf, prb = prodf[ml % 2], B_(f"prodf{ml % 2}")
                WRB = [wbuf(j) for j in range(8)]
                WGB = [wbuf(8 + j) for j in range(8)]
                for spn in range(len(SPW)):
                    w = SPW[spn]
                    sb0 = spn * 512
                    k.p((lambda ml, mc, w, sb0, hpc: lambda e: ([e.matmul(
                        hpc[:, 0:w],
                        wbig[:, kc * 2048 + ml * 128:kc * 2048 + ml * 128 + 128],
                        einp_fm[:, kc * ESL + sb0:kc * ESL + sb0 + w],
                        start=(kc == 0), stop=False) for kc in range(KD)]
                        and e.matmul(hpc[:, 0:w], bfcr[0:1, mc * 128:mc * 128 + 128],
                                     onesrb[0:1, 0:w], start=False, stop=True)))(
                        ml, mc, w, sb0, hpc),
                        reads=WRB + EFB + [B_("bfcr"), B_("onesrb")],
                        writes=[hpb])
                    k.p((lambda ml, mc, w, sb0, gpc: lambda e: ([e.matmul(
                        gpc[:, 0:w],
                        wbig[:, 16384 + kc * 2048 + ml * 128:16384 + kc * 2048 + ml * 128 + 128],
                        einp_fm[:, kc * ESL + sb0:kc * ESL + sb0 + w],
                        start=(kc == 0), stop=False) for kc in range(KD)]
                        and e.matmul(gpc[:, 0:w], bgtr[0:1, mc * 128:mc * 128 + 128],
                                     onesrb[0:1, 0:w], start=False, stop=True)))(
                        ml, mc, w, sb0, gpc),
                        reads=WGB + EFB + [B_("bgtr"), B_("onesrb")],
                        writes=[gpb])
                    hsc, hscb = hsb[ml % 2], B_(f"p4hsb{ml % 2}")
                    k.a((lambda w, hpc, hsc: lambda e: e.activation(
                        out=hsc[:, 0:w], in_=hpc[:, 0:w], func=AF.Copy))(
                        w, hpc, hsc),
                        reads=[hpb], writes=[hscb])
                    k.v((lambda w, hsc, gpc, prf: lambda e: e.tensor_tensor(
                        prf[:, 0:w], hsc[:, 0:w], gpc[:, 0:w], ALU.mult))(
                        w, hsc, gpc, prf),
                        reads=[hscb, gpb], writes=[prb])
                    k.a((lambda w, sb0, prf, aev: lambda e: e.activation(
                        out=aev[:, sb0:sb0 + w], in_=prf[:, 0:w], func=AF.Silu))(
                        w, sb0, prf, aev),
                        reads=[prb], writes=[aevb])
                em.dma("qsp", (lambda mc, aev: lambda e: e.dma_start(
                    out=acth_hbm[:, mc * ESL:(mc + 1) * ESL], in_=aev[:, :]))(mc, aev),
                    reads=[aevb], writes=[B_(f"acth{mc % 8}")])
        for j in range(32):
            em.dma(("qsp", "qact")[j % 2], (lambda ex, j: lambda e: e.dma_start(
                out=wbig[:, j * 1024:(j + 1) * 1024],
                in_=wcpb_hbm[ex * INNER + j * 128:ex * INNER + (j + 1) * 128, :]))(ex, j),
                reads=[B_("wcpb")],
                writes=[wbuf(j // 2) if j < 16 else wbuf(8 + (j - 16) // 2)])
        em.dma("qpool", (lambda ex: lambda e: e.dma_start(
            out=bcpr[:, :], in_=bcp_h[ex:ex + 1, :]))(ex), writes=[B_("bcpr")])
        WAB = [wbuf(j) for j in range(16)]
        NSC = (ESL + 127) // 128
        for sc in range(NSC):
            M = min(128, ESL - sc * 128)
            pA, pAb = psA[sc % 2], B_(f"p4psA{sc % 2}")
            pB, pBb = psB[sc % 2], B_(f"p4psB{sc % 2}")
            sev, sevb = soutev[sc % 2], B_(f"soutev{sc % 2}")
            a_, ab_ = at4[sc % 2], B_(f"at4_{sc % 2}")
            em.dma("qact", (lambda sc, M, a_: lambda e: e.dma_start(
                out=a_[:, 0:32 * M].rearrange("p (j m) -> p j m", j=32),
                in_=acth_hbm[:, :].rearrange("p (j s) -> p j s", j=32)[
                    :, :, sc * 128:sc * 128 + M]))(sc, M, a_),
                reads=ACB, writes=[ab_])
            for j in range(32):
                k.p((lambda j, M, a_, pA: lambda e: e.matmul(
                    pA[0:M, :], a_[:, j * M:(j + 1) * M],
                    wbig[:, j * 1024:j * 1024 + 512],
                    start=(j == 0), stop=False))(j, M, a_, pA),
                    reads=[ab_] + WAB, writes=[pAb])
                k.p((lambda j, M, a_, pB: lambda e: e.matmul(
                    pB[0:M, :], a_[:, j * M:(j + 1) * M],
                    wbig[:, j * 1024 + 512:j * 1024 + 1024],
                    start=(j == 0), stop=False))(j, M, a_, pB),
                    reads=[ab_] + WAB, writes=[pBb])
            k.p((lambda M, pA: lambda e: e.matmul(
                pA[0:M, :], onesrb[0:1, 0:M], bcpr[0:1, 0:512],
                start=False, stop=True))(M, pA),
                reads=[B_("onesrb"), B_("bcpr")], writes=[pAb])
            k.p((lambda M, pB: lambda e: e.matmul(
                pB[0:M, :], onesrb[0:1, 0:M], bcpr[0:1, 512:1024],
                start=False, stop=True))(M, pB),
                reads=[B_("onesrb"), B_("bcpr")], writes=[pBb])
            k.v((lambda M, pA, sev: lambda e: e.tensor_copy(
                sev[0:M, 0:512], pA[0:M, :]))(M, pA, sev),
                reads=[pAb], writes=[sevb])
            k.v((lambda M, pB, sev: lambda e: e.tensor_copy(
                sev[0:M, 512:1024], pB[0:M, :]))(M, pB, sev),
                reads=[pBb], writes=[sevb])
            em.dma("qsp", (lambda ex, sc, M, sev: lambda e: e.dma_start(
                out=sout_hbm[ex * ESL + sc * 128:ex * ESL + sc * 128 + M, :],
                in_=sev[0:M, :]))(ex, sc, M, sev),
                reads=[sevb], writes=[B_(f"sout_w{nso % 12}")])
            nso += 1
    k.pop()

    k.push()
    SOB = [B_(f"sout_w{i}") for i in range(12)]
    gbf = k.sb("gbf", [128, 64])
    gbi = k.sb("gbi", [128, 64], I32)
    xc4 = [k.sb(f"p4xc{i}", [128, KD * 128]) for i in range(4)]
    x3t = [k.sb(f"p4x3t{i}", [128, D]) for i in range(4)]
    g0sb = [k.sb(f"g0sb{i}", [128, D]) for i in range(4)]
    g1sb = [k.sb(f"g1sb{i}", [128, D]) for i in range(4)]
    out4 = [k.sb(f"out4_{i}", [128, D]) for i in range(4)]
    pst = [k.ps(f"p4pst{i}", [128, 128]) for i in range(2)]
    for kk in range(2):
        em.dma("qsp", (lambda kk: lambda e: e.dma_start(
            out=gbf[:, kk * NT:(kk + 1) * NT],
            in_=scr_sg[0:1, 2 * T + kk * T:2 * T + (kk + 1) * T].rearrange(
                "o (w p) -> (o p) w", p=128),
            allow_slow_non_contiguous=True))(kk),
            reads=[B_("scr_sg")], writes=[B_("gbf")])
    k.v(lambda e: e.tensor_copy(gbi[:, :], gbf[:, :]),
        reads=[B_("gbf")], writes=[B_("gbi")])
    for tile in range(NT):
        i2 = tile % 4
        xc, xcb = xc4[i2], B_(f"p4xc{i2}")
        xt, xtb = x3t[i2], B_(f"p4x3t{i2}")
        g0, g0b = g0sb[i2], B_(f"g0sb{i2}")
        g1, g1b = g1sb[i2], B_(f"g1sb{i2}")
        ot, otb = out4[i2], B_(f"out4_{i2}")
        em.dma("qsp", (lambda tile, xc: lambda e: e.dma_start(
            out=xc[:, :].rearrange("p (kc t) -> p kc t", kc=KD),
            in_=x3_hbm[:, tile * 128:(tile + 1) * 128].rearrange(
                "(kc p) t -> p kc t", p=128)))(tile, xc),
            reads=[B_("x3_hbm")], writes=[xcb])
        for kc in range(KD):
            pt, ptb = pst[kc % 2], B_(f"p4pst{kc % 2}")
            k.p((lambda kc, xc, pt: lambda e: e.transpose(
                pt[:, :], xc[:, kc * 128:(kc + 1) * 128], identf[:, :]))(kc, xc, pt),
                reads=[xcb, B_("identf")], writes=[ptb])
            k.v((lambda kc, xt, pt, ptb: lambda e: e.tensor_copy(
                xt[:, kc * 128:(kc + 1) * 128], pt[:, :]))(kc, xt, pt, ptb),
                reads=[ptb], writes=[xtb])
        em.dma("qpool", (lambda tile, g0: lambda e: e.indirect_dma_start(
            g0[:, :], None, sout_hbm[:, :],
            bass.IndirectOffsetOnAxis(gbi[:, tile:tile + 1], 0)))(tile, g0),
            reads=[B_("gbi")] + SOB, writes=[g0b])
        em.dma("qpool", (lambda tile, g1: lambda e: e.indirect_dma_start(
            g1[:, :], None, sout_hbm[:, :],
            bass.IndirectOffsetOnAxis(gbi[:, NT + tile:NT + tile + 1], 0)))(tile, g1),
            reads=[B_("gbi")] + SOB, writes=[g1b])
        k.v((lambda tile, g0, xt, ot: lambda e: e.scalar_tensor_tensor(
            ot[:, :], g0[:, :], rt_tm[:, tile * 8 + 2:tile * 8 + 3], xt[:, :],
            ALU.mult, ALU.add))(tile, g0, xt, ot),
            reads=[g0b, xtb, B_("rt_tm")], writes=[otb])
        k.v((lambda tile, g1, ot: lambda e: e.scalar_tensor_tensor(
            ot[:, :], g1[:, :], rt_tm[:, tile * 8 + 3:tile * 8 + 4], ot[:, :],
            ALU.mult, ALU.add))(tile, g1, ot),
            reads=[g1b, B_("rt_tm")], writes=[otb])
        em.dma("qsp", (lambda tile, ot: lambda e: e.dma_start(
            out=out_h[tile * 128:(tile + 1) * 128, :], in_=ot[:, :]))(tile, ot),
            reads=[otb], writes=[B_(f"outw{tile % 4}")])
    k.pop(barrier=False)


def _stage_inputs(full):
    f32 = np.float32
    x = np.asarray(full["L0_attn_x"], f32)
    import ml_dtypes
    identb = np.eye(128).astype(ml_dtypes.bfloat16)
    ones1f = np.ones((1, 128), f32)
    onescol = np.ones((128, 1), f32)
    sortw = np.asarray(full["L0_attn_sort_w"], f32)[0]
    in_maps = []
    for c in range(N_CORES):
        b, half = c // 2, c % 2
        im = {}
        im["x_fm"] = np.ascontiguousarray(x[b, half * T:(half + 1) * T, :].T)
        im["l0_wq"] = np.asarray(full["L0_attn_w_q"], f32)
        im["l0_wkv"] = np.asarray(full["L0_attn_w_kv"], f32)
        im["l0_wout"] = np.asarray(full["L0_attn_w_out"], f32)
        im["l0_bout"] = np.ascontiguousarray(
            np.asarray(full["L0_attn_b_out"], f32).reshape(KD, 128).T)
        im["l0_sortwq"] = np.ascontiguousarray(sortw[:, 0:64, :])
        im["ident_f32"] = np.eye(128, dtype=f32)
        im["l0_wg_t"] = np.ascontiguousarray(np.asarray(full["L0_ffn_w_gate"], f32).T)
        im["l0_wu_t"] = np.ascontiguousarray(np.asarray(full["L0_ffn_w_up"], f32).T)
        im["l0_wd_t"] = np.ascontiguousarray(np.asarray(full["L0_ffn_w_down"], f32).T)
        im["l0_sortwk"] = np.ascontiguousarray(sortw[:, 64:128, :])
        oh = np.zeros((64, 32), f32)
        for u in range(32):
            oh[half * 32 + u, u] = 1.0
        im["ownhot"] = oh
        neg = np.float32(-np.finfo(np.float32).max)
        mf = np.zeros((128, 384), f32)
        ml_ = np.zeros((128, 384), f32)
        if half == 0:
            mf[:, 0:128] = neg
        if half == 1:
            ml_[:, 256:384] = neg
        im["mask_first"] = mf
        im["mask_last"] = ml_
        im["ident_bf16"] = identb
        im["ones1_f32"] = ones1f
        im["onescol_f32"] = onescol
        im["kbm_idx"] = ((64 * b + np.arange(64))[:, None] * 4
                         + np.arange(4)[None, :]).astype(np.int32)
        lki = np.zeros((128, 34), np.int32)
        for w in range(34):
            gb = half * 32 + (w - 1)
            if gb < 0:
                gb = 0
            if gb > 63:
                gb = 63
            rank = 2 * b + gb // 32
            lki[:, w] = rank * T + (gb % 32) * 128 + np.arange(128)
        im["lkb_idx"] = lki
        ski = np.zeros((128, 32), np.int32)
        for u in range(32):
            ski[:, u] = (2 * b + half) * T + u * 128 + np.arange(128)
        im["sktile_idx"] = ski
        dh = half
        im["l1_wx_t"] = np.ascontiguousarray(np.asarray(full["L1_attn_Wx"], f32).T)
        im["l1_wz_t"] = np.ascontiguousarray(np.asarray(full["L1_attn_Wz"], f32).T)
        im["l1_wp_t"] = np.ascontiguousarray(np.asarray(full["L1_attn_Wp"], f32).T)
        wdt_t = np.asarray(full["L1_attn_Wdt"], f32).T
        im["l1_wdt_sl"] = np.ascontiguousarray(wdt_t[:, dh * 512:(dh + 1) * 512])
        im["l1_wo_t"] = np.ascontiguousarray(np.asarray(full["L1_attn_Wo"], f32).T)
        alog = np.asarray(full["L1_attn_A_log"], f32)[dh * 512:(dh + 1) * 512, :]
        im["l1_alog_sl"] = np.ascontiguousarray(
            alog.reshape(4, 128, NST).transpose(1, 0, 2).reshape(128, 4 * NST))
        bdt = np.asarray(full["L1_attn_bdt"], f32)[dh * 512:(dh + 1) * 512]
        im["l1_bdt_sl"] = np.ascontiguousarray(bdt.reshape(4, 128).T)
        dp = np.asarray(full["L1_attn_Dp"], f32)[dh * 512:(dh + 1) * 512]
        im["l1_dp_sl"] = np.ascontiguousarray(dp.reshape(4, 128).T)
        cw = np.asarray(full["L1_attn_conv_w"], f32)[dh * 512:(dh + 1) * 512, 0, :]
        im["l1_convw_sl"] = np.ascontiguousarray(
            cw.reshape(4, 128, 4).transpose(1, 0, 2).reshape(128, 16))
        p_ = np.arange(128)
        xpg = np.zeros((128, 8), np.int32)
        for dc in range(4):
            for r01 in range(2):
                xpg[:, dc * 2 + r01] = (2 * b + r01) * 1120 + dh * 512 + dc * 128 + p_
        im["xpg_idx"] = xpg
        zi = np.zeros((128, 64), np.int32)
        for dc in range(4):
            for tch in range(16):
                zi[:, dc * 16 + tch] = (((2 * b + tch // 8) * 1024 + dh * 512
                                         + dc * 128 + p_) * 8 + (tch % 8))
        im["z_idx"] = zi
        dti = np.zeros((64, 2), np.int32)
        for r01 in range(2):
            dti[:, r01] = (2 * b + r01) * 1120 + 1024 + np.arange(64)
        im["dtun_idx"] = dti
        bci = np.zeros((32, 2), np.int32)
        for r01 in range(2):
            bci[:, r01] = (2 * b + r01) * 1120 + 1088 + np.arange(32)
        im["bc_idx"] = bci
        yi = np.zeros((128, 64), np.int32)
        for kc in range(KD):
            for tc in range(8):
                yi[:, kc * 8 + tc] = (((2 * b + kc // 4) * 512 + (kc % 4) * 128 + p_) * 16
                                      + half * 8 + tc)
        im["y_idx"] = yi
        import ml_dtypes as _md
        bh = np.zeros((32, 32 * 128), np.float32)
        for n in range(32):
            bh[n, n * 128:(n + 1) * 128] = 1.0
        im["bchot"] = bh.astype(_md.bfloat16)
        im["l1_rw"] = np.asarray(full["L1_ffn_router_w"], f32)
        im["l1_rb_col"] = np.ascontiguousarray(
            np.asarray(full["L1_ffn_router_b"], f32).reshape(NEXP, 1))
        im["l1_wfc"] = np.asarray(full["L1_ffn_w_c_fc"], f32)
        im["l1_wgt"] = np.asarray(full["L1_ffn_w_gate"], f32)
        im["l1_wcp"] = np.asarray(full["L1_ffn_w_c_proj"], f32)
        im["l1_bfc"] = np.asarray(full["L1_ffn_b_c_fc"], f32).reshape(NEXP, INNER)
        im["l1_bgt"] = np.asarray(full["L1_ffn_b_gate"], f32).reshape(NEXP, INNER)
        im["l1_bcp"] = np.asarray(full["L1_ffn_b_c_proj"], f32).reshape(NEXP, D)
        im["eiota"] = np.arange(NEXP, dtype=f32).reshape(NEXP, 1)
        im["ebase"] = (np.arange(NEXP, dtype=f32) * ESL).reshape(NEXP, 1)
        im["ebase_i"] = (np.arange(NEXP) * ESL).astype(np.int32).reshape(NEXP, 1)
        im["hsel8"] = np.full((NEXP, 1), half, f32)
        im["cnt_idx"] = np.array([[2 * b], [2 * b + 1]], np.int32)
        im["cand_idx"] = (b * NEXP + np.arange(NEXP)).astype(np.int32).reshape(NEXP, 1)
        im["tok8"] = np.broadcast_to(np.arange(T, dtype=f32), (NEXP, T)).copy()
        tr = (TRASH + (np.arange(T) // 128) * 2).astype(f32)
        im["trashrow"] = np.concatenate([tr, tr + 1]).reshape(1, 2 * T)
        in_maps.append(im)
    return in_maps


_NC_CACHE = {}


def kernel(**inputs):
    n_phases = int(inputs.pop("_n_phases", 4))
    if n_phases not in _NC_CACHE:
        _NC_CACHE[n_phases] = build_program(n_phases)
    nc = _NC_CACHE[n_phases]
    in_maps = _stage_inputs(inputs)
    res = run_bass_kernel_spmd(nc, in_maps, core_ids=list(range(N_CORES)))
    out = np.zeros((B, TFULL, D), np.float32)
    key = "out_fm" if n_phases < 4 else "out_tm"
    for c in range(N_CORES):
        b, half = c // 2, c % 2
        o = res.results[c][key]
        out[b, half * T:(half + 1) * T, :] = o.T if n_phases < 4 else o
    return out
